# version 01
# baseline (speedup 1.0000x reference)
import contextlib
import numpy as np
import ml_dtypes
import concourse.bass as bass
import concourse.mybir as mybir
from concourse.bass_utils import run_bass_kernel_spmd

F32 = mybir.dt.float32
BF16 = mybir.dt.bfloat16
AF = mybir.ActivationFunctionType
ALU = mybir.AluOpType
AX = mybir.AxisListType.X
NCORE = 8
T = 2048
D = 2048
BFD = ml_dtypes.bfloat16

_prog_cache = {}


def _cached(key, fn):
    if key not in _prog_cache:
        _prog_cache[key] = fn()
    return _prog_cache[key]


def _run(nc, in_maps):
    res = run_bass_kernel_spmd(nc, in_maps, core_ids=list(range(NCORE)))
    return res.results


def build_ln(with_sum):
    nc = bass.Bass("TRN2", target_bir_lowering=False, detect_race_conditions=False)
    x = nc.dram_tensor("x", [T, D], BF16 if not with_sum else F32,
                       kind="ExternalInput").ap()
    xn = nc.dram_tensor("xn", [T, D], BF16, kind="ExternalOutput").ap()
    xs = nc.dram_tensor("xsum", [1, D], F32, kind="ExternalOutput").ap()
    ones_d = nc.inline_tensor(np.ones((128, 1), BFD), "ones_d").ap()
    eps_d = nc.inline_tensor(np.full((128, 1), 1e-5, np.float32), "eps_d").ap()
    NT = T // 128
    with contextlib.ExitStack() as st:
        E = st.enter_context
        xdt = F32 if with_sum else BF16
        xsb = [E(nc.sbuf_tensor(f"xsb{i}", [128, D], xdt)) for i in range(2)]
        sq = E(nc.sbuf_tensor("sq", [128, D], F32))
        xnb = [E(nc.sbuf_tensor(f"xnb{i}", [128, D], BF16)) for i in range(2)]
        stat = E(nc.sbuf_tensor("stat", [128, 8], F32))
        ones_sb = E(nc.sbuf_tensor("ones_sb", [128, 1], BF16))
        eps_sb = E(nc.sbuf_tensor("eps_sb", [128, 1], F32))
        xs_sb = E(nc.sbuf_tensor("xs_sb", [1, D], F32))
        ps = E(nc.psum_tensor("ps", [1, D], F32))
        olds = E(nc.semaphore("olds"))
        vs = E(nc.semaphore("vs"))
        ld2 = [E(nc.semaphore(f"ld{p}")) for p in range(2)]
        ev = E(nc.semaphore("ev"))
        sdone = E(nc.semaphore("sdone"))
        tmm = E(nc.semaphore("tmm"))
        stv = E(nc.semaphore("stv"))
        blk = E(nc.Block())

        @blk.sync
        def _(sy):
            sy.dma_start(out=ones_sb[:], in_=ones_d).then_inc(olds, 16)
            sy.dma_start(out=eps_sb[:], in_=eps_d).then_inc(olds, 16)
            for i in range(NT):
                if i >= 2:
                    sy.wait_ge(ev, i - 1)
                    sy.dma_start(out=xn[(i - 2) * 128:(i - 1) * 128, :],
                                 in_=xnb[(i - 2) % 2][:]).then_inc(stv, 16)
                sy.dma_start(out=xsb[i % 2][:], in_=x[i * 128:(i + 1) * 128, :]
                             ).then_inc(ld2[i % 2], 16)
            sy.wait_ge(ev, NT - 1)
            sy.dma_start(out=xn[(NT - 2) * 128:(NT - 1) * 128, :],
                         in_=xnb[(NT - 2) % 2][:]).then_inc(stv, 16)
            sy.wait_ge(ev, NT)
            sy.dma_start(out=xn[(NT - 1) * 128:NT * 128, :],
                         in_=xnb[(NT - 1) % 2][:]).then_inc(stv, 16)
            if with_sum:
                sy.wait_ge(ev, NT + 1)
                sy.dma_start(out=xs, in_=xs_sb[:]).then_inc(stv, 16)
            sy.wait_ge(stv, 16 * (NT + (1 if with_sum else 0)))

        @blk.vector
        def _(v):
            nvs = 0
            for i in range(NT):
                v.wait_ge(ld2[i % 2], 16 * (i // 2 + 1))
                if i >= 2:
                    if with_sum:
                        v.wait_ge(tmm, i - 1)
                    v.wait_ge(stv, 16 * (i - 1))
                b = xsb[i % 2]
                o = xnb[i % 2]
                v.reduce_sum(stat[:, 0:1], b[:], axis=AX)
                v.tensor_tensor(out=sq[:], in0=b[:], in1=b[:], op=ALU.mult)
                v.reduce_sum(stat[:, 2:3], sq[:], axis=AX).then_inc(vs, 1)
                nvs += 1
                v.wait_ge(vs, nvs)
                v.tensor_scalar(out=stat[:, 1:2], in0=stat[:, 0:1],
                                scalar1=1.0 / D, scalar2=None, op0=ALU.mult)
                v.tensor_scalar(out=stat[:, 3:4], in0=stat[:, 2:3],
                                scalar1=1.0 / D, scalar2=None,
                                op0=ALU.mult).then_inc(vs, 1)
                nvs += 1
                v.wait_ge(vs, nvs)
                v.tensor_tensor(out=stat[:, 4:5], in0=stat[:, 1:2],
                                in1=stat[:, 1:2], op=ALU.mult).then_inc(vs, 1)
                nvs += 1
                v.wait_ge(vs, nvs)
                v.tensor_tensor(out=stat[:, 5:6], in0=stat[:, 3:4],
                                in1=stat[:, 4:5],
                                op=ALU.subtract).then_inc(sdone, 1)
                v.wait_ge(sdone, 2 * (i + 1))
                v.reciprocal(stat[:, 7:8], stat[:, 6:7]).then_inc(vs, 1)
                nvs += 1
                v.wait_ge(vs, nvs)
                v.tensor_scalar(out=sq[:], in0=b[:], scalar1=stat[:, 1:2],
                                scalar2=None, op0=ALU.subtract)
                v.tensor_scalar(out=o[:], in0=sq[:], scalar1=stat[:, 7:8],
                                scalar2=None, op0=ALU.mult).then_inc(ev, 1)
            if with_sum:
                v.wait_ge(tmm, NT)
                v.tensor_copy(xs_sb[:], ps[:]).then_inc(ev, 1)

        @blk.scalar
        def _(s):
            s.wait_ge(olds, 32)
            for i in range(NT):
                s.wait_ge(sdone, 2 * i + 1)
                s.activation(out=stat[:, 6:7], in_=stat[:, 5:6],
                             func=AF.Sqrt, bias=eps_sb[:, 0:1],
                             scale=1.0).then_inc(sdone, 1)

        if with_sum:
            @blk.tensor
            def _(t):
                t.wait_ge(olds, 16)
                for i in range(NT):
                    t.wait_ge(ev, i + 1)
                    for c in range(D // 512):
                        ins = t.matmul(ps[0:1, c * 512:(c + 1) * 512], ones_sb[:],
                                       xnb[i % 2][:, c * 512:(c + 1) * 512],
                                       start=(i == 0), stop=(i == NT - 1))
                    ins.then_inc(tmm, 1)
    return nc


def run_ln0(x_tok_percore):
    nc = _cached("ln0", lambda: build_ln(True))
    maps = [{"x": xc} for xc in x_tok_percore]
    res = _run(nc, maps)
    return ([r["xn"] for r in res], [r["xsum"] for r in res])


def run_ln1(x1_tok_percore):
    nc = _cached("ln1", lambda: build_ln(False))
    maps = [{"x": xc} for xc in x1_tok_percore]
    res = _run(nc, maps)
    return [r["xn"] for r in res]


def build_gemm(key, k_list, fout, x_dtypes, w_dtypes, res_dtype, out_dtype,
               t_total=T):
    nc = bass.Bass("TRN2", target_bir_lowering=False, detect_race_conditions=False)
    nj = len(k_list)
    ws = [nc.dram_tensor(f"w{j}", [k_list[j], fout], w_dtypes[j],
                         kind="ExternalInput").ap() for j in range(nj)]
    xs = [nc.dram_tensor(f"x{j}", [k_list[j], t_total], x_dtypes[j],
                         kind="ExternalInput").ap() for j in range(nj)]
    res_in = (nc.dram_tensor("res", [fout, t_total], res_dtype,
                             kind="ExternalInput").ap()
              if res_dtype is not None else None)
    out = nc.dram_tensor("out", [fout, t_total], out_dtype,
                         kind="ExternalOutput").ap()
    OG = fout // 128
    TC = t_total // 512
    NKT = [k // 128 for k in k_list]
    nk_tot = sum(NKT)
    with contextlib.ExitStack() as st:
        E = st.enter_context
        x_sb = [E(nc.sbuf_tensor(f"x_sb{j}", [128, NKT[j] * t_total], BF16))
                for j in range(nj)]
        w_sb = [[E(nc.sbuf_tensor(f"w_sb{j}_{p}", [128, NKT[j] * 128], BF16))
                 for j in range(nj)] for p in range(2)]
        r_sb = [E(nc.sbuf_tensor(f"r_sb{p}", [128, 512],
                                 res_dtype if res_dtype else BF16))
                for p in range(2)] if res_dtype is not None else None
        o_sb = [E(nc.sbuf_tensor(f"o_sb{p}", [128, 512], out_dtype))
                for p in range(4)]
        ps = [E(nc.psum_tensor(f"ps{p}", [128, 512], F32)) for p in range(4)]
        xld = E(nc.semaphore("xld"))
        wld2 = [E(nc.semaphore(f"wld{p}")) for p in range(2)]
        rld = E(nc.semaphore("rld"))
        mmd = E(nc.semaphore("mmd"))
        evd = E(nc.semaphore("evd"))
        stv4 = [E(nc.semaphore(f"stv{p}")) for p in range(4)]
        blk = E(nc.Block())

        @blk.gpsimd
        def _(g):
            for j in range(nj):
                for kt in range(NKT[j]):
                    g.dma_start(out=x_sb[j][:, kt * t_total:(kt + 1) * t_total],
                                in_=xs[j][kt * 128:(kt + 1) * 128, :]
                                ).then_inc(xld, 16)
            for og in range(OG):
                if og >= 2:
                    g.wait_ge(mmd, (og - 1) * TC)
                for j in range(nj):
                    g.dma_start(
                        out=w_sb[og % 2][j].ap().rearrange(
                            "p (a f) -> p a f", f=128),
                        in_=ws[j][:, og * 128:(og + 1) * 128].rearrange(
                            "(a p) f -> p a f", p=128)
                    ).then_inc(wld2[og % 2], 16)

        @blk.sync
        def _(sy):
            for og in range(OG):
                for tc in range(TC):
                    gidx = og * TC + tc
                    if res_in is not None:
                        if gidx >= 2:
                            sy.wait_ge(evd, gidx - 1)
                        sy.dma_start(out=r_sb[gidx % 2][:],
                                     in_=res_in[og * 128:(og + 1) * 128,
                                                tc * 512:(tc + 1) * 512]
                                     ).then_inc(rld, 16)
                    sy.wait_ge(evd, gidx + 1)
                    sy.dma_start(out=out[og * 128:(og + 1) * 128,
                                         tc * 512:(tc + 1) * 512],
                                 in_=o_sb[gidx % 4][:]).then_inc(stv4[gidx % 4], 16)
            ngr = OG * TC
            for p in range(4):
                sy.wait_ge(stv4[p], 16 * ((ngr - 1 - p) // 4 + 1))

        @blk.tensor
        def _(t):
            t.wait_ge(xld, 16 * nk_tot)
            for og in range(OG):
                t.wait_ge(wld2[og % 2], 16 * nj * (og // 2 + 1))
                for tc in range(TC):
                    gidx = og * TC + tc
                    if gidx >= 4:
                        t.wait_ge(evd, gidx - 3)
                    p = ps[gidx % 4]
                    n = 0
                    for j in range(nj):
                        for kt in range(NKT[j]):
                            n += 1
                            ins = t.matmul(
                                p[:],
                                w_sb[og % 2][j][:, kt * 128:(kt + 1) * 128],
                                x_sb[j][:, kt * t_total + tc * 512:
                                        kt * t_total + (tc + 1) * 512],
                                start=(n == 1), stop=(n == nk_tot))
                    ins.then_inc(mmd, 1)

        @blk.vector
        def _(v):
            for og in range(OG):
                for tc in range(TC):
                    gidx = og * TC + tc
                    v.wait_ge(mmd, gidx + 1)
                    if gidx >= 4:
                        v.wait_ge(stv4[gidx % 4], 16 * ((gidx - 4) // 4 + 1))
                    if res_in is not None:
                        v.wait_ge(rld, 16 * (gidx + 1))
                        v.tensor_tensor(out=o_sb[gidx % 4][:],
                                        in0=ps[gidx % 4][:],
                                        in1=r_sb[gidx % 2][:],
                                        op=ALU.add).then_inc(evd, 1)
                    else:
                        v.tensor_copy(o_sb[gidx % 4][:],
                                      ps[gidx % 4][:]).then_inc(evd, 1)
    return nc


def run_gemm(key, w_full_list, x_percore_list, res_percore, out_dtype_np):
    k_list = [w.shape[0] for w in w_full_list]
    fout = w_full_list[0].shape[1]
    w_dt = [F32 if w.dtype == np.float32 else BF16 for w in w_full_list]
    x_dt = [BF16] * len(w_full_list)
    res_dt = None
    if res_percore is not None:
        res_dt = F32 if res_percore[0].dtype == np.float32 else BF16
    out_dt = F32 if out_dtype_np == np.float32 else BF16
    nc = _cached(("gemm", key), lambda: build_gemm(
        key, tuple(k_list), fout, tuple(x_dt), tuple(w_dt), res_dt, out_dt))
    maps = []
    for c in range(NCORE):
        m = {}
        for j, w in enumerate(w_full_list):
            m[f"w{j}"] = w
            m[f"x{j}"] = x_percore_list[j][c]
        if res_percore is not None:
            m["res"] = res_percore[c]
        maps.append(m)
    res = _run(nc, maps)
    return [r["out"] for r in res]



def build_l0attn():
    S = 4096
    NCH = 32
    NP = 8
    SCALE = 128.0 ** -0.5
    nc = bass.Bass("TRN2", target_bir_lowering=False, detect_race_conditions=False)
    qf = nc.dram_tensor("qf", [NP, 128, S], BF16, kind="ExternalInput").ap()
    kf = nc.dram_tensor("kf", [NP, 128, S], BF16, kind="ExternalInput").ap()
    kt = nc.dram_tensor("kt", [NP, 128, S], BF16, kind="ExternalInput").ap()
    vt = nc.dram_tensor("vt", [NP, 128, S], BF16, kind="ExternalInput").ap()
    mem = nc.dram_tensor("mem", [NP, 128, 520], F32, kind="ExternalInput").ap()
    mns = nc.dram_tensor("mns", [NP, 128, 4], F32, kind="ExternalInput").ap()
    gatev = nc.dram_tensor("gatev", [NP, 1, 1], F32, kind="ExternalInput").ap()
    mnflat = nc.dram_tensor("mnflat", [1, 8192], F32, kind="ExternalInput").ap()
    outm = nc.dram_tensor("outm", [NP, 128, S], BF16, kind="ExternalOutput").ap()
    mask_np = (np.arange(128)[None, :] >= np.arange(128)[:, None]).astype(BFD)
    mask_d = nc.inline_tensor(mask_np, "mask_d").ap()
    ones1_d = nc.inline_tensor(np.ones((1, 128), BFD), "ones1_d").ap()

    VN = 4 + 3 * NCH
    TN = 2 + 2 * NCH
    EN = 10

    def v_relq(p): return VN * p + 1
    def v_b6(p): return VN * p + 2
    def v_memw(p): return VN * p + 3
    def v_state0(p): return VN * p + 4
    def v_scorem(p, c): return VN * p + 5 + 3 * c
    def v_chunkev(p, c): return VN * p + 7 + 3 * c
    def v_end(p): return VN * (p + 1)

    def t_rel(p): return TN * p + 1
    def t_b6(p): return TN * p + 2
    def t_score(p, c): return TN * p + 3 + 2 * c
    def t_chunk(p, c): return TN * p + 4 + 2 * c

    def e_elu_v(p, i): return EN * p + 2 * i + 1
    def e_elu_s(p, i): return EN * p + 2 * i + 2
    def e_rel_v(p): return EN * p + 7
    def e_rel_s(p): return EN * p + 8
    def e_gate_v(p): return EN * p + 9
    def e_gate_s(p): return EN * p + 10

    with contextlib.ExitStack() as st:
        E = st.enter_context
        q_sb = [E(nc.sbuf_tensor(f"q_sb{p}", [128, S], BF16)) for p in range(2)]
        kf_sb = [E(nc.sbuf_tensor(f"kf_sb{p}", [128, S], BF16)) for p in range(2)]
        kt_sb = [E(nc.sbuf_tensor(f"kt_sb{p}", [128, S], BF16)) for p in range(2)]
        va_sb = [E(nc.sbuf_tensor(f"va_sb{p}", [128, NCH * 132], BF16))
                 for p in range(2)]
        mem_sb = [E(nc.sbuf_tensor(f"mem_sb{p}", [128, 520], F32)) for p in range(2)]
        mns_sb = [E(nc.sbuf_tensor(f"mns_sb{p}", [128, 4], BF16)) for p in range(2)]
        gt_sb = [E(nc.sbuf_tensor(f"gt_sb{p}", [1, 1], F32)) for p in range(2)]
        sgq = E(nc.sbuf_tensor("sgq", [128, S], BF16))
        kpf = E(nc.sbuf_tensor("kpf", [128, S], BF16))
        kpt = E(nc.sbuf_tensor("kpt", [128, S], BF16))
        tmp1 = E(nc.sbuf_tensor("tmp1", [128, S], BF16))
        tmp2 = E(nc.sbuf_tensor("tmp2", [128, S], BF16))
        memw = E(nc.sbuf_tensor("memw", [128, 520], BF16))
        ob = E(nc.sbuf_tensor("ob", [128, S], BF16))
        mask_sb = E(nc.sbuf_tensor("mask_sb", [128, 128], BF16))
        ones1_sb = E(nc.sbuf_tensor("ones1_sb", [1, 128], BF16))
        mnf_sb = E(nc.sbuf_tensor("mnf_sb", [1, 8192], F32))
        emt = E(nc.sbuf_tensor("emt", [1, 16], F32))
        relb = E(nc.sbuf_tensor("relb", [1, 32], F32))
        wg6 = E(nc.sbuf_tensor("wg6", [1, 6], BF16))
        wgb = E(nc.sbuf_tensor("wgb", [128, 6], F32))
        qsum = E(nc.sbuf_tensor("qsum", [128, 1], F32))
        qsum_bf = E(nc.sbuf_tensor("qsum_bf", [128, 1], BF16))
        S_f = [E(nc.sbuf_tensor(f"S_f{p}", [128, 129], F32)) for p in range(2)]
        S_bf = E(nc.sbuf_tensor("S_bf", [128, 129], BF16))
        pstm = E(nc.sbuf_tensor("pstm", [128, 128], BF16))
        den5 = E(nc.sbuf_tensor("den5", [128, 12], F32))
        loc = E(nc.sbuf_tensor("loc", [128, 128], F32))
        acc = [E(nc.sbuf_tensor(f"acc{p}", [128, 128], F32)) for p in range(2)]
        ps_rel = E(nc.psum_tensor("ps_rel", [1, 8], F32))
        ps_b6 = E(nc.psum_tensor("ps_b6", [128, 6], F32))
        ps_s = E(nc.psum_tensor("ps_s", [128, 128], F32))
        ps_out = E(nc.psum_tensor("ps_out", [128, 129], F32))
        ps_am1 = E(nc.psum_tensor("ps_am1", [128, 260], F32))
        ps_am2 = E(nc.psum_tensor("ps_am2", [128, 260], F32))
        ps_kv = E(nc.psum_tensor("ps_kv", [128, 129], F32))
        cld = E(nc.semaphore("cld"))
        msd = E(nc.semaphore("msd"))
        pld = [E(nc.semaphore(f"pld{p}")) for p in range(2)]
        tsem = E(nc.semaphore("tsem"))
        vsem = E(nc.semaphore("vsem"))
        esem = E(nc.semaphore("esem"))
        vs = E(nc.semaphore("vs"))
        stv = E(nc.semaphore("stv"))
        blk = E(nc.Block())

        @blk.gpsimd
        def _(g):
            g.dma_start(out=mask_sb[:], in_=mask_d).then_inc(cld, 16)
            g.dma_start(out=ones1_sb[:], in_=ones1_d).then_inc(cld, 16)
            g.dma_start(out=mnf_sb[:], in_=mnflat).then_inc(cld, 16)
            g.memset(va_sb[0][:], 1.0)
            g.memset(va_sb[1][:], 1.0)
            g.memset(S_f[0][:], 0.0)
            g.memset(S_bf[:], 0.0).then_inc(msd, 1)
            for p in range(NP):
                par = p % 2
                if p >= 2:
                    g.wait_ge(vsem, v_end(p - 2))
                    g.wait_ge(tsem, t_chunk(p - 2, NCH - 1))
                g.dma_start(out=q_sb[par][:], in_=qf[p]).then_inc(pld[par], 16)
                g.dma_start(out=kf_sb[par][:], in_=kf[p]).then_inc(pld[par], 16)
                g.dma_start(out=kt_sb[par][:], in_=kt[p]).then_inc(pld[par], 16)
                g.dma_start(
                    out=va_sb[par].ap().rearrange("p (c w) -> p c w", w=132
                                                  )[:, :, 0:128],
                    in_=vt[p].rearrange("p (c w) -> p c w", w=128)
                ).then_inc(pld[par], 16)
                g.dma_start(out=mem_sb[par][:], in_=mem[p]).then_inc(pld[par], 16)
                g.dma_start(out=mns_sb[par][:], in_=mns[p]).then_inc(pld[par], 16)
                g.dma_start(out=gt_sb[par][:], in_=gatev[p]).then_inc(pld[par], 16)

        @blk.vector
        def _(v):
            nvs = 0

            def sync(ins):
                nonlocal nvs
                nvs += 1
                ins.then_inc(vs, 1)
                v.wait_ge(vs, nvs)

            v.wait_ge(cld, 48)
            sync(v.tensor_reduce(emt[:, 0:4], mnf_sb.ap().rearrange(
                "p (m d) -> p m d", d=2048), axis=AX, op=ALU.add))
            sync(v.tensor_scalar(out=emt[:, 4:8], in0=emt[:, 0:4],
                                 scalar1=1e-6, scalar2=None, op0=ALU.is_lt))
            sync(v.tensor_reduce(emt[:, 8:9], emt[:, 4:8], axis=AX, op=ALU.add))
            sync(v.tensor_scalar(out=emt[:, 9:10], in0=emt[:, 8:9],
                                 scalar1=3.5, scalar2=None, op0=ALU.is_lt))

            for p in range(NP):
                par = p % 2
                v.wait_ge(pld[par], 16 * 7 * (p // 2 + 1))
                if p >= 1:
                    v.wait_ge(stv, 16 * p)
                for i, (srcb, dstb) in enumerate(
                        ((q_sb[par], sgq), (kf_sb[par], kpf), (kt_sb[par], kpt))):
                    v.tensor_scalar(out=tmp1[:], in0=srcb[:], scalar1=0.0,
                                    scalar2=None, op0=ALU.min).then_inc(esem, 1)
                    v.wait_ge(esem, e_elu_s(p, i))
                    v.scalar_tensor_tensor(out=dstb[:], in0=srcb[:], scalar=0.0,
                                           in1=tmp2[:], op0=ALU.max, op1=ALU.add)
                sync(v.reduce_sum(qsum[:], q_sb[par][:], axis=AX))
                v.tensor_copy(qsum_bf[:], qsum[:]).then_inc(vsem, 1)
                v.wait_ge(tsem, t_rel(p))
                sync(v.tensor_scalar(out=relb[:, 0:4], in0=ps_rel[0:1, 0:4],
                                     scalar1=SCALE / 4096.0, scalar2=None,
                                     op0=ALU.mult))
                sync(v.scalar_tensor_tensor(out=relb[:, 4:8], in0=emt[:, 4:8],
                                            scalar=-1e38, in1=relb[:, 0:4],
                                            op0=ALU.mult, op1=ALU.add))
                v.reduce_max(relb[:, 8:9], relb[:, 4:8], axis=AX,
                             negate=True).then_inc(esem, 1)
                v.wait_ge(esem, e_rel_s(p))
                sync(v.reduce_sum(relb[:, 14:15], relb[:, 10:14], axis=AX))
                sync(v.reciprocal(relb[:, 15:16], relb[:, 14:15]))
                sync(v.tensor_scalar(out=relb[:, 16:20], in0=relb[:, 10:14],
                                     scalar1=relb[:, 15:16], scalar2=None,
                                     op0=ALU.mult))
                v.tensor_scalar(out=relb[:, 24:28], in0=relb[:, 16:20],
                                scalar1=emt[:, 9:10], scalar2=None,
                                op0=ALU.mult).then_inc(esem, 1)
                v.wait_ge(esem, e_gate_s(p))
                sync(v.tensor_scalar(out=relb[:, 29:30], in0=relb[:, 28:29],
                                     scalar1=-1.0, scalar2=1.0, op0=ALU.mult,
                                     op1=ALU.add))
                v.tensor_copy(wg6[:], relb[:, 24:30]).then_inc(vsem, 1)
                v.wait_ge(tsem, t_b6(p))
                sync(v.tensor_copy(wgb[:], ps_b6[:]))
                mv = memw.ap().rearrange("p (m w) -> p m w", w=130)
                ms = mem_sb[par].ap().rearrange("p (m w) -> p m w", w=130)
                for m in range(4):
                    v.tensor_scalar(out=memw[:, m * 130:m * 130 + 128],
                                    in0=mem_sb[par][:, m * 130:m * 130 + 128],
                                    scalar1=wgb[:, m:m + 1], scalar2=None,
                                    op0=ALU.mult)
                v.tensor_copy(mv[:, :, 128:129],
                              ms[:, :, 128:129]).then_inc(vsem, 1)
                if p == 0:
                    v.wait_ge(msd, 1)
                v.tensor_scalar(out=S_f[0][:], in0=S_f[0][:], scalar1=0.0,
                                scalar2=None, op0=ALU.mult)
                v.tensor_scalar(out=S_bf[:], in0=S_f[0][:], scalar1=0.0,
                                scalar2=None, op0=ALU.mult).then_inc(vsem, 1)
                for c in range(NCH):
                    cpar = c % 2
                    v.wait_ge(tsem, t_score(p, c))
                    v.tensor_tensor(out=pstm[:], in0=ps_s[:], in1=mask_sb[:],
                                    op=ALU.mult).then_inc(vsem, 1)
                    v.wait_ge(tsem, t_chunk(p, c))
                    v.tensor_tensor(out=S_f[1 - cpar][:], in0=S_f[cpar][:],
                                    in1=ps_kv[:], op=ALU.add)
                    v.tensor_copy(S_bf[:], S_f[1 - cpar][:]).then_inc(vsem, 1)
                    v.tensor_copy(den5[:, 0:1], ps_out[:, 128:129])
                    v.tensor_copy(den5[:, 1:2], ps_am1[:, 128:129])
                    v.tensor_copy(den5[:, 2:3], ps_am1[:, 258:259])
                    v.tensor_copy(den5[:, 3:4], ps_am2[:, 128:129])
                    sync(v.tensor_copy(den5[:, 4:5], ps_am2[:, 258:259]))
                    sync(v.tensor_scalar(out=den5[:, 6:11], in0=den5[:, 0:5],
                                         scalar1=1e-6, scalar2=None,
                                         op0=ALU.max))
                    sync(v.reciprocal(den5[:, 0:5], den5[:, 6:11]))
                    v.tensor_scalar(out=loc[:], in0=ps_out[:, 0:128],
                                    scalar1=den5[:, 0:1], scalar2=None,
                                    op0=ALU.mult)
                    v.tensor_scalar(out=acc[0][:], in0=ps_am1[:, 0:128],
                                    scalar1=den5[:, 1:2], scalar2=None,
                                    op0=ALU.mult)
                    v.scalar_tensor_tensor(out=acc[1][:], in0=ps_am1[:, 130:258],
                                           scalar=den5[:, 2:3], in1=acc[0][:],
                                           op0=ALU.mult, op1=ALU.add)
                    v.scalar_tensor_tensor(out=acc[0][:], in0=ps_am2[:, 0:128],
                                           scalar=den5[:, 3:4], in1=acc[1][:],
                                           op0=ALU.mult, op1=ALU.add)
                    v.scalar_tensor_tensor(out=acc[1][:], in0=ps_am2[:, 130:258],
                                           scalar=den5[:, 4:5], in1=acc[0][:],
                                           op0=ALU.mult, op1=ALU.add)
                    v.tensor_scalar(out=loc[:], in0=loc[:], scalar1=wgb[:, 5:6],
                                    scalar2=None, op0=ALU.mult)
                    v.scalar_tensor_tensor(
                        out=ob[:, c * 128:(c + 1) * 128], in0=acc[1][:],
                        scalar=wgb[:, 4:5], in1=loc[:], op0=ALU.mult,
                        op1=ALU.add).then_inc(vsem, 1)

        @blk.scalar
        def _(s):
            for p in range(NP):
                for i in range(3):
                    s.wait_ge(esem, e_elu_v(p, i))
                    s.activation(out=tmp2[:], in_=tmp1[:],
                                 func=AF.Exp).then_inc(esem, 1)
                s.wait_ge(esem, e_rel_v(p))
                s.activation(out=relb[:, 10:14], in_=relb[:, 4:8], func=AF.Exp,
                             bias=relb[:, 8:9], scale=1.0).then_inc(esem, 1)
                s.wait_ge(esem, e_gate_v(p))
                s.activation(out=relb[:, 28:29], in_=gt_sb[p % 2][:],
                             func=AF.Sigmoid).then_inc(esem, 1)

        @blk.tensor
        def _(t):
            t.wait_ge(cld, 48)
            t.wait_ge(msd, 1)
            for p in range(NP):
                par = p % 2
                t.wait_ge(vsem, v_relq(p))
                t.matmul(ps_rel[0:1, 0:4], qsum_bf[:], mns_sb[par][:],
                         start=True, stop=True).then_inc(tsem, 1)
                t.wait_ge(vsem, v_b6(p))
                t.matmul(ps_b6[:], ones1_sb[:], wg6[:],
                         start=True, stop=True).then_inc(tsem, 1)
                t.wait_ge(vsem, v_memw(p))
                for c in range(NCH):
                    cs = slice(c * 128, (c + 1) * 128)
                    vsl = slice(c * 132, c * 132 + 129)
                    if c == 0:
                        t.wait_ge(vsem, v_state0(p))
                    else:
                        t.wait_ge(vsem, v_chunkev(p, c - 1))
                    t.matmul(ps_s[:], kpf[:, cs], sgq[:, cs],
                             start=True, stop=True).then_inc(tsem, 1)
                    t.matmul(ps_out[:], sgq[:, cs], S_bf[:],
                             start=True, stop=False)
                    t.matmul(ps_am1[:, 0:129], sgq[:, cs], memw[:, 0:129],
                             start=True, stop=True)
                    t.matmul(ps_am1[:, 130:259], sgq[:, cs], memw[:, 130:259],
                             start=True, stop=True)
                    t.matmul(ps_am2[:, 0:129], sgq[:, cs], memw[:, 260:389],
                             start=True, stop=True)
                    t.matmul(ps_am2[:, 130:259], sgq[:, cs], memw[:, 390:519],
                             start=True, stop=True)
                    t.wait_ge(vsem, v_scorem(p, c))
                    t.matmul(ps_out[:], pstm[:], va_sb[par][:, vsl],
                             start=False, stop=True)
                    t.matmul(ps_kv[:], kpt[:, cs], va_sb[par][:, vsl],
                             start=True, stop=True).then_inc(tsem, 1)

        @blk.sync
        def _(sy):
            for p in range(NP):
                sy.wait_ge(vsem, v_end(p))
                sy.dma_start(out=outm[p], in_=ob[:]).then_inc(stv, 16)
            sy.wait_ge(stv, 16 * NP)
    return nc


def run_l0attn(in_percore):
    nc = _cached("l0attn", build_l0attn)
    res = _run(nc, in_percore)
    return [r["outm"] for r in res]


def build_l0ffna():
    NK = D // 128
    NT = T // 128
    nc = bass.Bass("TRN2", target_bir_lowering=False, detect_race_conditions=False)
    xnf = nc.dram_tensor("xnf", [D, T], BF16, kind="ExternalInput").ap()
    rwt = nc.dram_tensor("rwt", [D, 8], F32, kind="ExternalInput").ap()
    dwt = nc.dram_tensor("dwt", [D, 128], F32, kind="ExternalInput").ap()
    aout_d = nc.dram_tensor("aout", [128, T], BF16, kind="ExternalOutput").ap()
    ident_d = nc.inline_tensor(np.eye(128, dtype=BFD), "ident_d").ap()

    with contextlib.ExitStack() as st:
        E = st.enter_context
        xn_sb = E(nc.sbuf_tensor("xn_sb", [128, NK * T], BF16))
        rw_sb = E(nc.sbuf_tensor("rw_sb", [128, NK * 8], BF16))
        dw_sb = E(nc.sbuf_tensor("dw_sb", [128, NK * 128], BF16))
        ident = E(nc.sbuf_tensor("ident", [128, 128], BF16))
        p8 = E(nc.sbuf_tensor("p8", [128, 8], F32))
        w8 = E(nc.sbuf_tensor("w8", [128, 8], F32))
        w2b = E(nc.sbuf_tensor("w2b", [128, 8], F32))
        g8 = E(nc.sbuf_tensor("g8", [128, 8], F32))
        stt_ = E(nc.sbuf_tensor("stt_", [128, 8], F32))
        a_sb = E(nc.sbuf_tensor("a_sb", [128, 128], F32))
        asc = E(nc.sbuf_tensor("asc", [128, 128], BF16))
        aout = E(nc.sbuf_tensor("aout_sb", [128, T], BF16))
        ps_r = E(nc.psum_tensor("ps_r", [128, 8], F32))
        ps_h = E(nc.psum_tensor("ps_h", [128, 128], F32))
        ps_t = E(nc.psum_tensor("ps_t", [128, 128], BF16))
        ld = E(nc.semaphore("ld"))
        tsem = E(nc.semaphore("tsem"))
        vsem = E(nc.semaphore("vsem"))
        esem = E(nc.semaphore("esem"))
        vs = E(nc.semaphore("vs"))
        stv = E(nc.semaphore("stv"))
        blk = E(nc.Block())

        @blk.gpsimd
        def _(g):
            g.dma_start(out=xn_sb.ap().rearrange("p (a t) -> p a t", t=T),
                        in_=xnf.rearrange("(a p) t -> p a t", p=128)
                        ).then_inc(ld, 16)
            g.dma_start(out=rw_sb.ap().rearrange("p (a f) -> p a f", f=8),
                        in_=rwt.rearrange("(a p) f -> p a f", p=128)
                        ).then_inc(ld, 16)
            g.dma_start(out=dw_sb.ap().rearrange("p (a f) -> p a f", f=128),
                        in_=dwt.rearrange("(a p) f -> p a f", p=128)
                        ).then_inc(ld, 16)
            g.dma_start(out=ident[:], in_=ident_d).then_inc(ld, 16)

        @blk.tensor
        def _(t):
            t.wait_ge(ld, 64)
            for i in range(NT):
                if i >= 1:
                    t.wait_ge(vsem, 2 * i)
                for k in range(NK):
                    t.matmul(ps_r[:], xn_sb[:, k * T + i * 128:k * T + (i + 1) * 128],
                             rw_sb[:, k * 8:(k + 1) * 8],
                             start=(k == 0), stop=(k == NK - 1))
                for k in range(NK):
                    ins = t.matmul(ps_h[:],
                                   xn_sb[:, k * T + i * 128:k * T + (i + 1) * 128],
                                   dw_sb[:, k * 128:(k + 1) * 128],
                                   start=(k == 0), stop=(k == NK - 1))
                ins.then_inc(tsem, 1)
                t.wait_ge(vsem, 2 * i + 1)
                t.transpose(ps_t[:], asc[:], ident[:]).then_inc(tsem, 1)

        @blk.vector
        def _(v):
            nvs = 0

            def sync(ins):
                nonlocal nvs
                nvs += 1
                ins.then_inc(vs, 1)
                v.wait_ge(vs, nvs)

            for i in range(NT):
                v.wait_ge(tsem, 2 * i + 1)
                v.reduce_max(stt_[:, 0:1], ps_r[:], axis=AX,
                             negate=True).then_inc(esem, 1)
                v.wait_ge(esem, 3 * i + 2)
                sync(v.reduce_sum(stt_[:, 1:2], p8[:], axis=AX))
                sync(v.reciprocal(stt_[:, 2:3], stt_[:, 1:2]))
                sync(v.tensor_scalar(out=w8[:], in0=p8[:], scalar1=stt_[:, 2:3],
                                     scalar2=None, op0=ALU.mult))
                sync(v.reduce_max(stt_[:, 3:4], w8[:], axis=AX))
                sync(v.tensor_scalar(out=w2b[:], in0=w8[:], scalar1=stt_[:, 3:4],
                                     scalar2=None, op0=ALU.is_ge))
                sync(v.scalar_tensor_tensor(out=w2b[:], in0=w2b[:], scalar=-1e38,
                                            in1=w8[:], op0=ALU.mult, op1=ALU.add))
                sync(v.reduce_max(stt_[:, 4:5], w2b[:], axis=AX))
                sync(v.tensor_scalar(out=g8[:], in0=w8[:], scalar1=stt_[:, 4:5],
                                     scalar2=None, op0=ALU.is_ge))
                sync(v.tensor_tensor(out=g8[:], in0=g8[:], in1=w8[:],
                                     op=ALU.mult))
                v.wait_ge(esem, 3 * i + 3)
                for e in range(8):
                    ins = v.tensor_scalar(out=asc[:, e * 16:(e + 1) * 16],
                                          in0=a_sb[:, e * 16:(e + 1) * 16],
                                          scalar1=g8[:, e:e + 1], scalar2=None,
                                          op0=ALU.mult)
                ins.then_inc(vsem, 1)
                v.wait_ge(tsem, 2 * i + 2)
                if i >= 1:
                    v.wait_ge(stv, 0)
                v.tensor_copy(aout[:, i * 128:(i + 1) * 128],
                              ps_t[:]).then_inc(vsem, 1)

        @blk.scalar
        def _(s):
            for i in range(NT):
                s.wait_ge(esem, 3 * i + 1)
                s.activation(out=p8[:], in_=ps_r[:], func=AF.Exp,
                             bias=stt_[:, 0:1], scale=1.0).then_inc(esem, 1)
                s.activation(out=a_sb[:], in_=ps_h[:],
                             func=AF.Silu).then_inc(esem, 1)

        @blk.sync
        def _(sy):
            sy.wait_ge(vsem, 2 * NT)
            sy.dma_start(out=aout_d, in_=aout[:]).then_inc(stv, 16)
            sy.wait_ge(stv, 16)
    return nc


def run_l0ffna(xnf_percore, rwt, dwt):
    nc = _cached("l0ffna", build_l0ffna)
    maps = [{"xnf": xnf_percore[c], "rwt": rwt, "dwt": dwt} for c in range(NCORE)]
    res = _run(nc, maps)
    return [r["aout"] for r in res]



def build_rope():
    NG = 18
    TC = T // 512
    nc = bass.Bass("TRN2", target_bir_lowering=False, detect_race_conditions=False)
    qkv = nc.dram_tensor("qkv", [2304, T], BF16, kind="ExternalInput").ap()
    cosr = nc.dram_tensor("cosr", [128, T], F32, kind="ExternalInput").ap()
    sins = nc.dram_tensor("sins", [128, T], F32, kind="ExternalInput").ap()
    out = nc.dram_tensor("out", [2304, T], BF16, kind="ExternalOutput").ap()
    pm = np.zeros((128, 128), BFD)
    for p in range(128):
        q = (p // 64) * 64 + ((p % 64) + 32) % 64
        pm[q, p] = 1.0
    perm_d = nc.inline_tensor(pm, "perm_d").ap()
    with contextlib.ExitStack() as st:
        E = st.enter_context
        xb = [E(nc.sbuf_tensor(f"xb{p}", [128, T], BF16)) for p in range(2)]
        yb = [E(nc.sbuf_tensor(f"yb{p}", [128, T], BF16)) for p in range(2)]
        cs = E(nc.sbuf_tensor("cs", [128, T], F32))
        sn = E(nc.sbuf_tensor("sn", [128, T], F32))
        pmat = E(nc.sbuf_tensor("pmat", [128, 128], BF16))
        t1 = E(nc.sbuf_tensor("t1", [128, 512], F32))
        t2 = E(nc.sbuf_tensor("t2", [128, 512], F32))
        ps_sw = [E(nc.psum_tensor(f"ps_sw{p}", [128, 512], F32)) for p in range(2)]
        cld = E(nc.semaphore("cld"))
        ld2 = [E(nc.semaphore(f"ld{p}")) for p in range(2)]
        ev = E(nc.semaphore("ev"))
        tsem = E(nc.semaphore("tsem"))
        stv = E(nc.semaphore("stv"))
        blk = E(nc.Block())

        @blk.sync
        def _(sy):
            sy.dma_start(out=cs[:], in_=cosr).then_inc(cld, 16)
            sy.dma_start(out=sn[:], in_=sins).then_inc(cld, 16)
            sy.dma_start(out=pmat[:], in_=perm_d).then_inc(cld, 16)
            for g in range(NG):
                if g >= 2:
                    sy.wait_ge(ev, (g - 1) * TC)
                    sy.dma_start(out=out[(g - 2) * 128:(g - 1) * 128, :],
                                 in_=yb[g % 2][:]).then_inc(stv, 16)
                sy.dma_start(out=xb[g % 2][:],
                             in_=qkv[g * 128:(g + 1) * 128, :]
                             ).then_inc(ld2[g % 2], 16)
            sy.wait_ge(ev, (NG - 1) * TC)
            sy.dma_start(out=out[(NG - 2) * 128:(NG - 1) * 128, :],
                         in_=yb[0][:]).then_inc(stv, 16)
            sy.wait_ge(ev, NG * TC)
            sy.dma_start(out=out[(NG - 1) * 128:NG * 128, :],
                         in_=yb[1][:]).then_inc(stv, 16)
            sy.wait_ge(stv, 16 * NG)

        @blk.tensor
        def _(t):
            t.wait_ge(cld, 48)
            for g in range(NG):
                t.wait_ge(ld2[g % 2], 16 * (g // 2 + 1))
                for cc in range(TC):
                    gi = g * TC + cc
                    if gi >= 2:
                        t.wait_ge(ev, gi - 1)
                    t.matmul(ps_sw[gi % 2][:], pmat[:],
                             xb[g % 2][:, cc * 512:(cc + 1) * 512],
                             start=True, stop=True).then_inc(tsem, 1)

        @blk.vector
        def _(v):
            v.wait_ge(cld, 48)
            for g in range(NG):
                v.wait_ge(ld2[g % 2], 16 * (g // 2 + 1))
                if g >= 2:
                    v.wait_ge(stv, 16 * (g - 1))
                for cc in range(TC):
                    gi = g * TC + cc
                    sl = slice(cc * 512, (cc + 1) * 512)
                    v.wait_ge(tsem, gi + 1)
                    v.tensor_tensor(out=t1[:], in0=xb[g % 2][:, sl],
                                    in1=cs[:, sl], op=ALU.mult)
                    v.tensor_tensor(out=t2[:], in0=ps_sw[gi % 2][:],
                                    in1=sn[:, sl], op=ALU.mult)
                    v.scalar_tensor_tensor(out=yb[g % 2][:, sl], in0=t1[:],
                                           scalar=1.0, in1=t2[:],
                                           op0=ALU.mult,
                                           op1=ALU.add).then_inc(ev, 1)
    return nc


def run_rope(qkv_percore, cos_half, sins_half):
    nc = _cached("rope", build_rope)
    maps = [{"qkv": qkv_percore[c], "cosr": cos_half[c % 2],
             "sins": sins_half[c % 2]} for c in range(NCORE)]
    res = _run(nc, maps)
    return [r["out"] for r in res]


def _l1_sched():
    NP, NQ = 8, 8
    T_ = {}
    V_ = {}
    E_ = {}
    ct = cv = ce = 0
    cv += 1; V_["ksq"] = cv
    for cc in range(16):
        ct += 1; T_[f"kn{cc}"] = ct
        cv += 1; V_[f"knr{cc}"] = cv
    g = 0
    for u in range(NP):
        ce += 1; E_[f"sq{u}"] = ce
        for cc in range(8):
            ct += 1; T_[f"qn{u},{cc}"] = ct
            cv += 1; V_[f"qnr{u},{cc}"] = cv
        cv += 1; V_[f"biasgo{u}"] = cv
        ce += 1; E_[f"sqrt{u}"] = ce
        cv += 1; V_[f"biasmm{u}"] = cv
        ct += 1; T_[f"bb{u}"] = ct
        cv += 1; V_[f"biasrdy{u}"] = cv
        for h in range(2):
            for q in range(NQ):
                nch = 4 * q + 4
                for c in range(nch):
                    ct += 1; T_[f"s{g + c}"] = ct
                    ce += 1; E_[f"exp{g + c}"] = ce
                    if c >= 4 * q:
                        cv += 1; V_[f"mask{g + c}"] = cv
                    ct += 1; T_[f"p{g + c}"] = ct
                g += nch
                cv += 1; V_[f"den{u},{h},{q}"] = cv
                ct += 1; T_[f"bc{u},{h},{q}"] = ct
                cv += 1; V_[f"evac{u},{h},{q}"] = cv
        V_[f"end{u}"] = cv
    return T_, V_, E_


def build_l1attn():
    S = 4096
    NQ = 8
    NP = 8
    T_, V_, E_ = _l1_sched()
    nc = bass.Bass("TRN2", target_bir_lowering=False, detect_race_conditions=False)
    qh = nc.dram_tensor("qh", [NP, 128, S], BF16, kind="ExternalInput").ap()
    kf2 = nc.dram_tensor("kf2", [2, 128, S], BF16, kind="ExternalInput").ap()
    vt2 = nc.dram_tensor("vt2", [2, 128, 2048], BF16, kind="ExternalInput").ap()
    oh = nc.dram_tensor("oh", [NP * 2, 64, S], BF16, kind="ExternalOutput").ap()
    stair = (np.arange(128)[:, None] <= np.arange(896)[None, :] - 384).astype(BFD)
    t8_d = nc.inline_tensor(stair, "t8_d").ap()
    ones1_d = nc.inline_tensor(np.ones((1, 128), BFD), "ones1_d").ap()
    onesc_d = nc.inline_tensor(np.ones((128, 1), BFD), "onesc_d").ap()

    with contextlib.ExitStack() as st:
        E = st.enter_context
        q_sb = [E(nc.sbuf_tensor(f"q_sb{p}", [128, S], BF16)) for p in range(2)]
        oh_sb = [E(nc.sbuf_tensor(f"oh_sb{p}", [64, S], BF16)) for p in range(2)]
        kf_sb = E(nc.sbuf_tensor("kf_sb", [128, 2 * S], BF16))
        va_sb = E(nc.sbuf_tensor("va_sb", [128, 2 * 2176], BF16))
        sq_sb = E(nc.sbuf_tensor("sq_sb", [128, 2 * S], BF16))
        t8_sb = E(nc.sbuf_tensor("t8_sb", [128, 896], BF16))
        ones1_sb = E(nc.sbuf_tensor("ones1_sb", [1, 128], BF16))
        onesc_sb = E(nc.sbuf_tensor("onesc_sb", [128, 1], BF16))
        pst = [E(nc.sbuf_tensor(f"pst{p}", [128, 512], BF16)) for p in range(2)]
        nrm = E(nc.sbuf_tensor("nrm", [1, 20], F32))
        nrmb = E(nc.sbuf_tensor("nrmb", [1, 1], BF16))
        biasb = E(nc.sbuf_tensor("biasb", [128, 1], F32))
        denf = E(nc.sbuf_tensor("denf", [1, 512], F32))
        denr = E(nc.sbuf_tensor("denr", [1, 512], BF16))
        denb = E(nc.sbuf_tensor("denb", [64, 512], BF16))
        ps_sT = [E(nc.psum_tensor(f"ps_sT{p}", [128, 512], F32)) for p in range(2)]
        ps_o = E(nc.psum_tensor("ps_o", [65, 512], F32))
        ps_bc = E(nc.psum_tensor("ps_bc", [64, 512], F32))
        ps_n = E(nc.psum_tensor("ps_n", [1, 512], F32))
        ps_bb = E(nc.psum_tensor("ps_bb", [128, 1], F32))
        cld = E(nc.semaphore("cld"))
        msd = E(nc.semaphore("msd"))
        pld = [E(nc.semaphore(f"pld{p}")) for p in range(2)]
        tsem = E(nc.semaphore("tsem"))
        vsem = E(nc.semaphore("vsem"))
        esem = E(nc.semaphore("esem"))
        vs = E(nc.semaphore("vs"))
        stv = E(nc.semaphore("stv"))
        blk = E(nc.Block())

        @blk.gpsimd
        def _(g):
            g.dma_start(out=t8_sb[:], in_=t8_d).then_inc(cld, 16)
            g.dma_start(out=ones1_sb[:], in_=ones1_d).then_inc(cld, 16)
            g.dma_start(out=onesc_sb[:], in_=onesc_d).then_inc(cld, 16)
            g.dma_start(out=kf_sb[:, 0:S], in_=kf2[0]).then_inc(cld, 16)
            g.dma_start(out=kf_sb[:, S:2 * S], in_=kf2[1]).then_inc(cld, 16)
            g.memset(va_sb[:], 1.0).then_inc(msd, 1)
            for kv in range(2):
                g.dma_start(
                    out=va_sb.ap()[:, kv * 2176:(kv + 1) * 2176].rearrange(
                        "p (c w) -> p c w", w=68)[:, :, 0:64],
                    in_=vt2[kv].rearrange("p (c w) -> p c w", w=64)
                ).then_inc(cld, 16)
            for u in range(NP):
                par = u % 2
                if u >= 2:
                    g.wait_ge(vsem, V_[f"end{u - 2}"])
                g.dma_start(out=q_sb[par][:], in_=qh[u]).then_inc(pld[par], 16)

        @blk.scalar
        def _(s):
            s.wait_ge(cld, 112)
            gg = 0
            for u in range(NP):
                par = u % 2
                kv = u // 4
                s.wait_ge(pld[par], 16 * (u // 2 + 1))
                if u >= 1:
                    s.wait_ge(tsem, T_[f"qn{u - 1},7"])
                else:
                    s.wait_ge(tsem, T_["kn7"])
                s.activation(out=sq_sb[:, 0:S], in_=q_sb[par][:],
                             func=AF.Square).then_inc(esem, 1)
                s.wait_ge(vsem, V_[f"biasgo{u}"])
                s.activation(out=nrm[:, 3:4], in_=nrm[:, 2:3],
                             func=AF.Sqrt).then_inc(esem, 1)
                s.wait_ge(vsem, V_[f"biasrdy{u}"])
                for h in range(2):
                    for q in range(NQ):
                        for c in range(4 * q + 4):
                            gi = gg + c
                            s.wait_ge(tsem, T_[f"s{gi}"])
                            if gi >= 2:
                                s.wait_ge(tsem, T_[f"p{gi - 2}"])
                            s.activation(out=pst[gi % 2][:],
                                         in_=ps_sT[gi % 2][:], func=AF.Exp,
                                         bias=biasb[:, 0:1],
                                         scale=0.125).then_inc(esem, 1)
                        gg += 4 * q + 4

        @blk.vector
        def _(v):
            nvs = 0

            def sync(ins):
                nonlocal nvs
                nvs += 1
                ins.then_inc(vs, 1)
                v.wait_ge(vs, nvs)

            v.wait_ge(cld, 112)
            v.tensor_tensor(out=sq_sb[:], in0=kf_sb[:], in1=kf_sb[:],
                            op=ALU.mult).then_inc(vsem, 1)
            for kv in range(2):
                for cc in range(8):
                    v.wait_ge(tsem, T_[f"kn{kv * 8 + cc}"])
                    v.reduce_max(nrm[:, 8 + cc:9 + cc], ps_n[:],
                                 axis=AX).then_inc(vsem, 1)
                v.wait_ge(vsem, V_[f"knr{kv * 8 + 7}"])
                sync(v.reduce_max(nrm[:, 4 + kv:5 + kv], nrm[:, 8:16], axis=AX))
            gg = 0
            for u in range(NP):
                par = u % 2
                kv = u // 4
                for cc in range(8):
                    v.wait_ge(tsem, T_[f"qn{u},{cc}"])
                    v.reduce_max(nrm[:, 8 + cc:9 + cc], ps_n[:],
                                 axis=AX).then_inc(vsem, 1)
                v.wait_ge(vsem, V_[f"qnr{u},7"])
                sync(v.reduce_max(nrm[:, 0:1], nrm[:, 8:16], axis=AX))
                sync(v.tensor_tensor(out=nrm[:, 2:3], in0=nrm[:, 0:1],
                                     in1=nrm[:, 4 + kv:5 + kv], op=ALU.mult))
                v.tensor_copy(nrm[:, 18:19],
                              nrm[:, 8:9]).then_inc(vsem, 1)
                v.wait_ge(esem, E_[f"sqrt{u}"])
                sync(v.tensor_scalar(out=nrm[:, 6:7], in0=nrm[:, 3:4],
                                     scalar1=-0.125, scalar2=None,
                                     op0=ALU.mult))
                sync(v.tensor_copy(nrmb[:], nrm[:, 6:7]))
                v.tensor_copy(nrm[:, 18:19],
                              nrm[:, 8:9]).then_inc(vsem, 1)
                v.wait_ge(tsem, T_[f"bb{u}"])
                v.tensor_copy(biasb[:], ps_bb[:]).then_inc(vsem, 1)
                for h in range(2):
                    uh = u * 2 + h
                    if uh >= 2:
                        v.wait_ge(stv, 16 * (uh - 1))
                    for q in range(NQ):
                        for c in range(4 * q, 4 * q + 4):
                            gi = gg + c
                            v.wait_ge(esem, E_[f"exp{gi}"])
                            delta = 128 * (4 * q - c)
                            v.tensor_tensor(
                                out=pst[gi % 2][:], in0=pst[gi % 2][:],
                                in1=t8_sb[:, 384 + delta:896 + delta],
                                op=ALU.mult).then_inc(vsem, 1)
                        gg += 4 * q + 4
                        v.wait_ge(tsem, T_[f"p{gg - 1}"])
                        sync(v.reciprocal(denf[:], ps_o[64:65, :]))
                        sync(v.tensor_copy(denr[:], denf[:]))
                        v.tensor_copy(nrm[:, 18:19],
                                      nrm[:, 8:9]).then_inc(vsem, 1)
                        v.wait_ge(tsem, T_[f"bc{u},{h},{q}"])
                        v.tensor_copy(denb[:], ps_bc[:])
                        v.tensor_tensor(
                            out=oh_sb[uh % 2][:, q * 512:(q + 1) * 512],
                            in0=ps_o[0:64, :], in1=denb[:],
                            op=ALU.mult).then_inc(vsem, 1)

        @blk.tensor
        def _(t):
            t.wait_ge(cld, 112)
            t.wait_ge(msd, 1)
            t.wait_ge(vsem, V_["ksq"])
            for cc in range(16):
                if cc >= 1:
                    t.wait_ge(vsem, V_[f"knr{cc - 1}"])
                t.matmul(ps_n[:], onesc_sb[:],
                         sq_sb[:, cc * 512:(cc + 1) * 512],
                         start=True, stop=True).then_inc(tsem, 1)
            gg = 0
            for u in range(NP):
                par = u % 2
                kv = u // 4
                t.wait_ge(esem, E_[f"sq{u}"])
                for cc in range(8):
                    if cc == 0:
                        t.wait_ge(vsem, V_["knr15"] if u == 0 else
                                  V_[f"qnr{u - 1},7"])
                    else:
                        t.wait_ge(vsem, V_[f"qnr{u},{cc - 1}"])
                    t.matmul(ps_n[:], onesc_sb[:],
                             sq_sb[:, cc * 512:(cc + 1) * 512],
                             start=True, stop=True).then_inc(tsem, 1)
                t.wait_ge(vsem, V_[f"biasmm{u}"])
                t.matmul(ps_bb[:], ones1_sb[:], nrmb[:],
                         start=True, stop=True).then_inc(tsem, 1)
                for h in range(2):
                    qrow = slice(h * 64, (h + 1) * 64)
                    for q in range(NQ):
                        nch = 4 * q + 4
                        for c in range(nch):
                            gi = gg + c
                            if gi >= 2:
                                t.wait_ge(esem, E_[f"exp{gi - 2}"])
                            t.matmul(ps_sT[gi % 2][:],
                                     kf_sb[qrow, kv * S + c * 128:
                                           kv * S + (c + 1) * 128],
                                     q_sb[par][qrow, q * 512:(q + 1) * 512],
                                     start=True, stop=True
                                     ).then_inc(tsem, 1)
                            if c == 0 and not (u == 0 and h == 0 and q == 0):
                                pu, ph, pq = (u, h, q - 1) if q > 0 else                                     ((u, h - 1, NQ - 1) if h > 0 else
                                     (u - 1, 1, NQ - 1))
                                t.wait_ge(vsem, V_[f"evac{pu},{ph},{pq}"])
                            if c >= 4 * q:
                                t.wait_ge(vsem, V_[f"mask{gi}"])
                            else:
                                t.wait_ge(esem, E_[f"exp{gi}"])
                            t.matmul(ps_o[:],
                                     va_sb[:, kv * 2176 + c * 68:
                                           kv * 2176 + c * 68 + 65],
                                     pst[gi % 2][:],
                                     start=(c == 0), stop=(c == nch - 1)
                                     ).then_inc(tsem, 1)
                        gg += nch
                        t.wait_ge(vsem, V_[f"den{u},{h},{q}"])
                        t.matmul(ps_bc[:], ones1_sb[0:1, 0:64], denr[:],
                                 start=True, stop=True
                                 ).then_inc(tsem, 1)

        @blk.sync
        def _(sy):
            for u in range(NP):
                for h in range(2):
                    uh = u * 2 + h
                    sy.wait_ge(vsem, V_[f"evac{u},{h},{NQ - 1}"])
                    sy.dma_start(out=oh[uh],
                                 in_=oh_sb[uh % 2][:]).then_inc(stv, 16)
            sy.wait_ge(stv, 16 * NP * 2)
    return nc


def run_l1attn(in_percore):
    nc = _cached("l1attn", build_l1attn)
    res = _run(nc, in_percore)
    return [r["oh"] for r in res]


def build_quant():
    nc = bass.Bass("TRN2", target_bir_lowering=False, detect_race_conditions=False)
    w1 = nc.dram_tensor("w1", [2048, 2048], F32, kind="ExternalInput").ap()
    w2 = nc.dram_tensor("w2", [1024, 2048], F32, kind="ExternalInput").ap()
    q1 = nc.dram_tensor("q1", [2048, 2048], BF16, kind="ExternalOutput").ap()
    q2 = nc.dram_tensor("q2", [1024, 2048], BF16, kind="ExternalOutput").ap()
    sel = np.zeros((128, 2), BFD)
    sel[0:64, 0] = 1.0
    sel[64:128, 1] = 1.0
    sel_d = nc.inline_tensor(sel, "sel_d").ap()
    selt_d = nc.inline_tensor(np.ascontiguousarray(sel.T), "selt_d").ap()
    tiles = [("w1", r, c) for r in range(16) for c in range(4)] + \
            [("w2", r, c) for r in range(8) for c in range(4)]
    NTL = len(tiles)

    with contextlib.ExitStack() as st:
        E = st.enter_context
        w_sb = [E(nc.sbuf_tensor(f"w_sb{p}", [128, 512], BF16)) for p in range(2)]
        wa = E(nc.sbuf_tensor("wa", [128, 512], BF16))
        sel_sb = E(nc.sbuf_tensor("sel_sb", [128, 2], BF16))
        selt_sb = E(nc.sbuf_tensor("selt_sb", [2, 128], BF16))
        sc = E(nc.sbuf_tensor("sc", [2, 512 * 4], F32))
        sc_bf = E(nc.sbuf_tensor("sc_bf", [2, 512], BF16))
        thr = E(nc.sbuf_tensor("thr", [128, 512], F32))
        mk = E(nc.sbuf_tensor("mk", [128, 512], BF16))
        sg = E(nc.sbuf_tensor("sg", [128, 512], BF16))
        tq = E(nc.sbuf_tensor("tq", [128, 512], BF16))
        wq = [E(nc.sbuf_tensor(f"wq{p}", [128, 512], BF16)) for p in range(2)]
        ps_s = E(nc.psum_tensor("ps_s", [2, 512], F32))
        ps_b2 = E(nc.psum_tensor("ps_b2", [128, 512], F32))
        cld = E(nc.semaphore("cld"))
        pld = [E(nc.semaphore(f"pld{p}")) for p in range(2)]
        tsem = E(nc.semaphore("tsem"))
        vsem = E(nc.semaphore("vsem"))
        esem = E(nc.semaphore("esem"))
        vs = E(nc.semaphore("vs"))
        stv = [E(nc.semaphore(f"stv{p}")) for p in range(2)]
        blk = E(nc.Block())

        def tsl(idx):
            name, r, c = tiles[idx]
            ap = w1 if name == "w1" else w2
            return ap[r * 128:(r + 1) * 128, c * 512:(c + 1) * 512]

        def osl(idx):
            name, r, c = tiles[idx]
            ap = q1 if name == "w1" else q2
            return ap[r * 128:(r + 1) * 128, c * 512:(c + 1) * 512]

        @blk.gpsimd
        def _(g):
            g.dma_start(out=sel_sb[:], in_=sel_d).then_inc(cld, 16)
            g.dma_start(out=selt_sb[:], in_=selt_d).then_inc(cld, 16)
            for i in range(NTL):
                if i >= 2:
                    g.wait_ge(stv[i % 2], 16 * (i // 2))
                g.dma_start(out=w_sb[i % 2][:], in_=tsl(i)).then_inc(pld[i % 2], 16)

        @blk.scalar
        def _(s):
            for i in range(NTL):
                s.wait_ge(pld[i % 2], 16 * (i // 2 + 1))
                if i >= 1:
                    s.wait_ge(vsem, 2 * i)
                s.activation(out=wa[:], in_=w_sb[i % 2][:],
                             func=AF.Abs).then_inc(esem, 1)
                s.activation(out=sg[:], in_=w_sb[i % 2][:],
                             func=AF.Sign).then_inc(esem, 1)

        @blk.tensor
        def _(t):
            t.wait_ge(cld, 32)
            for i in range(NTL):
                t.wait_ge(esem, 2 * i + 1)
                if i >= 1:
                    t.wait_ge(vsem, 2 * i)
                t.matmul(ps_s[:], sel_sb[:], wa[:],
                         start=True, stop=True).then_inc(tsem, 1)
                t.wait_ge(vsem, 2 * i + 1)
                t.matmul(ps_b2[:], selt_sb[:], sc_bf[:],
                         start=True, stop=True).then_inc(tsem, 1)

        @blk.vector
        def _(v):
            nvs = 0

            def sync(ins):
                nonlocal nvs
                nvs += 1
                ins.then_inc(vs, 1)
                v.wait_ge(vs, nvs)

            for i in range(NTL):
                v.wait_ge(tsem, 2 * i + 1)
                sync(v.tensor_scalar(out=sc[:, 0:512], in0=ps_s[:],
                                     scalar1=1.0 / 64, scalar2=None,
                                     op0=ALU.mult))
                v.tensor_scalar(out=sc_bf[:], in0=sc[:, 0:512],
                                scalar1=1e-8, scalar2=None,
                                op0=ALU.max).then_inc(vsem, 1)
                v.wait_ge(tsem, 2 * i + 2)
                v.tensor_scalar(out=thr[:], in0=ps_b2[:],
                                scalar1=0.5009765625, scalar2=None,
                                op0=ALU.mult)
                v.wait_ge(esem, 2 * i + 2)
                v.tensor_tensor(out=mk[:], in0=wa[:], in1=thr[:],
                                op=ALU.is_gt)
                v.tensor_tensor(out=tq[:], in0=mk[:], in1=sg[:], op=ALU.mult)
                if i >= 2:
                    v.wait_ge(stv[i % 2], 16 * (i // 2))
                v.tensor_tensor(out=wq[i % 2][:], in0=tq[:], in1=ps_b2[:],
                                op=ALU.mult).then_inc(vsem, 1)
        @blk.sync
        def _(sy):
            for i in range(NTL):
                sy.wait_ge(vsem, 2 * i + 2)
                sy.dma_start(out=osl(i), in_=wq[i % 2][:]).then_inc(stv[i % 2], 16)
            for p in range(2):
                sy.wait_ge(stv[p], 16 * ((NTL - 1 - p) // 2 + 1))
    return nc


def run_quant(w1_percore, w2_percore):
    nc = _cached("quant", build_quant)
    maps = [{"w1": w1_percore[c], "w2": w2_percore[c]} for c in range(NCORE)]
    res = _run(nc, maps)
    return [r["q1"] for r in res], [r["q2"] for r in res]


def build_ffn1():
    NK = D // 128
    NG = 64
    TC = T // 512
    OG = D // 128
    EPSF = float(np.finfo(np.float32).eps)
    nc = bass.Bass("TRN2", target_bir_lowering=False, detect_race_conditions=False)
    xnf = nc.dram_tensor("xnf", [D, T], BF16, kind="ExternalInput").ap()
    wgu = nc.dram_tensor("wgu", [D, 16384], BF16, kind="ExternalInput").ap()
    wpr = nc.dram_tensor("wpr", [8192, D], BF16, kind="ExternalInput").ap()
    p2 = nc.dram_tensor("p2", [D, T], BF16, kind="ExternalInput").ap()
    x2 = nc.dram_tensor("x2", [D, T], F32, kind="ExternalOutput").ap()
    ones1_d = nc.inline_tensor(np.ones((1, 128), BFD), "ones1_d").ap()
    onesc_d = nc.inline_tensor(np.ones((128, 1), BFD), "onesc_d").ap()

    def sch():
        Td, Vd, Ed = {}, {}, {}
        ct = cv = ce = 0
        for tc in range(TC):
            for gg in range(NG):
                gi = tc * NG + gg
                ct += 1; Td[f"g{gi}"] = ct
                ce += 1; Ed[f"s{gi}"] = ce
                cv += 1; Vd[f"h{gi}"] = cv
                ct += 1; Td[f"r{gi}"] = ct
            cv += 1; Vd[f"vsq{tc}"] = cv
            ce += 1; Ed[f"q{tc}"] = ce
            cv += 1; Vd[f"ib{tc}"] = cv
            ct += 1; Td[f"b{tc}"] = ct
            cv += 1; Vd[f"hn{tc}"] = cv
            for og in range(OG):
                oi = tc * OG + og
                ct += 1; Td[f"o{oi}"] = ct
                cv += 1; Vd[f"oev{oi}"] = cv
        return Td, Vd, Ed

    Td, Vd, Ed = sch()

    with contextlib.ExitStack() as st:
        E = st.enter_context
        xn_sb = E(nc.sbuf_tensor("xn_sb", [128, NK * T], BF16))
        wg_sb = [[E(nc.sbuf_tensor(f"wg_sb{p}_{j}", [128, NK * 128], BF16))
                  for j in range(2)] for p in range(2)]
        wp_sb = [E(nc.sbuf_tensor(f"wp_sb{p}", [128, 64 * 128], BF16))
                 for p in range(2)]
        hbuf = E(nc.sbuf_tensor("hbuf", [128, NG * 512], BF16))
        h2b = [E(nc.sbuf_tensor(f"h2b{p}", [128, 512], BF16)) for p in range(2)]
        sil = [E(nc.sbuf_tensor(f"sil{p}", [128, 512], F32)) for p in range(2)]
        rr0 = E(nc.sbuf_tensor("rr0", [1, 512], F32))
        rr1 = E(nc.sbuf_tensor("rr1", [1, 512], F32))
        rr2 = E(nc.sbuf_tensor("rr2", [1, 512], F32))
        ib_bf = E(nc.sbuf_tensor("ib_bf", [1, 512], BF16))
        ib_sb = E(nc.sbuf_tensor("ib_sb", [128, 512], BF16))
        p2_sb = [E(nc.sbuf_tensor(f"p2_sb{p}", [128, 512], BF16)) for p in range(2)]
        ob = [E(nc.sbuf_tensor(f"ob{p}", [128, 512], F32)) for p in range(2)]
        ones1_sb = E(nc.sbuf_tensor("ones1_sb", [1, 128], BF16))
        onesc_sb = E(nc.sbuf_tensor("onesc_sb", [128, 1], BF16))
        ps_g = [E(nc.psum_tensor(f"ps_g{p}", [128, 512], F32)) for p in range(2)]
        ps_u = [E(nc.psum_tensor(f"ps_u{p}", [128, 512], F32)) for p in range(2)]
        ps_o = [E(nc.psum_tensor(f"ps_o{p}", [128, 512], F32)) for p in range(2)]
        ps_r = E(nc.psum_tensor("ps_r", [1, 512], F32))
        ps_ib = E(nc.psum_tensor("ps_ib", [128, 512], F32))
        cld = E(nc.semaphore("cld"))
        wld = [E(nc.semaphore(f"wld{p}")) for p in range(2)]
        wpld = [E(nc.semaphore(f"wpld{p}")) for p in range(2)]
        rld = [E(nc.semaphore(f"rld{p}")) for p in range(2)]
        tsem = E(nc.semaphore("tsem"))
        vsem = E(nc.semaphore("vsem"))
        esem = E(nc.semaphore("esem"))
        vs = E(nc.semaphore("vs"))
        stv = [E(nc.semaphore(f"stv{p}")) for p in range(2)]
        blk = E(nc.Block())

        @blk.gpsimd
        def _(g):
            g.dma_start(out=ones1_sb[:], in_=ones1_d).then_inc(cld, 16)
            g.dma_start(out=onesc_sb[:], in_=onesc_d).then_inc(cld, 16)
            g.dma_start(out=xn_sb.ap().rearrange("p (a t) -> p a t", t=T),
                        in_=xnf.rearrange("(a p) t -> p a t", p=128)
                        ).then_inc(cld, 16)
            for tc in range(TC):
                for gg in range(NG):
                    gi = tc * NG + gg
                    if gi >= 2:
                        g.wait_ge(tsem, Td[f"g{gi - 2}"])
                    g.dma_start(
                        out=wg_sb[gi % 2][0].ap().rearrange(
                            "p (a f) -> p a f", f=128),
                        in_=wgu[:, gg * 128:(gg + 1) * 128].rearrange(
                            "(a p) f -> p a f", p=128)
                    ).then_inc(wld[gi % 2], 16)
                    g.dma_start(
                        out=wg_sb[gi % 2][1].ap().rearrange(
                            "p (a f) -> p a f", f=128),
                        in_=wgu[:, 8192 + gg * 128:8192 + (gg + 1) * 128
                                ].rearrange("(a p) f -> p a f", p=128)
                    ).then_inc(wld[gi % 2], 16)
                for og in range(OG):
                    oi = tc * OG + og
                    if oi >= 2:
                        g.wait_ge(tsem, Td[f"o{oi - 2}"])
                    g.dma_start(
                        out=wp_sb[oi % 2].ap().rearrange(
                            "p (a f) -> p a f", f=128),
                        in_=wpr[:, og * 128:(og + 1) * 128].rearrange(
                            "(a p) f -> p a f", p=128)
                    ).then_inc(wpld[oi % 2], 16)

        @blk.sync
        def _(sy):
            for tc in range(TC):
                for og in range(OG):
                    oi = tc * OG + og
                    if oi >= 2:
                        sy.wait_ge(vsem, Vd[f"oev{oi - 1}"])
                    sy.dma_start(out=p2_sb[oi % 2][:],
                                 in_=p2[og * 128:(og + 1) * 128,
                                        tc * 512:(tc + 1) * 512]
                                 ).then_inc(rld[oi % 2], 16)
                    sy.wait_ge(vsem, Vd[f"oev{oi}"])
                    sy.dma_start(out=x2[og * 128:(og + 1) * 128,
                                        tc * 512:(tc + 1) * 512],
                                 in_=ob[oi % 2][:]).then_inc(stv[oi % 2], 16)
            for p in range(2):
                sy.wait_ge(stv[p], 16 * ((TC * OG - 1 - p) // 2 + 1))

        @blk.scalar
        def _(s):
            for tc in range(TC):
                for gg in range(NG):
                    gi = tc * NG + gg
                    s.wait_ge(tsem, Td[f"g{gi}"])
                    if gi >= 2:
                        s.wait_ge(vsem, Vd[f"h{gi - 2}"])
                    s.activation(out=sil[gi % 2][:], in_=ps_g[gi % 2][:],
                                 func=AF.Silu).then_inc(esem, 1)
                s.wait_ge(vsem, Vd[f"vsq{tc}"])
                s.activation(out=rr1[:], in_=rr0[:],
                             func=AF.Sqrt).then_inc(esem, 1)

        @blk.vector
        def _(v):
            nvs = 0

            def sync(ins):
                nonlocal nvs
                nvs += 1
                ins.then_inc(vs, 1)
                v.wait_ge(vs, nvs)

            for tc in range(TC):
                if tc >= 1:
                    v.wait_ge(tsem, Td[f"o{tc * OG - 1}"])
                for gg in range(NG):
                    gi = tc * NG + gg
                    v.wait_ge(esem, Ed[f"s{gi}"])
                    if gi >= 2:
                        v.wait_ge(tsem, Td[f"r{gi - 2}"])
                    v.tensor_tensor(out=hbuf[:, gg * 512:(gg + 1) * 512],
                                    in0=sil[gi % 2][:], in1=ps_u[gi % 2][:],
                                    op=ALU.mult)
                    v.tensor_tensor(out=h2b[gi % 2][:],
                                    in0=hbuf[:, gg * 512:(gg + 1) * 512],
                                    in1=hbuf[:, gg * 512:(gg + 1) * 512],
                                    op=ALU.mult).then_inc(vsem, 1)
                v.wait_ge(tsem, Td[f"r{tc * NG + NG - 1}"])
                sync(v.tensor_scalar(out=rr0[:], in0=ps_r[:],
                                     scalar1=1.0 / 8192, scalar2=None,
                                     op0=ALU.mult))
                v.tensor_scalar(out=rr0[:], in0=rr0[:], scalar1=EPSF,
                                scalar2=None, op0=ALU.add).then_inc(vsem, 1)
                v.wait_ge(esem, Ed[f"q{tc}"])
                sync(v.reciprocal(rr2[:], rr1[:]))
                v.tensor_copy(ib_bf[:], rr2[:]).then_inc(vsem, 1)
                v.wait_ge(tsem, Td[f"b{tc}"])
                sync(v.tensor_copy(ib_sb[:], ps_ib[:]))
                for gg in range(NG):
                    ins = v.tensor_tensor(
                        out=hbuf[:, gg * 512:(gg + 1) * 512],
                        in0=hbuf[:, gg * 512:(gg + 1) * 512],
                        in1=ib_sb[:], op=ALU.mult)
                ins.then_inc(vsem, 1)
                for og in range(OG):
                    oi = tc * OG + og
                    v.wait_ge(tsem, Td[f"o{oi}"])
                    v.wait_ge(rld[oi % 2], 16 * (oi // 2 + 1))
                    if oi >= 2:
                        v.wait_ge(stv[oi % 2], 16 * (oi // 2))
                    v.tensor_tensor(out=ob[oi % 2][:], in0=ps_o[oi % 2][:],
                                    in1=p2_sb[oi % 2][:],
                                    op=ALU.add).then_inc(vsem, 1)

        @blk.tensor
        def _(t):
            t.wait_ge(cld, 48)
            for tc in range(TC):
                for gg in range(NG):
                    gi = tc * NG + gg
                    t.wait_ge(wld[gi % 2], 32 * (gi // 2 + 1))
                    if gi >= 2:
                        t.wait_ge(vsem, Vd[f"h{gi - 2}"])
                    for k in range(NK):
                        t.matmul(ps_g[gi % 2][:],
                                 wg_sb[gi % 2][0][:, k * 128:(k + 1) * 128],
                                 xn_sb[:, k * T + tc * 512:
                                       k * T + (tc + 1) * 512],
                                 start=(k == 0), stop=(k == NK - 1))
                    for k in range(NK):
                        ins = t.matmul(ps_u[gi % 2][:],
                                       wg_sb[gi % 2][1][:, k * 128:(k + 1) * 128],
                                       xn_sb[:, k * T + tc * 512:
                                             k * T + (tc + 1) * 512],
                                       start=(k == 0), stop=(k == NK - 1))
                    ins.then_inc(tsem, 1)
                    t.wait_ge(vsem, Vd[f"h{gi}"])
                    t.matmul(ps_r[:], onesc_sb[:], h2b[gi % 2][:],
                             start=(gg == 0), stop=(gg == NG - 1)
                             ).then_inc(tsem, 1)
                t.wait_ge(vsem, Vd[f"ib{tc}"])
                t.matmul(ps_ib[:], ones1_sb[:], ib_bf[:],
                         start=True, stop=True).then_inc(tsem, 1)
                for og in range(OG):
                    oi = tc * OG + og
                    t.wait_ge(wpld[oi % 2], 16 * (oi // 2 + 1))
                    t.wait_ge(vsem, Vd[f"hn{tc}"])
                    if oi >= 2:
                        t.wait_ge(vsem, Vd[f"oev{oi - 2}"])
                    for g2 in range(NG):
                        ins = t.matmul(ps_o[oi % 2][:],
                                       wp_sb[oi % 2][:, g2 * 128:(g2 + 1) * 128],
                                       hbuf[:, g2 * 512:(g2 + 1) * 512],
                                       start=(g2 == 0), stop=(g2 == NG - 1))
                    ins.then_inc(tsem, 1)
    return nc



def run_ffn1(xnf_percore, wgu, wpr, p2_percore):
    nc = _cached("ffn1", build_ffn1)
    maps = [{"xnf": xnf_percore[c], "wgu": wgu, "wpr": wpr,
             "p2": p2_percore[c]} for c in range(NCORE)]
    res = _run(nc, maps)
    return [r["x2"] for r in res]



def shard_tokens(x):
    out = []
    for c in range(NCORE):
        b, h = c // 2, c % 2
        out.append(np.ascontiguousarray(x[b, h * T:(h + 1) * T]))
    return out


def unshard_tokens(parts, d2):
    x = np.empty((4, 4096, d2), parts[0].dtype)
    for c in range(NCORE):
        b, h = c // 2, c % 2
        x[b, h * T:(h + 1) * T] = parts[c]
    return x


def tr(a):
    return np.ascontiguousarray(np.swapaxes(a, -1, -2))


def _rope_tables():
    inv = 1.0 / 10000.0 ** (np.arange(0, 64, 2, dtype=np.float64) / 64)
    sign = np.where((np.arange(128) % 64) < 32, -1.0, 1.0)
    cos_half, sin_half = [], []
    for hh in range(2):
        pos = np.arange(hh * T, (hh + 1) * T, dtype=np.float64)
        ang = pos[None, :] * inv[np.arange(128) % 32][:, None]
        cos_half.append(np.cos(ang).astype(np.float32))
        sin_half.append((sign[:, None] * np.sin(ang)).astype(np.float32))
    return cos_half, sin_half


def _tiled_tok(x, w):
    return np.ascontiguousarray(
        x.reshape(32, 128, w).transpose(1, 0, 2).reshape(128, 32 * w))


def _prep_l0attn(qkv0, memories, mnorm, gate):
    mnflat = np.ascontiguousarray(mnorm.reshape(1, 8192), np.float32)
    maps = []
    for c in range(NCORE):
        b = c // 2
        e, o = 2 * b, 2 * b + 1
        heads = range((c % 2) * 8, (c % 2) * 8 + 8)
        qf = np.empty((8, 128, 4096), BFD)
        kf = np.empty((8, 128, 4096), BFD)
        kt = np.empty((8, 128, 4096), BFD)
        vt = np.empty((8, 128, 4096), BFD)
        for j, h in enumerate(heads):
            rows = slice(h * 128, (h + 1) * 128)
            qf[j, :, 0:T] = qkv0[e][rows]
            qf[j, :, T:2 * T] = qkv0[o][rows]
            kf[j, :, 0:T] = qkv0[e][2048 + h * 128:2048 + (h + 1) * 128]
            kf[j, :, T:2 * T] = qkv0[o][2048 + h * 128:2048 + (h + 1) * 128]
            kt[j] = _tiled_tok(np.ascontiguousarray(kf[j].T), 128)
            vrows = slice(4096 + h * 128, 4096 + (h + 1) * 128)
            vfull = np.concatenate([qkv0[e][vrows], qkv0[o][vrows]], axis=1)
            vt[j] = _tiled_tok(np.ascontiguousarray(vfull.T), 128)
        memp = np.zeros((8, 128, 520), np.float32)
        mnsp = np.zeros((8, 128, 4), np.float32)
        gp = np.zeros((8, 1, 1), np.float32)
        for j, h in enumerate(heads):
            for m in range(4):
                memp[j, :, m * 130:m * 130 + 128] = memories[m, h]
                memp[j, :, m * 130 + 128] = mnorm[m, h]
                mnsp[j, :, m] = mnorm[m, h]
            gp[j, 0, 0] = gate[h]
        maps.append(dict(qf=qf, kf=kf, kt=kt, vt=vt, mem=memp, mns=mnsp,
                         gatev=gp, mnflat=mnflat))
    return maps


def _prep_l1attn(roped, v1):
    maps = []
    for c in range(NCORE):
        b, hh = c // 2, c % 2
        e, o = 2 * b, 2 * b + 1
        qh = np.empty((8, 128, 4096), BFD)
        for u in range(8):
            for j in range(2):
                h = hh * 16 + 2 * u + j
                rows = slice(h * 64, (h + 1) * 64)
                qh[u, j * 64:(j + 1) * 64, 0:T] = roped[e][rows]
                qh[u, j * 64:(j + 1) * 64, T:2 * T] = roped[o][rows]
        kf2 = np.empty((2, 128, 4096), BFD)
        vt2 = np.empty((2, 128, 2048), BFD)
        for j in range(2):
            kvh = hh * 2 + j
            rows = slice(2048 + kvh * 64, 2048 + (kvh + 1) * 64)
            kf2[j, 0:64, 0:T] = roped[e][rows]
            kf2[j, 0:64, T:2 * T] = roped[o][rows]
            kf2[j, 64:128] = kf2[j, 0:64]
            vrows = slice(kvh * 64, (kvh + 1) * 64)
            vfull = np.concatenate([v1[e][vrows], v1[o][vrows]], axis=1)
            vt2[j] = _tiled_tok(np.ascontiguousarray(vfull.T), 64)
        maps.append(dict(qh=qh, kf2=kf2, vt2=vt2))
    return maps


def kernel(L0_attn_hidden_states, L0_attn_memories, L0_attn_memory_norms,
           L0_attn_w_q, L0_attn_w_k, L0_attn_w_v, L0_attn_w_o, L0_attn_gate,
           L0_ffn_router_w, L0_ffn_down_w, L0_ffn_up_w,
           L1_attn_w_qkv, L1_attn_w_o, L1_ffn_w_gate_up, L1_ffn_w_proj):
    x = np.ascontiguousarray(np.asarray(L0_attn_hidden_states, np.float32))
    xs = shard_tokens(x)

    xn, _ = run_ln0(xs)
    xn_feat = [tr(a) for a in xn]
    wqkv0 = np.concatenate(
        [np.ascontiguousarray(np.asarray(w, np.float32).T)
         for w in (L0_attn_w_q, L0_attn_w_k, L0_attn_w_v)], axis=1)
    qkv0 = run_gemm("qkv0", [wqkv0], [xn_feat], None, BFD)

    maps0 = _prep_l0attn(qkv0, np.asarray(L0_attn_memories, np.float32),
                         np.asarray(L0_attn_memory_norms, np.float32),
                         np.asarray(L0_attn_gate, np.float32))
    outm = run_l0attn(maps0)

    rwt = np.ascontiguousarray(np.asarray(L0_ffn_router_w, np.float32).T)
    dwt = np.ascontiguousarray(
        np.transpose(np.asarray(L0_ffn_down_w, np.float32),
                     (2, 0, 1)).reshape(2048, 128))
    asct = run_l0ffna(xn_feat, rwt, dwt)

    mixf = []
    for c in range(NCORE):
        b, hh = c // 2, c % 2
        mf = np.empty((2048, T), BFD)
        for src_core, hbase in ((2 * b, 0), (2 * b + 1, 8)):
            om = np.asarray(outm[src_core]).reshape(8, 128, 32, 128)
            for j in range(8):
                h = hbase + j
                blk2 = om[j].transpose(1, 0, 2).reshape(4096, 128)
                mf[h * 128:(h + 1) * 128] = blk2[hh * T:(hh + 1) * T].T
        mixf.append(np.ascontiguousarray(mf))
    wot = np.ascontiguousarray(np.asarray(L0_attn_w_o, np.float32).T)
    upt = np.ascontiguousarray(
        np.transpose(np.asarray(L0_ffn_up_w, np.float32),
                     (0, 2, 1)).reshape(128, 2048))
    xres = [np.ascontiguousarray(x[c // 2, (c % 2) * T:(c % 2 + 1) * T].T)
            for c in range(NCORE)]
    x1 = run_gemm("oproj0", [wot, upt], [mixf, asct], xres, BFD)

    x1tok = [tr(a) for a in x1]
    xn1 = run_ln1(x1tok)
    xn1f = [tr(a) for a in xn1]
    wqkvt = np.ascontiguousarray(np.asarray(L1_attn_w_qkv, np.float32).T)
    qkv1 = run_gemm("qkv1", [wqkvt], [xn1f], None, BFD)
    cos_half, sin_half = _rope_tables()
    qk = [np.ascontiguousarray(qkv1[c][0:2304]) for c in range(NCORE)]
    roped = run_rope(qk, cos_half, sin_half)
    v1 = [np.ascontiguousarray(qkv1[c][2304:2560]) for c in range(NCORE)]

    maps1 = _prep_l1attn(roped, v1)
    oh = run_l1attn(maps1)

    ahf = []
    for c in range(NCORE):
        b, hh = c // 2, c % 2
        af = np.empty((2048, T), BFD)
        tok = slice(hh * T, (hh + 1) * T)
        for src_core, hbase in ((2 * b, 0), (2 * b + 1, 16)):
            blk2 = np.asarray(oh[src_core]).reshape(16, 64, 4096)
            for uh in range(16):
                h = hbase + uh
                af[h * 64:(h + 1) * 64] = blk2[uh][:, tok]
        ahf.append(np.ascontiguousarray(af))
    wot1 = np.ascontiguousarray(np.asarray(L1_attn_w_o, np.float32).T)
    p2 = run_gemm("oproj1", [wot1], [ahf], x1, BFD)

    wguT = np.ascontiguousarray(np.asarray(L1_ffn_w_gate_up, np.float32).T)
    wprT = np.ascontiguousarray(np.asarray(L1_ffn_w_proj, np.float32).T)
    w1s = [np.ascontiguousarray(wguT[:, c * 2048:(c + 1) * 2048])
           for c in range(NCORE)]
    w2s = [np.ascontiguousarray(wprT[c * 1024:(c + 1) * 1024])
           for c in range(NCORE)]
    q1, q2 = run_quant(w1s, w2s)
    wgu_q = np.ascontiguousarray(np.concatenate(
        [np.asarray(q) for q in q1], axis=1))
    wpr_q = np.ascontiguousarray(np.concatenate(
        [np.asarray(q) for q in q2], axis=0))

    x2 = run_ffn1(xn1f, wgu_q, wpr_q, p2)
    out = unshard_tokens([np.ascontiguousarray(tr(np.asarray(a, np.float32)))
                          for a in x2], 2048)
    return out


# version 02
# speedup vs baseline: 1.0284x; 1.0284x over previous
import contextlib
import numpy as np
import ml_dtypes
import concourse.bass as bass
import concourse.mybir as mybir
from concourse.bass_utils import run_bass_kernel_spmd

F32 = mybir.dt.float32
BF16 = mybir.dt.bfloat16
AF = mybir.ActivationFunctionType
ALU = mybir.AluOpType
AX = mybir.AxisListType.X
NCORE = 8
T = 2048
D = 2048
BFD = ml_dtypes.bfloat16

_prog_cache = {}


def _cached(key, fn):
    if key not in _prog_cache:
        _prog_cache[key] = fn()
    return _prog_cache[key]


def _run(nc, in_maps):
    res = run_bass_kernel_spmd(nc, in_maps, core_ids=list(range(NCORE)))
    return res.results


def build_ln(with_sum):
    nc = bass.Bass("TRN2", target_bir_lowering=False, detect_race_conditions=False)
    x = nc.dram_tensor("x", [T, D], BF16 if not with_sum else F32,
                       kind="ExternalInput").ap()
    xn = nc.dram_tensor("xn", [T, D], BF16, kind="ExternalOutput").ap()
    xs = nc.dram_tensor("xsum", [1, D], F32, kind="ExternalOutput").ap()
    ones_d = nc.inline_tensor(np.ones((128, 1), BFD), "ones_d").ap()
    eps_d = nc.inline_tensor(np.full((128, 1), 1e-5, np.float32), "eps_d").ap()
    NT = T // 128
    with contextlib.ExitStack() as st:
        E = st.enter_context
        xdt = F32 if with_sum else BF16
        xsb = [E(nc.sbuf_tensor(f"xsb{i}", [128, D], xdt)) for i in range(2)]
        sq = E(nc.sbuf_tensor("sq", [128, D], F32))
        xnb = [E(nc.sbuf_tensor(f"xnb{i}", [128, D], BF16)) for i in range(2)]
        stat = E(nc.sbuf_tensor("stat", [128, 8], F32))
        ones_sb = E(nc.sbuf_tensor("ones_sb", [128, 1], BF16))
        eps_sb = E(nc.sbuf_tensor("eps_sb", [128, 1], F32))
        xs_sb = E(nc.sbuf_tensor("xs_sb", [1, D], F32))
        ps = E(nc.psum_tensor("ps", [1, D], F32))
        olds = E(nc.semaphore("olds"))
        vs = E(nc.semaphore("vs"))
        ld2 = [E(nc.semaphore(f"ld{p}")) for p in range(2)]
        ev = E(nc.semaphore("ev"))
        sdone = E(nc.semaphore("sdone"))
        tmm = E(nc.semaphore("tmm"))
        stv = E(nc.semaphore("stv"))
        blk = E(nc.Block())

        @blk.sync
        def _(sy):
            sy.dma_start(out=ones_sb[:], in_=ones_d).then_inc(olds, 16)
            sy.dma_start(out=eps_sb[:], in_=eps_d).then_inc(olds, 16)
            for i in range(NT):
                if i >= 2:
                    sy.wait_ge(ev, i - 1)
                    sy.dma_start(out=xn[(i - 2) * 128:(i - 1) * 128, :],
                                 in_=xnb[(i - 2) % 2][:]).then_inc(stv, 16)
                sy.dma_start(out=xsb[i % 2][:], in_=x[i * 128:(i + 1) * 128, :]
                             ).then_inc(ld2[i % 2], 16)
            sy.wait_ge(ev, NT - 1)
            sy.dma_start(out=xn[(NT - 2) * 128:(NT - 1) * 128, :],
                         in_=xnb[(NT - 2) % 2][:]).then_inc(stv, 16)
            sy.wait_ge(ev, NT)
            sy.dma_start(out=xn[(NT - 1) * 128:NT * 128, :],
                         in_=xnb[(NT - 1) % 2][:]).then_inc(stv, 16)
            if with_sum:
                sy.wait_ge(ev, NT + 1)
                sy.dma_start(out=xs, in_=xs_sb[:]).then_inc(stv, 16)
            sy.wait_ge(stv, 16 * (NT + (1 if with_sum else 0)))

        @blk.vector
        def _(v):
            nvs = 0
            for i in range(NT):
                v.wait_ge(ld2[i % 2], 16 * (i // 2 + 1))
                if i >= 2:
                    if with_sum:
                        v.wait_ge(tmm, i - 1)
                    v.wait_ge(stv, 16 * (i - 1))
                b = xsb[i % 2]
                o = xnb[i % 2]
                v.reduce_sum(stat[:, 0:1], b[:], axis=AX)
                v.tensor_tensor(out=sq[:], in0=b[:], in1=b[:], op=ALU.mult)
                v.reduce_sum(stat[:, 2:3], sq[:], axis=AX).then_inc(vs, 1)
                nvs += 1
                v.wait_ge(vs, nvs)
                v.tensor_scalar(out=stat[:, 1:2], in0=stat[:, 0:1],
                                scalar1=1.0 / D, scalar2=None, op0=ALU.mult)
                v.tensor_scalar(out=stat[:, 3:4], in0=stat[:, 2:3],
                                scalar1=1.0 / D, scalar2=None,
                                op0=ALU.mult).then_inc(vs, 1)
                nvs += 1
                v.wait_ge(vs, nvs)
                v.tensor_tensor(out=stat[:, 4:5], in0=stat[:, 1:2],
                                in1=stat[:, 1:2], op=ALU.mult).then_inc(vs, 1)
                nvs += 1
                v.wait_ge(vs, nvs)
                v.tensor_tensor(out=stat[:, 5:6], in0=stat[:, 3:4],
                                in1=stat[:, 4:5],
                                op=ALU.subtract).then_inc(sdone, 1)
                v.wait_ge(sdone, 2 * (i + 1))
                v.reciprocal(stat[:, 7:8], stat[:, 6:7]).then_inc(vs, 1)
                nvs += 1
                v.wait_ge(vs, nvs)
                v.tensor_scalar(out=sq[:], in0=b[:], scalar1=stat[:, 1:2],
                                scalar2=None, op0=ALU.subtract)
                v.tensor_scalar(out=o[:], in0=sq[:], scalar1=stat[:, 7:8],
                                scalar2=None, op0=ALU.mult).then_inc(ev, 1)
            if with_sum:
                v.wait_ge(tmm, NT)
                v.tensor_copy(xs_sb[:], ps[:]).then_inc(ev, 1)

        @blk.scalar
        def _(s):
            s.wait_ge(olds, 32)
            for i in range(NT):
                s.wait_ge(sdone, 2 * i + 1)
                s.activation(out=stat[:, 6:7], in_=stat[:, 5:6],
                             func=AF.Sqrt, bias=eps_sb[:, 0:1],
                             scale=1.0).then_inc(sdone, 1)

        if with_sum:
            @blk.tensor
            def _(t):
                t.wait_ge(olds, 16)
                for i in range(NT):
                    t.wait_ge(ev, i + 1)
                    for c in range(D // 512):
                        ins = t.matmul(ps[0:1, c * 512:(c + 1) * 512], ones_sb[:],
                                       xnb[i % 2][:, c * 512:(c + 1) * 512],
                                       start=(i == 0), stop=(i == NT - 1))
                    ins.then_inc(tmm, 1)
    return nc


def run_ln0(x_tok_percore):
    nc = _cached("ln0", lambda: build_ln(True))
    maps = [{"x": xc} for xc in x_tok_percore]
    res = _run(nc, maps)
    return ([r["xn"] for r in res], [r["xsum"] for r in res])


def run_ln1(x1_tok_percore):
    nc = _cached("ln1", lambda: build_ln(False))
    maps = [{"x": xc} for xc in x1_tok_percore]
    res = _run(nc, maps)
    return [r["xn"] for r in res]


def build_gemm(key, k_list, fout, x_dtypes, w_dtypes, res_dtype, out_dtype,
               t_total=T):
    nc = bass.Bass("TRN2", target_bir_lowering=False, detect_race_conditions=False)
    nj = len(k_list)
    ws = [nc.dram_tensor(f"w{j}", [k_list[j], fout], w_dtypes[j],
                         kind="ExternalInput").ap() for j in range(nj)]
    xs = [nc.dram_tensor(f"x{j}", [k_list[j], t_total], x_dtypes[j],
                         kind="ExternalInput").ap() for j in range(nj)]
    res_in = (nc.dram_tensor("res", [fout, t_total], res_dtype,
                             kind="ExternalInput").ap()
              if res_dtype is not None else None)
    out = nc.dram_tensor("out", [fout, t_total], out_dtype,
                         kind="ExternalOutput").ap()
    OG = fout // 128
    TC = t_total // 512
    NKT = [k // 128 for k in k_list]
    nk_tot = sum(NKT)
    with contextlib.ExitStack() as st:
        E = st.enter_context
        x_sb = [E(nc.sbuf_tensor(f"x_sb{j}", [128, NKT[j] * t_total], BF16))
                for j in range(nj)]
        w_sb = [[E(nc.sbuf_tensor(f"w_sb{j}_{p}", [128, NKT[j] * 128], BF16))
                 for j in range(nj)] for p in range(2)]
        r_sb = [E(nc.sbuf_tensor(f"r_sb{p}", [128, 512],
                                 res_dtype if res_dtype else BF16))
                for p in range(2)] if res_dtype is not None else None
        o_sb = [E(nc.sbuf_tensor(f"o_sb{p}", [128, 512], out_dtype))
                for p in range(4)]
        ps = [E(nc.psum_tensor(f"ps{p}", [128, 512], F32)) for p in range(4)]
        xld = E(nc.semaphore("xld"))
        wld2 = [E(nc.semaphore(f"wld{p}")) for p in range(2)]
        rld = E(nc.semaphore("rld"))
        mmd = E(nc.semaphore("mmd"))
        evd = E(nc.semaphore("evd"))
        stv4 = [E(nc.semaphore(f"stv{p}")) for p in range(4)]
        blk = E(nc.Block())

        @blk.gpsimd
        def _(g):
            for j in range(nj):
                for kt in range(NKT[j]):
                    g.dma_start(out=x_sb[j][:, kt * t_total:(kt + 1) * t_total],
                                in_=xs[j][kt * 128:(kt + 1) * 128, :]
                                ).then_inc(xld, 16)
            for og in range(OG):
                if og >= 2:
                    g.wait_ge(mmd, (og - 1) * TC)
                for j in range(nj):
                    g.dma_start(
                        out=w_sb[og % 2][j].ap().rearrange(
                            "p (a f) -> p a f", f=128),
                        in_=ws[j][:, og * 128:(og + 1) * 128].rearrange(
                            "(a p) f -> p a f", p=128)
                    ).then_inc(wld2[og % 2], 16)

        @blk.sync
        def _(sy):
            for og in range(OG):
                for tc in range(TC):
                    gidx = og * TC + tc
                    if res_in is not None:
                        if gidx >= 2:
                            sy.wait_ge(evd, gidx - 1)
                        sy.dma_start(out=r_sb[gidx % 2][:],
                                     in_=res_in[og * 128:(og + 1) * 128,
                                                tc * 512:(tc + 1) * 512]
                                     ).then_inc(rld, 16)
                    sy.wait_ge(evd, gidx + 1)
                    sy.dma_start(out=out[og * 128:(og + 1) * 128,
                                         tc * 512:(tc + 1) * 512],
                                 in_=o_sb[gidx % 4][:]).then_inc(stv4[gidx % 4], 16)
            ngr = OG * TC
            for p in range(4):
                sy.wait_ge(stv4[p], 16 * ((ngr - 1 - p) // 4 + 1))

        @blk.tensor
        def _(t):
            t.wait_ge(xld, 16 * nk_tot)
            for og in range(OG):
                t.wait_ge(wld2[og % 2], 16 * nj * (og // 2 + 1))
                for tc in range(TC):
                    gidx = og * TC + tc
                    if gidx >= 4:
                        t.wait_ge(evd, gidx - 3)
                    p = ps[gidx % 4]
                    n = 0
                    for j in range(nj):
                        for kt in range(NKT[j]):
                            n += 1
                            ins = t.matmul(
                                p[:],
                                w_sb[og % 2][j][:, kt * 128:(kt + 1) * 128],
                                x_sb[j][:, kt * t_total + tc * 512:
                                        kt * t_total + (tc + 1) * 512],
                                start=(n == 1), stop=(n == nk_tot))
                    ins.then_inc(mmd, 1)

        @blk.vector
        def _(v):
            for og in range(OG):
                for tc in range(TC):
                    gidx = og * TC + tc
                    v.wait_ge(mmd, gidx + 1)
                    if gidx >= 4:
                        v.wait_ge(stv4[gidx % 4], 16 * ((gidx - 4) // 4 + 1))
                    if res_in is not None:
                        v.wait_ge(rld, 16 * (gidx + 1))
                        v.tensor_tensor(out=o_sb[gidx % 4][:],
                                        in0=ps[gidx % 4][:],
                                        in1=r_sb[gidx % 2][:],
                                        op=ALU.add).then_inc(evd, 1)
                    else:
                        v.tensor_copy(o_sb[gidx % 4][:],
                                      ps[gidx % 4][:]).then_inc(evd, 1)
    return nc


def run_gemm(key, w_full_list, x_percore_list, res_percore, out_dtype_np):
    k_list = [w.shape[0] for w in w_full_list]
    fout = w_full_list[0].shape[1]
    w_dt = [F32 if w.dtype == np.float32 else BF16 for w in w_full_list]
    x_dt = [BF16] * len(w_full_list)
    res_dt = None
    if res_percore is not None:
        res_dt = F32 if res_percore[0].dtype == np.float32 else BF16
    out_dt = F32 if out_dtype_np == np.float32 else BF16
    nc = _cached(("gemm", key), lambda: build_gemm(
        key, tuple(k_list), fout, tuple(x_dt), tuple(w_dt), res_dt, out_dt))
    maps = []
    for c in range(NCORE):
        m = {}
        for j, w in enumerate(w_full_list):
            m[f"w{j}"] = w
            m[f"x{j}"] = x_percore_list[j][c]
        if res_percore is not None:
            m["res"] = res_percore[c]
        maps.append(m)
    res = _run(nc, maps)
    return [r["out"] for r in res]



def build_l0attn():
    S = 4096
    NCH = 32
    NP = 8
    SCALE = 128.0 ** -0.5
    nc = bass.Bass("TRN2", target_bir_lowering=False, detect_race_conditions=False)
    qf = nc.dram_tensor("qf", [NP, 128, S], BF16, kind="ExternalInput").ap()
    kf = nc.dram_tensor("kf", [NP, 128, S], BF16, kind="ExternalInput").ap()
    kt = nc.dram_tensor("kt", [NP, 128, S], BF16, kind="ExternalInput").ap()
    vt = nc.dram_tensor("vt", [NP, 128, S], BF16, kind="ExternalInput").ap()
    mem = nc.dram_tensor("mem", [NP, 128, 520], F32, kind="ExternalInput").ap()
    mns = nc.dram_tensor("mns", [NP, 128, 4], F32, kind="ExternalInput").ap()
    gatev = nc.dram_tensor("gatev", [NP, 1, 1], F32, kind="ExternalInput").ap()
    mnflat = nc.dram_tensor("mnflat", [1, 8192], F32, kind="ExternalInput").ap()
    outm = nc.dram_tensor("outm", [NP, 128, S], BF16, kind="ExternalOutput").ap()
    mask_np = (np.arange(128)[None, :] >= np.arange(128)[:, None]).astype(BFD)
    mask_d = nc.inline_tensor(mask_np, "mask_d").ap()
    ones1_d = nc.inline_tensor(np.ones((1, 128), BFD), "ones1_d").ap()

    VN = 4 + 3 * NCH
    TN = 2 + 2 * NCH
    EN = 10

    def v_relq(p): return VN * p + 1
    def v_b6(p): return VN * p + 2
    def v_memw(p): return VN * p + 3
    def v_state0(p): return VN * p + 4
    def v_scorem(p, c): return VN * p + 5 + 3 * c
    def v_chunkev(p, c): return VN * p + 7 + 3 * c
    def v_end(p): return VN * (p + 1)

    def t_rel(p): return TN * p + 1
    def t_b6(p): return TN * p + 2
    def t_score(p, c): return TN * p + 3 + 2 * c
    def t_chunk(p, c): return TN * p + 4 + 2 * c

    def e_elu_v(p, i): return EN * p + 2 * i + 1
    def e_elu_s(p, i): return EN * p + 2 * i + 2
    def e_rel_v(p): return EN * p + 7
    def e_rel_s(p): return EN * p + 8
    def e_gate_v(p): return EN * p + 9
    def e_gate_s(p): return EN * p + 10

    with contextlib.ExitStack() as st:
        E = st.enter_context
        q_sb = [E(nc.sbuf_tensor(f"q_sb{p}", [128, S], BF16)) for p in range(2)]
        kf_sb = [E(nc.sbuf_tensor(f"kf_sb{p}", [128, S], BF16)) for p in range(2)]
        kt_sb = [E(nc.sbuf_tensor(f"kt_sb{p}", [128, S], BF16)) for p in range(2)]
        va_sb = [E(nc.sbuf_tensor(f"va_sb{p}", [128, NCH * 132], BF16))
                 for p in range(2)]
        mem_sb = [E(nc.sbuf_tensor(f"mem_sb{p}", [128, 520], F32)) for p in range(2)]
        mns_sb = [E(nc.sbuf_tensor(f"mns_sb{p}", [128, 4], BF16)) for p in range(2)]
        gt_sb = [E(nc.sbuf_tensor(f"gt_sb{p}", [1, 1], F32)) for p in range(2)]
        sgq = E(nc.sbuf_tensor("sgq", [128, S], BF16))
        kpf = E(nc.sbuf_tensor("kpf", [128, S], BF16))
        kpt = E(nc.sbuf_tensor("kpt", [128, S], BF16))
        tmp1 = E(nc.sbuf_tensor("tmp1", [128, S], BF16))
        tmp2 = E(nc.sbuf_tensor("tmp2", [128, S], BF16))
        memw = E(nc.sbuf_tensor("memw", [128, 520], BF16))
        ob = E(nc.sbuf_tensor("ob", [128, S], BF16))
        mask_sb = E(nc.sbuf_tensor("mask_sb", [128, 128], BF16))
        ones1_sb = E(nc.sbuf_tensor("ones1_sb", [1, 128], BF16))
        mnf_sb = E(nc.sbuf_tensor("mnf_sb", [1, 8192], F32))
        emt = E(nc.sbuf_tensor("emt", [1, 16], F32))
        relb = E(nc.sbuf_tensor("relb", [1, 32], F32))
        wg6 = E(nc.sbuf_tensor("wg6", [1, 6], BF16))
        wgb = E(nc.sbuf_tensor("wgb", [128, 6], F32))
        qsum = E(nc.sbuf_tensor("qsum", [128, 1], F32))
        qsum_bf = E(nc.sbuf_tensor("qsum_bf", [128, 1], BF16))
        S_f = [E(nc.sbuf_tensor(f"S_f{p}", [128, 129], F32)) for p in range(2)]
        S_bf = E(nc.sbuf_tensor("S_bf", [128, 129], BF16))
        pstm = E(nc.sbuf_tensor("pstm", [128, 128], BF16))
        den5 = E(nc.sbuf_tensor("den5", [128, 12], F32))
        loc = E(nc.sbuf_tensor("loc", [128, 128], F32))
        acc = [E(nc.sbuf_tensor(f"acc{p}", [128, 128], F32)) for p in range(2)]
        ps_rel = E(nc.psum_tensor("ps_rel", [1, 8], F32))
        ps_b6 = E(nc.psum_tensor("ps_b6", [128, 6], F32))
        ps_s = E(nc.psum_tensor("ps_s", [128, 128], F32))
        ps_out = E(nc.psum_tensor("ps_out", [128, 129], F32))
        ps_am1 = E(nc.psum_tensor("ps_am1", [128, 260], F32))
        ps_am2 = E(nc.psum_tensor("ps_am2", [128, 260], F32))
        ps_kv = E(nc.psum_tensor("ps_kv", [128, 129], F32))
        cld = E(nc.semaphore("cld"))
        msd = E(nc.semaphore("msd"))
        pld = [E(nc.semaphore(f"pld{p}")) for p in range(2)]
        tsem = E(nc.semaphore("tsem"))
        vsem = E(nc.semaphore("vsem"))
        esem = E(nc.semaphore("esem"))
        vs = E(nc.semaphore("vs"))
        stv = E(nc.semaphore("stv"))
        blk = E(nc.Block())

        @blk.gpsimd
        def _(g):
            g.dma_start(out=mask_sb[:], in_=mask_d).then_inc(cld, 16)
            g.dma_start(out=ones1_sb[:], in_=ones1_d).then_inc(cld, 16)
            g.dma_start(out=mnf_sb[:], in_=mnflat).then_inc(cld, 16)
            g.memset(va_sb[0][:], 1.0)
            g.memset(va_sb[1][:], 1.0)
            g.memset(S_f[0][:], 0.0)
            g.memset(S_bf[:], 0.0).then_inc(msd, 1)
            for p in range(NP):
                par = p % 2
                if p >= 2:
                    g.wait_ge(vsem, v_end(p - 2))
                    g.wait_ge(tsem, t_chunk(p - 2, NCH - 1))
                g.dma_start(out=q_sb[par][:], in_=qf[p]).then_inc(pld[par], 16)
                g.dma_start(out=kf_sb[par][:], in_=kf[p]).then_inc(pld[par], 16)
                g.dma_start(out=kt_sb[par][:], in_=kt[p]).then_inc(pld[par], 16)
                g.dma_start(
                    out=va_sb[par].ap().rearrange("p (c w) -> p c w", w=132
                                                  )[:, :, 0:128],
                    in_=vt[p].rearrange("p (c w) -> p c w", w=128)
                ).then_inc(pld[par], 16)
                g.dma_start(out=mem_sb[par][:], in_=mem[p]).then_inc(pld[par], 16)
                g.dma_start(out=mns_sb[par][:], in_=mns[p]).then_inc(pld[par], 16)
                g.dma_start(out=gt_sb[par][:], in_=gatev[p]).then_inc(pld[par], 16)

        @blk.vector
        def _(v):
            nvs = 0

            def sync(ins):
                nonlocal nvs
                nvs += 1
                ins.then_inc(vs, 1)
                v.wait_ge(vs, nvs)

            v.wait_ge(cld, 48)
            sync(v.tensor_reduce(emt[:, 0:4], mnf_sb.ap().rearrange(
                "p (m d) -> p m d", d=2048), axis=AX, op=ALU.add))
            sync(v.tensor_scalar(out=emt[:, 4:8], in0=emt[:, 0:4],
                                 scalar1=1e-6, scalar2=None, op0=ALU.is_lt))
            sync(v.tensor_reduce(emt[:, 8:9], emt[:, 4:8], axis=AX, op=ALU.add))
            sync(v.tensor_scalar(out=emt[:, 9:10], in0=emt[:, 8:9],
                                 scalar1=3.5, scalar2=None, op0=ALU.is_lt))

            for p in range(NP):
                par = p % 2
                v.wait_ge(pld[par], 16 * 7 * (p // 2 + 1))
                if p >= 1:
                    v.wait_ge(stv, 16 * p)
                for i, (srcb, dstb) in enumerate(
                        ((q_sb[par], sgq), (kf_sb[par], kpf), (kt_sb[par], kpt))):
                    v.tensor_scalar(out=tmp1[:], in0=srcb[:], scalar1=0.0,
                                    scalar2=None, op0=ALU.min).then_inc(esem, 1)
                    v.wait_ge(esem, e_elu_s(p, i))
                    v.scalar_tensor_tensor(out=dstb[:], in0=srcb[:], scalar=0.0,
                                           in1=tmp2[:], op0=ALU.max, op1=ALU.add)
                sync(v.reduce_sum(qsum[:], q_sb[par][:], axis=AX))
                v.tensor_copy(qsum_bf[:], qsum[:]).then_inc(vsem, 1)
                v.wait_ge(tsem, t_rel(p))
                sync(v.tensor_scalar(out=relb[:, 0:4], in0=ps_rel[0:1, 0:4],
                                     scalar1=SCALE / 4096.0, scalar2=None,
                                     op0=ALU.mult))
                sync(v.scalar_tensor_tensor(out=relb[:, 4:8], in0=emt[:, 4:8],
                                            scalar=-1e38, in1=relb[:, 0:4],
                                            op0=ALU.mult, op1=ALU.add))
                v.reduce_max(relb[:, 8:9], relb[:, 4:8], axis=AX,
                             negate=True).then_inc(esem, 1)
                v.wait_ge(esem, e_rel_s(p))
                sync(v.reduce_sum(relb[:, 14:15], relb[:, 10:14], axis=AX))
                sync(v.reciprocal(relb[:, 15:16], relb[:, 14:15]))
                sync(v.tensor_scalar(out=relb[:, 16:20], in0=relb[:, 10:14],
                                     scalar1=relb[:, 15:16], scalar2=None,
                                     op0=ALU.mult))
                v.tensor_scalar(out=relb[:, 24:28], in0=relb[:, 16:20],
                                scalar1=emt[:, 9:10], scalar2=None,
                                op0=ALU.mult).then_inc(esem, 1)
                v.wait_ge(esem, e_gate_s(p))
                sync(v.tensor_scalar(out=relb[:, 29:30], in0=relb[:, 28:29],
                                     scalar1=-1.0, scalar2=1.0, op0=ALU.mult,
                                     op1=ALU.add))
                v.tensor_copy(wg6[:], relb[:, 24:30]).then_inc(vsem, 1)
                v.wait_ge(tsem, t_b6(p))
                sync(v.tensor_copy(wgb[:], ps_b6[:]))
                mv = memw.ap().rearrange("p (m w) -> p m w", w=130)
                ms = mem_sb[par].ap().rearrange("p (m w) -> p m w", w=130)
                for m in range(4):
                    v.tensor_scalar(out=memw[:, m * 130:m * 130 + 128],
                                    in0=mem_sb[par][:, m * 130:m * 130 + 128],
                                    scalar1=wgb[:, m:m + 1], scalar2=None,
                                    op0=ALU.mult)
                v.tensor_copy(mv[:, :, 128:129],
                              ms[:, :, 128:129]).then_inc(vsem, 1)
                if p == 0:
                    v.wait_ge(msd, 1)
                v.tensor_scalar(out=S_f[0][:], in0=S_f[0][:], scalar1=0.0,
                                scalar2=None, op0=ALU.mult)
                v.tensor_scalar(out=S_bf[:], in0=S_f[0][:], scalar1=0.0,
                                scalar2=None, op0=ALU.mult).then_inc(vsem, 1)
                for c in range(NCH):
                    cpar = c % 2
                    v.wait_ge(tsem, t_score(p, c))
                    v.tensor_tensor(out=pstm[:], in0=ps_s[:], in1=mask_sb[:],
                                    op=ALU.mult).then_inc(vsem, 1)
                    v.wait_ge(tsem, t_chunk(p, c))
                    v.tensor_tensor(out=S_f[1 - cpar][:], in0=S_f[cpar][:],
                                    in1=ps_kv[:], op=ALU.add)
                    v.tensor_copy(S_bf[:], S_f[1 - cpar][:]).then_inc(vsem, 1)
                    v.tensor_copy(den5[:, 0:1], ps_out[:, 128:129])
                    v.tensor_copy(den5[:, 1:2], ps_am1[:, 128:129])
                    v.tensor_copy(den5[:, 2:3], ps_am1[:, 258:259])
                    v.tensor_copy(den5[:, 3:4], ps_am2[:, 128:129])
                    sync(v.tensor_copy(den5[:, 4:5], ps_am2[:, 258:259]))
                    sync(v.tensor_scalar(out=den5[:, 6:11], in0=den5[:, 0:5],
                                         scalar1=1e-6, scalar2=None,
                                         op0=ALU.max))
                    sync(v.reciprocal(den5[:, 0:5], den5[:, 6:11]))
                    v.tensor_scalar(out=loc[:], in0=ps_out[:, 0:128],
                                    scalar1=den5[:, 0:1], scalar2=None,
                                    op0=ALU.mult)
                    v.tensor_scalar(out=acc[0][:], in0=ps_am1[:, 0:128],
                                    scalar1=den5[:, 1:2], scalar2=None,
                                    op0=ALU.mult)
                    v.scalar_tensor_tensor(out=acc[1][:], in0=ps_am1[:, 130:258],
                                           scalar=den5[:, 2:3], in1=acc[0][:],
                                           op0=ALU.mult, op1=ALU.add)
                    v.scalar_tensor_tensor(out=acc[0][:], in0=ps_am2[:, 0:128],
                                           scalar=den5[:, 3:4], in1=acc[1][:],
                                           op0=ALU.mult, op1=ALU.add)
                    v.scalar_tensor_tensor(out=acc[1][:], in0=ps_am2[:, 130:258],
                                           scalar=den5[:, 4:5], in1=acc[0][:],
                                           op0=ALU.mult, op1=ALU.add)
                    v.tensor_scalar(out=loc[:], in0=loc[:], scalar1=wgb[:, 5:6],
                                    scalar2=None, op0=ALU.mult)
                    v.scalar_tensor_tensor(
                        out=ob[:, c * 128:(c + 1) * 128], in0=acc[1][:],
                        scalar=wgb[:, 4:5], in1=loc[:], op0=ALU.mult,
                        op1=ALU.add).then_inc(vsem, 1)

        @blk.scalar
        def _(s):
            for p in range(NP):
                for i in range(3):
                    s.wait_ge(esem, e_elu_v(p, i))
                    s.activation(out=tmp2[:], in_=tmp1[:],
                                 func=AF.Exp).then_inc(esem, 1)
                s.wait_ge(esem, e_rel_v(p))
                s.activation(out=relb[:, 10:14], in_=relb[:, 4:8], func=AF.Exp,
                             bias=relb[:, 8:9], scale=1.0).then_inc(esem, 1)
                s.wait_ge(esem, e_gate_v(p))
                s.activation(out=relb[:, 28:29], in_=gt_sb[p % 2][:],
                             func=AF.Sigmoid).then_inc(esem, 1)

        @blk.tensor
        def _(t):
            t.wait_ge(cld, 48)
            t.wait_ge(msd, 1)
            for p in range(NP):
                par = p % 2
                t.wait_ge(vsem, v_relq(p))
                t.matmul(ps_rel[0:1, 0:4], qsum_bf[:], mns_sb[par][:],
                         start=True, stop=True).then_inc(tsem, 1)
                t.wait_ge(vsem, v_b6(p))
                t.matmul(ps_b6[:], ones1_sb[:], wg6[:],
                         start=True, stop=True).then_inc(tsem, 1)
                t.wait_ge(vsem, v_memw(p))
                for c in range(NCH):
                    cs = slice(c * 128, (c + 1) * 128)
                    vsl = slice(c * 132, c * 132 + 129)
                    if c == 0:
                        t.wait_ge(vsem, v_state0(p))
                    else:
                        t.wait_ge(vsem, v_chunkev(p, c - 1))
                    t.matmul(ps_s[:], kpf[:, cs], sgq[:, cs],
                             start=True, stop=True).then_inc(tsem, 1)
                    t.matmul(ps_out[:], sgq[:, cs], S_bf[:],
                             start=True, stop=False)
                    t.matmul(ps_am1[:, 0:129], sgq[:, cs], memw[:, 0:129],
                             start=True, stop=True)
                    t.matmul(ps_am1[:, 130:259], sgq[:, cs], memw[:, 130:259],
                             start=True, stop=True)
                    t.matmul(ps_am2[:, 0:129], sgq[:, cs], memw[:, 260:389],
                             start=True, stop=True)
                    t.matmul(ps_am2[:, 130:259], sgq[:, cs], memw[:, 390:519],
                             start=True, stop=True)
                    t.wait_ge(vsem, v_scorem(p, c))
                    t.matmul(ps_out[:], pstm[:], va_sb[par][:, vsl],
                             start=False, stop=True)
                    t.matmul(ps_kv[:], kpt[:, cs], va_sb[par][:, vsl],
                             start=True, stop=True).then_inc(tsem, 1)

        @blk.sync
        def _(sy):
            for p in range(NP):
                sy.wait_ge(vsem, v_end(p))
                sy.dma_start(out=outm[p], in_=ob[:]).then_inc(stv, 16)
            sy.wait_ge(stv, 16 * NP)
    return nc


def run_l0attn(in_percore):
    nc = _cached("l0attn", build_l0attn)
    res = _run(nc, in_percore)
    return [r["outm"] for r in res]


def build_l0ffna():
    NK = D // 128
    NT = T // 128
    nc = bass.Bass("TRN2", target_bir_lowering=False, detect_race_conditions=False)
    xnf = nc.dram_tensor("xnf", [D, T], BF16, kind="ExternalInput").ap()
    rwt = nc.dram_tensor("rwt", [D, 8], F32, kind="ExternalInput").ap()
    dwt = nc.dram_tensor("dwt", [D, 128], F32, kind="ExternalInput").ap()
    aout_d = nc.dram_tensor("aout", [128, T], BF16, kind="ExternalOutput").ap()
    ident_d = nc.inline_tensor(np.eye(128, dtype=BFD), "ident_d").ap()

    with contextlib.ExitStack() as st:
        E = st.enter_context
        xn_sb = E(nc.sbuf_tensor("xn_sb", [128, NK * T], BF16))
        rw_sb = E(nc.sbuf_tensor("rw_sb", [128, NK * 8], BF16))
        dw_sb = E(nc.sbuf_tensor("dw_sb", [128, NK * 128], BF16))
        ident = E(nc.sbuf_tensor("ident", [128, 128], BF16))
        p8 = E(nc.sbuf_tensor("p8", [128, 8], F32))
        w8 = E(nc.sbuf_tensor("w8", [128, 8], F32))
        w2b = E(nc.sbuf_tensor("w2b", [128, 8], F32))
        g8 = E(nc.sbuf_tensor("g8", [128, 8], F32))
        stt_ = E(nc.sbuf_tensor("stt_", [128, 8], F32))
        a_sb = E(nc.sbuf_tensor("a_sb", [128, 128], F32))
        asc = E(nc.sbuf_tensor("asc", [128, 128], BF16))
        aout = E(nc.sbuf_tensor("aout_sb", [128, T], BF16))
        ps_r = E(nc.psum_tensor("ps_r", [128, 8], F32))
        ps_h = E(nc.psum_tensor("ps_h", [128, 128], F32))
        ps_t = E(nc.psum_tensor("ps_t", [128, 128], BF16))
        ld = E(nc.semaphore("ld"))
        tsem = E(nc.semaphore("tsem"))
        vsem = E(nc.semaphore("vsem"))
        esem = E(nc.semaphore("esem"))
        vs = E(nc.semaphore("vs"))
        stv = E(nc.semaphore("stv"))
        blk = E(nc.Block())

        @blk.gpsimd
        def _(g):
            g.dma_start(out=xn_sb.ap().rearrange("p (a t) -> p a t", t=T),
                        in_=xnf.rearrange("(a p) t -> p a t", p=128)
                        ).then_inc(ld, 16)
            g.dma_start(out=rw_sb.ap().rearrange("p (a f) -> p a f", f=8),
                        in_=rwt.rearrange("(a p) f -> p a f", p=128)
                        ).then_inc(ld, 16)
            g.dma_start(out=dw_sb.ap().rearrange("p (a f) -> p a f", f=128),
                        in_=dwt.rearrange("(a p) f -> p a f", p=128)
                        ).then_inc(ld, 16)
            g.dma_start(out=ident[:], in_=ident_d).then_inc(ld, 16)

        @blk.tensor
        def _(t):
            t.wait_ge(ld, 64)
            for i in range(NT):
                if i >= 1:
                    t.wait_ge(vsem, 2 * i)
                for k in range(NK):
                    t.matmul(ps_r[:], xn_sb[:, k * T + i * 128:k * T + (i + 1) * 128],
                             rw_sb[:, k * 8:(k + 1) * 8],
                             start=(k == 0), stop=(k == NK - 1))
                for k in range(NK):
                    ins = t.matmul(ps_h[:],
                                   xn_sb[:, k * T + i * 128:k * T + (i + 1) * 128],
                                   dw_sb[:, k * 128:(k + 1) * 128],
                                   start=(k == 0), stop=(k == NK - 1))
                ins.then_inc(tsem, 1)
                t.wait_ge(vsem, 2 * i + 1)
                t.transpose(ps_t[:], asc[:], ident[:]).then_inc(tsem, 1)

        @blk.vector
        def _(v):
            nvs = 0

            def sync(ins):
                nonlocal nvs
                nvs += 1
                ins.then_inc(vs, 1)
                v.wait_ge(vs, nvs)

            for i in range(NT):
                v.wait_ge(tsem, 2 * i + 1)
                v.reduce_max(stt_[:, 0:1], ps_r[:], axis=AX,
                             negate=True).then_inc(esem, 1)
                v.wait_ge(esem, 3 * i + 2)
                sync(v.reduce_sum(stt_[:, 1:2], p8[:], axis=AX))
                sync(v.reciprocal(stt_[:, 2:3], stt_[:, 1:2]))
                sync(v.tensor_scalar(out=w8[:], in0=p8[:], scalar1=stt_[:, 2:3],
                                     scalar2=None, op0=ALU.mult))
                sync(v.reduce_max(stt_[:, 3:4], w8[:], axis=AX))
                sync(v.tensor_scalar(out=w2b[:], in0=w8[:], scalar1=stt_[:, 3:4],
                                     scalar2=None, op0=ALU.is_ge))
                sync(v.scalar_tensor_tensor(out=w2b[:], in0=w2b[:], scalar=-1e38,
                                            in1=w8[:], op0=ALU.mult, op1=ALU.add))
                sync(v.reduce_max(stt_[:, 4:5], w2b[:], axis=AX))
                sync(v.tensor_scalar(out=g8[:], in0=w8[:], scalar1=stt_[:, 4:5],
                                     scalar2=None, op0=ALU.is_ge))
                sync(v.tensor_tensor(out=g8[:], in0=g8[:], in1=w8[:],
                                     op=ALU.mult))
                v.wait_ge(esem, 3 * i + 3)
                for e in range(8):
                    ins = v.tensor_scalar(out=asc[:, e * 16:(e + 1) * 16],
                                          in0=a_sb[:, e * 16:(e + 1) * 16],
                                          scalar1=g8[:, e:e + 1], scalar2=None,
                                          op0=ALU.mult)
                ins.then_inc(vsem, 1)
                v.wait_ge(tsem, 2 * i + 2)
                if i >= 1:
                    v.wait_ge(stv, 0)
                v.tensor_copy(aout[:, i * 128:(i + 1) * 128],
                              ps_t[:]).then_inc(vsem, 1)

        @blk.scalar
        def _(s):
            for i in range(NT):
                s.wait_ge(esem, 3 * i + 1)
                s.activation(out=p8[:], in_=ps_r[:], func=AF.Exp,
                             bias=stt_[:, 0:1], scale=1.0).then_inc(esem, 1)
                s.activation(out=a_sb[:], in_=ps_h[:],
                             func=AF.Silu).then_inc(esem, 1)

        @blk.sync
        def _(sy):
            sy.wait_ge(vsem, 2 * NT)
            sy.dma_start(out=aout_d, in_=aout[:]).then_inc(stv, 16)
            sy.wait_ge(stv, 16)
    return nc


def run_l0ffna(xnf_percore, rwt, dwt):
    nc = _cached("l0ffna", build_l0ffna)
    maps = [{"xnf": xnf_percore[c], "rwt": rwt, "dwt": dwt} for c in range(NCORE)]
    res = _run(nc, maps)
    return [r["aout"] for r in res]



def build_rope():
    NG = 18
    TC = T // 512
    nc = bass.Bass("TRN2", target_bir_lowering=False, detect_race_conditions=False)
    qkv = nc.dram_tensor("qkv", [2304, T], BF16, kind="ExternalInput").ap()
    cosr = nc.dram_tensor("cosr", [128, T], F32, kind="ExternalInput").ap()
    sins = nc.dram_tensor("sins", [128, T], F32, kind="ExternalInput").ap()
    out = nc.dram_tensor("out", [2304, T], BF16, kind="ExternalOutput").ap()
    pm = np.zeros((128, 128), BFD)
    for p in range(128):
        q = (p // 64) * 64 + ((p % 64) + 32) % 64
        pm[q, p] = 1.0
    perm_d = nc.inline_tensor(pm, "perm_d").ap()
    with contextlib.ExitStack() as st:
        E = st.enter_context
        xb = [E(nc.sbuf_tensor(f"xb{p}", [128, T], BF16)) for p in range(2)]
        yb = [E(nc.sbuf_tensor(f"yb{p}", [128, T], BF16)) for p in range(2)]
        cs = E(nc.sbuf_tensor("cs", [128, T], F32))
        sn = E(nc.sbuf_tensor("sn", [128, T], F32))
        pmat = E(nc.sbuf_tensor("pmat", [128, 128], BF16))
        t1 = E(nc.sbuf_tensor("t1", [128, 512], F32))
        t2 = E(nc.sbuf_tensor("t2", [128, 512], F32))
        ps_sw = [E(nc.psum_tensor(f"ps_sw{p}", [128, 512], F32)) for p in range(2)]
        cld = E(nc.semaphore("cld"))
        ld2 = [E(nc.semaphore(f"ld{p}")) for p in range(2)]
        ev = E(nc.semaphore("ev"))
        tsem = E(nc.semaphore("tsem"))
        stv = E(nc.semaphore("stv"))
        blk = E(nc.Block())

        @blk.sync
        def _(sy):
            sy.dma_start(out=cs[:], in_=cosr).then_inc(cld, 16)
            sy.dma_start(out=sn[:], in_=sins).then_inc(cld, 16)
            sy.dma_start(out=pmat[:], in_=perm_d).then_inc(cld, 16)
            for g in range(NG):
                if g >= 2:
                    sy.wait_ge(ev, (g - 1) * TC)
                    sy.dma_start(out=out[(g - 2) * 128:(g - 1) * 128, :],
                                 in_=yb[g % 2][:]).then_inc(stv, 16)
                sy.dma_start(out=xb[g % 2][:],
                             in_=qkv[g * 128:(g + 1) * 128, :]
                             ).then_inc(ld2[g % 2], 16)
            sy.wait_ge(ev, (NG - 1) * TC)
            sy.dma_start(out=out[(NG - 2) * 128:(NG - 1) * 128, :],
                         in_=yb[0][:]).then_inc(stv, 16)
            sy.wait_ge(ev, NG * TC)
            sy.dma_start(out=out[(NG - 1) * 128:NG * 128, :],
                         in_=yb[1][:]).then_inc(stv, 16)
            sy.wait_ge(stv, 16 * NG)

        @blk.tensor
        def _(t):
            t.wait_ge(cld, 48)
            for g in range(NG):
                t.wait_ge(ld2[g % 2], 16 * (g // 2 + 1))
                for cc in range(TC):
                    gi = g * TC + cc
                    if gi >= 2:
                        t.wait_ge(ev, gi - 1)
                    t.matmul(ps_sw[gi % 2][:], pmat[:],
                             xb[g % 2][:, cc * 512:(cc + 1) * 512],
                             start=True, stop=True).then_inc(tsem, 1)

        @blk.vector
        def _(v):
            v.wait_ge(cld, 48)
            for g in range(NG):
                v.wait_ge(ld2[g % 2], 16 * (g // 2 + 1))
                if g >= 2:
                    v.wait_ge(stv, 16 * (g - 1))
                for cc in range(TC):
                    gi = g * TC + cc
                    sl = slice(cc * 512, (cc + 1) * 512)
                    v.wait_ge(tsem, gi + 1)
                    v.tensor_tensor(out=t1[:], in0=xb[g % 2][:, sl],
                                    in1=cs[:, sl], op=ALU.mult)
                    v.tensor_tensor(out=t2[:], in0=ps_sw[gi % 2][:],
                                    in1=sn[:, sl], op=ALU.mult)
                    v.scalar_tensor_tensor(out=yb[g % 2][:, sl], in0=t1[:],
                                           scalar=1.0, in1=t2[:],
                                           op0=ALU.mult,
                                           op1=ALU.add).then_inc(ev, 1)
    return nc


def run_rope(qkv_percore, cos_half, sins_half):
    nc = _cached("rope", build_rope)
    maps = [{"qkv": qkv_percore[c], "cosr": cos_half[c % 2],
             "sins": sins_half[c % 2]} for c in range(NCORE)]
    res = _run(nc, maps)
    return [r["out"] for r in res]


def _l1_sched():
    NP, NQ = 8, 8
    T_ = {}
    V_ = {}
    E_ = {}
    ct = cv = ce = 0
    cv += 1; V_["ksq"] = cv
    for cc in range(16):
        ct += 1; T_[f"kn{cc}"] = ct
        cv += 1; V_[f"knr{cc}"] = cv
    g = 0
    for u in range(NP):
        ce += 1; E_[f"sq{u}"] = ce
        for cc in range(8):
            ct += 1; T_[f"qn{u},{cc}"] = ct
            cv += 1; V_[f"qnr{u},{cc}"] = cv
        cv += 1; V_[f"biasgo{u}"] = cv
        ce += 1; E_[f"sqrt{u}"] = ce
        cv += 1; V_[f"biasmm{u}"] = cv
        ct += 1; T_[f"bb{u}"] = ct
        cv += 1; V_[f"biasrdy{u}"] = cv
        for h in range(2):
            for q in range(NQ):
                nch = 4 * q + 4
                for c in range(nch):
                    ct += 1; T_[f"s{g + c}"] = ct
                    ce += 1; E_[f"exp{g + c}"] = ce
                    if c >= 4 * q:
                        cv += 1; V_[f"mask{g + c}"] = cv
                    ct += 1; T_[f"p{g + c}"] = ct
                g += nch
                cv += 1; V_[f"den{u},{h},{q}"] = cv
                ct += 1; T_[f"bc{u},{h},{q}"] = ct
                cv += 1; V_[f"evac{u},{h},{q}"] = cv
        V_[f"end{u}"] = cv
    return T_, V_, E_


def build_l1attn():
    S = 4096
    NQ = 8
    NP = 8
    T_, V_, E_ = _l1_sched()
    nc = bass.Bass("TRN2", target_bir_lowering=False, detect_race_conditions=False)
    qh = nc.dram_tensor("qh", [NP, 128, S], BF16, kind="ExternalInput").ap()
    kf2 = nc.dram_tensor("kf2", [2, 128, S], BF16, kind="ExternalInput").ap()
    vt2 = nc.dram_tensor("vt2", [2, 128, 2048], BF16, kind="ExternalInput").ap()
    oh = nc.dram_tensor("oh", [NP * 2, 64, S], BF16, kind="ExternalOutput").ap()
    stair = (np.arange(128)[:, None] <= np.arange(896)[None, :] - 384).astype(BFD)
    t8_d = nc.inline_tensor(stair, "t8_d").ap()
    ones1_d = nc.inline_tensor(np.ones((1, 128), BFD), "ones1_d").ap()
    onesc_d = nc.inline_tensor(np.ones((128, 1), BFD), "onesc_d").ap()

    with contextlib.ExitStack() as st:
        E = st.enter_context
        q_sb = [E(nc.sbuf_tensor(f"q_sb{p}", [128, S], BF16)) for p in range(2)]
        oh_sb = [E(nc.sbuf_tensor(f"oh_sb{p}", [64, S], BF16)) for p in range(2)]
        kf_sb = E(nc.sbuf_tensor("kf_sb", [128, 2 * S], BF16))
        va_sb = E(nc.sbuf_tensor("va_sb", [128, 2 * 2176], BF16))
        sq_sb = E(nc.sbuf_tensor("sq_sb", [128, 2 * S], BF16))
        t8_sb = E(nc.sbuf_tensor("t8_sb", [128, 896], BF16))
        ones1_sb = E(nc.sbuf_tensor("ones1_sb", [1, 128], BF16))
        onesc_sb = E(nc.sbuf_tensor("onesc_sb", [128, 1], BF16))
        pst = [E(nc.sbuf_tensor(f"pst{p}", [128, 512], BF16)) for p in range(4)]
        nrm = E(nc.sbuf_tensor("nrm", [1, 20], F32))
        nrmb = E(nc.sbuf_tensor("nrmb", [1, 1], BF16))
        biasb = E(nc.sbuf_tensor("biasb", [128, 1], F32))
        denf = E(nc.sbuf_tensor("denf", [1, 512], F32))
        denr = E(nc.sbuf_tensor("denr", [1, 512], BF16))
        denb = E(nc.sbuf_tensor("denb", [64, 512], BF16))
        ps_sT = [E(nc.psum_tensor(f"ps_sT{p}", [128, 512], F32)) for p in range(4)]
        ps_o = E(nc.psum_tensor("ps_o", [65, 512], F32))
        ps_bc = E(nc.psum_tensor("ps_bc", [64, 512], F32))
        ps_n = E(nc.psum_tensor("ps_n", [1, 512], F32))
        ps_bb = E(nc.psum_tensor("ps_bb", [128, 1], F32))
        cld = E(nc.semaphore("cld"))
        msd = E(nc.semaphore("msd"))
        pld = [E(nc.semaphore(f"pld{p}")) for p in range(2)]
        tsem = E(nc.semaphore("tsem"))
        vsem = E(nc.semaphore("vsem"))
        esem = E(nc.semaphore("esem"))
        vs = E(nc.semaphore("vs"))
        stv = E(nc.semaphore("stv"))
        blk = E(nc.Block())

        @blk.gpsimd
        def _(g):
            g.dma_start(out=t8_sb[:], in_=t8_d).then_inc(cld, 16)
            g.dma_start(out=ones1_sb[:], in_=ones1_d).then_inc(cld, 16)
            g.dma_start(out=onesc_sb[:], in_=onesc_d).then_inc(cld, 16)
            g.dma_start(out=kf_sb[:, 0:S], in_=kf2[0]).then_inc(cld, 16)
            g.dma_start(out=kf_sb[:, S:2 * S], in_=kf2[1]).then_inc(cld, 16)
            g.memset(va_sb[:], 1.0).then_inc(msd, 1)
            for kv in range(2):
                g.dma_start(
                    out=va_sb.ap()[:, kv * 2176:(kv + 1) * 2176].rearrange(
                        "p (c w) -> p c w", w=68)[:, :, 0:64],
                    in_=vt2[kv].rearrange("p (c w) -> p c w", w=64)
                ).then_inc(cld, 16)
            for u in range(NP):
                par = u % 2
                if u >= 2:
                    g.wait_ge(vsem, V_[f"end{u - 2}"])
                g.dma_start(out=q_sb[par][:], in_=qh[u]).then_inc(pld[par], 16)

        @blk.scalar
        def _(s):
            s.wait_ge(cld, 112)
            gg = 0
            for u in range(NP):
                par = u % 2
                kv = u // 4
                s.wait_ge(pld[par], 16 * (u // 2 + 1))
                if u >= 1:
                    s.wait_ge(tsem, T_[f"qn{u - 1},7"])
                else:
                    s.wait_ge(tsem, T_["kn7"])
                s.activation(out=sq_sb[:, 0:S], in_=q_sb[par][:],
                             func=AF.Square).then_inc(esem, 1)
                s.wait_ge(vsem, V_[f"biasgo{u}"])
                s.activation(out=nrm[:, 3:4], in_=nrm[:, 2:3],
                             func=AF.Sqrt).then_inc(esem, 1)
                s.wait_ge(vsem, V_[f"biasrdy{u}"])
                for h in range(2):
                    for q in range(NQ):
                        for c in range(4 * q + 4):
                            gi = gg + c
                            s.wait_ge(tsem, T_[f"s{gi}"])
                            if gi >= 4:
                                s.wait_ge(tsem, T_[f"p{gi - 4}"])
                            s.activation(out=pst[gi % 4][:],
                                         in_=ps_sT[gi % 4][:], func=AF.Exp,
                                         bias=biasb[:, 0:1],
                                         scale=0.125).then_inc(esem, 1)
                        gg += 4 * q + 4

        @blk.vector
        def _(v):
            nvs = 0

            def sync(ins):
                nonlocal nvs
                nvs += 1
                ins.then_inc(vs, 1)
                v.wait_ge(vs, nvs)

            v.wait_ge(cld, 112)
            v.tensor_tensor(out=sq_sb[:], in0=kf_sb[:], in1=kf_sb[:],
                            op=ALU.mult).then_inc(vsem, 1)
            for kv in range(2):
                for cc in range(8):
                    v.wait_ge(tsem, T_[f"kn{kv * 8 + cc}"])
                    v.reduce_max(nrm[:, 8 + cc:9 + cc], ps_n[:],
                                 axis=AX).then_inc(vsem, 1)
                v.wait_ge(vsem, V_[f"knr{kv * 8 + 7}"])
                sync(v.reduce_max(nrm[:, 4 + kv:5 + kv], nrm[:, 8:16], axis=AX))
            gg = 0
            for u in range(NP):
                par = u % 2
                kv = u // 4
                for cc in range(8):
                    v.wait_ge(tsem, T_[f"qn{u},{cc}"])
                    v.reduce_max(nrm[:, 8 + cc:9 + cc], ps_n[:],
                                 axis=AX).then_inc(vsem, 1)
                v.wait_ge(vsem, V_[f"qnr{u},7"])
                sync(v.reduce_max(nrm[:, 0:1], nrm[:, 8:16], axis=AX))
                sync(v.tensor_tensor(out=nrm[:, 2:3], in0=nrm[:, 0:1],
                                     in1=nrm[:, 4 + kv:5 + kv], op=ALU.mult))
                v.tensor_copy(nrm[:, 18:19],
                              nrm[:, 8:9]).then_inc(vsem, 1)
                v.wait_ge(esem, E_[f"sqrt{u}"])
                sync(v.tensor_scalar(out=nrm[:, 6:7], in0=nrm[:, 3:4],
                                     scalar1=-0.125, scalar2=None,
                                     op0=ALU.mult))
                sync(v.tensor_copy(nrmb[:], nrm[:, 6:7]))
                v.tensor_copy(nrm[:, 18:19],
                              nrm[:, 8:9]).then_inc(vsem, 1)
                v.wait_ge(tsem, T_[f"bb{u}"])
                v.tensor_copy(biasb[:], ps_bb[:]).then_inc(vsem, 1)
                for h in range(2):
                    uh = u * 2 + h
                    if uh >= 2:
                        v.wait_ge(stv, 16 * (uh - 1))
                    for q in range(NQ):
                        for c in range(4 * q, 4 * q + 4):
                            gi = gg + c
                            v.wait_ge(esem, E_[f"exp{gi}"])
                            delta = 128 * (4 * q - c)
                            v.tensor_tensor(
                                out=pst[gi % 4][:], in0=pst[gi % 4][:],
                                in1=t8_sb[:, 384 + delta:896 + delta],
                                op=ALU.mult).then_inc(vsem, 1)
                        gg += 4 * q + 4
                        v.wait_ge(tsem, T_[f"p{gg - 1}"])
                        sync(v.reciprocal(denf[:], ps_o[64:65, :]))
                        sync(v.tensor_copy(denr[:], denf[:]))
                        v.tensor_copy(nrm[:, 18:19],
                                      nrm[:, 8:9]).then_inc(vsem, 1)
                        v.wait_ge(tsem, T_[f"bc{u},{h},{q}"])
                        v.tensor_copy(denb[:], ps_bc[:])
                        v.tensor_tensor(
                            out=oh_sb[uh % 2][:, q * 512:(q + 1) * 512],
                            in0=ps_o[0:64, :], in1=denb[:],
                            op=ALU.mult).then_inc(vsem, 1)

        @blk.tensor
        def _(t):
            t.wait_ge(cld, 112)
            t.wait_ge(msd, 1)
            t.wait_ge(vsem, V_["ksq"])
            for cc in range(16):
                if cc >= 1:
                    t.wait_ge(vsem, V_[f"knr{cc - 1}"])
                t.matmul(ps_n[:], onesc_sb[:],
                         sq_sb[:, cc * 512:(cc + 1) * 512],
                         start=True, stop=True).then_inc(tsem, 1)
            gg = 0
            for u in range(NP):
                par = u % 2
                kv = u // 4
                t.wait_ge(esem, E_[f"sq{u}"])
                for cc in range(8):
                    if cc == 0:
                        t.wait_ge(vsem, V_["knr15"] if u == 0 else
                                  V_[f"qnr{u - 1},7"])
                    else:
                        t.wait_ge(vsem, V_[f"qnr{u},{cc - 1}"])
                    t.matmul(ps_n[:], onesc_sb[:],
                             sq_sb[:, cc * 512:(cc + 1) * 512],
                             start=True, stop=True).then_inc(tsem, 1)
                t.wait_ge(vsem, V_[f"biasmm{u}"])
                t.matmul(ps_bb[:], ones1_sb[:], nrmb[:],
                         start=True, stop=True).then_inc(tsem, 1)
                for h in range(2):
                    qrow = slice(h * 64, (h + 1) * 64)
                    for q in range(NQ):
                        nch = 4 * q + 4
                        for c in range(nch):
                            gi = gg + c
                            if gi >= 4:
                                t.wait_ge(esem, E_[f"exp{gi - 4}"])
                            t.matmul(ps_sT[gi % 4][:],
                                     kf_sb[qrow, kv * S + c * 128:
                                           kv * S + (c + 1) * 128],
                                     q_sb[par][qrow, q * 512:(q + 1) * 512],
                                     start=True, stop=True
                                     ).then_inc(tsem, 1)
                            if c == 0 and not (u == 0 and h == 0 and q == 0):
                                pu, ph, pq = (u, h, q - 1) if q > 0 else                                     ((u, h - 1, NQ - 1) if h > 0 else
                                     (u - 1, 1, NQ - 1))
                                t.wait_ge(vsem, V_[f"evac{pu},{ph},{pq}"])
                            if c >= 4 * q:
                                t.wait_ge(vsem, V_[f"mask{gi}"])
                            else:
                                t.wait_ge(esem, E_[f"exp{gi}"])
                            t.matmul(ps_o[:],
                                     va_sb[:, kv * 2176 + c * 68:
                                           kv * 2176 + c * 68 + 65],
                                     pst[gi % 4][:],
                                     start=(c == 0), stop=(c == nch - 1)
                                     ).then_inc(tsem, 1)
                        gg += nch
                        t.wait_ge(vsem, V_[f"den{u},{h},{q}"])
                        t.matmul(ps_bc[:], ones1_sb[0:1, 0:64], denr[:],
                                 start=True, stop=True
                                 ).then_inc(tsem, 1)

        @blk.sync
        def _(sy):
            for u in range(NP):
                for h in range(2):
                    uh = u * 2 + h
                    sy.wait_ge(vsem, V_[f"evac{u},{h},{NQ - 1}"])
                    sy.dma_start(out=oh[uh],
                                 in_=oh_sb[uh % 2][:]).then_inc(stv, 16)
            sy.wait_ge(stv, 16 * NP * 2)
    return nc


def run_l1attn(in_percore):
    nc = _cached("l1attn", build_l1attn)
    res = _run(nc, in_percore)
    return [r["oh"] for r in res]


def build_quant():
    nc = bass.Bass("TRN2", target_bir_lowering=False, detect_race_conditions=False)
    w1 = nc.dram_tensor("w1", [2048, 2048], F32, kind="ExternalInput").ap()
    w2 = nc.dram_tensor("w2", [1024, 2048], F32, kind="ExternalInput").ap()
    q1 = nc.dram_tensor("q1", [2048, 2048], BF16, kind="ExternalOutput").ap()
    q2 = nc.dram_tensor("q2", [1024, 2048], BF16, kind="ExternalOutput").ap()
    sel = np.zeros((128, 2), BFD)
    sel[0:64, 0] = 1.0
    sel[64:128, 1] = 1.0
    sel_d = nc.inline_tensor(sel, "sel_d").ap()
    selt_d = nc.inline_tensor(np.ascontiguousarray(sel.T), "selt_d").ap()
    tiles = [("w1", r, c) for r in range(16) for c in range(4)] + \
            [("w2", r, c) for r in range(8) for c in range(4)]
    NTL = len(tiles)

    with contextlib.ExitStack() as st:
        E = st.enter_context
        w_sb = [E(nc.sbuf_tensor(f"w_sb{p}", [128, 512], BF16)) for p in range(2)]
        wa = E(nc.sbuf_tensor("wa", [128, 512], BF16))
        sel_sb = E(nc.sbuf_tensor("sel_sb", [128, 2], BF16))
        selt_sb = E(nc.sbuf_tensor("selt_sb", [2, 128], BF16))
        sc = E(nc.sbuf_tensor("sc", [2, 512 * 4], F32))
        sc_bf = E(nc.sbuf_tensor("sc_bf", [2, 512], BF16))
        thr = E(nc.sbuf_tensor("thr", [128, 512], F32))
        mk = E(nc.sbuf_tensor("mk", [128, 512], BF16))
        sg = E(nc.sbuf_tensor("sg", [128, 512], BF16))
        tq = E(nc.sbuf_tensor("tq", [128, 512], BF16))
        wq = [E(nc.sbuf_tensor(f"wq{p}", [128, 512], BF16)) for p in range(2)]
        ps_s = E(nc.psum_tensor("ps_s", [2, 512], F32))
        ps_b2 = E(nc.psum_tensor("ps_b2", [128, 512], F32))
        cld = E(nc.semaphore("cld"))
        pld = [E(nc.semaphore(f"pld{p}")) for p in range(2)]
        tsem = E(nc.semaphore("tsem"))
        vsem = E(nc.semaphore("vsem"))
        esem = E(nc.semaphore("esem"))
        vs = E(nc.semaphore("vs"))
        stv = [E(nc.semaphore(f"stv{p}")) for p in range(2)]
        blk = E(nc.Block())

        def tsl(idx):
            name, r, c = tiles[idx]
            ap = w1 if name == "w1" else w2
            return ap[r * 128:(r + 1) * 128, c * 512:(c + 1) * 512]

        def osl(idx):
            name, r, c = tiles[idx]
            ap = q1 if name == "w1" else q2
            return ap[r * 128:(r + 1) * 128, c * 512:(c + 1) * 512]

        @blk.gpsimd
        def _(g):
            g.dma_start(out=sel_sb[:], in_=sel_d).then_inc(cld, 16)
            g.dma_start(out=selt_sb[:], in_=selt_d).then_inc(cld, 16)
            for i in range(NTL):
                if i >= 2:
                    g.wait_ge(stv[i % 2], 16 * (i // 2))
                g.dma_start(out=w_sb[i % 2][:], in_=tsl(i)).then_inc(pld[i % 2], 16)

        @blk.scalar
        def _(s):
            for i in range(NTL):
                s.wait_ge(pld[i % 2], 16 * (i // 2 + 1))
                if i >= 1:
                    s.wait_ge(vsem, 2 * i)
                s.activation(out=wa[:], in_=w_sb[i % 2][:],
                             func=AF.Abs).then_inc(esem, 1)
                s.activation(out=sg[:], in_=w_sb[i % 2][:],
                             func=AF.Sign).then_inc(esem, 1)

        @blk.tensor
        def _(t):
            t.wait_ge(cld, 32)
            for i in range(NTL):
                t.wait_ge(esem, 2 * i + 1)
                if i >= 1:
                    t.wait_ge(vsem, 2 * i)
                t.matmul(ps_s[:], sel_sb[:], wa[:],
                         start=True, stop=True).then_inc(tsem, 1)
                t.wait_ge(vsem, 2 * i + 1)
                t.matmul(ps_b2[:], selt_sb[:], sc_bf[:],
                         start=True, stop=True).then_inc(tsem, 1)

        @blk.vector
        def _(v):
            nvs = 0

            def sync(ins):
                nonlocal nvs
                nvs += 1
                ins.then_inc(vs, 1)
                v.wait_ge(vs, nvs)

            for i in range(NTL):
                v.wait_ge(tsem, 2 * i + 1)
                sync(v.tensor_scalar(out=sc[:, 0:512], in0=ps_s[:],
                                     scalar1=1.0 / 64, scalar2=None,
                                     op0=ALU.mult))
                v.tensor_scalar(out=sc_bf[:], in0=sc[:, 0:512],
                                scalar1=1e-8, scalar2=None,
                                op0=ALU.max).then_inc(vsem, 1)
                v.wait_ge(tsem, 2 * i + 2)
                v.tensor_scalar(out=thr[:], in0=ps_b2[:],
                                scalar1=0.5009765625, scalar2=None,
                                op0=ALU.mult)
                v.wait_ge(esem, 2 * i + 2)
                v.tensor_tensor(out=mk[:], in0=wa[:], in1=thr[:],
                                op=ALU.is_gt)
                v.tensor_tensor(out=tq[:], in0=mk[:], in1=sg[:], op=ALU.mult)
                if i >= 2:
                    v.wait_ge(stv[i % 2], 16 * (i // 2))
                v.tensor_tensor(out=wq[i % 2][:], in0=tq[:], in1=ps_b2[:],
                                op=ALU.mult).then_inc(vsem, 1)
        @blk.sync
        def _(sy):
            for i in range(NTL):
                sy.wait_ge(vsem, 2 * i + 2)
                sy.dma_start(out=osl(i), in_=wq[i % 2][:]).then_inc(stv[i % 2], 16)
            for p in range(2):
                sy.wait_ge(stv[p], 16 * ((NTL - 1 - p) // 2 + 1))
    return nc


def run_quant(w1_percore, w2_percore):
    nc = _cached("quant", build_quant)
    maps = [{"w1": w1_percore[c], "w2": w2_percore[c]} for c in range(NCORE)]
    res = _run(nc, maps)
    return [r["q1"] for r in res], [r["q2"] for r in res]


def build_ffn1():
    NK = D // 128
    NG = 64
    TC = T // 512
    OG = D // 128
    EPSF = float(np.finfo(np.float32).eps)
    nc = bass.Bass("TRN2", target_bir_lowering=False, detect_race_conditions=False)
    xnf = nc.dram_tensor("xnf", [D, T], BF16, kind="ExternalInput").ap()
    wgu = nc.dram_tensor("wgu", [D, 16384], BF16, kind="ExternalInput").ap()
    wpr = nc.dram_tensor("wpr", [8192, D], BF16, kind="ExternalInput").ap()
    p2 = nc.dram_tensor("p2", [D, T], BF16, kind="ExternalInput").ap()
    x2 = nc.dram_tensor("x2", [D, T], F32, kind="ExternalOutput").ap()
    ones1_d = nc.inline_tensor(np.ones((1, 128), BFD), "ones1_d").ap()
    onesc_d = nc.inline_tensor(np.ones((128, 1), BFD), "onesc_d").ap()

    def sch():
        Td, Vd, Ed = {}, {}, {}
        ct = cv = ce = 0
        for tc in range(TC):
            for gg in range(NG):
                gi = tc * NG + gg
                if gg >= 2:
                    ct += 1; Td[f"r{gi - 2}"] = ct
                ct += 1; Td[f"g{gi}"] = ct
                ce += 1; Ed[f"s{gi}"] = ce
                cv += 1; Vd[f"h{gi}"] = cv
            ct += 1; Td[f"r{tc * NG + NG - 2}"] = ct
            ct += 1; Td[f"r{tc * NG + NG - 1}"] = ct
            cv += 1; Vd[f"vsq{tc}"] = cv
            ce += 1; Ed[f"q{tc}"] = ce
            cv += 1; Vd[f"ib{tc}"] = cv
            ct += 1; Td[f"b{tc}"] = ct
            cv += 1; Vd[f"hn{tc}"] = cv
            for og in range(OG):
                oi = tc * OG + og
                ct += 1; Td[f"o{oi}"] = ct
                cv += 1; Vd[f"oev{oi}"] = cv
        return Td, Vd, Ed

    Td, Vd, Ed = sch()

    with contextlib.ExitStack() as st:
        E = st.enter_context
        xn_sb = E(nc.sbuf_tensor("xn_sb", [128, NK * T], BF16))
        wg_sb = [[E(nc.sbuf_tensor(f"wg_sb{p}_{j}", [128, NK * 128], BF16))
                  for j in range(2)] for p in range(2)]
        wp_sb = [E(nc.sbuf_tensor(f"wp_sb{p}", [128, 64 * 128], BF16))
                 for p in range(2)]
        hbuf = E(nc.sbuf_tensor("hbuf", [128, NG * 512], BF16))
        h2b = [E(nc.sbuf_tensor(f"h2b{p}", [128, 512], BF16)) for p in range(2)]
        sil = [E(nc.sbuf_tensor(f"sil{p}", [128, 512], F32)) for p in range(2)]
        rr0 = E(nc.sbuf_tensor("rr0", [1, 512], F32))
        rr1 = E(nc.sbuf_tensor("rr1", [1, 512], F32))
        rr2 = E(nc.sbuf_tensor("rr2", [1, 512], F32))
        ib_bf = E(nc.sbuf_tensor("ib_bf", [1, 512], BF16))
        ib_sb = E(nc.sbuf_tensor("ib_sb", [128, 512], BF16))
        p2_sb = [E(nc.sbuf_tensor(f"p2_sb{p}", [128, 512], BF16)) for p in range(2)]
        ob = [E(nc.sbuf_tensor(f"ob{p}", [128, 512], F32)) for p in range(2)]
        ones1_sb = E(nc.sbuf_tensor("ones1_sb", [1, 128], BF16))
        onesc_sb = E(nc.sbuf_tensor("onesc_sb", [128, 1], BF16))
        ps_g = [E(nc.psum_tensor(f"ps_g{p}", [128, 512], F32)) for p in range(2)]
        ps_u = [E(nc.psum_tensor(f"ps_u{p}", [128, 512], F32)) for p in range(2)]
        ps_o = [E(nc.psum_tensor(f"ps_o{p}", [128, 512], F32)) for p in range(2)]
        ps_r = E(nc.psum_tensor("ps_r", [1, 512], F32))
        ps_ib = E(nc.psum_tensor("ps_ib", [128, 512], F32))
        cld = E(nc.semaphore("cld"))
        wld = [E(nc.semaphore(f"wld{p}")) for p in range(2)]
        wpld = [E(nc.semaphore(f"wpld{p}")) for p in range(2)]
        rld = [E(nc.semaphore(f"rld{p}")) for p in range(2)]
        tsem = E(nc.semaphore("tsem"))
        vsem = E(nc.semaphore("vsem"))
        esem = E(nc.semaphore("esem"))
        vs = E(nc.semaphore("vs"))
        stv = [E(nc.semaphore(f"stv{p}")) for p in range(2)]
        blk = E(nc.Block())

        @blk.gpsimd
        def _(g):
            g.dma_start(out=ones1_sb[:], in_=ones1_d).then_inc(cld, 16)
            g.dma_start(out=onesc_sb[:], in_=onesc_d).then_inc(cld, 16)
            g.dma_start(out=xn_sb.ap().rearrange("p (a t) -> p a t", t=T),
                        in_=xnf.rearrange("(a p) t -> p a t", p=128)
                        ).then_inc(cld, 16)
            for tc in range(TC):
                for gg in range(NG):
                    gi = tc * NG + gg
                    if gi >= 2:
                        g.wait_ge(tsem, Td[f"g{gi - 2}"])
                    g.dma_start(
                        out=wg_sb[gi % 2][0].ap().rearrange(
                            "p (a f) -> p a f", f=128),
                        in_=wgu[:, gg * 128:(gg + 1) * 128].rearrange(
                            "(a p) f -> p a f", p=128)
                    ).then_inc(wld[gi % 2], 16)
                    g.dma_start(
                        out=wg_sb[gi % 2][1].ap().rearrange(
                            "p (a f) -> p a f", f=128),
                        in_=wgu[:, 8192 + gg * 128:8192 + (gg + 1) * 128
                                ].rearrange("(a p) f -> p a f", p=128)
                    ).then_inc(wld[gi % 2], 16)
                for og in range(OG):
                    oi = tc * OG + og
                    if oi >= 2:
                        g.wait_ge(tsem, Td[f"o{oi - 2}"])
                    g.dma_start(
                        out=wp_sb[oi % 2].ap().rearrange(
                            "p (a f) -> p a f", f=128),
                        in_=wpr[:, og * 128:(og + 1) * 128].rearrange(
                            "(a p) f -> p a f", p=128)
                    ).then_inc(wpld[oi % 2], 16)

        @blk.sync
        def _(sy):
            for tc in range(TC):
                for og in range(OG):
                    oi = tc * OG + og
                    if oi >= 2:
                        sy.wait_ge(vsem, Vd[f"oev{oi - 1}"])
                    sy.dma_start(out=p2_sb[oi % 2][:],
                                 in_=p2[og * 128:(og + 1) * 128,
                                        tc * 512:(tc + 1) * 512]
                                 ).then_inc(rld[oi % 2], 16)
                    sy.wait_ge(vsem, Vd[f"oev{oi}"])
                    sy.dma_start(out=x2[og * 128:(og + 1) * 128,
                                        tc * 512:(tc + 1) * 512],
                                 in_=ob[oi % 2][:]).then_inc(stv[oi % 2], 16)
            for p in range(2):
                sy.wait_ge(stv[p], 16 * ((TC * OG - 1 - p) // 2 + 1))

        @blk.scalar
        def _(s):
            for tc in range(TC):
                for gg in range(NG):
                    gi = tc * NG + gg
                    s.wait_ge(tsem, Td[f"g{gi}"])
                    if gi >= 2:
                        s.wait_ge(vsem, Vd[f"h{gi - 2}"])
                    s.activation(out=sil[gi % 2][:], in_=ps_g[gi % 2][:],
                                 func=AF.Silu).then_inc(esem, 1)
                s.wait_ge(vsem, Vd[f"vsq{tc}"])
                s.activation(out=rr1[:], in_=rr0[:],
                             func=AF.Sqrt).then_inc(esem, 1)

        @blk.vector
        def _(v):
            nvs = 0

            def sync(ins):
                nonlocal nvs
                nvs += 1
                ins.then_inc(vs, 1)
                v.wait_ge(vs, nvs)

            for tc in range(TC):
                if tc >= 1:
                    v.wait_ge(tsem, Td[f"o{tc * OG - 1}"])
                for gg in range(NG):
                    gi = tc * NG + gg
                    v.wait_ge(esem, Ed[f"s{gi}"])
                    if gi >= 2:
                        v.wait_ge(tsem, Td[f"r{gi - 2}"])
                    v.tensor_tensor(out=hbuf[:, gg * 512:(gg + 1) * 512],
                                    in0=sil[gi % 2][:], in1=ps_u[gi % 2][:],
                                    op=ALU.mult)
                    v.tensor_tensor(out=h2b[gi % 2][:],
                                    in0=hbuf[:, gg * 512:(gg + 1) * 512],
                                    in1=hbuf[:, gg * 512:(gg + 1) * 512],
                                    op=ALU.mult).then_inc(vsem, 1)
                v.wait_ge(tsem, Td[f"r{tc * NG + NG - 1}"])
                sync(v.tensor_scalar(out=rr0[:], in0=ps_r[:],
                                     scalar1=1.0 / 8192, scalar2=None,
                                     op0=ALU.mult))
                v.tensor_scalar(out=rr0[:], in0=rr0[:], scalar1=EPSF,
                                scalar2=None, op0=ALU.add).then_inc(vsem, 1)
                v.wait_ge(esem, Ed[f"q{tc}"])
                sync(v.reciprocal(rr2[:], rr1[:]))
                v.tensor_copy(ib_bf[:], rr2[:]).then_inc(vsem, 1)
                v.wait_ge(tsem, Td[f"b{tc}"])
                sync(v.tensor_copy(ib_sb[:], ps_ib[:]))
                for gg in range(NG):
                    ins = v.tensor_tensor(
                        out=hbuf[:, gg * 512:(gg + 1) * 512],
                        in0=hbuf[:, gg * 512:(gg + 1) * 512],
                        in1=ib_sb[:], op=ALU.mult)
                ins.then_inc(vsem, 1)
                for og in range(OG):
                    oi = tc * OG + og
                    v.wait_ge(tsem, Td[f"o{oi}"])
                    v.wait_ge(rld[oi % 2], 16 * (oi // 2 + 1))
                    if oi >= 2:
                        v.wait_ge(stv[oi % 2], 16 * (oi // 2))
                    v.tensor_tensor(out=ob[oi % 2][:], in0=ps_o[oi % 2][:],
                                    in1=p2_sb[oi % 2][:],
                                    op=ALU.add).then_inc(vsem, 1)

        @blk.tensor
        def _(t):
            t.wait_ge(cld, 48)
            for tc in range(TC):
                for gg in range(NG):
                    gi = tc * NG + gg
                    if gg >= 2:
                        t.wait_ge(vsem, Vd[f"h{gi - 2}"])
                        t.matmul(ps_r[:], onesc_sb[:], h2b[gi % 2][:],
                                 start=(gg - 2 == 0), stop=False
                                 ).then_inc(tsem, 1)
                    t.wait_ge(wld[gi % 2], 32 * (gi // 2 + 1))
                    if gi >= 2:
                        t.wait_ge(vsem, Vd[f"h{gi - 2}"])
                    for k in range(NK):
                        t.matmul(ps_g[gi % 2][:],
                                 wg_sb[gi % 2][0][:, k * 128:(k + 1) * 128],
                                 xn_sb[:, k * T + tc * 512:
                                       k * T + (tc + 1) * 512],
                                 start=(k == 0), stop=(k == NK - 1))
                    for k in range(NK):
                        ins = t.matmul(ps_u[gi % 2][:],
                                       wg_sb[gi % 2][1][:, k * 128:(k + 1) * 128],
                                       xn_sb[:, k * T + tc * 512:
                                             k * T + (tc + 1) * 512],
                                       start=(k == 0), stop=(k == NK - 1))
                    ins.then_inc(tsem, 1)
                for gg in (NG - 2, NG - 1):
                    gi = tc * NG + gg
                    t.wait_ge(vsem, Vd[f"h{gi}"])
                    t.matmul(ps_r[:], onesc_sb[:], h2b[gi % 2][:],
                             start=False, stop=(gg == NG - 1)
                             ).then_inc(tsem, 1)
                t.wait_ge(vsem, Vd[f"ib{tc}"])
                t.matmul(ps_ib[:], ones1_sb[:], ib_bf[:],
                         start=True, stop=True).then_inc(tsem, 1)
                for og in range(OG):
                    oi = tc * OG + og
                    t.wait_ge(wpld[oi % 2], 16 * (oi // 2 + 1))
                    t.wait_ge(vsem, Vd[f"hn{tc}"])
                    if oi >= 2:
                        t.wait_ge(vsem, Vd[f"oev{oi - 2}"])
                    for g2 in range(NG):
                        ins = t.matmul(ps_o[oi % 2][:],
                                       wp_sb[oi % 2][:, g2 * 128:(g2 + 1) * 128],
                                       hbuf[:, g2 * 512:(g2 + 1) * 512],
                                       start=(g2 == 0), stop=(g2 == NG - 1))
                    ins.then_inc(tsem, 1)
    return nc



def run_ffn1(xnf_percore, wgu, wpr, p2_percore):
    nc = _cached("ffn1", build_ffn1)
    maps = [{"xnf": xnf_percore[c], "wgu": wgu, "wpr": wpr,
             "p2": p2_percore[c]} for c in range(NCORE)]
    res = _run(nc, maps)
    return [r["x2"] for r in res]



def shard_tokens(x):
    out = []
    for c in range(NCORE):
        b, h = c // 2, c % 2
        out.append(np.ascontiguousarray(x[b, h * T:(h + 1) * T]))
    return out


def unshard_tokens(parts, d2):
    x = np.empty((4, 4096, d2), parts[0].dtype)
    for c in range(NCORE):
        b, h = c // 2, c % 2
        x[b, h * T:(h + 1) * T] = parts[c]
    return x


def tr(a):
    return np.ascontiguousarray(np.swapaxes(a, -1, -2))


def _rope_tables():
    inv = 1.0 / 10000.0 ** (np.arange(0, 64, 2, dtype=np.float64) / 64)
    sign = np.where((np.arange(128) % 64) < 32, -1.0, 1.0)
    cos_half, sin_half = [], []
    for hh in range(2):
        pos = np.arange(hh * T, (hh + 1) * T, dtype=np.float64)
        ang = pos[None, :] * inv[np.arange(128) % 32][:, None]
        cos_half.append(np.cos(ang).astype(np.float32))
        sin_half.append((sign[:, None] * np.sin(ang)).astype(np.float32))
    return cos_half, sin_half


def _tiled_tok(x, w):
    return np.ascontiguousarray(
        x.reshape(32, 128, w).transpose(1, 0, 2).reshape(128, 32 * w))


def _prep_l0attn(qkv0, memories, mnorm, gate):
    mnflat = np.ascontiguousarray(mnorm.reshape(1, 8192), np.float32)
    maps = []
    for c in range(NCORE):
        b = c // 2
        e, o = 2 * b, 2 * b + 1
        heads = range((c % 2) * 8, (c % 2) * 8 + 8)
        qf = np.empty((8, 128, 4096), BFD)
        kf = np.empty((8, 128, 4096), BFD)
        kt = np.empty((8, 128, 4096), BFD)
        vt = np.empty((8, 128, 4096), BFD)
        for j, h in enumerate(heads):
            rows = slice(h * 128, (h + 1) * 128)
            qf[j, :, 0:T] = qkv0[e][rows]
            qf[j, :, T:2 * T] = qkv0[o][rows]
            kf[j, :, 0:T] = qkv0[e][2048 + h * 128:2048 + (h + 1) * 128]
            kf[j, :, T:2 * T] = qkv0[o][2048 + h * 128:2048 + (h + 1) * 128]
            kt[j] = _tiled_tok(np.ascontiguousarray(kf[j].T), 128)
            vrows = slice(4096 + h * 128, 4096 + (h + 1) * 128)
            vfull = np.concatenate([qkv0[e][vrows], qkv0[o][vrows]], axis=1)
            vt[j] = _tiled_tok(np.ascontiguousarray(vfull.T), 128)
        memp = np.zeros((8, 128, 520), np.float32)
        mnsp = np.zeros((8, 128, 4), np.float32)
        gp = np.zeros((8, 1, 1), np.float32)
        for j, h in enumerate(heads):
            for m in range(4):
                memp[j, :, m * 130:m * 130 + 128] = memories[m, h]
                memp[j, :, m * 130 + 128] = mnorm[m, h]
                mnsp[j, :, m] = mnorm[m, h]
            gp[j, 0, 0] = gate[h]
        maps.append(dict(qf=qf, kf=kf, kt=kt, vt=vt, mem=memp, mns=mnsp,
                         gatev=gp, mnflat=mnflat))
    return maps


def _prep_l1attn(roped, v1):
    maps = []
    for c in range(NCORE):
        b, hh = c // 2, c % 2
        e, o = 2 * b, 2 * b + 1
        qh = np.empty((8, 128, 4096), BFD)
        for u in range(8):
            for j in range(2):
                h = hh * 16 + 2 * u + j
                rows = slice(h * 64, (h + 1) * 64)
                qh[u, j * 64:(j + 1) * 64, 0:T] = roped[e][rows]
                qh[u, j * 64:(j + 1) * 64, T:2 * T] = roped[o][rows]
        kf2 = np.empty((2, 128, 4096), BFD)
        vt2 = np.empty((2, 128, 2048), BFD)
        for j in range(2):
            kvh = hh * 2 + j
            rows = slice(2048 + kvh * 64, 2048 + (kvh + 1) * 64)
            kf2[j, 0:64, 0:T] = roped[e][rows]
            kf2[j, 0:64, T:2 * T] = roped[o][rows]
            kf2[j, 64:128] = kf2[j, 0:64]
            vrows = slice(kvh * 64, (kvh + 1) * 64)
            vfull = np.concatenate([v1[e][vrows], v1[o][vrows]], axis=1)
            vt2[j] = _tiled_tok(np.ascontiguousarray(vfull.T), 64)
        maps.append(dict(qh=qh, kf2=kf2, vt2=vt2))
    return maps


def kernel(L0_attn_hidden_states, L0_attn_memories, L0_attn_memory_norms,
           L0_attn_w_q, L0_attn_w_k, L0_attn_w_v, L0_attn_w_o, L0_attn_gate,
           L0_ffn_router_w, L0_ffn_down_w, L0_ffn_up_w,
           L1_attn_w_qkv, L1_attn_w_o, L1_ffn_w_gate_up, L1_ffn_w_proj):
    x = np.ascontiguousarray(np.asarray(L0_attn_hidden_states, np.float32))
    xs = shard_tokens(x)

    xn, _ = run_ln0(xs)
    xn_feat = [tr(a) for a in xn]
    wqkv0 = np.concatenate(
        [np.ascontiguousarray(np.asarray(w, np.float32).T)
         for w in (L0_attn_w_q, L0_attn_w_k, L0_attn_w_v)], axis=1)
    qkv0 = run_gemm("qkv0", [wqkv0], [xn_feat], None, BFD)

    maps0 = _prep_l0attn(qkv0, np.asarray(L0_attn_memories, np.float32),
                         np.asarray(L0_attn_memory_norms, np.float32),
                         np.asarray(L0_attn_gate, np.float32))
    outm = run_l0attn(maps0)

    rwt = np.ascontiguousarray(np.asarray(L0_ffn_router_w, np.float32).T)
    dwt = np.ascontiguousarray(
        np.transpose(np.asarray(L0_ffn_down_w, np.float32),
                     (2, 0, 1)).reshape(2048, 128))
    asct = run_l0ffna(xn_feat, rwt, dwt)

    mixf = []
    for c in range(NCORE):
        b, hh = c // 2, c % 2
        mf = np.empty((2048, T), BFD)
        for src_core, hbase in ((2 * b, 0), (2 * b + 1, 8)):
            om = np.asarray(outm[src_core]).reshape(8, 128, 32, 128)
            for j in range(8):
                h = hbase + j
                blk2 = om[j].transpose(1, 0, 2).reshape(4096, 128)
                mf[h * 128:(h + 1) * 128] = blk2[hh * T:(hh + 1) * T].T
        mixf.append(np.ascontiguousarray(mf))
    wot = np.ascontiguousarray(np.asarray(L0_attn_w_o, np.float32).T)
    upt = np.ascontiguousarray(
        np.transpose(np.asarray(L0_ffn_up_w, np.float32),
                     (0, 2, 1)).reshape(128, 2048))
    xres = [np.ascontiguousarray(x[c // 2, (c % 2) * T:(c % 2 + 1) * T].T)
            for c in range(NCORE)]
    x1 = run_gemm("oproj0", [wot, upt], [mixf, asct], xres, BFD)

    x1tok = [tr(a) for a in x1]
    xn1 = run_ln1(x1tok)
    xn1f = [tr(a) for a in xn1]
    wqkvt = np.ascontiguousarray(np.asarray(L1_attn_w_qkv, np.float32).T)
    qkv1 = run_gemm("qkv1", [wqkvt], [xn1f], None, BFD)
    cos_half, sin_half = _rope_tables()
    qk = [np.ascontiguousarray(qkv1[c][0:2304]) for c in range(NCORE)]
    roped = run_rope(qk, cos_half, sin_half)
    v1 = [np.ascontiguousarray(qkv1[c][2304:2560]) for c in range(NCORE)]

    maps1 = _prep_l1attn(roped, v1)
    oh = run_l1attn(maps1)

    ahf = []
    for c in range(NCORE):
        b, hh = c // 2, c % 2
        af = np.empty((2048, T), BFD)
        tok = slice(hh * T, (hh + 1) * T)
        for src_core, hbase in ((2 * b, 0), (2 * b + 1, 16)):
            blk2 = np.asarray(oh[src_core]).reshape(16, 64, 4096)
            for uh in range(16):
                h = hbase + uh
                af[h * 64:(h + 1) * 64] = blk2[uh][:, tok]
        ahf.append(np.ascontiguousarray(af))
    wot1 = np.ascontiguousarray(np.asarray(L1_attn_w_o, np.float32).T)
    p2 = run_gemm("oproj1", [wot1], [ahf], x1, BFD)

    wguT = np.ascontiguousarray(np.asarray(L1_ffn_w_gate_up, np.float32).T)
    wprT = np.ascontiguousarray(np.asarray(L1_ffn_w_proj, np.float32).T)
    w1s = [np.ascontiguousarray(wguT[:, c * 2048:(c + 1) * 2048])
           for c in range(NCORE)]
    w2s = [np.ascontiguousarray(wprT[c * 1024:(c + 1) * 1024])
           for c in range(NCORE)]
    q1, q2 = run_quant(w1s, w2s)
    wgu_q = np.ascontiguousarray(np.concatenate(
        [np.asarray(q) for q in q1], axis=1))
    wpr_q = np.ascontiguousarray(np.concatenate(
        [np.asarray(q) for q in q2], axis=0))

    x2 = run_ffn1(xn1f, wgu_q, wpr_q, p2)
    out = unshard_tokens([np.ascontiguousarray(tr(np.asarray(a, np.float32)))
                          for a in x2], 2048)
    return out


# version 05
# speedup vs baseline: 1.1225x; 1.0914x over previous
import contextlib
import numpy as np
import ml_dtypes
import concourse.bass as bass
import concourse.mybir as mybir
from concourse.bass_utils import run_bass_kernel_spmd

F32 = mybir.dt.float32
BF16 = mybir.dt.bfloat16
AF = mybir.ActivationFunctionType
ALU = mybir.AluOpType
AX = mybir.AxisListType.X
NCORE = 8
T = 2048
D = 2048
BFD = ml_dtypes.bfloat16

_prog_cache = {}


def _cached(key, fn):
    if key not in _prog_cache:
        _prog_cache[key] = fn()
    return _prog_cache[key]


def _run(nc, in_maps):
    res = run_bass_kernel_spmd(nc, in_maps, core_ids=list(range(NCORE)))
    return res.results


def build_ln(with_sum):
    nc = bass.Bass("TRN2", target_bir_lowering=False, detect_race_conditions=False)
    x = nc.dram_tensor("x", [T, D], BF16 if not with_sum else F32,
                       kind="ExternalInput").ap()
    xn = nc.dram_tensor("xn", [T, D], BF16, kind="ExternalOutput").ap()
    xs = nc.dram_tensor("xsum", [1, D], F32, kind="ExternalOutput").ap()
    ones_d = nc.inline_tensor(np.ones((128, 1), BFD), "ones_d").ap()
    eps_d = nc.inline_tensor(np.full((128, 1), 1e-5, np.float32), "eps_d").ap()
    NT = T // 128
    with contextlib.ExitStack() as st:
        E = st.enter_context
        xdt = F32 if with_sum else BF16
        xsb = [E(nc.sbuf_tensor(f"xsb{i}", [128, D], xdt)) for i in range(2)]
        sq = E(nc.sbuf_tensor("sq", [128, D], F32))
        xnb = [E(nc.sbuf_tensor(f"xnb{i}", [128, D], BF16)) for i in range(2)]
        stat = E(nc.sbuf_tensor("stat", [128, 8], F32))
        ones_sb = E(nc.sbuf_tensor("ones_sb", [128, 1], BF16))
        eps_sb = E(nc.sbuf_tensor("eps_sb", [128, 1], F32))
        xs_sb = E(nc.sbuf_tensor("xs_sb", [1, D], F32))
        ps = E(nc.psum_tensor("ps", [1, D], F32))
        olds = E(nc.semaphore("olds"))
        vs = E(nc.semaphore("vs"))
        ld2 = [E(nc.semaphore(f"ld{p}")) for p in range(2)]
        ev = E(nc.semaphore("ev"))
        sdone = E(nc.semaphore("sdone"))
        tmm = E(nc.semaphore("tmm"))
        stv = E(nc.semaphore("stv"))
        blk = E(nc.Block())

        @blk.sync
        def _(sy):
            sy.dma_start(out=ones_sb[:], in_=ones_d).then_inc(olds, 16)
            sy.dma_start(out=eps_sb[:], in_=eps_d).then_inc(olds, 16)
            for i in range(NT):
                if i >= 2:
                    sy.wait_ge(ev, i - 1)
                    sy.dma_start(out=xn[(i - 2) * 128:(i - 1) * 128, :],
                                 in_=xnb[(i - 2) % 2][:]).then_inc(stv, 16)
                sy.dma_start(out=xsb[i % 2][:], in_=x[i * 128:(i + 1) * 128, :]
                             ).then_inc(ld2[i % 2], 16)
            sy.wait_ge(ev, NT - 1)
            sy.dma_start(out=xn[(NT - 2) * 128:(NT - 1) * 128, :],
                         in_=xnb[(NT - 2) % 2][:]).then_inc(stv, 16)
            sy.wait_ge(ev, NT)
            sy.dma_start(out=xn[(NT - 1) * 128:NT * 128, :],
                         in_=xnb[(NT - 1) % 2][:]).then_inc(stv, 16)
            if with_sum:
                sy.wait_ge(ev, NT + 1)
                sy.dma_start(out=xs, in_=xs_sb[:]).then_inc(stv, 16)
            sy.wait_ge(stv, 16 * (NT + (1 if with_sum else 0)))

        @blk.vector
        def _(v):
            nvs = 0
            for i in range(NT):
                v.wait_ge(ld2[i % 2], 16 * (i // 2 + 1))
                if i >= 2:
                    if with_sum:
                        v.wait_ge(tmm, i - 1)
                    v.wait_ge(stv, 16 * (i - 1))
                b = xsb[i % 2]
                o = xnb[i % 2]
                v.reduce_sum(stat[:, 0:1], b[:], axis=AX)
                v.tensor_tensor(out=sq[:], in0=b[:], in1=b[:], op=ALU.mult)
                v.reduce_sum(stat[:, 2:3], sq[:], axis=AX).then_inc(vs, 1)
                nvs += 1
                v.wait_ge(vs, nvs)
                v.tensor_scalar(out=stat[:, 1:2], in0=stat[:, 0:1],
                                scalar1=1.0 / D, scalar2=None, op0=ALU.mult)
                v.tensor_scalar(out=stat[:, 3:4], in0=stat[:, 2:3],
                                scalar1=1.0 / D, scalar2=None,
                                op0=ALU.mult).then_inc(vs, 1)
                nvs += 1
                v.wait_ge(vs, nvs)
                v.tensor_tensor(out=stat[:, 4:5], in0=stat[:, 1:2],
                                in1=stat[:, 1:2], op=ALU.mult).then_inc(vs, 1)
                nvs += 1
                v.wait_ge(vs, nvs)
                v.tensor_tensor(out=stat[:, 5:6], in0=stat[:, 3:4],
                                in1=stat[:, 4:5],
                                op=ALU.subtract).then_inc(sdone, 1)
                v.wait_ge(sdone, 2 * (i + 1))
                v.reciprocal(stat[:, 7:8], stat[:, 6:7]).then_inc(vs, 1)
                nvs += 1
                v.wait_ge(vs, nvs)
                v.tensor_scalar(out=sq[:], in0=b[:], scalar1=stat[:, 1:2],
                                scalar2=None, op0=ALU.subtract)
                v.tensor_scalar(out=o[:], in0=sq[:], scalar1=stat[:, 7:8],
                                scalar2=None, op0=ALU.mult).then_inc(ev, 1)
            if with_sum:
                v.wait_ge(tmm, NT)
                v.tensor_copy(xs_sb[:], ps[:]).then_inc(ev, 1)

        @blk.scalar
        def _(s):
            s.wait_ge(olds, 32)
            for i in range(NT):
                s.wait_ge(sdone, 2 * i + 1)
                s.activation(out=stat[:, 6:7], in_=stat[:, 5:6],
                             func=AF.Sqrt, bias=eps_sb[:, 0:1],
                             scale=1.0).then_inc(sdone, 1)

        if with_sum:
            @blk.tensor
            def _(t):
                t.wait_ge(olds, 16)
                for i in range(NT):
                    t.wait_ge(ev, i + 1)
                    for c in range(D // 512):
                        ins = t.matmul(ps[0:1, c * 512:(c + 1) * 512], ones_sb[:],
                                       xnb[i % 2][:, c * 512:(c + 1) * 512],
                                       start=(i == 0), stop=(i == NT - 1))
                    ins.then_inc(tmm, 1)
    return nc


def run_ln0(x_tok_percore):
    nc = _cached("ln0", lambda: build_ln(True))
    maps = [{"x": xc} for xc in x_tok_percore]
    res = _run(nc, maps)
    return ([r["xn"] for r in res], [r["xsum"] for r in res])


def run_ln1(x1_tok_percore):
    nc = _cached("ln1", lambda: build_ln(False))
    maps = [{"x": xc} for xc in x1_tok_percore]
    res = _run(nc, maps)
    return [r["xn"] for r in res]


def build_gemm(key, k_list, fout, x_dtypes, w_dtypes, res_dtype, out_dtype,
               t_total=T):
    nc = bass.Bass("TRN2", target_bir_lowering=False, detect_race_conditions=False)
    nj = len(k_list)
    ws = [nc.dram_tensor(f"w{j}", [k_list[j], fout], w_dtypes[j],
                         kind="ExternalInput").ap() for j in range(nj)]
    xs = [nc.dram_tensor(f"x{j}", [k_list[j], t_total], x_dtypes[j],
                         kind="ExternalInput").ap() for j in range(nj)]
    res_in = (nc.dram_tensor("res", [fout, t_total], res_dtype,
                             kind="ExternalInput").ap()
              if res_dtype is not None else None)
    out = nc.dram_tensor("out", [fout, t_total], out_dtype,
                         kind="ExternalOutput").ap()
    OG = fout // 128
    TC = t_total // 512
    NKT = [k // 128 for k in k_list]
    nk_tot = sum(NKT)
    with contextlib.ExitStack() as st:
        E = st.enter_context
        x_sb = [E(nc.sbuf_tensor(f"x_sb{j}", [128, NKT[j] * t_total], BF16))
                for j in range(nj)]
        w_sb = [[E(nc.sbuf_tensor(f"w_sb{j}_{p}", [128, NKT[j] * 128], BF16))
                 for j in range(nj)] for p in range(2)]
        r_sb = [E(nc.sbuf_tensor(f"r_sb{p}", [128, 512],
                                 res_dtype if res_dtype else BF16))
                for p in range(2)] if res_dtype is not None else None
        o_sb = [E(nc.sbuf_tensor(f"o_sb{p}", [128, 512], out_dtype))
                for p in range(4)]
        ps = [E(nc.psum_tensor(f"ps{p}", [128, 512], F32)) for p in range(4)]
        xld = E(nc.semaphore("xld"))
        wld2 = [E(nc.semaphore(f"wld{p}")) for p in range(2)]
        rld = E(nc.semaphore("rld"))
        mmd = E(nc.semaphore("mmd"))
        evd = E(nc.semaphore("evd"))
        stv4 = [E(nc.semaphore(f"stv{p}")) for p in range(4)]
        blk = E(nc.Block())

        @blk.gpsimd
        def _(g):
            for j in range(nj):
                for kt in range(NKT[j]):
                    g.dma_start(out=x_sb[j][:, kt * t_total:(kt + 1) * t_total],
                                in_=xs[j][kt * 128:(kt + 1) * 128, :]
                                ).then_inc(xld, 16)
            for og in range(OG):
                if og >= 2:
                    g.wait_ge(mmd, (og - 1) * TC)
                for j in range(nj):
                    g.dma_start(
                        out=w_sb[og % 2][j].ap().rearrange(
                            "p (a f) -> p a f", f=128),
                        in_=ws[j][:, og * 128:(og + 1) * 128].rearrange(
                            "(a p) f -> p a f", p=128)
                    ).then_inc(wld2[og % 2], 16)

        @blk.sync
        def _(sy):
            for og in range(OG):
                for tc in range(TC):
                    gidx = og * TC + tc
                    if res_in is not None:
                        if gidx >= 2:
                            sy.wait_ge(evd, gidx - 1)
                        sy.dma_start(out=r_sb[gidx % 2][:],
                                     in_=res_in[og * 128:(og + 1) * 128,
                                                tc * 512:(tc + 1) * 512]
                                     ).then_inc(rld, 16)
                    sy.wait_ge(evd, gidx + 1)
                    sy.dma_start(out=out[og * 128:(og + 1) * 128,
                                         tc * 512:(tc + 1) * 512],
                                 in_=o_sb[gidx % 4][:]).then_inc(stv4[gidx % 4], 16)
            ngr = OG * TC
            for p in range(4):
                sy.wait_ge(stv4[p], 16 * ((ngr - 1 - p) // 4 + 1))

        @blk.tensor
        def _(t):
            t.wait_ge(xld, 16 * nk_tot)
            for og in range(OG):
                t.wait_ge(wld2[og % 2], 16 * nj * (og // 2 + 1))
                for tc in range(TC):
                    gidx = og * TC + tc
                    if gidx >= 4:
                        t.wait_ge(evd, gidx - 3)
                    p = ps[gidx % 4]
                    n = 0
                    for j in range(nj):
                        for kt in range(NKT[j]):
                            n += 1
                            ins = t.matmul(
                                p[:],
                                w_sb[og % 2][j][:, kt * 128:(kt + 1) * 128],
                                x_sb[j][:, kt * t_total + tc * 512:
                                        kt * t_total + (tc + 1) * 512],
                                start=(n == 1), stop=(n == nk_tot))
                    ins.then_inc(mmd, 1)

        @blk.vector
        def _(v):
            for og in range(OG):
                for tc in range(TC):
                    gidx = og * TC + tc
                    v.wait_ge(mmd, gidx + 1)
                    if gidx >= 4:
                        v.wait_ge(stv4[gidx % 4], 16 * ((gidx - 4) // 4 + 1))
                    if res_in is not None:
                        v.wait_ge(rld, 16 * (gidx + 1))
                        v.tensor_tensor(out=o_sb[gidx % 4][:],
                                        in0=ps[gidx % 4][:],
                                        in1=r_sb[gidx % 2][:],
                                        op=ALU.add).then_inc(evd, 1)
                    else:
                        v.tensor_copy(o_sb[gidx % 4][:],
                                      ps[gidx % 4][:]).then_inc(evd, 1)
    return nc


def run_gemm(key, w_full_list, x_percore_list, res_percore, out_dtype_np):
    k_list = [w.shape[0] for w in w_full_list]
    fout = w_full_list[0].shape[1]
    w_dt = [F32 if w.dtype == np.float32 else BF16 for w in w_full_list]
    x_dt = [BF16] * len(w_full_list)
    res_dt = None
    if res_percore is not None:
        res_dt = F32 if res_percore[0].dtype == np.float32 else BF16
    out_dt = F32 if out_dtype_np == np.float32 else BF16
    nc = _cached(("gemm", key), lambda: build_gemm(
        key, tuple(k_list), fout, tuple(x_dt), tuple(w_dt), res_dt, out_dt))
    maps = []
    for c in range(NCORE):
        m = {}
        for j, w in enumerate(w_full_list):
            m[f"w{j}"] = w
            m[f"x{j}"] = x_percore_list[j][c]
        if res_percore is not None:
            m["res"] = res_percore[c]
        maps.append(m)
    res = _run(nc, maps)
    return [r["out"] for r in res]



def build_l0attn():
    S = 4096
    NCH = 32
    NP = 8
    SCALE = 128.0 ** -0.5
    nc = bass.Bass("TRN2", target_bir_lowering=False, detect_race_conditions=False)
    qf = nc.dram_tensor("qf", [NP, 128, S], BF16, kind="ExternalInput").ap()
    kf = nc.dram_tensor("kf", [NP, 128, S], BF16, kind="ExternalInput").ap()
    kt = nc.dram_tensor("kt", [NP, 128, S], BF16, kind="ExternalInput").ap()
    vt = nc.dram_tensor("vt", [NP, 128, S], BF16, kind="ExternalInput").ap()
    mem = nc.dram_tensor("mem", [NP, 128, 520], F32, kind="ExternalInput").ap()
    mns = nc.dram_tensor("mns", [NP, 128, 4], F32, kind="ExternalInput").ap()
    gatev = nc.dram_tensor("gatev", [NP, 1, 1], F32, kind="ExternalInput").ap()
    mnflat = nc.dram_tensor("mnflat", [1, 8192], F32, kind="ExternalInput").ap()
    outm = nc.dram_tensor("outm", [NP, 128, S], BF16, kind="ExternalOutput").ap()
    mask_np = (np.arange(128)[None, :] >= np.arange(128)[:, None]).astype(BFD)
    mask_d = nc.inline_tensor(mask_np, "mask_d").ap()
    ones1_d = nc.inline_tensor(np.ones((1, 128), BFD), "ones1_d").ap()

    VN = 4 + 3 * NCH
    TN = 2 + 2 * NCH
    EN = 10

    def v_relq(p): return VN * p + 1
    def v_b6(p): return VN * p + 2
    def v_memw(p): return VN * p + 3
    def v_state0(p): return VN * p + 4
    def v_scorem(p, c): return VN * p + 5 + 3 * c
    def v_chunkev(p, c): return VN * p + 7 + 3 * c
    def v_end(p): return VN * (p + 1)

    def t_rel(p): return TN * p + 1
    def t_b6(p): return TN * p + 2
    def t_score(p, c): return TN * p + 3 + 2 * c
    def t_chunk(p, c): return TN * p + 4 + 2 * c

    def e_elu_v(p, i): return EN * p + 2 * i + 1
    def e_elu_s(p, i): return EN * p + 2 * i + 2
    def e_rel_v(p): return EN * p + 7
    def e_rel_s(p): return EN * p + 8
    def e_gate_v(p): return EN * p + 9
    def e_gate_s(p): return EN * p + 10

    with contextlib.ExitStack() as st:
        E = st.enter_context
        q_sb = [E(nc.sbuf_tensor(f"q_sb{p}", [128, S], BF16)) for p in range(2)]
        kf_sb = [E(nc.sbuf_tensor(f"kf_sb{p}", [128, S], BF16)) for p in range(2)]
        kt_sb = [E(nc.sbuf_tensor(f"kt_sb{p}", [128, S], BF16)) for p in range(2)]
        va_sb = [E(nc.sbuf_tensor(f"va_sb{p}", [128, NCH * 132], BF16))
                 for p in range(2)]
        mem_sb = [E(nc.sbuf_tensor(f"mem_sb{p}", [128, 520], F32)) for p in range(2)]
        mns_sb = [E(nc.sbuf_tensor(f"mns_sb{p}", [128, 4], BF16)) for p in range(2)]
        gt_sb = [E(nc.sbuf_tensor(f"gt_sb{p}", [1, 1], F32)) for p in range(2)]
        sgq = E(nc.sbuf_tensor("sgq", [128, S], BF16))
        kpf = E(nc.sbuf_tensor("kpf", [128, S], BF16))
        kpt = E(nc.sbuf_tensor("kpt", [128, S], BF16))
        tmp1 = E(nc.sbuf_tensor("tmp1", [128, S], BF16))
        tmp2 = E(nc.sbuf_tensor("tmp2", [128, S], BF16))
        memw = E(nc.sbuf_tensor("memw", [128, 520], BF16))
        ob = E(nc.sbuf_tensor("ob", [128, S], BF16))
        mask_sb = E(nc.sbuf_tensor("mask_sb", [128, 128], BF16))
        ones1_sb = E(nc.sbuf_tensor("ones1_sb", [1, 128], BF16))
        mnf_sb = E(nc.sbuf_tensor("mnf_sb", [1, 8192], F32))
        emt = E(nc.sbuf_tensor("emt", [1, 16], F32))
        relb = E(nc.sbuf_tensor("relb", [1, 32], F32))
        wg6 = E(nc.sbuf_tensor("wg6", [1, 6], BF16))
        wgb = E(nc.sbuf_tensor("wgb", [128, 6], F32))
        qsum = E(nc.sbuf_tensor("qsum", [128, 1], F32))
        qsum_bf = E(nc.sbuf_tensor("qsum_bf", [128, 1], BF16))
        S_f = [E(nc.sbuf_tensor(f"S_f{p}", [128, 129], F32)) for p in range(2)]
        S_bf = E(nc.sbuf_tensor("S_bf", [128, 129], BF16))
        pstm = E(nc.sbuf_tensor("pstm", [128, 128], BF16))
        den5 = E(nc.sbuf_tensor("den5", [128, 12], F32))
        loc = E(nc.sbuf_tensor("loc", [128, 128], F32))
        acc = [E(nc.sbuf_tensor(f"acc{p}", [128, 128], F32)) for p in range(2)]
        ps_rel = E(nc.psum_tensor("ps_rel", [1, 8], F32))
        ps_b6 = E(nc.psum_tensor("ps_b6", [128, 6], F32))
        ps_s = E(nc.psum_tensor("ps_s", [128, 128], F32))
        ps_out = E(nc.psum_tensor("ps_out", [128, 129], F32))
        ps_am1 = E(nc.psum_tensor("ps_am1", [128, 260], F32))
        ps_am2 = E(nc.psum_tensor("ps_am2", [128, 260], F32))
        ps_kv = E(nc.psum_tensor("ps_kv", [128, 129], F32))
        cld = E(nc.semaphore("cld"))
        msd = E(nc.semaphore("msd"))
        pld = [E(nc.semaphore(f"pld{p}")) for p in range(2)]
        tsem = E(nc.semaphore("tsem"))
        vsem = E(nc.semaphore("vsem"))
        esem = E(nc.semaphore("esem"))
        vs = E(nc.semaphore("vs"))
        stv = E(nc.semaphore("stv"))
        blk = E(nc.Block())

        @blk.gpsimd
        def _(g):
            g.dma_start(out=mask_sb[:], in_=mask_d).then_inc(cld, 16)
            g.dma_start(out=ones1_sb[:], in_=ones1_d).then_inc(cld, 16)
            g.dma_start(out=mnf_sb[:], in_=mnflat).then_inc(cld, 16)
            g.memset(va_sb[0][:], 1.0)
            g.memset(va_sb[1][:], 1.0)
            g.memset(S_f[0][:], 0.0)
            g.memset(S_bf[:], 0.0).then_inc(msd, 1)
            for p in range(NP):
                par = p % 2
                if p >= 2:
                    g.wait_ge(vsem, v_end(p - 2))
                    g.wait_ge(tsem, t_chunk(p - 2, NCH - 1))
                g.dma_start(out=q_sb[par][:], in_=qf[p]).then_inc(pld[par], 16)
                g.dma_start(out=kf_sb[par][:], in_=kf[p]).then_inc(pld[par], 16)
                g.dma_start(out=kt_sb[par][:], in_=kt[p]).then_inc(pld[par], 16)
                g.dma_start(
                    out=va_sb[par].ap().rearrange("p (c w) -> p c w", w=132
                                                  )[:, :, 0:128],
                    in_=vt[p].rearrange("p (c w) -> p c w", w=128)
                ).then_inc(pld[par], 16)
                g.dma_start(out=mem_sb[par][:], in_=mem[p]).then_inc(pld[par], 16)
                g.dma_start(out=mns_sb[par][:], in_=mns[p]).then_inc(pld[par], 16)
                g.dma_start(out=gt_sb[par][:], in_=gatev[p]).then_inc(pld[par], 16)

        @blk.vector
        def _(v):
            nvs = 0

            def sync(ins):
                nonlocal nvs
                nvs += 1
                ins.then_inc(vs, 1)
                v.wait_ge(vs, nvs)

            v.wait_ge(cld, 48)
            sync(v.tensor_reduce(emt[:, 0:4], mnf_sb.ap().rearrange(
                "p (m d) -> p m d", d=2048), axis=AX, op=ALU.add))
            sync(v.tensor_scalar(out=emt[:, 4:8], in0=emt[:, 0:4],
                                 scalar1=1e-6, scalar2=None, op0=ALU.is_lt))
            sync(v.tensor_reduce(emt[:, 8:9], emt[:, 4:8], axis=AX, op=ALU.add))
            sync(v.tensor_scalar(out=emt[:, 9:10], in0=emt[:, 8:9],
                                 scalar1=3.5, scalar2=None, op0=ALU.is_lt))

            for p in range(NP):
                par = p % 2
                v.wait_ge(pld[par], 16 * 7 * (p // 2 + 1))
                if p >= 1:
                    v.wait_ge(stv, 16 * p)
                for i, (srcb, dstb) in enumerate(
                        ((q_sb[par], sgq), (kf_sb[par], kpf), (kt_sb[par], kpt))):
                    v.tensor_scalar(out=tmp1[:], in0=srcb[:], scalar1=0.0,
                                    scalar2=None, op0=ALU.min).then_inc(esem, 1)
                    v.wait_ge(esem, e_elu_s(p, i))
                    v.scalar_tensor_tensor(out=dstb[:], in0=srcb[:], scalar=0.0,
                                           in1=tmp2[:], op0=ALU.max, op1=ALU.add)
                sync(v.reduce_sum(qsum[:], q_sb[par][:], axis=AX))
                v.tensor_copy(qsum_bf[:], qsum[:]).then_inc(vsem, 1)
                v.wait_ge(tsem, t_rel(p))
                sync(v.tensor_scalar(out=relb[:, 0:4], in0=ps_rel[0:1, 0:4],
                                     scalar1=SCALE / 4096.0, scalar2=None,
                                     op0=ALU.mult))
                sync(v.scalar_tensor_tensor(out=relb[:, 4:8], in0=emt[:, 4:8],
                                            scalar=-1e38, in1=relb[:, 0:4],
                                            op0=ALU.mult, op1=ALU.add))
                v.reduce_max(relb[:, 8:9], relb[:, 4:8], axis=AX,
                             negate=True).then_inc(esem, 1)
                v.wait_ge(esem, e_rel_s(p))
                sync(v.reduce_sum(relb[:, 14:15], relb[:, 10:14], axis=AX))
                sync(v.reciprocal(relb[:, 15:16], relb[:, 14:15]))
                sync(v.tensor_scalar(out=relb[:, 16:20], in0=relb[:, 10:14],
                                     scalar1=relb[:, 15:16], scalar2=None,
                                     op0=ALU.mult))
                v.tensor_scalar(out=relb[:, 24:28], in0=relb[:, 16:20],
                                scalar1=emt[:, 9:10], scalar2=None,
                                op0=ALU.mult).then_inc(esem, 1)
                v.wait_ge(esem, e_gate_s(p))
                sync(v.tensor_scalar(out=relb[:, 29:30], in0=relb[:, 28:29],
                                     scalar1=-1.0, scalar2=1.0, op0=ALU.mult,
                                     op1=ALU.add))
                v.tensor_copy(wg6[:], relb[:, 24:30]).then_inc(vsem, 1)
                v.wait_ge(tsem, t_b6(p))
                sync(v.tensor_copy(wgb[:], ps_b6[:]))
                mv = memw.ap().rearrange("p (m w) -> p m w", w=130)
                ms = mem_sb[par].ap().rearrange("p (m w) -> p m w", w=130)
                for m in range(4):
                    v.tensor_scalar(out=memw[:, m * 130:m * 130 + 128],
                                    in0=mem_sb[par][:, m * 130:m * 130 + 128],
                                    scalar1=wgb[:, m:m + 1], scalar2=None,
                                    op0=ALU.mult)
                v.tensor_copy(mv[:, :, 128:129],
                              ms[:, :, 128:129]).then_inc(vsem, 1)
                if p == 0:
                    v.wait_ge(msd, 1)
                v.tensor_scalar(out=S_f[0][:], in0=S_f[0][:], scalar1=0.0,
                                scalar2=None, op0=ALU.mult)
                v.tensor_scalar(out=S_bf[:], in0=S_f[0][:], scalar1=0.0,
                                scalar2=None, op0=ALU.mult).then_inc(vsem, 1)
                for c in range(NCH):
                    cpar = c % 2
                    v.wait_ge(tsem, t_score(p, c))
                    v.tensor_tensor(out=pstm[:], in0=ps_s[:], in1=mask_sb[:],
                                    op=ALU.mult).then_inc(vsem, 1)
                    v.wait_ge(tsem, t_chunk(p, c))
                    v.tensor_tensor(out=S_f[1 - cpar][:], in0=S_f[cpar][:],
                                    in1=ps_kv[:], op=ALU.add)
                    v.tensor_copy(S_bf[:], S_f[1 - cpar][:]).then_inc(vsem, 1)
                    v.tensor_copy(den5[:, 0:1], ps_out[:, 128:129])
                    v.tensor_copy(den5[:, 1:2], ps_am1[:, 128:129])
                    v.tensor_copy(den5[:, 2:3], ps_am1[:, 258:259])
                    v.tensor_copy(den5[:, 3:4], ps_am2[:, 128:129])
                    sync(v.tensor_copy(den5[:, 4:5], ps_am2[:, 258:259]))
                    sync(v.tensor_scalar(out=den5[:, 6:11], in0=den5[:, 0:5],
                                         scalar1=1e-6, scalar2=None,
                                         op0=ALU.max))
                    sync(v.reciprocal(den5[:, 0:5], den5[:, 6:11]))
                    v.tensor_scalar(out=loc[:], in0=ps_out[:, 0:128],
                                    scalar1=den5[:, 0:1], scalar2=None,
                                    op0=ALU.mult)
                    v.tensor_scalar(out=acc[0][:], in0=ps_am1[:, 0:128],
                                    scalar1=den5[:, 1:2], scalar2=None,
                                    op0=ALU.mult)
                    v.scalar_tensor_tensor(out=acc[1][:], in0=ps_am1[:, 130:258],
                                           scalar=den5[:, 2:3], in1=acc[0][:],
                                           op0=ALU.mult, op1=ALU.add)
                    v.scalar_tensor_tensor(out=acc[0][:], in0=ps_am2[:, 0:128],
                                           scalar=den5[:, 3:4], in1=acc[1][:],
                                           op0=ALU.mult, op1=ALU.add)
                    v.scalar_tensor_tensor(out=acc[1][:], in0=ps_am2[:, 130:258],
                                           scalar=den5[:, 4:5], in1=acc[0][:],
                                           op0=ALU.mult, op1=ALU.add)
                    v.tensor_scalar(out=loc[:], in0=loc[:], scalar1=wgb[:, 5:6],
                                    scalar2=None, op0=ALU.mult)
                    v.scalar_tensor_tensor(
                        out=ob[:, c * 128:(c + 1) * 128], in0=acc[1][:],
                        scalar=wgb[:, 4:5], in1=loc[:], op0=ALU.mult,
                        op1=ALU.add).then_inc(vsem, 1)

        @blk.scalar
        def _(s):
            for p in range(NP):
                for i in range(3):
                    s.wait_ge(esem, e_elu_v(p, i))
                    s.activation(out=tmp2[:], in_=tmp1[:],
                                 func=AF.Exp).then_inc(esem, 1)
                s.wait_ge(esem, e_rel_v(p))
                s.activation(out=relb[:, 10:14], in_=relb[:, 4:8], func=AF.Exp,
                             bias=relb[:, 8:9], scale=1.0).then_inc(esem, 1)
                s.wait_ge(esem, e_gate_v(p))
                s.activation(out=relb[:, 28:29], in_=gt_sb[p % 2][:],
                             func=AF.Sigmoid).then_inc(esem, 1)

        @blk.tensor
        def _(t):
            t.wait_ge(cld, 48)
            t.wait_ge(msd, 1)
            for p in range(NP):
                par = p % 2
                t.wait_ge(vsem, v_relq(p))
                t.matmul(ps_rel[0:1, 0:4], qsum_bf[:], mns_sb[par][:],
                         start=True, stop=True).then_inc(tsem, 1)
                t.wait_ge(vsem, v_b6(p))
                t.matmul(ps_b6[:], ones1_sb[:], wg6[:],
                         start=True, stop=True).then_inc(tsem, 1)
                t.wait_ge(vsem, v_memw(p))
                for c in range(NCH):
                    cs = slice(c * 128, (c + 1) * 128)
                    vsl = slice(c * 132, c * 132 + 129)
                    if c == 0:
                        t.wait_ge(vsem, v_state0(p))
                    else:
                        t.wait_ge(vsem, v_chunkev(p, c - 1))
                    t.matmul(ps_s[:], kpf[:, cs], sgq[:, cs],
                             start=True, stop=True).then_inc(tsem, 1)
                    t.matmul(ps_out[:], sgq[:, cs], S_bf[:],
                             start=True, stop=False)
                    t.matmul(ps_am1[:, 0:129], sgq[:, cs], memw[:, 0:129],
                             start=True, stop=True)
                    t.matmul(ps_am1[:, 130:259], sgq[:, cs], memw[:, 130:259],
                             start=True, stop=True)
                    t.matmul(ps_am2[:, 0:129], sgq[:, cs], memw[:, 260:389],
                             start=True, stop=True)
                    t.matmul(ps_am2[:, 130:259], sgq[:, cs], memw[:, 390:519],
                             start=True, stop=True)
                    t.wait_ge(vsem, v_scorem(p, c))
                    t.matmul(ps_out[:], pstm[:], va_sb[par][:, vsl],
                             start=False, stop=True)
                    t.matmul(ps_kv[:], kpt[:, cs], va_sb[par][:, vsl],
                             start=True, stop=True).then_inc(tsem, 1)

        @blk.sync
        def _(sy):
            for p in range(NP):
                sy.wait_ge(vsem, v_end(p))
                sy.dma_start(out=outm[p], in_=ob[:]).then_inc(stv, 16)
            sy.wait_ge(stv, 16 * NP)
    return nc


def run_l0attn(in_percore):
    nc = _cached("l0attn", build_l0attn)
    res = _run(nc, in_percore)
    return [r["outm"] for r in res]


def build_l0ffna():
    NK = D // 128
    NT = T // 128
    nc = bass.Bass("TRN2", target_bir_lowering=False, detect_race_conditions=False)
    xnf = nc.dram_tensor("xnf", [D, T], BF16, kind="ExternalInput").ap()
    rwt = nc.dram_tensor("rwt", [D, 8], F32, kind="ExternalInput").ap()
    dwt = nc.dram_tensor("dwt", [D, 128], F32, kind="ExternalInput").ap()
    aout_d = nc.dram_tensor("aout", [128, T], BF16, kind="ExternalOutput").ap()
    ident_d = nc.inline_tensor(np.eye(128, dtype=BFD), "ident_d").ap()

    with contextlib.ExitStack() as st:
        E = st.enter_context
        xn_sb = E(nc.sbuf_tensor("xn_sb", [128, NK * T], BF16))
        rw_sb = E(nc.sbuf_tensor("rw_sb", [128, NK * 8], BF16))
        dw_sb = E(nc.sbuf_tensor("dw_sb", [128, NK * 128], BF16))
        ident = E(nc.sbuf_tensor("ident", [128, 128], BF16))
        p8 = E(nc.sbuf_tensor("p8", [128, 8], F32))
        w8 = E(nc.sbuf_tensor("w8", [128, 8], F32))
        w2b = E(nc.sbuf_tensor("w2b", [128, 8], F32))
        g8 = E(nc.sbuf_tensor("g8", [128, 8], F32))
        stt_ = E(nc.sbuf_tensor("stt_", [128, 8], F32))
        a_sb = E(nc.sbuf_tensor("a_sb", [128, 128], F32))
        asc = E(nc.sbuf_tensor("asc", [128, 128], BF16))
        aout = E(nc.sbuf_tensor("aout_sb", [128, T], BF16))
        ps_r = E(nc.psum_tensor("ps_r", [128, 8], F32))
        ps_h = E(nc.psum_tensor("ps_h", [128, 128], F32))
        ps_t = E(nc.psum_tensor("ps_t", [128, 128], BF16))
        ld = E(nc.semaphore("ld"))
        tsem = E(nc.semaphore("tsem"))
        vsem = E(nc.semaphore("vsem"))
        esem = E(nc.semaphore("esem"))
        vs = E(nc.semaphore("vs"))
        stv = E(nc.semaphore("stv"))
        blk = E(nc.Block())

        @blk.gpsimd
        def _(g):
            g.dma_start(out=xn_sb.ap().rearrange("p (a t) -> p a t", t=T),
                        in_=xnf.rearrange("(a p) t -> p a t", p=128)
                        ).then_inc(ld, 16)
            g.dma_start(out=rw_sb.ap().rearrange("p (a f) -> p a f", f=8),
                        in_=rwt.rearrange("(a p) f -> p a f", p=128)
                        ).then_inc(ld, 16)
            g.dma_start(out=dw_sb.ap().rearrange("p (a f) -> p a f", f=128),
                        in_=dwt.rearrange("(a p) f -> p a f", p=128)
                        ).then_inc(ld, 16)
            g.dma_start(out=ident[:], in_=ident_d).then_inc(ld, 16)

        @blk.tensor
        def _(t):
            t.wait_ge(ld, 64)
            for i in range(NT):
                if i >= 1:
                    t.wait_ge(vsem, 2 * i)
                for k in range(NK):
                    t.matmul(ps_r[:], xn_sb[:, k * T + i * 128:k * T + (i + 1) * 128],
                             rw_sb[:, k * 8:(k + 1) * 8],
                             start=(k == 0), stop=(k == NK - 1))
                for k in range(NK):
                    ins = t.matmul(ps_h[:],
                                   xn_sb[:, k * T + i * 128:k * T + (i + 1) * 128],
                                   dw_sb[:, k * 128:(k + 1) * 128],
                                   start=(k == 0), stop=(k == NK - 1))
                ins.then_inc(tsem, 1)
                t.wait_ge(vsem, 2 * i + 1)
                t.transpose(ps_t[:], asc[:], ident[:]).then_inc(tsem, 1)

        @blk.vector
        def _(v):
            nvs = 0

            def sync(ins):
                nonlocal nvs
                nvs += 1
                ins.then_inc(vs, 1)
                v.wait_ge(vs, nvs)

            for i in range(NT):
                v.wait_ge(tsem, 2 * i + 1)
                v.reduce_max(stt_[:, 0:1], ps_r[:], axis=AX,
                             negate=True).then_inc(esem, 1)
                v.wait_ge(esem, 3 * i + 2)
                sync(v.reduce_sum(stt_[:, 1:2], p8[:], axis=AX))
                sync(v.reciprocal(stt_[:, 2:3], stt_[:, 1:2]))
                sync(v.tensor_scalar(out=w8[:], in0=p8[:], scalar1=stt_[:, 2:3],
                                     scalar2=None, op0=ALU.mult))
                sync(v.reduce_max(stt_[:, 3:4], w8[:], axis=AX))
                sync(v.tensor_scalar(out=w2b[:], in0=w8[:], scalar1=stt_[:, 3:4],
                                     scalar2=None, op0=ALU.is_ge))
                sync(v.scalar_tensor_tensor(out=w2b[:], in0=w2b[:], scalar=-1e38,
                                            in1=w8[:], op0=ALU.mult, op1=ALU.add))
                sync(v.reduce_max(stt_[:, 4:5], w2b[:], axis=AX))
                sync(v.tensor_scalar(out=g8[:], in0=w8[:], scalar1=stt_[:, 4:5],
                                     scalar2=None, op0=ALU.is_ge))
                sync(v.tensor_tensor(out=g8[:], in0=g8[:], in1=w8[:],
                                     op=ALU.mult))
                v.wait_ge(esem, 3 * i + 3)
                for e in range(8):
                    ins = v.tensor_scalar(out=asc[:, e * 16:(e + 1) * 16],
                                          in0=a_sb[:, e * 16:(e + 1) * 16],
                                          scalar1=g8[:, e:e + 1], scalar2=None,
                                          op0=ALU.mult)
                ins.then_inc(vsem, 1)
                v.wait_ge(tsem, 2 * i + 2)
                if i >= 1:
                    v.wait_ge(stv, 0)
                v.tensor_copy(aout[:, i * 128:(i + 1) * 128],
                              ps_t[:]).then_inc(vsem, 1)

        @blk.scalar
        def _(s):
            for i in range(NT):
                s.wait_ge(esem, 3 * i + 1)
                s.activation(out=p8[:], in_=ps_r[:], func=AF.Exp,
                             bias=stt_[:, 0:1], scale=1.0).then_inc(esem, 1)
                s.activation(out=a_sb[:], in_=ps_h[:],
                             func=AF.Silu).then_inc(esem, 1)

        @blk.sync
        def _(sy):
            sy.wait_ge(vsem, 2 * NT)
            sy.dma_start(out=aout_d, in_=aout[:]).then_inc(stv, 16)
            sy.wait_ge(stv, 16)
    return nc


def run_l0ffna(xnf_percore, rwt, dwt):
    nc = _cached("l0ffna", build_l0ffna)
    maps = [{"xnf": xnf_percore[c], "rwt": rwt, "dwt": dwt} for c in range(NCORE)]
    res = _run(nc, maps)
    return [r["aout"] for r in res]



def build_rope():
    NG = 18
    TC = T // 512
    nc = bass.Bass("TRN2", target_bir_lowering=False, detect_race_conditions=False)
    qkv = nc.dram_tensor("qkv", [2304, T], BF16, kind="ExternalInput").ap()
    cosr = nc.dram_tensor("cosr", [128, T], F32, kind="ExternalInput").ap()
    sins = nc.dram_tensor("sins", [128, T], F32, kind="ExternalInput").ap()
    out = nc.dram_tensor("out", [2304, T], BF16, kind="ExternalOutput").ap()
    pm = np.zeros((128, 128), BFD)
    for p in range(128):
        q = (p // 64) * 64 + ((p % 64) + 32) % 64
        pm[q, p] = 1.0
    perm_d = nc.inline_tensor(pm, "perm_d").ap()
    with contextlib.ExitStack() as st:
        E = st.enter_context
        xb = [E(nc.sbuf_tensor(f"xb{p}", [128, T], BF16)) for p in range(2)]
        yb = [E(nc.sbuf_tensor(f"yb{p}", [128, T], BF16)) for p in range(2)]
        cs = E(nc.sbuf_tensor("cs", [128, T], F32))
        sn = E(nc.sbuf_tensor("sn", [128, T], F32))
        pmat = E(nc.sbuf_tensor("pmat", [128, 128], BF16))
        t1 = E(nc.sbuf_tensor("t1", [128, 512], F32))
        t2 = E(nc.sbuf_tensor("t2", [128, 512], F32))
        ps_sw = [E(nc.psum_tensor(f"ps_sw{p}", [128, 512], F32)) for p in range(2)]
        cld = E(nc.semaphore("cld"))
        ld2 = [E(nc.semaphore(f"ld{p}")) for p in range(2)]
        ev = E(nc.semaphore("ev"))
        tsem = E(nc.semaphore("tsem"))
        stv = E(nc.semaphore("stv"))
        blk = E(nc.Block())

        @blk.sync
        def _(sy):
            sy.dma_start(out=cs[:], in_=cosr).then_inc(cld, 16)
            sy.dma_start(out=sn[:], in_=sins).then_inc(cld, 16)
            sy.dma_start(out=pmat[:], in_=perm_d).then_inc(cld, 16)
            for g in range(NG):
                if g >= 2:
                    sy.wait_ge(ev, (g - 1) * TC)
                    sy.dma_start(out=out[(g - 2) * 128:(g - 1) * 128, :],
                                 in_=yb[g % 2][:]).then_inc(stv, 16)
                sy.dma_start(out=xb[g % 2][:],
                             in_=qkv[g * 128:(g + 1) * 128, :]
                             ).then_inc(ld2[g % 2], 16)
            sy.wait_ge(ev, (NG - 1) * TC)
            sy.dma_start(out=out[(NG - 2) * 128:(NG - 1) * 128, :],
                         in_=yb[0][:]).then_inc(stv, 16)
            sy.wait_ge(ev, NG * TC)
            sy.dma_start(out=out[(NG - 1) * 128:NG * 128, :],
                         in_=yb[1][:]).then_inc(stv, 16)
            sy.wait_ge(stv, 16 * NG)

        @blk.tensor
        def _(t):
            t.wait_ge(cld, 48)
            for g in range(NG):
                t.wait_ge(ld2[g % 2], 16 * (g // 2 + 1))
                for cc in range(TC):
                    gi = g * TC + cc
                    if gi >= 2:
                        t.wait_ge(ev, gi - 1)
                    t.matmul(ps_sw[gi % 2][:], pmat[:],
                             xb[g % 2][:, cc * 512:(cc + 1) * 512],
                             start=True, stop=True).then_inc(tsem, 1)

        @blk.vector
        def _(v):
            v.wait_ge(cld, 48)
            for g in range(NG):
                v.wait_ge(ld2[g % 2], 16 * (g // 2 + 1))
                if g >= 2:
                    v.wait_ge(stv, 16 * (g - 1))
                for cc in range(TC):
                    gi = g * TC + cc
                    sl = slice(cc * 512, (cc + 1) * 512)
                    v.wait_ge(tsem, gi + 1)
                    v.tensor_tensor(out=t1[:], in0=xb[g % 2][:, sl],
                                    in1=cs[:, sl], op=ALU.mult)
                    v.tensor_tensor(out=t2[:], in0=ps_sw[gi % 2][:],
                                    in1=sn[:, sl], op=ALU.mult)
                    v.scalar_tensor_tensor(out=yb[g % 2][:, sl], in0=t1[:],
                                           scalar=1.0, in1=t2[:],
                                           op0=ALU.mult,
                                           op1=ALU.add).then_inc(ev, 1)
    return nc


def run_rope(qkv_percore, cos_half, sins_half):
    nc = _cached("rope", build_rope)
    maps = [{"qkv": qkv_percore[c], "cosr": cos_half[c % 2],
             "sins": sins_half[c % 2]} for c in range(NCORE)]
    res = _run(nc, maps)
    return [r["out"] for r in res]


def _l1_sched():
    NP, NQ = 8, 8
    T_ = {}
    V_ = {}
    E_ = {}
    ct = cv = ce = 0
    cv += 1; V_["ksq"] = cv
    for cc in range(16):
        ct += 1; T_[f"kn{cc}"] = ct
        cv += 1; V_[f"knr{cc}"] = cv
    g = 0
    for u in range(NP):
        ce += 1; E_[f"sq{u}"] = ce
        for cc in range(8):
            ct += 1; T_[f"qn{u},{cc}"] = ct
            cv += 1; V_[f"qnr{u},{cc}"] = cv
        cv += 1; V_[f"biasgo{u}"] = cv
        ce += 1; E_[f"sqrt{u}"] = ce
        cv += 1; V_[f"biasmm{u}"] = cv
        ct += 1; T_[f"bb{u}"] = ct
        cv += 1; V_[f"biasrdy{u}"] = cv
        for h in range(2):
            for q in range(NQ):
                nch = 4 * q + 4
                for c0 in range(0, nch, 2):
                    for c in (c0, c0 + 1):
                        ct += 1; T_[f"s{g + c}"] = ct
                    ce += 1; E_[f"exp{(g + c0) // 2}"] = ce
                    for c in (c0, c0 + 1):
                        if c >= 4 * q:
                            cv += 1; V_[f"mask{g + c}"] = cv
                    for c in (c0, c0 + 1):
                        ct += 1; T_[f"p{g + c}"] = ct
                g += nch
                cv += 1; V_[f"den{u},{h},{q}"] = cv
                ct += 1; T_[f"bc{u},{h},{q}"] = ct
                cv += 1; V_[f"evac{u},{h},{q}"] = cv
        V_[f"end{u}"] = cv
    return T_, V_, E_


def build_l1attn():
    S = 4096
    NQ = 8
    NP = 8
    T_, V_, E_ = _l1_sched()
    nc = bass.Bass("TRN2", target_bir_lowering=False, detect_race_conditions=False)
    qh = nc.dram_tensor("qh", [NP, 128, S], BF16, kind="ExternalInput").ap()
    kf2 = nc.dram_tensor("kf2", [2, 128, S], BF16, kind="ExternalInput").ap()
    vt2 = nc.dram_tensor("vt2", [2, 128, 2048], BF16, kind="ExternalInput").ap()
    oh = nc.dram_tensor("oh", [NP * 2, 64, S], BF16, kind="ExternalOutput").ap()
    stair = (np.arange(128)[:, None] <= np.arange(896)[None, :] - 384).astype(BFD)
    t8_d = nc.inline_tensor(stair, "t8_d").ap()
    ones1_d = nc.inline_tensor(np.ones((1, 128), BFD), "ones1_d").ap()
    onesc_d = nc.inline_tensor(np.ones((128, 1), BFD), "onesc_d").ap()

    with contextlib.ExitStack() as st:
        E = st.enter_context
        q_sb = [E(nc.sbuf_tensor(f"q_sb{p}", [128, S], BF16)) for p in range(2)]
        oh_sb = [E(nc.sbuf_tensor(f"oh_sb{p}", [64, S], BF16)) for p in range(2)]
        kf_sb = E(nc.sbuf_tensor("kf_sb", [128, 2 * S], BF16))
        va_sb = E(nc.sbuf_tensor("va_sb", [128, 2 * 2176], BF16))
        sq_sb = E(nc.sbuf_tensor("sq_sb", [128, 2 * S], BF16))
        t8_sb = E(nc.sbuf_tensor("t8_sb", [128, 896], BF16))
        ones1_sb = E(nc.sbuf_tensor("ones1_sb", [1, 128], BF16))
        onesc_sb = E(nc.sbuf_tensor("onesc_sb", [128, 1], BF16))
        pst = [E(nc.sbuf_tensor(f"pst{p}", [128, 1024], BF16)) for p in range(2)]
        nrm = E(nc.sbuf_tensor("nrm", [1, 20], F32))
        nrmb = E(nc.sbuf_tensor("nrmb", [1, 1], BF16))
        biasb = E(nc.sbuf_tensor("biasb", [128, 1], F32))
        denf = E(nc.sbuf_tensor("denf", [1, 512], F32))
        denr = E(nc.sbuf_tensor("denr", [1, 512], BF16))
        denb = E(nc.sbuf_tensor("denb", [64, 512], BF16))
        ps_sT = [E(nc.psum_tensor(f"ps_sT{p}", [128, 1024], F32)) for p in range(2)]
        ps_o = E(nc.psum_tensor("ps_o", [65, 512], F32))
        ps_bc = E(nc.psum_tensor("ps_bc", [64, 512], F32))
        ps_n = E(nc.psum_tensor("ps_n", [1, 512], F32))
        ps_bb = E(nc.psum_tensor("ps_bb", [128, 1], F32))
        cld = E(nc.semaphore("cld"))
        msd = E(nc.semaphore("msd"))
        pld = [E(nc.semaphore(f"pld{p}")) for p in range(2)]
        tsem = E(nc.semaphore("tsem"))
        vsem = E(nc.semaphore("vsem"))
        esem = E(nc.semaphore("esem"))
        vs = E(nc.semaphore("vs"))
        stv = E(nc.semaphore("stv"))
        blk = E(nc.Block())

        @blk.gpsimd
        def _(g):
            g.dma_start(out=t8_sb[:], in_=t8_d).then_inc(cld, 16)
            g.dma_start(out=ones1_sb[:], in_=ones1_d).then_inc(cld, 16)
            g.dma_start(out=onesc_sb[:], in_=onesc_d).then_inc(cld, 16)
            g.dma_start(out=kf_sb[:, 0:S], in_=kf2[0]).then_inc(cld, 16)
            g.dma_start(out=kf_sb[:, S:2 * S], in_=kf2[1]).then_inc(cld, 16)
            g.memset(va_sb[:], 1.0).then_inc(msd, 1)
            for kv in range(2):
                g.dma_start(
                    out=va_sb.ap()[:, kv * 2176:(kv + 1) * 2176].rearrange(
                        "p (c w) -> p c w", w=68)[:, :, 0:64],
                    in_=vt2[kv].rearrange("p (c w) -> p c w", w=64)
                ).then_inc(cld, 16)
            for u in range(NP):
                par = u % 2
                if u >= 2:
                    g.wait_ge(vsem, V_[f"end{u - 2}"])
                g.dma_start(out=q_sb[par][:], in_=qh[u]).then_inc(pld[par], 16)

        @blk.scalar
        def _(s):
            s.wait_ge(cld, 112)
            gg = 0
            for u in range(NP):
                par = u % 2
                kv = u // 4
                s.wait_ge(pld[par], 16 * (u // 2 + 1))
                if u >= 1:
                    s.wait_ge(tsem, T_[f"qn{u - 1},7"])
                else:
                    s.wait_ge(tsem, T_["kn7"])
                s.activation(out=sq_sb[:, 0:S], in_=q_sb[par][:],
                             func=AF.Square).then_inc(esem, 1)
                s.wait_ge(vsem, V_[f"biasgo{u}"])
                s.activation(out=nrm[:, 3:4], in_=nrm[:, 2:3],
                             func=AF.Sqrt).then_inc(esem, 1)
                s.wait_ge(vsem, V_[f"biasrdy{u}"])
                for h in range(2):
                    for q in range(NQ):
                        for c in range(1, 4 * q + 4, 2):
                            gi = gg + c
                            di = gi // 2
                            s.wait_ge(tsem, T_[f"s{gi}"])
                            if di >= 2:
                                s.wait_ge(tsem, T_[f"p{2 * di - 3}"])
                            s.activation(out=pst[di % 2][:],
                                         in_=ps_sT[di % 2][:], func=AF.Exp,
                                         bias=biasb[:, 0:1],
                                         scale=0.125).then_inc(esem, 1)
                        gg += 4 * q + 4

        @blk.vector
        def _(v):
            nvs = 0

            def sync(ins):
                nonlocal nvs
                nvs += 1
                ins.then_inc(vs, 1)
                v.wait_ge(vs, nvs)

            v.wait_ge(cld, 112)
            v.tensor_tensor(out=sq_sb[:], in0=kf_sb[:], in1=kf_sb[:],
                            op=ALU.mult).then_inc(vsem, 1)
            for kv in range(2):
                for cc in range(8):
                    v.wait_ge(tsem, T_[f"kn{kv * 8 + cc}"])
                    v.reduce_max(nrm[:, 8 + cc:9 + cc], ps_n[:],
                                 axis=AX).then_inc(vsem, 1)
                v.wait_ge(vsem, V_[f"knr{kv * 8 + 7}"])
                sync(v.reduce_max(nrm[:, 4 + kv:5 + kv], nrm[:, 8:16], axis=AX))
            gg = 0
            for u in range(NP):
                par = u % 2
                kv = u // 4
                for cc in range(8):
                    v.wait_ge(tsem, T_[f"qn{u},{cc}"])
                    v.reduce_max(nrm[:, 8 + cc:9 + cc], ps_n[:],
                                 axis=AX).then_inc(vsem, 1)
                v.wait_ge(vsem, V_[f"qnr{u},7"])
                sync(v.reduce_max(nrm[:, 0:1], nrm[:, 8:16], axis=AX))
                sync(v.tensor_tensor(out=nrm[:, 2:3], in0=nrm[:, 0:1],
                                     in1=nrm[:, 4 + kv:5 + kv], op=ALU.mult))
                v.tensor_copy(nrm[:, 18:19],
                              nrm[:, 8:9]).then_inc(vsem, 1)
                v.wait_ge(esem, E_[f"sqrt{u}"])
                sync(v.tensor_scalar(out=nrm[:, 6:7], in0=nrm[:, 3:4],
                                     scalar1=-0.125, scalar2=None,
                                     op0=ALU.mult))
                sync(v.tensor_copy(nrmb[:], nrm[:, 6:7]))
                v.tensor_copy(nrm[:, 18:19],
                              nrm[:, 8:9]).then_inc(vsem, 1)
                v.wait_ge(tsem, T_[f"bb{u}"])
                v.tensor_copy(biasb[:], ps_bb[:]).then_inc(vsem, 1)
                for h in range(2):
                    uh = u * 2 + h
                    if uh >= 2:
                        v.wait_ge(stv, 16 * (uh - 1))
                    for q in range(NQ):
                        for c in range(4 * q, 4 * q + 4):
                            gi = gg + c
                            di = gi // 2
                            hs = (gi % 2) * 512
                            v.wait_ge(esem, E_[f"exp{di}"])
                            delta = 128 * (4 * q - c)
                            v.tensor_tensor(
                                out=pst[di % 2][:, hs:hs + 512],
                                in0=pst[di % 2][:, hs:hs + 512],
                                in1=t8_sb[:, 384 + delta:896 + delta],
                                op=ALU.mult).then_inc(vsem, 1)
                        gg += 4 * q + 4
                        v.wait_ge(tsem, T_[f"p{gg - 1}"])
                        sync(v.reciprocal(denf[:], ps_o[64:65, :]))
                        sync(v.tensor_copy(denr[:], denf[:]))
                        v.tensor_copy(nrm[:, 18:19],
                                      nrm[:, 8:9]).then_inc(vsem, 1)
                        v.wait_ge(tsem, T_[f"bc{u},{h},{q}"])
                        v.tensor_copy(denb[:], ps_bc[:])
                        v.tensor_tensor(
                            out=oh_sb[uh % 2][:, q * 512:(q + 1) * 512],
                            in0=ps_o[0:64, :], in1=denb[:],
                            op=ALU.mult).then_inc(vsem, 1)

        @blk.tensor
        def _(t):
            t.wait_ge(cld, 112)
            t.wait_ge(msd, 1)
            t.wait_ge(vsem, V_["ksq"])
            for cc in range(16):
                if cc >= 1:
                    t.wait_ge(vsem, V_[f"knr{cc - 1}"])
                t.matmul(ps_n[:], onesc_sb[:],
                         sq_sb[:, cc * 512:(cc + 1) * 512],
                         start=True, stop=True).then_inc(tsem, 1)
            gg = 0
            for u in range(NP):
                par = u % 2
                kv = u // 4
                t.wait_ge(esem, E_[f"sq{u}"])
                for cc in range(8):
                    if cc == 0:
                        t.wait_ge(vsem, V_["knr15"] if u == 0 else
                                  V_[f"qnr{u - 1},7"])
                    else:
                        t.wait_ge(vsem, V_[f"qnr{u},{cc - 1}"])
                    t.matmul(ps_n[:], onesc_sb[:],
                             sq_sb[:, cc * 512:(cc + 1) * 512],
                             start=True, stop=True).then_inc(tsem, 1)
                t.wait_ge(vsem, V_[f"biasmm{u}"])
                t.matmul(ps_bb[:], ones1_sb[:], nrmb[:],
                         start=True, stop=True).then_inc(tsem, 1)
                for h in range(2):
                    qrow = slice(h * 64, (h + 1) * 64)
                    for q in range(NQ):
                        nch = 4 * q + 4
                        for c0 in range(0, nch, 2):
                            for c in (c0, c0 + 1):
                                gi = gg + c
                                di = gi // 2
                                hs = (gi % 2) * 512
                                if gi % 2 == 0 and di >= 2:
                                    t.wait_ge(esem, E_[f"exp{di - 2}"])
                                t.matmul(ps_sT[di % 2][:, hs:hs + 512],
                                         kf_sb[qrow, kv * S + c * 128:
                                               kv * S + (c + 1) * 128],
                                         q_sb[par][qrow,
                                                   q * 512:(q + 1) * 512],
                                         start=True, stop=True
                                         ).then_inc(tsem, 1)
                            for c in (c0, c0 + 1):
                                gi = gg + c
                                di = gi // 2
                                hs = (gi % 2) * 512
                                if c == 0 and not (u == 0 and h == 0
                                                   and q == 0):
                                    pu, ph, pq = (u, h, q - 1) if q > 0 else (
                                        (u, h - 1, NQ - 1) if h > 0 else
                                        (u - 1, 1, NQ - 1))
                                    t.wait_ge(vsem,
                                              V_[f"evac{pu},{ph},{pq}"])
                                if c >= 4 * q:
                                    t.wait_ge(vsem, V_[f"mask{gi}"])
                                else:
                                    t.wait_ge(esem, E_[f"exp{di}"])
                                t.matmul(ps_o[:],
                                         va_sb[:, kv * 2176 + c * 68:
                                               kv * 2176 + c * 68 + 65],
                                         pst[di % 2][:, hs:hs + 512],
                                         start=(c == 0), stop=(c == nch - 1)
                                         ).then_inc(tsem, 1)
                        gg += nch
                        t.wait_ge(vsem, V_[f"den{u},{h},{q}"])
                        t.matmul(ps_bc[:], ones1_sb[0:1, 0:64], denr[:],
                                 start=True, stop=True
                                 ).then_inc(tsem, 1)

        @blk.sync
        def _(sy):
            for u in range(NP):
                for h in range(2):
                    uh = u * 2 + h
                    sy.wait_ge(vsem, V_[f"evac{u},{h},{NQ - 1}"])
                    sy.dma_start(out=oh[uh],
                                 in_=oh_sb[uh % 2][:]).then_inc(stv, 16)
            sy.wait_ge(stv, 16 * NP * 2)
    return nc


def run_l1attn(in_percore):
    nc = _cached("l1attn", build_l1attn)
    res = _run(nc, in_percore)
    return [r["oh"] for r in res]


def build_quant():
    nc = bass.Bass("TRN2", target_bir_lowering=False, detect_race_conditions=False)
    w1 = nc.dram_tensor("w1", [2048, 2048], F32, kind="ExternalInput").ap()
    w2 = nc.dram_tensor("w2", [1024, 2048], F32, kind="ExternalInput").ap()
    q1 = nc.dram_tensor("q1", [2048, 2048], BF16, kind="ExternalOutput").ap()
    q2 = nc.dram_tensor("q2", [1024, 2048], BF16, kind="ExternalOutput").ap()
    sel = np.zeros((128, 2), BFD)
    sel[0:64, 0] = 1.0
    sel[64:128, 1] = 1.0
    sel_d = nc.inline_tensor(sel, "sel_d").ap()
    selt_d = nc.inline_tensor(np.ascontiguousarray(sel.T), "selt_d").ap()
    tiles = [("w1", r, c) for r in range(16) for c in range(4)] + \
            [("w2", r, c) for r in range(8) for c in range(4)]
    NTL = len(tiles)

    with contextlib.ExitStack() as st:
        E = st.enter_context
        w_sb = [E(nc.sbuf_tensor(f"w_sb{p}", [128, 512], BF16)) for p in range(2)]
        wa = E(nc.sbuf_tensor("wa", [128, 512], BF16))
        sel_sb = E(nc.sbuf_tensor("sel_sb", [128, 2], BF16))
        selt_sb = E(nc.sbuf_tensor("selt_sb", [2, 128], BF16))
        sc = E(nc.sbuf_tensor("sc", [2, 512 * 4], F32))
        sc_bf = E(nc.sbuf_tensor("sc_bf", [2, 512], BF16))
        thr = E(nc.sbuf_tensor("thr", [128, 512], F32))
        mk = E(nc.sbuf_tensor("mk", [128, 512], BF16))
        sg = E(nc.sbuf_tensor("sg", [128, 512], BF16))
        tq = E(nc.sbuf_tensor("tq", [128, 512], BF16))
        wq = [E(nc.sbuf_tensor(f"wq{p}", [128, 512], BF16)) for p in range(2)]
        ps_s = E(nc.psum_tensor("ps_s", [2, 512], F32))
        ps_b2 = E(nc.psum_tensor("ps_b2", [128, 512], F32))
        cld = E(nc.semaphore("cld"))
        pld = [E(nc.semaphore(f"pld{p}")) for p in range(2)]
        tsem = E(nc.semaphore("tsem"))
        vsem = E(nc.semaphore("vsem"))
        esem = E(nc.semaphore("esem"))
        vs = E(nc.semaphore("vs"))
        stv = [E(nc.semaphore(f"stv{p}")) for p in range(2)]
        blk = E(nc.Block())

        def tsl(idx):
            name, r, c = tiles[idx]
            ap = w1 if name == "w1" else w2
            return ap[r * 128:(r + 1) * 128, c * 512:(c + 1) * 512]

        def osl(idx):
            name, r, c = tiles[idx]
            ap = q1 if name == "w1" else q2
            return ap[r * 128:(r + 1) * 128, c * 512:(c + 1) * 512]

        @blk.gpsimd
        def _(g):
            g.dma_start(out=sel_sb[:], in_=sel_d).then_inc(cld, 16)
            g.dma_start(out=selt_sb[:], in_=selt_d).then_inc(cld, 16)
            for i in range(NTL):
                if i >= 2:
                    g.wait_ge(stv[i % 2], 16 * (i // 2))
                g.dma_start(out=w_sb[i % 2][:], in_=tsl(i)).then_inc(pld[i % 2], 16)

        @blk.scalar
        def _(s):
            for i in range(NTL):
                s.wait_ge(pld[i % 2], 16 * (i // 2 + 1))
                if i >= 1:
                    s.wait_ge(vsem, 2 * i)
                s.activation(out=wa[:], in_=w_sb[i % 2][:],
                             func=AF.Abs).then_inc(esem, 1)
                s.activation(out=sg[:], in_=w_sb[i % 2][:],
                             func=AF.Sign).then_inc(esem, 1)

        @blk.tensor
        def _(t):
            t.wait_ge(cld, 32)
            for i in range(NTL):
                t.wait_ge(esem, 2 * i + 1)
                if i >= 1:
                    t.wait_ge(vsem, 2 * i)
                t.matmul(ps_s[:], sel_sb[:], wa[:],
                         start=True, stop=True).then_inc(tsem, 1)
                t.wait_ge(vsem, 2 * i + 1)
                t.matmul(ps_b2[:], selt_sb[:], sc_bf[:],
                         start=True, stop=True).then_inc(tsem, 1)

        @blk.vector
        def _(v):
            nvs = 0

            def sync(ins):
                nonlocal nvs
                nvs += 1
                ins.then_inc(vs, 1)
                v.wait_ge(vs, nvs)

            for i in range(NTL):
                v.wait_ge(tsem, 2 * i + 1)
                sync(v.tensor_scalar(out=sc[:, 0:512], in0=ps_s[:],
                                     scalar1=1.0 / 64, scalar2=None,
                                     op0=ALU.mult))
                v.tensor_scalar(out=sc_bf[:], in0=sc[:, 0:512],
                                scalar1=1e-8, scalar2=None,
                                op0=ALU.max).then_inc(vsem, 1)
                v.wait_ge(tsem, 2 * i + 2)
                v.tensor_scalar(out=thr[:], in0=ps_b2[:],
                                scalar1=0.5009765625, scalar2=None,
                                op0=ALU.mult)
                v.wait_ge(esem, 2 * i + 2)
                v.tensor_tensor(out=mk[:], in0=wa[:], in1=thr[:],
                                op=ALU.is_gt)
                v.tensor_tensor(out=tq[:], in0=mk[:], in1=sg[:], op=ALU.mult)
                if i >= 2:
                    v.wait_ge(stv[i % 2], 16 * (i // 2))
                v.tensor_tensor(out=wq[i % 2][:], in0=tq[:], in1=ps_b2[:],
                                op=ALU.mult).then_inc(vsem, 1)
        @blk.sync
        def _(sy):
            for i in range(NTL):
                sy.wait_ge(vsem, 2 * i + 2)
                sy.dma_start(out=osl(i), in_=wq[i % 2][:]).then_inc(stv[i % 2], 16)
            for p in range(2):
                sy.wait_ge(stv[p], 16 * ((NTL - 1 - p) // 2 + 1))
    return nc


def run_quant(w1_percore, w2_percore):
    nc = _cached("quant", build_quant)
    maps = [{"w1": w1_percore[c], "w2": w2_percore[c]} for c in range(NCORE)]
    res = _run(nc, maps)
    return [r["q1"] for r in res], [r["q2"] for r in res]


def build_ffn1():
    NK = D // 128
    NG = 64
    TC = T // 512
    OG = D // 128
    EPSF = float(np.finfo(np.float32).eps)
    nc = bass.Bass("TRN2", target_bir_lowering=False, detect_race_conditions=False)
    xnf = nc.dram_tensor("xnf", [D, T], BF16, kind="ExternalInput").ap()
    wgu = nc.dram_tensor("wgu", [D, 16384], BF16, kind="ExternalInput").ap()
    wpr = nc.dram_tensor("wpr", [8192, D], BF16, kind="ExternalInput").ap()
    p2 = nc.dram_tensor("p2", [D, T], BF16, kind="ExternalInput").ap()
    x2 = nc.dram_tensor("x2", [D, T], F32, kind="ExternalOutput").ap()
    ones1_d = nc.inline_tensor(np.ones((1, 128), BFD), "ones1_d").ap()
    onesc_d = nc.inline_tensor(np.ones((128, 1), BFD), "onesc_d").ap()

    def sch():
        Td, Vd, Ed = {}, {}, {}
        ct = cv = ce = 0
        for tc in range(TC):
            for gg in range(NG):
                gi = tc * NG + gg
                if gg >= 2:
                    ct += 1; Td[f"r{gi - 2}"] = ct
                ct += 1; Td[f"g{gi}"] = ct
                ce += 1; Ed[f"s{gi}"] = ce
                cv += 1; Vd[f"h{gi}"] = cv
            ct += 1; Td[f"r{tc * NG + NG - 2}"] = ct
            ct += 1; Td[f"r{tc * NG + NG - 1}"] = ct
            cv += 1; Vd[f"vsq{tc}"] = cv
            ce += 1; Ed[f"q{tc}"] = ce
            cv += 1; Vd[f"ib{tc}"] = cv
            ct += 1; Td[f"b{tc}"] = ct
            cv += 1; Vd[f"hn{tc}"] = cv
            for og in range(OG):
                oi = tc * OG + og
                ct += 1; Td[f"o{oi}"] = ct
                cv += 1; Vd[f"oev{oi}"] = cv
        return Td, Vd, Ed

    Td, Vd, Ed = sch()

    with contextlib.ExitStack() as st:
        E = st.enter_context
        xn_sb = E(nc.sbuf_tensor("xn_sb", [128, NK * T], BF16))
        wg_sb = [[E(nc.sbuf_tensor(f"wg_sb{p}_{j}", [128, NK * 128], BF16))
                  for j in range(2)] for p in range(2)]
        wp_sb = [E(nc.sbuf_tensor(f"wp_sb{p}", [128, 64 * 128], BF16))
                 for p in range(2)]
        hbuf = E(nc.sbuf_tensor("hbuf", [128, NG * 512], BF16))
        h2b = [E(nc.sbuf_tensor(f"h2b{p}", [128, 512], BF16)) for p in range(2)]
        sil = [E(nc.sbuf_tensor(f"sil{p}", [128, 512], F32)) for p in range(2)]
        rr0 = E(nc.sbuf_tensor("rr0", [1, 512], F32))
        rr1 = E(nc.sbuf_tensor("rr1", [1, 512], F32))
        rr2 = E(nc.sbuf_tensor("rr2", [1, 512], F32))
        ib_bf = E(nc.sbuf_tensor("ib_bf", [1, 512], BF16))
        ib_sb = E(nc.sbuf_tensor("ib_sb", [128, 512], BF16))
        p2_sb = [E(nc.sbuf_tensor(f"p2_sb{p}", [128, 512], BF16)) for p in range(2)]
        ob = [E(nc.sbuf_tensor(f"ob{p}", [128, 512], F32)) for p in range(2)]
        ones1_sb = E(nc.sbuf_tensor("ones1_sb", [1, 128], BF16))
        onesc_sb = E(nc.sbuf_tensor("onesc_sb", [128, 1], BF16))
        ps_g = [E(nc.psum_tensor(f"ps_g{p}", [128, 512], F32)) for p in range(2)]
        ps_u = [E(nc.psum_tensor(f"ps_u{p}", [128, 512], F32)) for p in range(2)]
        ps_o = [E(nc.psum_tensor(f"ps_o{p}", [128, 512], F32)) for p in range(2)]
        ps_r = E(nc.psum_tensor("ps_r", [1, 512], F32))
        ps_ib = E(nc.psum_tensor("ps_ib", [128, 512], F32))
        cld = E(nc.semaphore("cld"))
        wld = [E(nc.semaphore(f"wld{p}")) for p in range(2)]
        wpld = [E(nc.semaphore(f"wpld{p}")) for p in range(2)]
        rld = [E(nc.semaphore(f"rld{p}")) for p in range(2)]
        tsem = E(nc.semaphore("tsem"))
        vsem = E(nc.semaphore("vsem"))
        esem = E(nc.semaphore("esem"))
        vs = E(nc.semaphore("vs"))
        stv = [E(nc.semaphore(f"stv{p}")) for p in range(2)]
        blk = E(nc.Block())

        @blk.gpsimd
        def _(g):
            g.dma_start(out=ones1_sb[:], in_=ones1_d).then_inc(cld, 16)
            g.dma_start(out=onesc_sb[:], in_=onesc_d).then_inc(cld, 16)
            g.dma_start(out=xn_sb.ap().rearrange("p (a t) -> p a t", t=T),
                        in_=xnf.rearrange("(a p) t -> p a t", p=128)
                        ).then_inc(cld, 16)
            for tc in range(TC):
                for gg in range(NG):
                    gi = tc * NG + gg
                    if gi >= 2:
                        g.wait_ge(tsem, Td[f"g{gi - 2}"])
                    g.dma_start(
                        out=wg_sb[gi % 2][0].ap().rearrange(
                            "p (a f) -> p a f", f=128),
                        in_=wgu[:, gg * 128:(gg + 1) * 128].rearrange(
                            "(a p) f -> p a f", p=128)
                    ).then_inc(wld[gi % 2], 16)
                    g.dma_start(
                        out=wg_sb[gi % 2][1].ap().rearrange(
                            "p (a f) -> p a f", f=128),
                        in_=wgu[:, 8192 + gg * 128:8192 + (gg + 1) * 128
                                ].rearrange("(a p) f -> p a f", p=128)
                    ).then_inc(wld[gi % 2], 16)
                for og in range(OG):
                    oi = tc * OG + og
                    if oi >= 2:
                        g.wait_ge(tsem, Td[f"o{oi - 2}"])
                    g.dma_start(
                        out=wp_sb[oi % 2].ap().rearrange(
                            "p (a f) -> p a f", f=128),
                        in_=wpr[:, og * 128:(og + 1) * 128].rearrange(
                            "(a p) f -> p a f", p=128)
                    ).then_inc(wpld[oi % 2], 16)

        @blk.sync
        def _(sy):
            for tc in range(TC):
                for og in range(OG):
                    oi = tc * OG + og
                    if oi >= 2:
                        sy.wait_ge(vsem, Vd[f"oev{oi - 1}"])
                    sy.dma_start(out=p2_sb[oi % 2][:],
                                 in_=p2[og * 128:(og + 1) * 128,
                                        tc * 512:(tc + 1) * 512]
                                 ).then_inc(rld[oi % 2], 16)
                    sy.wait_ge(vsem, Vd[f"oev{oi}"])
                    sy.dma_start(out=x2[og * 128:(og + 1) * 128,
                                        tc * 512:(tc + 1) * 512],
                                 in_=ob[oi % 2][:]).then_inc(stv[oi % 2], 16)
            for p in range(2):
                sy.wait_ge(stv[p], 16 * ((TC * OG - 1 - p) // 2 + 1))

        @blk.scalar
        def _(s):
            for tc in range(TC):
                for gg in range(NG):
                    gi = tc * NG + gg
                    s.wait_ge(tsem, Td[f"g{gi}"])
                    if gi >= 2:
                        s.wait_ge(vsem, Vd[f"h{gi - 2}"])
                    s.activation(out=sil[gi % 2][:], in_=ps_g[gi % 2][:],
                                 func=AF.Silu).then_inc(esem, 1)
                s.wait_ge(vsem, Vd[f"vsq{tc}"])
                s.activation(out=rr1[:], in_=rr0[:],
                             func=AF.Sqrt).then_inc(esem, 1)

        @blk.vector
        def _(v):
            nvs = 0

            def sync(ins):
                nonlocal nvs
                nvs += 1
                ins.then_inc(vs, 1)
                v.wait_ge(vs, nvs)

            for tc in range(TC):
                if tc >= 1:
                    v.wait_ge(tsem, Td[f"o{tc * OG - 1}"])
                for gg in range(NG):
                    gi = tc * NG + gg
                    v.wait_ge(esem, Ed[f"s{gi}"])
                    if gi >= 2:
                        v.wait_ge(tsem, Td[f"r{gi - 2}"])
                    v.tensor_tensor(out=hbuf[:, gg * 512:(gg + 1) * 512],
                                    in0=sil[gi % 2][:], in1=ps_u[gi % 2][:],
                                    op=ALU.mult)
                    v.tensor_tensor(out=h2b[gi % 2][:],
                                    in0=hbuf[:, gg * 512:(gg + 1) * 512],
                                    in1=hbuf[:, gg * 512:(gg + 1) * 512],
                                    op=ALU.mult).then_inc(vsem, 1)
                v.wait_ge(tsem, Td[f"r{tc * NG + NG - 1}"])
                sync(v.tensor_scalar(out=rr0[:], in0=ps_r[:],
                                     scalar1=1.0 / 8192, scalar2=None,
                                     op0=ALU.mult))
                v.tensor_scalar(out=rr0[:], in0=rr0[:], scalar1=EPSF,
                                scalar2=None, op0=ALU.add).then_inc(vsem, 1)
                v.wait_ge(esem, Ed[f"q{tc}"])
                sync(v.reciprocal(rr2[:], rr1[:]))
                v.tensor_copy(ib_bf[:], rr2[:]).then_inc(vsem, 1)
                v.wait_ge(tsem, Td[f"b{tc}"])
                sync(v.tensor_copy(ib_sb[:], ps_ib[:]))
                for gg in range(NG):
                    ins = v.tensor_tensor(
                        out=hbuf[:, gg * 512:(gg + 1) * 512],
                        in0=hbuf[:, gg * 512:(gg + 1) * 512],
                        in1=ib_sb[:], op=ALU.mult)
                ins.then_inc(vsem, 1)
                for og in range(OG):
                    oi = tc * OG + og
                    v.wait_ge(tsem, Td[f"o{oi}"])
                    v.wait_ge(rld[oi % 2], 16 * (oi // 2 + 1))
                    if oi >= 2:
                        v.wait_ge(stv[oi % 2], 16 * (oi // 2))
                    v.tensor_tensor(out=ob[oi % 2][:], in0=ps_o[oi % 2][:],
                                    in1=p2_sb[oi % 2][:],
                                    op=ALU.add).then_inc(vsem, 1)

        @blk.tensor
        def _(t):
            t.wait_ge(cld, 48)
            for tc in range(TC):
                for gg in range(NG):
                    gi = tc * NG + gg
                    if gg >= 2:
                        t.wait_ge(vsem, Vd[f"h{gi - 2}"])
                        t.matmul(ps_r[:], onesc_sb[:], h2b[gi % 2][:],
                                 start=(gg - 2 == 0), stop=False
                                 ).then_inc(tsem, 1)
                    t.wait_ge(wld[gi % 2], 32 * (gi // 2 + 1))
                    if gi >= 2:
                        t.wait_ge(vsem, Vd[f"h{gi - 2}"])
                    for k in range(NK):
                        t.matmul(ps_g[gi % 2][:],
                                 wg_sb[gi % 2][0][:, k * 128:(k + 1) * 128],
                                 xn_sb[:, k * T + tc * 512:
                                       k * T + (tc + 1) * 512],
                                 start=(k == 0), stop=(k == NK - 1))
                    for k in range(NK):
                        ins = t.matmul(ps_u[gi % 2][:],
                                       wg_sb[gi % 2][1][:, k * 128:(k + 1) * 128],
                                       xn_sb[:, k * T + tc * 512:
                                             k * T + (tc + 1) * 512],
                                       start=(k == 0), stop=(k == NK - 1))
                    ins.then_inc(tsem, 1)
                for gg in (NG - 2, NG - 1):
                    gi = tc * NG + gg
                    t.wait_ge(vsem, Vd[f"h{gi}"])
                    t.matmul(ps_r[:], onesc_sb[:], h2b[gi % 2][:],
                             start=False, stop=(gg == NG - 1)
                             ).then_inc(tsem, 1)
                t.wait_ge(vsem, Vd[f"ib{tc}"])
                t.matmul(ps_ib[:], ones1_sb[:], ib_bf[:],
                         start=True, stop=True).then_inc(tsem, 1)
                for og in range(OG):
                    oi = tc * OG + og
                    t.wait_ge(wpld[oi % 2], 16 * (oi // 2 + 1))
                    t.wait_ge(vsem, Vd[f"hn{tc}"])
                    if oi >= 2:
                        t.wait_ge(vsem, Vd[f"oev{oi - 2}"])
                    for g2 in range(NG):
                        ins = t.matmul(ps_o[oi % 2][:],
                                       wp_sb[oi % 2][:, g2 * 128:(g2 + 1) * 128],
                                       hbuf[:, g2 * 512:(g2 + 1) * 512],
                                       start=(g2 == 0), stop=(g2 == NG - 1))
                    ins.then_inc(tsem, 1)
    return nc



def run_ffn1(xnf_percore, wgu, wpr, p2_percore):
    nc = _cached("ffn1", build_ffn1)
    maps = [{"xnf": xnf_percore[c], "wgu": wgu, "wpr": wpr,
             "p2": p2_percore[c]} for c in range(NCORE)]
    res = _run(nc, maps)
    return [r["x2"] for r in res]



def shard_tokens(x):
    out = []
    for c in range(NCORE):
        b, h = c // 2, c % 2
        out.append(np.ascontiguousarray(x[b, h * T:(h + 1) * T]))
    return out


def unshard_tokens(parts, d2):
    x = np.empty((4, 4096, d2), parts[0].dtype)
    for c in range(NCORE):
        b, h = c // 2, c % 2
        x[b, h * T:(h + 1) * T] = parts[c]
    return x


def tr(a):
    return np.ascontiguousarray(np.swapaxes(a, -1, -2))


def _rope_tables():
    inv = 1.0 / 10000.0 ** (np.arange(0, 64, 2, dtype=np.float64) / 64)
    sign = np.where((np.arange(128) % 64) < 32, -1.0, 1.0)
    cos_half, sin_half = [], []
    for hh in range(2):
        pos = np.arange(hh * T, (hh + 1) * T, dtype=np.float64)
        ang = pos[None, :] * inv[np.arange(128) % 32][:, None]
        cos_half.append(np.cos(ang).astype(np.float32))
        sin_half.append((sign[:, None] * np.sin(ang)).astype(np.float32))
    return cos_half, sin_half


def _tiled_tok(x, w):
    return np.ascontiguousarray(
        x.reshape(32, 128, w).transpose(1, 0, 2).reshape(128, 32 * w))


def _prep_l0attn(qkv0, memories, mnorm, gate):
    mnflat = np.ascontiguousarray(mnorm.reshape(1, 8192), np.float32)
    maps = []
    for c in range(NCORE):
        b = c // 2
        e, o = 2 * b, 2 * b + 1
        heads = range((c % 2) * 8, (c % 2) * 8 + 8)
        qf = np.empty((8, 128, 4096), BFD)
        kf = np.empty((8, 128, 4096), BFD)
        kt = np.empty((8, 128, 4096), BFD)
        vt = np.empty((8, 128, 4096), BFD)
        for j, h in enumerate(heads):
            rows = slice(h * 128, (h + 1) * 128)
            qf[j, :, 0:T] = qkv0[e][rows]
            qf[j, :, T:2 * T] = qkv0[o][rows]
            kf[j, :, 0:T] = qkv0[e][2048 + h * 128:2048 + (h + 1) * 128]
            kf[j, :, T:2 * T] = qkv0[o][2048 + h * 128:2048 + (h + 1) * 128]
            kt[j] = _tiled_tok(np.ascontiguousarray(kf[j].T), 128)
            vrows = slice(4096 + h * 128, 4096 + (h + 1) * 128)
            vfull = np.concatenate([qkv0[e][vrows], qkv0[o][vrows]], axis=1)
            vt[j] = _tiled_tok(np.ascontiguousarray(vfull.T), 128)
        memp = np.zeros((8, 128, 520), np.float32)
        mnsp = np.zeros((8, 128, 4), np.float32)
        gp = np.zeros((8, 1, 1), np.float32)
        for j, h in enumerate(heads):
            for m in range(4):
                memp[j, :, m * 130:m * 130 + 128] = memories[m, h]
                memp[j, :, m * 130 + 128] = mnorm[m, h]
                mnsp[j, :, m] = mnorm[m, h]
            gp[j, 0, 0] = gate[h]
        maps.append(dict(qf=qf, kf=kf, kt=kt, vt=vt, mem=memp, mns=mnsp,
                         gatev=gp, mnflat=mnflat))
    return maps


def _prep_l1attn(roped, v1):
    maps = []
    for c in range(NCORE):
        b, hh = c // 2, c % 2
        e, o = 2 * b, 2 * b + 1
        qh = np.empty((8, 128, 4096), BFD)
        for u in range(8):
            for j in range(2):
                h = hh * 16 + 2 * u + j
                rows = slice(h * 64, (h + 1) * 64)
                qh[u, j * 64:(j + 1) * 64, 0:T] = roped[e][rows]
                qh[u, j * 64:(j + 1) * 64, T:2 * T] = roped[o][rows]
        kf2 = np.empty((2, 128, 4096), BFD)
        vt2 = np.empty((2, 128, 2048), BFD)
        for j in range(2):
            kvh = hh * 2 + j
            rows = slice(2048 + kvh * 64, 2048 + (kvh + 1) * 64)
            kf2[j, 0:64, 0:T] = roped[e][rows]
            kf2[j, 0:64, T:2 * T] = roped[o][rows]
            kf2[j, 64:128] = kf2[j, 0:64]
            vrows = slice(kvh * 64, (kvh + 1) * 64)
            vfull = np.concatenate([v1[e][vrows], v1[o][vrows]], axis=1)
            vt2[j] = _tiled_tok(np.ascontiguousarray(vfull.T), 64)
        maps.append(dict(qh=qh, kf2=kf2, vt2=vt2))
    return maps


def kernel(L0_attn_hidden_states, L0_attn_memories, L0_attn_memory_norms,
           L0_attn_w_q, L0_attn_w_k, L0_attn_w_v, L0_attn_w_o, L0_attn_gate,
           L0_ffn_router_w, L0_ffn_down_w, L0_ffn_up_w,
           L1_attn_w_qkv, L1_attn_w_o, L1_ffn_w_gate_up, L1_ffn_w_proj):
    x = np.ascontiguousarray(np.asarray(L0_attn_hidden_states, np.float32))
    xs = shard_tokens(x)

    xn, _ = run_ln0(xs)
    xn_feat = [tr(a) for a in xn]
    wqkv0 = np.concatenate(
        [np.ascontiguousarray(np.asarray(w, np.float32).T)
         for w in (L0_attn_w_q, L0_attn_w_k, L0_attn_w_v)], axis=1)
    qkv0 = run_gemm("qkv0", [wqkv0], [xn_feat], None, BFD)

    maps0 = _prep_l0attn(qkv0, np.asarray(L0_attn_memories, np.float32),
                         np.asarray(L0_attn_memory_norms, np.float32),
                         np.asarray(L0_attn_gate, np.float32))
    outm = run_l0attn(maps0)

    rwt = np.ascontiguousarray(np.asarray(L0_ffn_router_w, np.float32).T)
    dwt = np.ascontiguousarray(
        np.transpose(np.asarray(L0_ffn_down_w, np.float32),
                     (2, 0, 1)).reshape(2048, 128))
    asct = run_l0ffna(xn_feat, rwt, dwt)

    mixf = []
    for c in range(NCORE):
        b, hh = c // 2, c % 2
        mf = np.empty((2048, T), BFD)
        for src_core, hbase in ((2 * b, 0), (2 * b + 1, 8)):
            om = np.asarray(outm[src_core]).reshape(8, 128, 32, 128)
            for j in range(8):
                h = hbase + j
                blk2 = om[j].transpose(1, 0, 2).reshape(4096, 128)
                mf[h * 128:(h + 1) * 128] = blk2[hh * T:(hh + 1) * T].T
        mixf.append(np.ascontiguousarray(mf))
    wot = np.ascontiguousarray(np.asarray(L0_attn_w_o, np.float32).T)
    upt = np.ascontiguousarray(
        np.transpose(np.asarray(L0_ffn_up_w, np.float32),
                     (0, 2, 1)).reshape(128, 2048))
    xres = [np.ascontiguousarray(x[c // 2, (c % 2) * T:(c % 2 + 1) * T].T)
            for c in range(NCORE)]
    x1 = run_gemm("oproj0", [wot, upt], [mixf, asct], xres, BFD)

    x1tok = [tr(a) for a in x1]
    xn1 = run_ln1(x1tok)
    xn1f = [tr(a) for a in xn1]
    wqkvt = np.ascontiguousarray(np.asarray(L1_attn_w_qkv, np.float32).T)
    qkv1 = run_gemm("qkv1", [wqkvt], [xn1f], None, BFD)
    cos_half, sin_half = _rope_tables()
    qk = [np.ascontiguousarray(qkv1[c][0:2304]) for c in range(NCORE)]
    roped = run_rope(qk, cos_half, sin_half)
    v1 = [np.ascontiguousarray(qkv1[c][2304:2560]) for c in range(NCORE)]

    maps1 = _prep_l1attn(roped, v1)
    oh = run_l1attn(maps1)

    ahf = []
    for c in range(NCORE):
        b, hh = c // 2, c % 2
        af = np.empty((2048, T), BFD)
        tok = slice(hh * T, (hh + 1) * T)
        for src_core, hbase in ((2 * b, 0), (2 * b + 1, 16)):
            blk2 = np.asarray(oh[src_core]).reshape(16, 64, 4096)
            for uh in range(16):
                h = hbase + uh
                af[h * 64:(h + 1) * 64] = blk2[uh][:, tok]
        ahf.append(np.ascontiguousarray(af))
    wot1 = np.ascontiguousarray(np.asarray(L1_attn_w_o, np.float32).T)
    p2 = run_gemm("oproj1", [wot1], [ahf], x1, BFD)

    wguT = np.ascontiguousarray(np.asarray(L1_ffn_w_gate_up, np.float32).T)
    wprT = np.ascontiguousarray(np.asarray(L1_ffn_w_proj, np.float32).T)
    w1s = [np.ascontiguousarray(wguT[:, c * 2048:(c + 1) * 2048])
           for c in range(NCORE)]
    w2s = [np.ascontiguousarray(wprT[c * 1024:(c + 1) * 1024])
           for c in range(NCORE)]
    q1, q2 = run_quant(w1s, w2s)
    wgu_q = np.ascontiguousarray(np.concatenate(
        [np.asarray(q) for q in q1], axis=1))
    wpr_q = np.ascontiguousarray(np.concatenate(
        [np.asarray(q) for q in q2], axis=0))

    x2 = run_ffn1(xn1f, wgu_q, wpr_q, p2)
    out = unshard_tokens([np.ascontiguousarray(tr(np.asarray(a, np.float32)))
                          for a in x2], 2048)
    return out


# version 06
# speedup vs baseline: 1.2879x; 1.1474x over previous
import contextlib
import numpy as np
import ml_dtypes
import concourse.bass as bass
import concourse.mybir as mybir
from concourse.bass_utils import run_bass_kernel_spmd

F32 = mybir.dt.float32
BF16 = mybir.dt.bfloat16
AF = mybir.ActivationFunctionType
ALU = mybir.AluOpType
AX = mybir.AxisListType.X
NCORE = 8
T = 2048
D = 2048
BFD = ml_dtypes.bfloat16

_prog_cache = {}


def _cached(key, fn):
    if key not in _prog_cache:
        _prog_cache[key] = fn()
    return _prog_cache[key]


def _run(nc, in_maps):
    res = run_bass_kernel_spmd(nc, in_maps, core_ids=list(range(NCORE)))
    return res.results


def build_ln(with_sum):
    nc = bass.Bass("TRN2", target_bir_lowering=False, detect_race_conditions=False)
    x = nc.dram_tensor("x", [T, D], BF16 if not with_sum else F32,
                       kind="ExternalInput").ap()
    xn = nc.dram_tensor("xn", [T, D], BF16, kind="ExternalOutput").ap()
    xs = nc.dram_tensor("xsum", [1, D], F32, kind="ExternalOutput").ap()
    ones_d = nc.inline_tensor(np.ones((128, 1), BFD), "ones_d").ap()
    eps_d = nc.inline_tensor(np.full((128, 1), 1e-5, np.float32), "eps_d").ap()
    NT = T // 128
    with contextlib.ExitStack() as st:
        E = st.enter_context
        xdt = F32 if with_sum else BF16
        xsb = [E(nc.sbuf_tensor(f"xsb{i}", [128, D], xdt)) for i in range(2)]
        sq = E(nc.sbuf_tensor("sq", [128, D], F32))
        xnb = [E(nc.sbuf_tensor(f"xnb{i}", [128, D], BF16)) for i in range(2)]
        stat = E(nc.sbuf_tensor("stat", [128, 8], F32))
        ones_sb = E(nc.sbuf_tensor("ones_sb", [128, 1], BF16))
        eps_sb = E(nc.sbuf_tensor("eps_sb", [128, 1], F32))
        xs_sb = E(nc.sbuf_tensor("xs_sb", [1, D], F32))
        ps = E(nc.psum_tensor("ps", [1, D], F32))
        olds = E(nc.semaphore("olds"))
        vs = E(nc.semaphore("vs"))
        ld2 = [E(nc.semaphore(f"ld{p}")) for p in range(2)]
        ev = E(nc.semaphore("ev"))
        sdone = E(nc.semaphore("sdone"))
        tmm = E(nc.semaphore("tmm"))
        stv = E(nc.semaphore("stv"))
        blk = E(nc.Block())

        @blk.sync
        def _(sy):
            sy.dma_start(out=ones_sb[:], in_=ones_d).then_inc(olds, 16)
            sy.dma_start(out=eps_sb[:], in_=eps_d).then_inc(olds, 16)
            for i in range(NT):
                if i >= 2:
                    sy.wait_ge(ev, i - 1)
                    sy.dma_start(out=xn[(i - 2) * 128:(i - 1) * 128, :],
                                 in_=xnb[(i - 2) % 2][:]).then_inc(stv, 16)
                sy.dma_start(out=xsb[i % 2][:], in_=x[i * 128:(i + 1) * 128, :]
                             ).then_inc(ld2[i % 2], 16)
            sy.wait_ge(ev, NT - 1)
            sy.dma_start(out=xn[(NT - 2) * 128:(NT - 1) * 128, :],
                         in_=xnb[(NT - 2) % 2][:]).then_inc(stv, 16)
            sy.wait_ge(ev, NT)
            sy.dma_start(out=xn[(NT - 1) * 128:NT * 128, :],
                         in_=xnb[(NT - 1) % 2][:]).then_inc(stv, 16)
            if with_sum:
                sy.wait_ge(ev, NT + 1)
                sy.dma_start(out=xs, in_=xs_sb[:]).then_inc(stv, 16)
            sy.wait_ge(stv, 16 * (NT + (1 if with_sum else 0)))

        @blk.vector
        def _(v):
            nvs = 0
            for i in range(NT):
                v.wait_ge(ld2[i % 2], 16 * (i // 2 + 1))
                if i >= 2:
                    if with_sum:
                        v.wait_ge(tmm, i - 1)
                    v.wait_ge(stv, 16 * (i - 1))
                b = xsb[i % 2]
                o = xnb[i % 2]
                v.reduce_sum(stat[:, 0:1], b[:], axis=AX)
                v.tensor_tensor(out=sq[:], in0=b[:], in1=b[:], op=ALU.mult)
                v.reduce_sum(stat[:, 2:3], sq[:], axis=AX).then_inc(vs, 1)
                nvs += 1
                v.wait_ge(vs, nvs)
                v.tensor_scalar(out=stat[:, 1:2], in0=stat[:, 0:1],
                                scalar1=1.0 / D, scalar2=None, op0=ALU.mult)
                v.tensor_scalar(out=stat[:, 3:4], in0=stat[:, 2:3],
                                scalar1=1.0 / D, scalar2=None,
                                op0=ALU.mult).then_inc(vs, 1)
                nvs += 1
                v.wait_ge(vs, nvs)
                v.tensor_tensor(out=stat[:, 4:5], in0=stat[:, 1:2],
                                in1=stat[:, 1:2], op=ALU.mult).then_inc(vs, 1)
                nvs += 1
                v.wait_ge(vs, nvs)
                v.tensor_tensor(out=stat[:, 5:6], in0=stat[:, 3:4],
                                in1=stat[:, 4:5],
                                op=ALU.subtract).then_inc(sdone, 1)
                v.wait_ge(sdone, 2 * (i + 1))
                v.reciprocal(stat[:, 7:8], stat[:, 6:7]).then_inc(vs, 1)
                nvs += 1
                v.wait_ge(vs, nvs)
                v.tensor_scalar(out=sq[:], in0=b[:], scalar1=stat[:, 1:2],
                                scalar2=None, op0=ALU.subtract)
                v.tensor_scalar(out=o[:], in0=sq[:], scalar1=stat[:, 7:8],
                                scalar2=None, op0=ALU.mult).then_inc(ev, 1)
            if with_sum:
                v.wait_ge(tmm, NT)
                v.tensor_copy(xs_sb[:], ps[:]).then_inc(ev, 1)

        @blk.scalar
        def _(s):
            s.wait_ge(olds, 32)
            for i in range(NT):
                s.wait_ge(sdone, 2 * i + 1)
                s.activation(out=stat[:, 6:7], in_=stat[:, 5:6],
                             func=AF.Sqrt, bias=eps_sb[:, 0:1],
                             scale=1.0).then_inc(sdone, 1)

        if with_sum:
            @blk.tensor
            def _(t):
                t.wait_ge(olds, 16)
                for i in range(NT):
                    t.wait_ge(ev, i + 1)
                    for c in range(D // 512):
                        ins = t.matmul(ps[0:1, c * 512:(c + 1) * 512], ones_sb[:],
                                       xnb[i % 2][:, c * 512:(c + 1) * 512],
                                       start=(i == 0), stop=(i == NT - 1))
                    ins.then_inc(tmm, 1)
    return nc


def run_ln0(x_tok_percore):
    nc = _cached("ln0", lambda: build_ln(True))
    maps = [{"x": xc} for xc in x_tok_percore]
    res = _run(nc, maps)
    return ([r["xn"] for r in res], [r["xsum"] for r in res])


def run_ln1(x1_tok_percore):
    nc = _cached("ln1", lambda: build_ln(False))
    maps = [{"x": xc} for xc in x1_tok_percore]
    res = _run(nc, maps)
    return [r["xn"] for r in res]


def build_gemm(key, k_list, fout, x_dtypes, w_dtypes, res_dtype, out_dtype,
               t_total=T):
    nc = bass.Bass("TRN2", target_bir_lowering=False, detect_race_conditions=False)
    nj = len(k_list)
    ws = [nc.dram_tensor(f"w{j}", [k_list[j], fout], w_dtypes[j],
                         kind="ExternalInput").ap() for j in range(nj)]
    xs = [nc.dram_tensor(f"x{j}", [k_list[j], t_total], x_dtypes[j],
                         kind="ExternalInput").ap() for j in range(nj)]
    res_in = (nc.dram_tensor("res", [fout, t_total], res_dtype,
                             kind="ExternalInput").ap()
              if res_dtype is not None else None)
    out = nc.dram_tensor("out", [fout, t_total], out_dtype,
                         kind="ExternalOutput").ap()
    OG = fout // 128
    TC = t_total // 512
    NKT = [k // 128 for k in k_list]
    nk_tot = sum(NKT)
    with contextlib.ExitStack() as st:
        E = st.enter_context
        x_sb = [E(nc.sbuf_tensor(f"x_sb{j}", [128, NKT[j] * t_total], BF16))
                for j in range(nj)]
        w_sb = [[E(nc.sbuf_tensor(f"w_sb{j}_{p}", [128, NKT[j] * 128], BF16))
                 for j in range(nj)] for p in range(2)]
        r_sb = [E(nc.sbuf_tensor(f"r_sb{p}", [128, 512],
                                 res_dtype if res_dtype else BF16))
                for p in range(2)] if res_dtype is not None else None
        o_sb = [E(nc.sbuf_tensor(f"o_sb{p}", [128, 512], out_dtype))
                for p in range(4)]
        ps = [E(nc.psum_tensor(f"ps{p}", [128, 512], F32)) for p in range(4)]
        xld = E(nc.semaphore("xld"))
        wld2 = [E(nc.semaphore(f"wld{p}")) for p in range(2)]
        rld = E(nc.semaphore("rld"))
        mmd = E(nc.semaphore("mmd"))
        evd = E(nc.semaphore("evd"))
        stv4 = [E(nc.semaphore(f"stv{p}")) for p in range(4)]
        blk = E(nc.Block())

        @blk.gpsimd
        def _(g):
            for j in range(nj):
                for kt in range(NKT[j]):
                    g.dma_start(out=x_sb[j][:, kt * t_total:(kt + 1) * t_total],
                                in_=xs[j][kt * 128:(kt + 1) * 128, :]
                                ).then_inc(xld, 16)
            for og in range(OG):
                if og >= 2:
                    g.wait_ge(mmd, (og - 1) * TC)
                for j in range(nj):
                    g.dma_start(
                        out=w_sb[og % 2][j].ap().rearrange(
                            "p (a f) -> p a f", f=128),
                        in_=ws[j][:, og * 128:(og + 1) * 128].rearrange(
                            "(a p) f -> p a f", p=128)
                    ).then_inc(wld2[og % 2], 16)

        @blk.sync
        def _(sy):
            for og in range(OG):
                for tc in range(TC):
                    gidx = og * TC + tc
                    if res_in is not None:
                        if gidx >= 2:
                            sy.wait_ge(evd, gidx - 1)
                        sy.dma_start(out=r_sb[gidx % 2][:],
                                     in_=res_in[og * 128:(og + 1) * 128,
                                                tc * 512:(tc + 1) * 512]
                                     ).then_inc(rld, 16)
                    sy.wait_ge(evd, gidx + 1)
                    sy.dma_start(out=out[og * 128:(og + 1) * 128,
                                         tc * 512:(tc + 1) * 512],
                                 in_=o_sb[gidx % 4][:]).then_inc(stv4[gidx % 4], 16)
            ngr = OG * TC
            for p in range(4):
                sy.wait_ge(stv4[p], 16 * ((ngr - 1 - p) // 4 + 1))

        @blk.tensor
        def _(t):
            t.wait_ge(xld, 16 * nk_tot)
            for og in range(OG):
                t.wait_ge(wld2[og % 2], 16 * nj * (og // 2 + 1))
                for tc in range(TC):
                    gidx = og * TC + tc
                    if gidx >= 4:
                        t.wait_ge(evd, gidx - 3)
                    p = ps[gidx % 4]
                    n = 0
                    for j in range(nj):
                        for kt in range(NKT[j]):
                            n += 1
                            ins = t.matmul(
                                p[:],
                                w_sb[og % 2][j][:, kt * 128:(kt + 1) * 128],
                                x_sb[j][:, kt * t_total + tc * 512:
                                        kt * t_total + (tc + 1) * 512],
                                start=(n == 1), stop=(n == nk_tot))
                    ins.then_inc(mmd, 1)

        @blk.vector
        def _(v):
            for og in range(OG):
                for tc in range(TC):
                    gidx = og * TC + tc
                    v.wait_ge(mmd, gidx + 1)
                    if gidx >= 4:
                        v.wait_ge(stv4[gidx % 4], 16 * ((gidx - 4) // 4 + 1))
                    if res_in is not None:
                        v.wait_ge(rld, 16 * (gidx + 1))
                        v.tensor_tensor(out=o_sb[gidx % 4][:],
                                        in0=ps[gidx % 4][:],
                                        in1=r_sb[gidx % 2][:],
                                        op=ALU.add).then_inc(evd, 1)
                    else:
                        v.tensor_copy(o_sb[gidx % 4][:],
                                      ps[gidx % 4][:]).then_inc(evd, 1)
    return nc


def run_gemm(key, w_full_list, x_percore_list, res_percore, out_dtype_np):
    k_list = [w.shape[0] for w in w_full_list]
    fout = w_full_list[0].shape[1]
    w_dt = [F32 if w.dtype == np.float32 else BF16 for w in w_full_list]
    x_dt = [BF16] * len(w_full_list)
    res_dt = None
    if res_percore is not None:
        res_dt = F32 if res_percore[0].dtype == np.float32 else BF16
    out_dt = F32 if out_dtype_np == np.float32 else BF16
    nc = _cached(("gemm", key), lambda: build_gemm(
        key, tuple(k_list), fout, tuple(x_dt), tuple(w_dt), res_dt, out_dt))
    maps = []
    for c in range(NCORE):
        m = {}
        for j, w in enumerate(w_full_list):
            m[f"w{j}"] = w
            m[f"x{j}"] = x_percore_list[j][c]
        if res_percore is not None:
            m["res"] = res_percore[c]
        maps.append(m)
    res = _run(nc, maps)
    return [r["out"] for r in res]



def build_l0attn():
    S = 4096
    NCH = 32
    NP = 8
    SCALE = 128.0 ** -0.5
    nc = bass.Bass("TRN2", target_bir_lowering=False, detect_race_conditions=False)
    qf = nc.dram_tensor("qf", [NP, 128, S], BF16, kind="ExternalInput").ap()
    kf = nc.dram_tensor("kf", [NP, 128, S], BF16, kind="ExternalInput").ap()
    kt = nc.dram_tensor("kt", [NP, 128, S], BF16, kind="ExternalInput").ap()
    vt = nc.dram_tensor("vt", [NP, 128, S], BF16, kind="ExternalInput").ap()
    mem = nc.dram_tensor("mem", [NP, 128, 520], F32, kind="ExternalInput").ap()
    mns = nc.dram_tensor("mns", [NP, 128, 4], F32, kind="ExternalInput").ap()
    gatev = nc.dram_tensor("gatev", [NP, 1, 1], F32, kind="ExternalInput").ap()
    mnflat = nc.dram_tensor("mnflat", [1, 8192], F32, kind="ExternalInput").ap()
    outm = nc.dram_tensor("outm", [NP, 128, S], BF16, kind="ExternalOutput").ap()
    mask_np = (np.arange(128)[None, :] >= np.arange(128)[:, None]).astype(BFD)
    mask_d = nc.inline_tensor(mask_np, "mask_d").ap()
    ones1_d = nc.inline_tensor(np.ones((1, 128), BFD), "ones1_d").ap()

    VN = 4 + 3 * NCH
    TN = 2 + 2 * NCH
    EN = 10

    def v_relq(p): return VN * p + 1
    def v_b6(p): return VN * p + 2
    def v_memw(p): return VN * p + 3
    def v_state0(p): return VN * p + 4
    def v_scorem(p, c): return VN * p + 5 + 3 * c
    def v_chunkev(p, c): return VN * p + 7 + 3 * c
    def v_end(p): return VN * (p + 1)

    def t_rel(p): return TN * p + 1
    def t_b6(p): return TN * p + 2
    def t_score(p, c): return TN * p + 3 + 2 * c
    def t_chunk(p, c): return TN * p + 4 + 2 * c

    def e_elu_v(p, i): return EN * p + 2 * i + 1
    def e_elu_s(p, i): return EN * p + 2 * i + 2
    def e_rel_v(p): return EN * p + 7
    def e_rel_s(p): return EN * p + 8
    def e_gate_v(p): return EN * p + 9
    def e_gate_s(p): return EN * p + 10

    with contextlib.ExitStack() as st:
        E = st.enter_context
        q_sb = [E(nc.sbuf_tensor(f"q_sb{p}", [128, S], BF16)) for p in range(2)]
        kf_sb = [E(nc.sbuf_tensor(f"kf_sb{p}", [128, S], BF16)) for p in range(2)]
        kt_sb = [E(nc.sbuf_tensor(f"kt_sb{p}", [128, S], BF16)) for p in range(2)]
        va_sb = [E(nc.sbuf_tensor(f"va_sb{p}", [128, NCH * 132], BF16))
                 for p in range(2)]
        mem_sb = [E(nc.sbuf_tensor(f"mem_sb{p}", [128, 520], F32)) for p in range(2)]
        mns_sb = [E(nc.sbuf_tensor(f"mns_sb{p}", [128, 4], BF16)) for p in range(2)]
        gt_sb = [E(nc.sbuf_tensor(f"gt_sb{p}", [1, 1], F32)) for p in range(2)]
        sgq = E(nc.sbuf_tensor("sgq", [128, S], BF16))
        kpf = E(nc.sbuf_tensor("kpf", [128, S], BF16))
        kpt = E(nc.sbuf_tensor("kpt", [128, S], BF16))
        tmp1 = E(nc.sbuf_tensor("tmp1", [128, S], BF16))
        tmp2 = E(nc.sbuf_tensor("tmp2", [128, S], BF16))
        memw = E(nc.sbuf_tensor("memw", [128, 520], BF16))
        ob = E(nc.sbuf_tensor("ob", [128, S], BF16))
        mask_sb = E(nc.sbuf_tensor("mask_sb", [128, 128], BF16))
        ones1_sb = E(nc.sbuf_tensor("ones1_sb", [1, 128], BF16))
        mnf_sb = E(nc.sbuf_tensor("mnf_sb", [1, 8192], F32))
        emt = E(nc.sbuf_tensor("emt", [1, 16], F32))
        relb = E(nc.sbuf_tensor("relb", [1, 32], F32))
        wg6 = E(nc.sbuf_tensor("wg6", [1, 6], BF16))
        wgb = E(nc.sbuf_tensor("wgb", [128, 6], F32))
        qsum = E(nc.sbuf_tensor("qsum", [128, 1], F32))
        qsum_bf = E(nc.sbuf_tensor("qsum_bf", [128, 1], BF16))
        S_f = [E(nc.sbuf_tensor(f"S_f{p}", [128, 129], F32)) for p in range(2)]
        S_bf = E(nc.sbuf_tensor("S_bf", [128, 129], BF16))
        pstm = E(nc.sbuf_tensor("pstm", [128, 128], BF16))
        den5 = E(nc.sbuf_tensor("den5", [128, 12], F32))
        loc = E(nc.sbuf_tensor("loc", [128, 128], F32))
        acc = [E(nc.sbuf_tensor(f"acc{p}", [128, 128], F32)) for p in range(2)]
        ps_rel = E(nc.psum_tensor("ps_rel", [1, 8], F32))
        ps_b6 = E(nc.psum_tensor("ps_b6", [128, 6], F32))
        ps_s = E(nc.psum_tensor("ps_s", [128, 128], F32))
        ps_out = E(nc.psum_tensor("ps_out", [128, 129], F32))
        ps_am1 = E(nc.psum_tensor("ps_am1", [128, 260], F32))
        ps_am2 = E(nc.psum_tensor("ps_am2", [128, 260], F32))
        ps_kv = E(nc.psum_tensor("ps_kv", [128, 129], F32))
        cld = E(nc.semaphore("cld"))
        msd = E(nc.semaphore("msd"))
        pld = [E(nc.semaphore(f"pld{p}")) for p in range(2)]
        tsem = E(nc.semaphore("tsem"))
        vsem = E(nc.semaphore("vsem"))
        esem = E(nc.semaphore("esem"))
        vs = E(nc.semaphore("vs"))
        stv = E(nc.semaphore("stv"))
        blk = E(nc.Block())

        @blk.gpsimd
        def _(g):
            g.dma_start(out=mask_sb[:], in_=mask_d).then_inc(cld, 16)
            g.dma_start(out=ones1_sb[:], in_=ones1_d).then_inc(cld, 16)
            g.dma_start(out=mnf_sb[:], in_=mnflat).then_inc(cld, 16)
            g.memset(va_sb[0][:], 1.0)
            g.memset(va_sb[1][:], 1.0)
            g.memset(S_f[0][:], 0.0)
            g.memset(S_bf[:], 0.0).then_inc(msd, 1)
            for p in range(NP):
                par = p % 2
                if p >= 2:
                    g.wait_ge(vsem, v_end(p - 2))
                    g.wait_ge(tsem, t_chunk(p - 2, NCH - 1))
                g.dma_start(out=q_sb[par][:], in_=qf[p]).then_inc(pld[par], 16)
                g.dma_start(out=kf_sb[par][:], in_=kf[p]).then_inc(pld[par], 16)
                g.dma_start(out=kt_sb[par][:], in_=kt[p]).then_inc(pld[par], 16)
                g.dma_start(
                    out=va_sb[par].ap().rearrange("p (c w) -> p c w", w=132
                                                  )[:, :, 0:128],
                    in_=vt[p].rearrange("p (c w) -> p c w", w=128)
                ).then_inc(pld[par], 16)
                g.dma_start(out=mem_sb[par][:], in_=mem[p]).then_inc(pld[par], 16)
                g.dma_start(out=mns_sb[par][:], in_=mns[p]).then_inc(pld[par], 16)
                g.dma_start(out=gt_sb[par][:], in_=gatev[p]).then_inc(pld[par], 16)

        @blk.vector
        def _(v):
            nvs = 0

            def sync(ins):
                nonlocal nvs
                nvs += 1
                ins.then_inc(vs, 1)
                v.wait_ge(vs, nvs)

            v.wait_ge(cld, 48)
            sync(v.tensor_reduce(emt[:, 0:4], mnf_sb.ap().rearrange(
                "p (m d) -> p m d", d=2048), axis=AX, op=ALU.add))
            sync(v.tensor_scalar(out=emt[:, 4:8], in0=emt[:, 0:4],
                                 scalar1=1e-6, scalar2=None, op0=ALU.is_lt))
            sync(v.tensor_reduce(emt[:, 8:9], emt[:, 4:8], axis=AX, op=ALU.add))
            sync(v.tensor_scalar(out=emt[:, 9:10], in0=emt[:, 8:9],
                                 scalar1=3.5, scalar2=None, op0=ALU.is_lt))

            for p in range(NP):
                par = p % 2
                v.wait_ge(pld[par], 16 * 7 * (p // 2 + 1))
                if p >= 1:
                    v.wait_ge(stv, 16 * p)
                for i, (srcb, dstb) in enumerate(
                        ((q_sb[par], sgq), (kf_sb[par], kpf), (kt_sb[par], kpt))):
                    v.tensor_scalar(out=tmp1[:], in0=srcb[:], scalar1=0.0,
                                    scalar2=None, op0=ALU.min).then_inc(esem, 1)
                    v.wait_ge(esem, e_elu_s(p, i))
                    v.scalar_tensor_tensor(out=dstb[:], in0=srcb[:], scalar=0.0,
                                           in1=tmp2[:], op0=ALU.max, op1=ALU.add)
                sync(v.reduce_sum(qsum[:], q_sb[par][:], axis=AX))
                v.tensor_copy(qsum_bf[:], qsum[:]).then_inc(vsem, 1)
                v.wait_ge(tsem, t_rel(p))
                sync(v.tensor_scalar(out=relb[:, 0:4], in0=ps_rel[0:1, 0:4],
                                     scalar1=SCALE / 4096.0, scalar2=None,
                                     op0=ALU.mult))
                sync(v.scalar_tensor_tensor(out=relb[:, 4:8], in0=emt[:, 4:8],
                                            scalar=-1e38, in1=relb[:, 0:4],
                                            op0=ALU.mult, op1=ALU.add))
                v.reduce_max(relb[:, 8:9], relb[:, 4:8], axis=AX,
                             negate=True).then_inc(esem, 1)
                v.wait_ge(esem, e_rel_s(p))
                sync(v.reduce_sum(relb[:, 14:15], relb[:, 10:14], axis=AX))
                sync(v.reciprocal(relb[:, 15:16], relb[:, 14:15]))
                sync(v.tensor_scalar(out=relb[:, 16:20], in0=relb[:, 10:14],
                                     scalar1=relb[:, 15:16], scalar2=None,
                                     op0=ALU.mult))
                v.tensor_scalar(out=relb[:, 24:28], in0=relb[:, 16:20],
                                scalar1=emt[:, 9:10], scalar2=None,
                                op0=ALU.mult).then_inc(esem, 1)
                v.wait_ge(esem, e_gate_s(p))
                sync(v.tensor_scalar(out=relb[:, 29:30], in0=relb[:, 28:29],
                                     scalar1=-1.0, scalar2=1.0, op0=ALU.mult,
                                     op1=ALU.add))
                v.tensor_copy(wg6[:], relb[:, 24:30]).then_inc(vsem, 1)
                v.wait_ge(tsem, t_b6(p))
                sync(v.tensor_copy(wgb[:], ps_b6[:]))
                mv = memw.ap().rearrange("p (m w) -> p m w", w=130)
                ms = mem_sb[par].ap().rearrange("p (m w) -> p m w", w=130)
                for m in range(4):
                    v.tensor_scalar(out=memw[:, m * 130:m * 130 + 128],
                                    in0=mem_sb[par][:, m * 130:m * 130 + 128],
                                    scalar1=wgb[:, m:m + 1], scalar2=None,
                                    op0=ALU.mult)
                v.tensor_copy(mv[:, :, 128:129],
                              ms[:, :, 128:129]).then_inc(vsem, 1)
                if p == 0:
                    v.wait_ge(msd, 1)
                v.tensor_scalar(out=S_f[0][:], in0=S_f[0][:], scalar1=0.0,
                                scalar2=None, op0=ALU.mult)
                v.tensor_scalar(out=S_bf[:], in0=S_f[0][:], scalar1=0.0,
                                scalar2=None, op0=ALU.mult).then_inc(vsem, 1)
                for c in range(NCH):
                    cpar = c % 2
                    v.wait_ge(tsem, t_score(p, c))
                    v.tensor_tensor(out=pstm[:], in0=ps_s[:], in1=mask_sb[:],
                                    op=ALU.mult).then_inc(vsem, 1)
                    v.wait_ge(tsem, t_chunk(p, c))
                    v.tensor_tensor(out=S_f[1 - cpar][:], in0=S_f[cpar][:],
                                    in1=ps_kv[:], op=ALU.add)
                    v.tensor_copy(S_bf[:], S_f[1 - cpar][:]).then_inc(vsem, 1)
                    v.tensor_copy(den5[:, 0:1], ps_out[:, 128:129])
                    v.tensor_copy(den5[:, 1:2], ps_am1[:, 128:129])
                    v.tensor_copy(den5[:, 2:3], ps_am1[:, 258:259])
                    v.tensor_copy(den5[:, 3:4], ps_am2[:, 128:129])
                    sync(v.tensor_copy(den5[:, 4:5], ps_am2[:, 258:259]))
                    sync(v.tensor_scalar(out=den5[:, 6:11], in0=den5[:, 0:5],
                                         scalar1=1e-6, scalar2=None,
                                         op0=ALU.max))
                    sync(v.reciprocal(den5[:, 0:5], den5[:, 6:11]))
                    v.tensor_scalar(out=loc[:], in0=ps_out[:, 0:128],
                                    scalar1=den5[:, 0:1], scalar2=None,
                                    op0=ALU.mult)
                    v.tensor_scalar(out=acc[0][:], in0=ps_am1[:, 0:128],
                                    scalar1=den5[:, 1:2], scalar2=None,
                                    op0=ALU.mult)
                    v.scalar_tensor_tensor(out=acc[1][:], in0=ps_am1[:, 130:258],
                                           scalar=den5[:, 2:3], in1=acc[0][:],
                                           op0=ALU.mult, op1=ALU.add)
                    v.scalar_tensor_tensor(out=acc[0][:], in0=ps_am2[:, 0:128],
                                           scalar=den5[:, 3:4], in1=acc[1][:],
                                           op0=ALU.mult, op1=ALU.add)
                    v.scalar_tensor_tensor(out=acc[1][:], in0=ps_am2[:, 130:258],
                                           scalar=den5[:, 4:5], in1=acc[0][:],
                                           op0=ALU.mult, op1=ALU.add)
                    v.tensor_scalar(out=loc[:], in0=loc[:], scalar1=wgb[:, 5:6],
                                    scalar2=None, op0=ALU.mult)
                    v.scalar_tensor_tensor(
                        out=ob[:, c * 128:(c + 1) * 128], in0=acc[1][:],
                        scalar=wgb[:, 4:5], in1=loc[:], op0=ALU.mult,
                        op1=ALU.add).then_inc(vsem, 1)

        @blk.scalar
        def _(s):
            for p in range(NP):
                for i in range(3):
                    s.wait_ge(esem, e_elu_v(p, i))
                    s.activation(out=tmp2[:], in_=tmp1[:],
                                 func=AF.Exp).then_inc(esem, 1)
                s.wait_ge(esem, e_rel_v(p))
                s.activation(out=relb[:, 10:14], in_=relb[:, 4:8], func=AF.Exp,
                             bias=relb[:, 8:9], scale=1.0).then_inc(esem, 1)
                s.wait_ge(esem, e_gate_v(p))
                s.activation(out=relb[:, 28:29], in_=gt_sb[p % 2][:],
                             func=AF.Sigmoid).then_inc(esem, 1)

        @blk.tensor
        def _(t):
            t.wait_ge(cld, 48)
            t.wait_ge(msd, 1)
            for p in range(NP):
                par = p % 2
                t.wait_ge(vsem, v_relq(p))
                t.matmul(ps_rel[0:1, 0:4], qsum_bf[:], mns_sb[par][:],
                         start=True, stop=True).then_inc(tsem, 1)
                t.wait_ge(vsem, v_b6(p))
                t.matmul(ps_b6[:], ones1_sb[:], wg6[:],
                         start=True, stop=True).then_inc(tsem, 1)
                t.wait_ge(vsem, v_memw(p))
                for c in range(NCH):
                    cs = slice(c * 128, (c + 1) * 128)
                    vsl = slice(c * 132, c * 132 + 129)
                    if c == 0:
                        t.wait_ge(vsem, v_state0(p))
                    else:
                        t.wait_ge(vsem, v_chunkev(p, c - 1))
                    t.matmul(ps_s[:], kpf[:, cs], sgq[:, cs],
                             start=True, stop=True).then_inc(tsem, 1)
                    t.matmul(ps_out[:], sgq[:, cs], S_bf[:],
                             start=True, stop=False)
                    t.matmul(ps_am1[:, 0:129], sgq[:, cs], memw[:, 0:129],
                             start=True, stop=True)
                    t.matmul(ps_am1[:, 130:259], sgq[:, cs], memw[:, 130:259],
                             start=True, stop=True)
                    t.matmul(ps_am2[:, 0:129], sgq[:, cs], memw[:, 260:389],
                             start=True, stop=True)
                    t.matmul(ps_am2[:, 130:259], sgq[:, cs], memw[:, 390:519],
                             start=True, stop=True)
                    t.wait_ge(vsem, v_scorem(p, c))
                    t.matmul(ps_out[:], pstm[:], va_sb[par][:, vsl],
                             start=False, stop=True)
                    t.matmul(ps_kv[:], kpt[:, cs], va_sb[par][:, vsl],
                             start=True, stop=True).then_inc(tsem, 1)

        @blk.sync
        def _(sy):
            for p in range(NP):
                sy.wait_ge(vsem, v_end(p))
                sy.dma_start(out=outm[p], in_=ob[:]).then_inc(stv, 16)
            sy.wait_ge(stv, 16 * NP)
    return nc


def run_l0attn(in_percore):
    nc = _cached("l0attn", build_l0attn)
    res = _run(nc, in_percore)
    return [r["outm"] for r in res]


def build_l0ffna():
    NK = D // 128
    NT = T // 128
    nc = bass.Bass("TRN2", target_bir_lowering=False, detect_race_conditions=False)
    xnf = nc.dram_tensor("xnf", [D, T], BF16, kind="ExternalInput").ap()
    rwt = nc.dram_tensor("rwt", [D, 8], F32, kind="ExternalInput").ap()
    dwt = nc.dram_tensor("dwt", [D, 128], F32, kind="ExternalInput").ap()
    aout_d = nc.dram_tensor("aout", [128, T], BF16, kind="ExternalOutput").ap()
    ident_d = nc.inline_tensor(np.eye(128, dtype=BFD), "ident_d").ap()

    with contextlib.ExitStack() as st:
        E = st.enter_context
        xn_sb = E(nc.sbuf_tensor("xn_sb", [128, NK * T], BF16))
        rw_sb = E(nc.sbuf_tensor("rw_sb", [128, NK * 8], BF16))
        dw_sb = E(nc.sbuf_tensor("dw_sb", [128, NK * 128], BF16))
        ident = E(nc.sbuf_tensor("ident", [128, 128], BF16))
        p8 = E(nc.sbuf_tensor("p8", [128, 8], F32))
        w8 = E(nc.sbuf_tensor("w8", [128, 8], F32))
        w2b = E(nc.sbuf_tensor("w2b", [128, 8], F32))
        g8 = E(nc.sbuf_tensor("g8", [128, 8], F32))
        stt_ = E(nc.sbuf_tensor("stt_", [128, 8], F32))
        a_sb = E(nc.sbuf_tensor("a_sb", [128, 128], F32))
        asc = E(nc.sbuf_tensor("asc", [128, 128], BF16))
        aout = E(nc.sbuf_tensor("aout_sb", [128, T], BF16))
        ps_r = E(nc.psum_tensor("ps_r", [128, 8], F32))
        ps_h = E(nc.psum_tensor("ps_h", [128, 128], F32))
        ps_t = E(nc.psum_tensor("ps_t", [128, 128], BF16))
        ld = E(nc.semaphore("ld"))
        tsem = E(nc.semaphore("tsem"))
        vsem = E(nc.semaphore("vsem"))
        esem = E(nc.semaphore("esem"))
        vs = E(nc.semaphore("vs"))
        stv = E(nc.semaphore("stv"))
        blk = E(nc.Block())

        @blk.gpsimd
        def _(g):
            g.dma_start(out=xn_sb.ap().rearrange("p (a t) -> p a t", t=T),
                        in_=xnf.rearrange("(a p) t -> p a t", p=128)
                        ).then_inc(ld, 16)
            g.dma_start(out=rw_sb.ap().rearrange("p (a f) -> p a f", f=8),
                        in_=rwt.rearrange("(a p) f -> p a f", p=128)
                        ).then_inc(ld, 16)
            g.dma_start(out=dw_sb.ap().rearrange("p (a f) -> p a f", f=128),
                        in_=dwt.rearrange("(a p) f -> p a f", p=128)
                        ).then_inc(ld, 16)
            g.dma_start(out=ident[:], in_=ident_d).then_inc(ld, 16)

        @blk.tensor
        def _(t):
            t.wait_ge(ld, 64)
            for i in range(NT):
                if i >= 1:
                    t.wait_ge(vsem, 2 * i)
                for k in range(NK):
                    t.matmul(ps_r[:], xn_sb[:, k * T + i * 128:k * T + (i + 1) * 128],
                             rw_sb[:, k * 8:(k + 1) * 8],
                             start=(k == 0), stop=(k == NK - 1))
                for k in range(NK):
                    ins = t.matmul(ps_h[:],
                                   xn_sb[:, k * T + i * 128:k * T + (i + 1) * 128],
                                   dw_sb[:, k * 128:(k + 1) * 128],
                                   start=(k == 0), stop=(k == NK - 1))
                ins.then_inc(tsem, 1)
                t.wait_ge(vsem, 2 * i + 1)
                t.transpose(ps_t[:], asc[:], ident[:]).then_inc(tsem, 1)

        @blk.vector
        def _(v):
            nvs = 0

            def sync(ins):
                nonlocal nvs
                nvs += 1
                ins.then_inc(vs, 1)
                v.wait_ge(vs, nvs)

            for i in range(NT):
                v.wait_ge(tsem, 2 * i + 1)
                v.reduce_max(stt_[:, 0:1], ps_r[:], axis=AX,
                             negate=True).then_inc(esem, 1)
                v.wait_ge(esem, 3 * i + 2)
                sync(v.reduce_sum(stt_[:, 1:2], p8[:], axis=AX))
                sync(v.reciprocal(stt_[:, 2:3], stt_[:, 1:2]))
                sync(v.tensor_scalar(out=w8[:], in0=p8[:], scalar1=stt_[:, 2:3],
                                     scalar2=None, op0=ALU.mult))
                sync(v.reduce_max(stt_[:, 3:4], w8[:], axis=AX))
                sync(v.tensor_scalar(out=w2b[:], in0=w8[:], scalar1=stt_[:, 3:4],
                                     scalar2=None, op0=ALU.is_ge))
                sync(v.scalar_tensor_tensor(out=w2b[:], in0=w2b[:], scalar=-1e38,
                                            in1=w8[:], op0=ALU.mult, op1=ALU.add))
                sync(v.reduce_max(stt_[:, 4:5], w2b[:], axis=AX))
                sync(v.tensor_scalar(out=g8[:], in0=w8[:], scalar1=stt_[:, 4:5],
                                     scalar2=None, op0=ALU.is_ge))
                sync(v.tensor_tensor(out=g8[:], in0=g8[:], in1=w8[:],
                                     op=ALU.mult))
                v.wait_ge(esem, 3 * i + 3)
                for e in range(8):
                    ins = v.tensor_scalar(out=asc[:, e * 16:(e + 1) * 16],
                                          in0=a_sb[:, e * 16:(e + 1) * 16],
                                          scalar1=g8[:, e:e + 1], scalar2=None,
                                          op0=ALU.mult)
                ins.then_inc(vsem, 1)
                v.wait_ge(tsem, 2 * i + 2)
                if i >= 1:
                    v.wait_ge(stv, 0)
                v.tensor_copy(aout[:, i * 128:(i + 1) * 128],
                              ps_t[:]).then_inc(vsem, 1)

        @blk.scalar
        def _(s):
            for i in range(NT):
                s.wait_ge(esem, 3 * i + 1)
                s.activation(out=p8[:], in_=ps_r[:], func=AF.Exp,
                             bias=stt_[:, 0:1], scale=1.0).then_inc(esem, 1)
                s.activation(out=a_sb[:], in_=ps_h[:],
                             func=AF.Silu).then_inc(esem, 1)

        @blk.sync
        def _(sy):
            sy.wait_ge(vsem, 2 * NT)
            sy.dma_start(out=aout_d, in_=aout[:]).then_inc(stv, 16)
            sy.wait_ge(stv, 16)
    return nc


def run_l0ffna(xnf_percore, rwt, dwt):
    nc = _cached("l0ffna", build_l0ffna)
    maps = [{"xnf": xnf_percore[c], "rwt": rwt, "dwt": dwt} for c in range(NCORE)]
    res = _run(nc, maps)
    return [r["aout"] for r in res]



def build_rope():
    NG = 18
    TC = T // 512
    nc = bass.Bass("TRN2", target_bir_lowering=False, detect_race_conditions=False)
    qkv = nc.dram_tensor("qkv", [2304, T], BF16, kind="ExternalInput").ap()
    cosr = nc.dram_tensor("cosr", [128, T], F32, kind="ExternalInput").ap()
    sins = nc.dram_tensor("sins", [128, T], F32, kind="ExternalInput").ap()
    out = nc.dram_tensor("out", [2304, T], BF16, kind="ExternalOutput").ap()
    pm = np.zeros((128, 128), BFD)
    for p in range(128):
        q = (p // 64) * 64 + ((p % 64) + 32) % 64
        pm[q, p] = 1.0
    perm_d = nc.inline_tensor(pm, "perm_d").ap()
    with contextlib.ExitStack() as st:
        E = st.enter_context
        xb = [E(nc.sbuf_tensor(f"xb{p}", [128, T], BF16)) for p in range(2)]
        yb = [E(nc.sbuf_tensor(f"yb{p}", [128, T], BF16)) for p in range(2)]
        cs = E(nc.sbuf_tensor("cs", [128, T], F32))
        sn = E(nc.sbuf_tensor("sn", [128, T], F32))
        pmat = E(nc.sbuf_tensor("pmat", [128, 128], BF16))
        t1 = E(nc.sbuf_tensor("t1", [128, 512], F32))
        t2 = E(nc.sbuf_tensor("t2", [128, 512], F32))
        ps_sw = [E(nc.psum_tensor(f"ps_sw{p}", [128, 512], F32)) for p in range(2)]
        cld = E(nc.semaphore("cld"))
        ld2 = [E(nc.semaphore(f"ld{p}")) for p in range(2)]
        ev = E(nc.semaphore("ev"))
        tsem = E(nc.semaphore("tsem"))
        stv = E(nc.semaphore("stv"))
        blk = E(nc.Block())

        @blk.sync
        def _(sy):
            sy.dma_start(out=cs[:], in_=cosr).then_inc(cld, 16)
            sy.dma_start(out=sn[:], in_=sins).then_inc(cld, 16)
            sy.dma_start(out=pmat[:], in_=perm_d).then_inc(cld, 16)
            for g in range(NG):
                if g >= 2:
                    sy.wait_ge(ev, (g - 1) * TC)
                    sy.dma_start(out=out[(g - 2) * 128:(g - 1) * 128, :],
                                 in_=yb[g % 2][:]).then_inc(stv, 16)
                sy.dma_start(out=xb[g % 2][:],
                             in_=qkv[g * 128:(g + 1) * 128, :]
                             ).then_inc(ld2[g % 2], 16)
            sy.wait_ge(ev, (NG - 1) * TC)
            sy.dma_start(out=out[(NG - 2) * 128:(NG - 1) * 128, :],
                         in_=yb[0][:]).then_inc(stv, 16)
            sy.wait_ge(ev, NG * TC)
            sy.dma_start(out=out[(NG - 1) * 128:NG * 128, :],
                         in_=yb[1][:]).then_inc(stv, 16)
            sy.wait_ge(stv, 16 * NG)

        @blk.tensor
        def _(t):
            t.wait_ge(cld, 48)
            for g in range(NG):
                t.wait_ge(ld2[g % 2], 16 * (g // 2 + 1))
                for cc in range(TC):
                    gi = g * TC + cc
                    if gi >= 2:
                        t.wait_ge(ev, gi - 1)
                    t.matmul(ps_sw[gi % 2][:], pmat[:],
                             xb[g % 2][:, cc * 512:(cc + 1) * 512],
                             start=True, stop=True).then_inc(tsem, 1)

        @blk.vector
        def _(v):
            v.wait_ge(cld, 48)
            for g in range(NG):
                v.wait_ge(ld2[g % 2], 16 * (g // 2 + 1))
                if g >= 2:
                    v.wait_ge(stv, 16 * (g - 1))
                for cc in range(TC):
                    gi = g * TC + cc
                    sl = slice(cc * 512, (cc + 1) * 512)
                    v.wait_ge(tsem, gi + 1)
                    v.tensor_tensor(out=t1[:], in0=xb[g % 2][:, sl],
                                    in1=cs[:, sl], op=ALU.mult)
                    v.tensor_tensor(out=t2[:], in0=ps_sw[gi % 2][:],
                                    in1=sn[:, sl], op=ALU.mult)
                    v.scalar_tensor_tensor(out=yb[g % 2][:, sl], in0=t1[:],
                                           scalar=1.0, in1=t2[:],
                                           op0=ALU.mult,
                                           op1=ALU.add).then_inc(ev, 1)
    return nc


def run_rope(qkv_percore, cos_half, sins_half):
    nc = _cached("rope", build_rope)
    maps = [{"qkv": qkv_percore[c], "cosr": cos_half[c % 2],
             "sins": sins_half[c % 2]} for c in range(NCORE)]
    res = _run(nc, maps)
    return [r["out"] for r in res]


def _l1_sched():
    NP, NQ = 8, 8
    T_ = {}
    V_ = {}
    E_ = {}
    ct = cv = ce = 0
    cv += 1; V_["ksq"] = cv
    for cc in range(16):
        ct += 1; T_[f"kn{cc}"] = ct
        cv += 1; V_[f"knr{cc}"] = cv
    g = 0
    for u in range(NP):
        ce += 1; E_[f"sq{u}"] = ce
        for cc in range(8):
            ct += 1; T_[f"qn{u},{cc}"] = ct
            cv += 1; V_[f"qnr{u},{cc}"] = cv
        cv += 1; V_[f"biasgo{u}"] = cv
        ce += 1; E_[f"sqrt{u}"] = ce
        cv += 1; V_[f"biasmm{u}"] = cv
        ct += 1; T_[f"bb{u}"] = ct
        cv += 1; V_[f"biasrdy{u}"] = cv
        for h in range(2):
            for q in range(NQ):
                nch = 4 * q + 4
                prev = None
                for c0 in range(0, nch, 2):
                    for c in (c0, c0 + 1):
                        ct += 1; T_[f"s{g + c}"] = ct
                    ce += 1; E_[f"exp{(g + c0) // 2}"] = ce
                    for c in (c0, c0 + 1):
                        if c >= 4 * q:
                            cv += 1; V_[f"mask{g + c}"] = cv
                    if prev is not None:
                        for c in (prev, prev + 1):
                            ct += 1; T_[f"p{g + c}"] = ct
                    prev = c0
                for c in (prev, prev + 1):
                    ct += 1; T_[f"p{g + c}"] = ct
                g += nch
                cv += 1; V_[f"den{u},{h},{q}"] = cv
                ct += 1; T_[f"bc{u},{h},{q}"] = ct
                cv += 1; V_[f"evac{u},{h},{q}"] = cv
        V_[f"end{u}"] = cv
    return T_, V_, E_


def build_l1attn():
    S = 4096
    NQ = 8
    NP = 8
    T_, V_, E_ = _l1_sched()
    nc = bass.Bass("TRN2", target_bir_lowering=False, detect_race_conditions=False)
    qh = nc.dram_tensor("qh", [NP, 128, S], BF16, kind="ExternalInput").ap()
    kf2 = nc.dram_tensor("kf2", [2, 128, S], BF16, kind="ExternalInput").ap()
    vt2 = nc.dram_tensor("vt2", [2, 128, 2048], BF16, kind="ExternalInput").ap()
    oh = nc.dram_tensor("oh", [NP * 2, 64, S], BF16, kind="ExternalOutput").ap()
    stair = (np.arange(128)[:, None] <= np.arange(896)[None, :] - 384).astype(BFD)
    t8_d = nc.inline_tensor(stair, "t8_d").ap()
    ones1_d = nc.inline_tensor(np.ones((1, 128), BFD), "ones1_d").ap()
    onesc_d = nc.inline_tensor(np.ones((128, 1), BFD), "onesc_d").ap()

    with contextlib.ExitStack() as st:
        E = st.enter_context
        q_sb = [E(nc.sbuf_tensor(f"q_sb{p}", [128, S], BF16)) for p in range(2)]
        oh_sb = [E(nc.sbuf_tensor(f"oh_sb{p}", [64, S], BF16)) for p in range(2)]
        kf_sb = E(nc.sbuf_tensor("kf_sb", [128, 2 * S], BF16))
        va_sb = E(nc.sbuf_tensor("va_sb", [128, 2 * 2176], BF16))
        sq_sb = E(nc.sbuf_tensor("sq_sb", [128, 2 * S], BF16))
        t8_sb = E(nc.sbuf_tensor("t8_sb", [128, 896], BF16))
        ones1_sb = E(nc.sbuf_tensor("ones1_sb", [1, 128], BF16))
        onesc_sb = E(nc.sbuf_tensor("onesc_sb", [128, 1], BF16))
        pst = [E(nc.sbuf_tensor(f"pst{p}", [128, 1024], BF16)) for p in range(2)]
        nrm = E(nc.sbuf_tensor("nrm", [1, 20], F32))
        nrmb = E(nc.sbuf_tensor("nrmb", [1, 1], BF16))
        biasb = E(nc.sbuf_tensor("biasb", [128, 1], F32))
        denf = E(nc.sbuf_tensor("denf", [1, 512], F32))
        denr = E(nc.sbuf_tensor("denr", [1, 512], BF16))
        denb = E(nc.sbuf_tensor("denb", [64, 512], BF16))
        ps_sT = [E(nc.psum_tensor(f"ps_sT{p}", [128, 1024], F32)) for p in range(2)]
        ps_o = E(nc.psum_tensor("ps_o", [65, 512], F32))
        ps_bc = E(nc.psum_tensor("ps_bc", [64, 512], F32))
        ps_n = E(nc.psum_tensor("ps_n", [1, 512], F32))
        ps_bb = E(nc.psum_tensor("ps_bb", [128, 1], F32))
        cld = E(nc.semaphore("cld"))
        msd = E(nc.semaphore("msd"))
        pld = [E(nc.semaphore(f"pld{p}")) for p in range(2)]
        tsem = E(nc.semaphore("tsem"))
        vsem = E(nc.semaphore("vsem"))
        esem = E(nc.semaphore("esem"))
        vs = E(nc.semaphore("vs"))
        stv = E(nc.semaphore("stv"))
        blk = E(nc.Block())

        @blk.gpsimd
        def _(g):
            g.dma_start(out=t8_sb[:], in_=t8_d).then_inc(cld, 16)
            g.dma_start(out=ones1_sb[:], in_=ones1_d).then_inc(cld, 16)
            g.dma_start(out=onesc_sb[:], in_=onesc_d).then_inc(cld, 16)
            g.dma_start(out=kf_sb[:, 0:S], in_=kf2[0]).then_inc(cld, 16)
            g.dma_start(out=kf_sb[:, S:2 * S], in_=kf2[1]).then_inc(cld, 16)
            g.memset(va_sb[:], 1.0).then_inc(msd, 1)
            for kv in range(2):
                g.dma_start(
                    out=va_sb.ap()[:, kv * 2176:(kv + 1) * 2176].rearrange(
                        "p (c w) -> p c w", w=68)[:, :, 0:64],
                    in_=vt2[kv].rearrange("p (c w) -> p c w", w=64)
                ).then_inc(cld, 16)
            for u in range(NP):
                par = u % 2
                if u >= 2:
                    g.wait_ge(vsem, V_[f"end{u - 2}"])
                g.dma_start(out=q_sb[par][:], in_=qh[u]).then_inc(pld[par], 16)

        @blk.scalar
        def _(s):
            s.wait_ge(cld, 112)
            gg = 0
            for u in range(NP):
                par = u % 2
                kv = u // 4
                s.wait_ge(pld[par], 16 * (u // 2 + 1))
                if u >= 1:
                    s.wait_ge(tsem, T_[f"qn{u - 1},7"])
                else:
                    s.wait_ge(tsem, T_["kn7"])
                s.activation(out=sq_sb[:, 0:S], in_=q_sb[par][:],
                             func=AF.Square).then_inc(esem, 1)
                s.wait_ge(vsem, V_[f"biasgo{u}"])
                s.activation(out=nrm[:, 3:4], in_=nrm[:, 2:3],
                             func=AF.Sqrt).then_inc(esem, 1)
                s.wait_ge(vsem, V_[f"biasrdy{u}"])
                for h in range(2):
                    for q in range(NQ):
                        for c in range(1, 4 * q + 4, 2):
                            gi = gg + c
                            di = gi // 2
                            s.wait_ge(tsem, T_[f"s{gi}"])
                            if di >= 2:
                                s.wait_ge(tsem, T_[f"p{2 * di - 3}"])
                            s.activation(out=pst[di % 2][:],
                                         in_=ps_sT[di % 2][:], func=AF.Exp,
                                         bias=biasb[:, 0:1],
                                         scale=0.125).then_inc(esem, 1)
                        gg += 4 * q + 4

        @blk.vector
        def _(v):
            nvs = 0

            def sync(ins):
                nonlocal nvs
                nvs += 1
                ins.then_inc(vs, 1)
                v.wait_ge(vs, nvs)

            v.wait_ge(cld, 112)
            v.tensor_tensor(out=sq_sb[:], in0=kf_sb[:], in1=kf_sb[:],
                            op=ALU.mult).then_inc(vsem, 1)
            for kv in range(2):
                for cc in range(8):
                    v.wait_ge(tsem, T_[f"kn{kv * 8 + cc}"])
                    v.reduce_max(nrm[:, 8 + cc:9 + cc], ps_n[:],
                                 axis=AX).then_inc(vsem, 1)
                v.wait_ge(vsem, V_[f"knr{kv * 8 + 7}"])
                sync(v.reduce_max(nrm[:, 4 + kv:5 + kv], nrm[:, 8:16], axis=AX))
            gg = 0
            for u in range(NP):
                par = u % 2
                kv = u // 4
                for cc in range(8):
                    v.wait_ge(tsem, T_[f"qn{u},{cc}"])
                    v.reduce_max(nrm[:, 8 + cc:9 + cc], ps_n[:],
                                 axis=AX).then_inc(vsem, 1)
                v.wait_ge(vsem, V_[f"qnr{u},7"])
                sync(v.reduce_max(nrm[:, 0:1], nrm[:, 8:16], axis=AX))
                sync(v.tensor_tensor(out=nrm[:, 2:3], in0=nrm[:, 0:1],
                                     in1=nrm[:, 4 + kv:5 + kv], op=ALU.mult))
                v.tensor_copy(nrm[:, 18:19],
                              nrm[:, 8:9]).then_inc(vsem, 1)
                v.wait_ge(esem, E_[f"sqrt{u}"])
                sync(v.tensor_scalar(out=nrm[:, 6:7], in0=nrm[:, 3:4],
                                     scalar1=-0.125, scalar2=None,
                                     op0=ALU.mult))
                sync(v.tensor_copy(nrmb[:], nrm[:, 6:7]))
                v.tensor_copy(nrm[:, 18:19],
                              nrm[:, 8:9]).then_inc(vsem, 1)
                v.wait_ge(tsem, T_[f"bb{u}"])
                v.tensor_copy(biasb[:], ps_bb[:]).then_inc(vsem, 1)
                for h in range(2):
                    uh = u * 2 + h
                    if uh >= 2:
                        v.wait_ge(stv, 16 * (uh - 1))
                    for q in range(NQ):
                        for c in range(4 * q, 4 * q + 4):
                            gi = gg + c
                            di = gi // 2
                            hs = (gi % 2) * 512
                            v.wait_ge(esem, E_[f"exp{di}"])
                            delta = 128 * (4 * q - c)
                            v.tensor_tensor(
                                out=pst[di % 2][:, hs:hs + 512],
                                in0=pst[di % 2][:, hs:hs + 512],
                                in1=t8_sb[:, 384 + delta:896 + delta],
                                op=ALU.mult).then_inc(vsem, 1)
                        gg += 4 * q + 4
                        v.wait_ge(tsem, T_[f"p{gg - 1}"])
                        sync(v.reciprocal(denf[:], ps_o[64:65, :]))
                        sync(v.tensor_copy(denr[:], denf[:]))
                        v.tensor_copy(nrm[:, 18:19],
                                      nrm[:, 8:9]).then_inc(vsem, 1)
                        v.wait_ge(tsem, T_[f"bc{u},{h},{q}"])
                        v.tensor_copy(denb[:], ps_bc[:])
                        v.tensor_tensor(
                            out=oh_sb[uh % 2][:, q * 512:(q + 1) * 512],
                            in0=ps_o[0:64, :], in1=denb[:],
                            op=ALU.mult).then_inc(vsem, 1)

        @blk.tensor
        def _(t):
            t.wait_ge(cld, 112)
            t.wait_ge(msd, 1)
            t.wait_ge(vsem, V_["ksq"])
            for cc in range(16):
                if cc >= 1:
                    t.wait_ge(vsem, V_[f"knr{cc - 1}"])
                t.matmul(ps_n[:], onesc_sb[:],
                         sq_sb[:, cc * 512:(cc + 1) * 512],
                         start=True, stop=True).then_inc(tsem, 1)
            gg = 0
            for u in range(NP):
                par = u % 2
                kv = u // 4
                t.wait_ge(esem, E_[f"sq{u}"])
                for cc in range(8):
                    if cc == 0:
                        t.wait_ge(vsem, V_["knr15"] if u == 0 else
                                  V_[f"qnr{u - 1},7"])
                    else:
                        t.wait_ge(vsem, V_[f"qnr{u},{cc - 1}"])
                    t.matmul(ps_n[:], onesc_sb[:],
                             sq_sb[:, cc * 512:(cc + 1) * 512],
                             start=True, stop=True).then_inc(tsem, 1)
                t.wait_ge(vsem, V_[f"biasmm{u}"])
                t.matmul(ps_bb[:], ones1_sb[:], nrmb[:],
                         start=True, stop=True).then_inc(tsem, 1)
                for h in range(2):
                    qrow = slice(h * 64, (h + 1) * 64)
                    for q in range(NQ):
                        nch = 4 * q + 4
                        def _pvs(d0):
                            for c in (d0, d0 + 1):
                                gi = gg + c
                                di = gi // 2
                                hs = (gi % 2) * 512
                                if c == 0 and not (u == 0 and h == 0
                                                   and q == 0):
                                    pu, ph, pq = (u, h, q - 1) if q > 0 else (
                                        (u, h - 1, NQ - 1) if h > 0 else
                                        (u - 1, 1, NQ - 1))
                                    t.wait_ge(vsem,
                                              V_[f"evac{pu},{ph},{pq}"])
                                if c >= 4 * q:
                                    t.wait_ge(vsem, V_[f"mask{gi}"])
                                else:
                                    t.wait_ge(esem, E_[f"exp{di}"])
                                t.matmul(ps_o[:],
                                         va_sb[:, kv * 2176 + c * 68:
                                               kv * 2176 + c * 68 + 65],
                                         pst[di % 2][:, hs:hs + 512],
                                         start=(c == 0), stop=(c == nch - 1)
                                         ).then_inc(tsem, 1)

                        prev = None
                        for c0 in range(0, nch, 2):
                            for c in (c0, c0 + 1):
                                gi = gg + c
                                di = gi // 2
                                hs = (gi % 2) * 512
                                if gi % 2 == 0 and di >= 2:
                                    t.wait_ge(esem, E_[f"exp{di - 2}"])
                                t.matmul(ps_sT[di % 2][:, hs:hs + 512],
                                         kf_sb[qrow, kv * S + c * 128:
                                               kv * S + (c + 1) * 128],
                                         q_sb[par][qrow,
                                                   q * 512:(q + 1) * 512],
                                         start=True, stop=True
                                         ).then_inc(tsem, 1)
                            if prev is not None:
                                _pvs(prev)
                            prev = c0
                        _pvs(prev)
                        gg += nch
                        t.wait_ge(vsem, V_[f"den{u},{h},{q}"])
                        t.matmul(ps_bc[:], ones1_sb[0:1, 0:64], denr[:],
                                 start=True, stop=True
                                 ).then_inc(tsem, 1)

        @blk.sync
        def _(sy):
            for u in range(NP):
                for h in range(2):
                    uh = u * 2 + h
                    sy.wait_ge(vsem, V_[f"evac{u},{h},{NQ - 1}"])
                    sy.dma_start(out=oh[uh],
                                 in_=oh_sb[uh % 2][:]).then_inc(stv, 16)
            sy.wait_ge(stv, 16 * NP * 2)
    return nc


def run_l1attn(in_percore):
    nc = _cached("l1attn", build_l1attn)
    res = _run(nc, in_percore)
    return [r["oh"] for r in res]


def build_quant():
    nc = bass.Bass("TRN2", target_bir_lowering=False, detect_race_conditions=False)
    w1 = nc.dram_tensor("w1", [2048, 2048], F32, kind="ExternalInput").ap()
    w2 = nc.dram_tensor("w2", [1024, 2048], F32, kind="ExternalInput").ap()
    q1 = nc.dram_tensor("q1", [2048, 2048], BF16, kind="ExternalOutput").ap()
    q2 = nc.dram_tensor("q2", [1024, 2048], BF16, kind="ExternalOutput").ap()
    sel = np.zeros((128, 2), BFD)
    sel[0:64, 0] = 1.0
    sel[64:128, 1] = 1.0
    sel_d = nc.inline_tensor(sel, "sel_d").ap()
    selt_d = nc.inline_tensor(np.ascontiguousarray(sel.T), "selt_d").ap()
    tiles = [("w1", r, c) for r in range(16) for c in range(4)] + \
            [("w2", r, c) for r in range(8) for c in range(4)]
    NTL = len(tiles)

    with contextlib.ExitStack() as st:
        E = st.enter_context
        w_sb = [E(nc.sbuf_tensor(f"w_sb{p}", [128, 512], BF16)) for p in range(2)]
        wa = E(nc.sbuf_tensor("wa", [128, 512], BF16))
        sel_sb = E(nc.sbuf_tensor("sel_sb", [128, 2], BF16))
        selt_sb = E(nc.sbuf_tensor("selt_sb", [2, 128], BF16))
        sc = E(nc.sbuf_tensor("sc", [2, 512 * 4], F32))
        sc_bf = E(nc.sbuf_tensor("sc_bf", [2, 512], BF16))
        thr = E(nc.sbuf_tensor("thr", [128, 512], F32))
        mk = E(nc.sbuf_tensor("mk", [128, 512], BF16))
        sg = E(nc.sbuf_tensor("sg", [128, 512], BF16))
        tq = E(nc.sbuf_tensor("tq", [128, 512], BF16))
        wq = [E(nc.sbuf_tensor(f"wq{p}", [128, 512], BF16)) for p in range(2)]
        ps_s = E(nc.psum_tensor("ps_s", [2, 512], F32))
        ps_b2 = E(nc.psum_tensor("ps_b2", [128, 512], F32))
        cld = E(nc.semaphore("cld"))
        pld = [E(nc.semaphore(f"pld{p}")) for p in range(2)]
        tsem = E(nc.semaphore("tsem"))
        vsem = E(nc.semaphore("vsem"))
        esem = E(nc.semaphore("esem"))
        vs = E(nc.semaphore("vs"))
        stv = [E(nc.semaphore(f"stv{p}")) for p in range(2)]
        blk = E(nc.Block())

        def tsl(idx):
            name, r, c = tiles[idx]
            ap = w1 if name == "w1" else w2
            return ap[r * 128:(r + 1) * 128, c * 512:(c + 1) * 512]

        def osl(idx):
            name, r, c = tiles[idx]
            ap = q1 if name == "w1" else q2
            return ap[r * 128:(r + 1) * 128, c * 512:(c + 1) * 512]

        @blk.gpsimd
        def _(g):
            g.dma_start(out=sel_sb[:], in_=sel_d).then_inc(cld, 16)
            g.dma_start(out=selt_sb[:], in_=selt_d).then_inc(cld, 16)
            for i in range(NTL):
                if i >= 2:
                    g.wait_ge(stv[i % 2], 16 * (i // 2))
                g.dma_start(out=w_sb[i % 2][:], in_=tsl(i)).then_inc(pld[i % 2], 16)

        @blk.scalar
        def _(s):
            for i in range(NTL):
                s.wait_ge(pld[i % 2], 16 * (i // 2 + 1))
                if i >= 1:
                    s.wait_ge(vsem, 2 * i)
                s.activation(out=wa[:], in_=w_sb[i % 2][:],
                             func=AF.Abs).then_inc(esem, 1)
                s.activation(out=sg[:], in_=w_sb[i % 2][:],
                             func=AF.Sign).then_inc(esem, 1)

        @blk.tensor
        def _(t):
            t.wait_ge(cld, 32)
            for i in range(NTL):
                t.wait_ge(esem, 2 * i + 1)
                if i >= 1:
                    t.wait_ge(vsem, 2 * i)
                t.matmul(ps_s[:], sel_sb[:], wa[:],
                         start=True, stop=True).then_inc(tsem, 1)
                t.wait_ge(vsem, 2 * i + 1)
                t.matmul(ps_b2[:], selt_sb[:], sc_bf[:],
                         start=True, stop=True).then_inc(tsem, 1)

        @blk.vector
        def _(v):
            nvs = 0

            def sync(ins):
                nonlocal nvs
                nvs += 1
                ins.then_inc(vs, 1)
                v.wait_ge(vs, nvs)

            for i in range(NTL):
                v.wait_ge(tsem, 2 * i + 1)
                sync(v.tensor_scalar(out=sc[:, 0:512], in0=ps_s[:],
                                     scalar1=1.0 / 64, scalar2=None,
                                     op0=ALU.mult))
                v.tensor_scalar(out=sc_bf[:], in0=sc[:, 0:512],
                                scalar1=1e-8, scalar2=None,
                                op0=ALU.max).then_inc(vsem, 1)
                v.wait_ge(tsem, 2 * i + 2)
                v.tensor_scalar(out=thr[:], in0=ps_b2[:],
                                scalar1=0.5009765625, scalar2=None,
                                op0=ALU.mult)
                v.wait_ge(esem, 2 * i + 2)
                v.tensor_tensor(out=mk[:], in0=wa[:], in1=thr[:],
                                op=ALU.is_gt)
                v.tensor_tensor(out=tq[:], in0=mk[:], in1=sg[:], op=ALU.mult)
                if i >= 2:
                    v.wait_ge(stv[i % 2], 16 * (i // 2))
                v.tensor_tensor(out=wq[i % 2][:], in0=tq[:], in1=ps_b2[:],
                                op=ALU.mult).then_inc(vsem, 1)
        @blk.sync
        def _(sy):
            for i in range(NTL):
                sy.wait_ge(vsem, 2 * i + 2)
                sy.dma_start(out=osl(i), in_=wq[i % 2][:]).then_inc(stv[i % 2], 16)
            for p in range(2):
                sy.wait_ge(stv[p], 16 * ((NTL - 1 - p) // 2 + 1))
    return nc


def run_quant(w1_percore, w2_percore):
    nc = _cached("quant", build_quant)
    maps = [{"w1": w1_percore[c], "w2": w2_percore[c]} for c in range(NCORE)]
    res = _run(nc, maps)
    return [r["q1"] for r in res], [r["q2"] for r in res]


def build_ffn1():
    NK = D // 128
    NG = 64
    TC = T // 512
    OG = D // 128
    EPSF = float(np.finfo(np.float32).eps)
    nc = bass.Bass("TRN2", target_bir_lowering=False, detect_race_conditions=False)
    xnf = nc.dram_tensor("xnf", [D, T], BF16, kind="ExternalInput").ap()
    wgu = nc.dram_tensor("wgu", [D, 16384], BF16, kind="ExternalInput").ap()
    wpr = nc.dram_tensor("wpr", [8192, D], BF16, kind="ExternalInput").ap()
    p2 = nc.dram_tensor("p2", [D, T], BF16, kind="ExternalInput").ap()
    x2 = nc.dram_tensor("x2", [D, T], F32, kind="ExternalOutput").ap()
    ones1_d = nc.inline_tensor(np.ones((1, 128), BFD), "ones1_d").ap()
    onesc_d = nc.inline_tensor(np.ones((128, 1), BFD), "onesc_d").ap()

    def sch():
        Td, Vd, Ed = {}, {}, {}
        ct = cv = ce = 0
        for tc in range(TC):
            for gg in range(NG):
                gi = tc * NG + gg
                if gg >= 2:
                    ct += 1; Td[f"r{gi - 2}"] = ct
                ct += 1; Td[f"g{gi}"] = ct
                ce += 1; Ed[f"s{gi}"] = ce
                cv += 1; Vd[f"h{gi}"] = cv
            ct += 1; Td[f"r{tc * NG + NG - 2}"] = ct
            ct += 1; Td[f"r{tc * NG + NG - 1}"] = ct
            cv += 1; Vd[f"vsq{tc}"] = cv
            ce += 1; Ed[f"q{tc}"] = ce
            cv += 1; Vd[f"ib{tc}"] = cv
            ct += 1; Td[f"b{tc}"] = ct
            cv += 1; Vd[f"hn{tc}"] = cv
            for og in range(OG):
                oi = tc * OG + og
                ct += 1; Td[f"o{oi}"] = ct
                cv += 1; Vd[f"oev{oi}"] = cv
        return Td, Vd, Ed

    Td, Vd, Ed = sch()

    with contextlib.ExitStack() as st:
        E = st.enter_context
        xn_sb = E(nc.sbuf_tensor("xn_sb", [128, NK * T], BF16))
        wg_sb = [[E(nc.sbuf_tensor(f"wg_sb{p}_{j}", [128, NK * 128], BF16))
                  for j in range(2)] for p in range(2)]
        wp_sb = [E(nc.sbuf_tensor(f"wp_sb{p}", [128, 64 * 128], BF16))
                 for p in range(2)]
        hbuf = E(nc.sbuf_tensor("hbuf", [128, NG * 512], BF16))
        h2b = [E(nc.sbuf_tensor(f"h2b{p}", [128, 512], BF16)) for p in range(2)]
        sil = [E(nc.sbuf_tensor(f"sil{p}", [128, 512], F32)) for p in range(2)]
        rr0 = E(nc.sbuf_tensor("rr0", [1, 512], F32))
        rr1 = E(nc.sbuf_tensor("rr1", [1, 512], F32))
        rr2 = E(nc.sbuf_tensor("rr2", [1, 512], F32))
        ib_bf = E(nc.sbuf_tensor("ib_bf", [1, 512], BF16))
        ib_sb = E(nc.sbuf_tensor("ib_sb", [128, 512], BF16))
        p2_sb = [E(nc.sbuf_tensor(f"p2_sb{p}", [128, 512], BF16)) for p in range(2)]
        ob = [E(nc.sbuf_tensor(f"ob{p}", [128, 512], F32)) for p in range(2)]
        ones1_sb = E(nc.sbuf_tensor("ones1_sb", [1, 128], BF16))
        onesc_sb = E(nc.sbuf_tensor("onesc_sb", [128, 1], BF16))
        ps_g = [E(nc.psum_tensor(f"ps_g{p}", [128, 512], F32)) for p in range(2)]
        ps_u = [E(nc.psum_tensor(f"ps_u{p}", [128, 512], F32)) for p in range(2)]
        ps_o = [E(nc.psum_tensor(f"ps_o{p}", [128, 512], F32)) for p in range(2)]
        ps_r = E(nc.psum_tensor("ps_r", [1, 512], F32))
        ps_ib = E(nc.psum_tensor("ps_ib", [128, 512], F32))
        cld = E(nc.semaphore("cld"))
        wld = [E(nc.semaphore(f"wld{p}")) for p in range(2)]
        wpld = [E(nc.semaphore(f"wpld{p}")) for p in range(2)]
        rld = [E(nc.semaphore(f"rld{p}")) for p in range(2)]
        tsem = E(nc.semaphore("tsem"))
        vsem = E(nc.semaphore("vsem"))
        esem = E(nc.semaphore("esem"))
        vs = E(nc.semaphore("vs"))
        stv = [E(nc.semaphore(f"stv{p}")) for p in range(2)]
        blk = E(nc.Block())

        @blk.gpsimd
        def _(g):
            g.dma_start(out=ones1_sb[:], in_=ones1_d).then_inc(cld, 16)
            g.dma_start(out=onesc_sb[:], in_=onesc_d).then_inc(cld, 16)
            g.dma_start(out=xn_sb.ap().rearrange("p (a t) -> p a t", t=T),
                        in_=xnf.rearrange("(a p) t -> p a t", p=128)
                        ).then_inc(cld, 16)
            for tc in range(TC):
                for gg in range(NG):
                    gi = tc * NG + gg
                    if gi >= 2:
                        g.wait_ge(tsem, Td[f"g{gi - 2}"])
                    g.dma_start(
                        out=wg_sb[gi % 2][0].ap().rearrange(
                            "p (a f) -> p a f", f=128),
                        in_=wgu[:, gg * 128:(gg + 1) * 128].rearrange(
                            "(a p) f -> p a f", p=128)
                    ).then_inc(wld[gi % 2], 16)
                    g.dma_start(
                        out=wg_sb[gi % 2][1].ap().rearrange(
                            "p (a f) -> p a f", f=128),
                        in_=wgu[:, 8192 + gg * 128:8192 + (gg + 1) * 128
                                ].rearrange("(a p) f -> p a f", p=128)
                    ).then_inc(wld[gi % 2], 16)
                for og in range(OG):
                    oi = tc * OG + og
                    if oi >= 2:
                        g.wait_ge(tsem, Td[f"o{oi - 2}"])
                    g.dma_start(
                        out=wp_sb[oi % 2].ap().rearrange(
                            "p (a f) -> p a f", f=128),
                        in_=wpr[:, og * 128:(og + 1) * 128].rearrange(
                            "(a p) f -> p a f", p=128)
                    ).then_inc(wpld[oi % 2], 16)

        @blk.sync
        def _(sy):
            for tc in range(TC):
                for og in range(OG):
                    oi = tc * OG + og
                    if oi >= 2:
                        sy.wait_ge(vsem, Vd[f"oev{oi - 1}"])
                    sy.dma_start(out=p2_sb[oi % 2][:],
                                 in_=p2[og * 128:(og + 1) * 128,
                                        tc * 512:(tc + 1) * 512]
                                 ).then_inc(rld[oi % 2], 16)
                    sy.wait_ge(vsem, Vd[f"oev{oi}"])
                    sy.dma_start(out=x2[og * 128:(og + 1) * 128,
                                        tc * 512:(tc + 1) * 512],
                                 in_=ob[oi % 2][:]).then_inc(stv[oi % 2], 16)
            for p in range(2):
                sy.wait_ge(stv[p], 16 * ((TC * OG - 1 - p) // 2 + 1))

        @blk.scalar
        def _(s):
            for tc in range(TC):
                for gg in range(NG):
                    gi = tc * NG + gg
                    s.wait_ge(tsem, Td[f"g{gi}"])
                    if gi >= 2:
                        s.wait_ge(vsem, Vd[f"h{gi - 2}"])
                    s.activation(out=sil[gi % 2][:], in_=ps_g[gi % 2][:],
                                 func=AF.Silu).then_inc(esem, 1)
                s.wait_ge(vsem, Vd[f"vsq{tc}"])
                s.activation(out=rr1[:], in_=rr0[:],
                             func=AF.Sqrt).then_inc(esem, 1)

        @blk.vector
        def _(v):
            nvs = 0

            def sync(ins):
                nonlocal nvs
                nvs += 1
                ins.then_inc(vs, 1)
                v.wait_ge(vs, nvs)

            for tc in range(TC):
                if tc >= 1:
                    v.wait_ge(tsem, Td[f"o{tc * OG - 1}"])
                for gg in range(NG):
                    gi = tc * NG + gg
                    v.wait_ge(esem, Ed[f"s{gi}"])
                    if gi >= 2:
                        v.wait_ge(tsem, Td[f"r{gi - 2}"])
                    v.tensor_tensor(out=hbuf[:, gg * 512:(gg + 1) * 512],
                                    in0=sil[gi % 2][:], in1=ps_u[gi % 2][:],
                                    op=ALU.mult)
                    v.tensor_tensor(out=h2b[gi % 2][:],
                                    in0=hbuf[:, gg * 512:(gg + 1) * 512],
                                    in1=hbuf[:, gg * 512:(gg + 1) * 512],
                                    op=ALU.mult).then_inc(vsem, 1)
                v.wait_ge(tsem, Td[f"r{tc * NG + NG - 1}"])
                sync(v.tensor_scalar(out=rr0[:], in0=ps_r[:],
                                     scalar1=1.0 / 8192, scalar2=None,
                                     op0=ALU.mult))
                v.tensor_scalar(out=rr0[:], in0=rr0[:], scalar1=EPSF,
                                scalar2=None, op0=ALU.add).then_inc(vsem, 1)
                v.wait_ge(esem, Ed[f"q{tc}"])
                sync(v.reciprocal(rr2[:], rr1[:]))
                v.tensor_copy(ib_bf[:], rr2[:]).then_inc(vsem, 1)
                v.wait_ge(tsem, Td[f"b{tc}"])
                sync(v.tensor_copy(ib_sb[:], ps_ib[:]))
                for gg in range(NG):
                    ins = v.tensor_tensor(
                        out=hbuf[:, gg * 512:(gg + 1) * 512],
                        in0=hbuf[:, gg * 512:(gg + 1) * 512],
                        in1=ib_sb[:], op=ALU.mult)
                ins.then_inc(vsem, 1)
                for og in range(OG):
                    oi = tc * OG + og
                    v.wait_ge(tsem, Td[f"o{oi}"])
                    v.wait_ge(rld[oi % 2], 16 * (oi // 2 + 1))
                    if oi >= 2:
                        v.wait_ge(stv[oi % 2], 16 * (oi // 2))
                    v.tensor_tensor(out=ob[oi % 2][:], in0=ps_o[oi % 2][:],
                                    in1=p2_sb[oi % 2][:],
                                    op=ALU.add).then_inc(vsem, 1)

        @blk.tensor
        def _(t):
            t.wait_ge(cld, 48)
            for tc in range(TC):
                for gg in range(NG):
                    gi = tc * NG + gg
                    if gg >= 2:
                        t.wait_ge(vsem, Vd[f"h{gi - 2}"])
                        t.matmul(ps_r[:], onesc_sb[:], h2b[gi % 2][:],
                                 start=(gg - 2 == 0), stop=False
                                 ).then_inc(tsem, 1)
                    t.wait_ge(wld[gi % 2], 32 * (gi // 2 + 1))
                    if gi >= 2:
                        t.wait_ge(vsem, Vd[f"h{gi - 2}"])
                    for k in range(NK):
                        t.matmul(ps_g[gi % 2][:],
                                 wg_sb[gi % 2][0][:, k * 128:(k + 1) * 128],
                                 xn_sb[:, k * T + tc * 512:
                                       k * T + (tc + 1) * 512],
                                 start=(k == 0), stop=(k == NK - 1))
                    for k in range(NK):
                        ins = t.matmul(ps_u[gi % 2][:],
                                       wg_sb[gi % 2][1][:, k * 128:(k + 1) * 128],
                                       xn_sb[:, k * T + tc * 512:
                                             k * T + (tc + 1) * 512],
                                       start=(k == 0), stop=(k == NK - 1))
                    ins.then_inc(tsem, 1)
                for gg in (NG - 2, NG - 1):
                    gi = tc * NG + gg
                    t.wait_ge(vsem, Vd[f"h{gi}"])
                    t.matmul(ps_r[:], onesc_sb[:], h2b[gi % 2][:],
                             start=False, stop=(gg == NG - 1)
                             ).then_inc(tsem, 1)
                t.wait_ge(vsem, Vd[f"ib{tc}"])
                t.matmul(ps_ib[:], ones1_sb[:], ib_bf[:],
                         start=True, stop=True).then_inc(tsem, 1)
                for og in range(OG):
                    oi = tc * OG + og
                    t.wait_ge(wpld[oi % 2], 16 * (oi // 2 + 1))
                    t.wait_ge(vsem, Vd[f"hn{tc}"])
                    if oi >= 2:
                        t.wait_ge(vsem, Vd[f"oev{oi - 2}"])
                    for g2 in range(NG):
                        ins = t.matmul(ps_o[oi % 2][:],
                                       wp_sb[oi % 2][:, g2 * 128:(g2 + 1) * 128],
                                       hbuf[:, g2 * 512:(g2 + 1) * 512],
                                       start=(g2 == 0), stop=(g2 == NG - 1))
                    ins.then_inc(tsem, 1)
    return nc



def run_ffn1(xnf_percore, wgu, wpr, p2_percore):
    nc = _cached("ffn1", build_ffn1)
    maps = [{"xnf": xnf_percore[c], "wgu": wgu, "wpr": wpr,
             "p2": p2_percore[c]} for c in range(NCORE)]
    res = _run(nc, maps)
    return [r["x2"] for r in res]



def shard_tokens(x):
    out = []
    for c in range(NCORE):
        b, h = c // 2, c % 2
        out.append(np.ascontiguousarray(x[b, h * T:(h + 1) * T]))
    return out


def unshard_tokens(parts, d2):
    x = np.empty((4, 4096, d2), parts[0].dtype)
    for c in range(NCORE):
        b, h = c // 2, c % 2
        x[b, h * T:(h + 1) * T] = parts[c]
    return x


def tr(a):
    return np.ascontiguousarray(np.swapaxes(a, -1, -2))


def _rope_tables():
    inv = 1.0 / 10000.0 ** (np.arange(0, 64, 2, dtype=np.float64) / 64)
    sign = np.where((np.arange(128) % 64) < 32, -1.0, 1.0)
    cos_half, sin_half = [], []
    for hh in range(2):
        pos = np.arange(hh * T, (hh + 1) * T, dtype=np.float64)
        ang = pos[None, :] * inv[np.arange(128) % 32][:, None]
        cos_half.append(np.cos(ang).astype(np.float32))
        sin_half.append((sign[:, None] * np.sin(ang)).astype(np.float32))
    return cos_half, sin_half


def _tiled_tok(x, w):
    return np.ascontiguousarray(
        x.reshape(32, 128, w).transpose(1, 0, 2).reshape(128, 32 * w))


def _prep_l0attn(qkv0, memories, mnorm, gate):
    mnflat = np.ascontiguousarray(mnorm.reshape(1, 8192), np.float32)
    maps = []
    for c in range(NCORE):
        b = c // 2
        e, o = 2 * b, 2 * b + 1
        heads = range((c % 2) * 8, (c % 2) * 8 + 8)
        qf = np.empty((8, 128, 4096), BFD)
        kf = np.empty((8, 128, 4096), BFD)
        kt = np.empty((8, 128, 4096), BFD)
        vt = np.empty((8, 128, 4096), BFD)
        for j, h in enumerate(heads):
            rows = slice(h * 128, (h + 1) * 128)
            qf[j, :, 0:T] = qkv0[e][rows]
            qf[j, :, T:2 * T] = qkv0[o][rows]
            kf[j, :, 0:T] = qkv0[e][2048 + h * 128:2048 + (h + 1) * 128]
            kf[j, :, T:2 * T] = qkv0[o][2048 + h * 128:2048 + (h + 1) * 128]
            kt[j] = _tiled_tok(np.ascontiguousarray(kf[j].T), 128)
            vrows = slice(4096 + h * 128, 4096 + (h + 1) * 128)
            vfull = np.concatenate([qkv0[e][vrows], qkv0[o][vrows]], axis=1)
            vt[j] = _tiled_tok(np.ascontiguousarray(vfull.T), 128)
        memp = np.zeros((8, 128, 520), np.float32)
        mnsp = np.zeros((8, 128, 4), np.float32)
        gp = np.zeros((8, 1, 1), np.float32)
        for j, h in enumerate(heads):
            for m in range(4):
                memp[j, :, m * 130:m * 130 + 128] = memories[m, h]
                memp[j, :, m * 130 + 128] = mnorm[m, h]
                mnsp[j, :, m] = mnorm[m, h]
            gp[j, 0, 0] = gate[h]
        maps.append(dict(qf=qf, kf=kf, kt=kt, vt=vt, mem=memp, mns=mnsp,
                         gatev=gp, mnflat=mnflat))
    return maps


def _prep_l1attn(roped, v1):
    maps = []
    for c in range(NCORE):
        b, hh = c // 2, c % 2
        e, o = 2 * b, 2 * b + 1
        qh = np.empty((8, 128, 4096), BFD)
        for u in range(8):
            for j in range(2):
                h = hh * 16 + 2 * u + j
                rows = slice(h * 64, (h + 1) * 64)
                qh[u, j * 64:(j + 1) * 64, 0:T] = roped[e][rows]
                qh[u, j * 64:(j + 1) * 64, T:2 * T] = roped[o][rows]
        kf2 = np.empty((2, 128, 4096), BFD)
        vt2 = np.empty((2, 128, 2048), BFD)
        for j in range(2):
            kvh = hh * 2 + j
            rows = slice(2048 + kvh * 64, 2048 + (kvh + 1) * 64)
            kf2[j, 0:64, 0:T] = roped[e][rows]
            kf2[j, 0:64, T:2 * T] = roped[o][rows]
            kf2[j, 64:128] = kf2[j, 0:64]
            vrows = slice(kvh * 64, (kvh + 1) * 64)
            vfull = np.concatenate([v1[e][vrows], v1[o][vrows]], axis=1)
            vt2[j] = _tiled_tok(np.ascontiguousarray(vfull.T), 64)
        maps.append(dict(qh=qh, kf2=kf2, vt2=vt2))
    return maps


def kernel(L0_attn_hidden_states, L0_attn_memories, L0_attn_memory_norms,
           L0_attn_w_q, L0_attn_w_k, L0_attn_w_v, L0_attn_w_o, L0_attn_gate,
           L0_ffn_router_w, L0_ffn_down_w, L0_ffn_up_w,
           L1_attn_w_qkv, L1_attn_w_o, L1_ffn_w_gate_up, L1_ffn_w_proj):
    x = np.ascontiguousarray(np.asarray(L0_attn_hidden_states, np.float32))
    xs = shard_tokens(x)

    xn, _ = run_ln0(xs)
    xn_feat = [tr(a) for a in xn]
    wqkv0 = np.concatenate(
        [np.ascontiguousarray(np.asarray(w, np.float32).T)
         for w in (L0_attn_w_q, L0_attn_w_k, L0_attn_w_v)], axis=1)
    qkv0 = run_gemm("qkv0", [wqkv0], [xn_feat], None, BFD)

    maps0 = _prep_l0attn(qkv0, np.asarray(L0_attn_memories, np.float32),
                         np.asarray(L0_attn_memory_norms, np.float32),
                         np.asarray(L0_attn_gate, np.float32))
    outm = run_l0attn(maps0)

    rwt = np.ascontiguousarray(np.asarray(L0_ffn_router_w, np.float32).T)
    dwt = np.ascontiguousarray(
        np.transpose(np.asarray(L0_ffn_down_w, np.float32),
                     (2, 0, 1)).reshape(2048, 128))
    asct = run_l0ffna(xn_feat, rwt, dwt)

    mixf = []
    for c in range(NCORE):
        b, hh = c // 2, c % 2
        mf = np.empty((2048, T), BFD)
        for src_core, hbase in ((2 * b, 0), (2 * b + 1, 8)):
            om = np.asarray(outm[src_core]).reshape(8, 128, 32, 128)
            for j in range(8):
                h = hbase + j
                blk2 = om[j].transpose(1, 0, 2).reshape(4096, 128)
                mf[h * 128:(h + 1) * 128] = blk2[hh * T:(hh + 1) * T].T
        mixf.append(np.ascontiguousarray(mf))
    wot = np.ascontiguousarray(np.asarray(L0_attn_w_o, np.float32).T)
    upt = np.ascontiguousarray(
        np.transpose(np.asarray(L0_ffn_up_w, np.float32),
                     (0, 2, 1)).reshape(128, 2048))
    xres = [np.ascontiguousarray(x[c // 2, (c % 2) * T:(c % 2 + 1) * T].T)
            for c in range(NCORE)]
    x1 = run_gemm("oproj0", [wot, upt], [mixf, asct], xres, BFD)

    x1tok = [tr(a) for a in x1]
    xn1 = run_ln1(x1tok)
    xn1f = [tr(a) for a in xn1]
    wqkvt = np.ascontiguousarray(np.asarray(L1_attn_w_qkv, np.float32).T)
    qkv1 = run_gemm("qkv1", [wqkvt], [xn1f], None, BFD)
    cos_half, sin_half = _rope_tables()
    qk = [np.ascontiguousarray(qkv1[c][0:2304]) for c in range(NCORE)]
    roped = run_rope(qk, cos_half, sin_half)
    v1 = [np.ascontiguousarray(qkv1[c][2304:2560]) for c in range(NCORE)]

    maps1 = _prep_l1attn(roped, v1)
    oh = run_l1attn(maps1)

    ahf = []
    for c in range(NCORE):
        b, hh = c // 2, c % 2
        af = np.empty((2048, T), BFD)
        tok = slice(hh * T, (hh + 1) * T)
        for src_core, hbase in ((2 * b, 0), (2 * b + 1, 16)):
            blk2 = np.asarray(oh[src_core]).reshape(16, 64, 4096)
            for uh in range(16):
                h = hbase + uh
                af[h * 64:(h + 1) * 64] = blk2[uh][:, tok]
        ahf.append(np.ascontiguousarray(af))
    wot1 = np.ascontiguousarray(np.asarray(L1_attn_w_o, np.float32).T)
    p2 = run_gemm("oproj1", [wot1], [ahf], x1, BFD)

    wguT = np.ascontiguousarray(np.asarray(L1_ffn_w_gate_up, np.float32).T)
    wprT = np.ascontiguousarray(np.asarray(L1_ffn_w_proj, np.float32).T)
    w1s = [np.ascontiguousarray(wguT[:, c * 2048:(c + 1) * 2048])
           for c in range(NCORE)]
    w2s = [np.ascontiguousarray(wprT[c * 1024:(c + 1) * 1024])
           for c in range(NCORE)]
    q1, q2 = run_quant(w1s, w2s)
    wgu_q = np.ascontiguousarray(np.concatenate(
        [np.asarray(q) for q in q1], axis=1))
    wpr_q = np.ascontiguousarray(np.concatenate(
        [np.asarray(q) for q in q2], axis=0))

    x2 = run_ffn1(xn1f, wgu_q, wpr_q, p2)
    out = unshard_tokens([np.ascontiguousarray(tr(np.asarray(a, np.float32)))
                          for a in x2], 2048)
    return out


# version 07 (final)
# speedup vs baseline: 1.2956x; 1.0060x over previous
import contextlib
import numpy as np
import ml_dtypes
import concourse.bass as bass
import concourse.mybir as mybir
from concourse.bass_utils import run_bass_kernel_spmd

F32 = mybir.dt.float32
BF16 = mybir.dt.bfloat16
AF = mybir.ActivationFunctionType
ALU = mybir.AluOpType
AX = mybir.AxisListType.X
NCORE = 8
T = 2048
D = 2048
BFD = ml_dtypes.bfloat16

_prog_cache = {}


def _cached(key, fn):
    if key not in _prog_cache:
        _prog_cache[key] = fn()
    return _prog_cache[key]


def _run(nc, in_maps):
    res = run_bass_kernel_spmd(nc, in_maps, core_ids=list(range(NCORE)))
    return res.results


def build_ln(with_sum):
    nc = bass.Bass("TRN2", target_bir_lowering=False, detect_race_conditions=False)
    x = nc.dram_tensor("x", [T, D], BF16 if not with_sum else F32,
                       kind="ExternalInput").ap()
    xn = nc.dram_tensor("xn", [T, D], BF16, kind="ExternalOutput").ap()
    xs = nc.dram_tensor("xsum", [1, D], F32, kind="ExternalOutput").ap()
    ones_d = nc.inline_tensor(np.ones((128, 1), BFD), "ones_d").ap()
    eps_d = nc.inline_tensor(np.full((128, 1), 1e-5, np.float32), "eps_d").ap()
    NT = T // 128
    with contextlib.ExitStack() as st:
        E = st.enter_context
        xdt = F32 if with_sum else BF16
        xsb = [E(nc.sbuf_tensor(f"xsb{i}", [128, D], xdt)) for i in range(2)]
        sq = E(nc.sbuf_tensor("sq", [128, D], F32))
        xnb = [E(nc.sbuf_tensor(f"xnb{i}", [128, D], BF16)) for i in range(2)]
        stat = E(nc.sbuf_tensor("stat", [128, 8], F32))
        ones_sb = E(nc.sbuf_tensor("ones_sb", [128, 1], BF16))
        eps_sb = E(nc.sbuf_tensor("eps_sb", [128, 1], F32))
        xs_sb = E(nc.sbuf_tensor("xs_sb", [1, D], F32))
        ps = E(nc.psum_tensor("ps", [1, D], F32))
        olds = E(nc.semaphore("olds"))
        vs = E(nc.semaphore("vs"))
        ld2 = [E(nc.semaphore(f"ld{p}")) for p in range(2)]
        ev = E(nc.semaphore("ev"))
        sdone = E(nc.semaphore("sdone"))
        tmm = E(nc.semaphore("tmm"))
        stv = E(nc.semaphore("stv"))
        blk = E(nc.Block())

        @blk.sync
        def _(sy):
            sy.dma_start(out=ones_sb[:], in_=ones_d).then_inc(olds, 16)
            sy.dma_start(out=eps_sb[:], in_=eps_d).then_inc(olds, 16)
            for i in range(NT):
                if i >= 2:
                    sy.wait_ge(ev, i - 1)
                    sy.dma_start(out=xn[(i - 2) * 128:(i - 1) * 128, :],
                                 in_=xnb[(i - 2) % 2][:]).then_inc(stv, 16)
                sy.dma_start(out=xsb[i % 2][:], in_=x[i * 128:(i + 1) * 128, :]
                             ).then_inc(ld2[i % 2], 16)
            sy.wait_ge(ev, NT - 1)
            sy.dma_start(out=xn[(NT - 2) * 128:(NT - 1) * 128, :],
                         in_=xnb[(NT - 2) % 2][:]).then_inc(stv, 16)
            sy.wait_ge(ev, NT)
            sy.dma_start(out=xn[(NT - 1) * 128:NT * 128, :],
                         in_=xnb[(NT - 1) % 2][:]).then_inc(stv, 16)
            if with_sum:
                sy.wait_ge(ev, NT + 1)
                sy.dma_start(out=xs, in_=xs_sb[:]).then_inc(stv, 16)
            sy.wait_ge(stv, 16 * (NT + (1 if with_sum else 0)))

        @blk.vector
        def _(v):
            nvs = 0
            for i in range(NT):
                v.wait_ge(ld2[i % 2], 16 * (i // 2 + 1))
                if i >= 2:
                    if with_sum:
                        v.wait_ge(tmm, i - 1)
                    v.wait_ge(stv, 16 * (i - 1))
                b = xsb[i % 2]
                o = xnb[i % 2]
                v.reduce_sum(stat[:, 0:1], b[:], axis=AX)
                v.tensor_tensor(out=sq[:], in0=b[:], in1=b[:], op=ALU.mult)
                v.reduce_sum(stat[:, 2:3], sq[:], axis=AX).then_inc(vs, 1)
                nvs += 1
                v.wait_ge(vs, nvs)
                v.tensor_scalar(out=stat[:, 1:2], in0=stat[:, 0:1],
                                scalar1=1.0 / D, scalar2=None, op0=ALU.mult)
                v.tensor_scalar(out=stat[:, 3:4], in0=stat[:, 2:3],
                                scalar1=1.0 / D, scalar2=None,
                                op0=ALU.mult).then_inc(vs, 1)
                nvs += 1
                v.wait_ge(vs, nvs)
                v.tensor_tensor(out=stat[:, 4:5], in0=stat[:, 1:2],
                                in1=stat[:, 1:2], op=ALU.mult).then_inc(vs, 1)
                nvs += 1
                v.wait_ge(vs, nvs)
                v.tensor_tensor(out=stat[:, 5:6], in0=stat[:, 3:4],
                                in1=stat[:, 4:5],
                                op=ALU.subtract).then_inc(sdone, 1)
                v.wait_ge(sdone, 2 * (i + 1))
                v.reciprocal(stat[:, 7:8], stat[:, 6:7]).then_inc(vs, 1)
                nvs += 1
                v.wait_ge(vs, nvs)
                v.tensor_scalar(out=sq[:], in0=b[:], scalar1=stat[:, 1:2],
                                scalar2=None, op0=ALU.subtract)
                v.tensor_scalar(out=o[:], in0=sq[:], scalar1=stat[:, 7:8],
                                scalar2=None, op0=ALU.mult).then_inc(ev, 1)
            if with_sum:
                v.wait_ge(tmm, NT)
                v.tensor_copy(xs_sb[:], ps[:]).then_inc(ev, 1)

        @blk.scalar
        def _(s):
            s.wait_ge(olds, 32)
            for i in range(NT):
                s.wait_ge(sdone, 2 * i + 1)
                s.activation(out=stat[:, 6:7], in_=stat[:, 5:6],
                             func=AF.Sqrt, bias=eps_sb[:, 0:1],
                             scale=1.0).then_inc(sdone, 1)

        if with_sum:
            @blk.tensor
            def _(t):
                t.wait_ge(olds, 16)
                for i in range(NT):
                    t.wait_ge(ev, i + 1)
                    for c in range(D // 512):
                        ins = t.matmul(ps[0:1, c * 512:(c + 1) * 512], ones_sb[:],
                                       xnb[i % 2][:, c * 512:(c + 1) * 512],
                                       start=(i == 0), stop=(i == NT - 1))
                    ins.then_inc(tmm, 1)
    return nc


def run_ln0(x_tok_percore):
    nc = _cached("ln0", lambda: build_ln(True))
    maps = [{"x": xc} for xc in x_tok_percore]
    res = _run(nc, maps)
    return ([r["xn"] for r in res], [r["xsum"] for r in res])


def run_ln1(x1_tok_percore):
    nc = _cached("ln1", lambda: build_ln(False))
    maps = [{"x": xc} for xc in x1_tok_percore]
    res = _run(nc, maps)
    return [r["xn"] for r in res]


def build_gemm(key, k_list, fout, x_dtypes, w_dtypes, res_dtype, out_dtype,
               t_total=T, nc=None, pfx=""):
    if nc is None:
        nc = bass.Bass("TRN2", target_bir_lowering=False,
                       detect_race_conditions=False)
    nj = len(k_list)
    ws = [nc.dram_tensor(f"{pfx}w{j}", [k_list[j], fout], w_dtypes[j],
                         kind="ExternalInput").ap() for j in range(nj)]
    xs = [nc.dram_tensor(f"{pfx}x{j}", [k_list[j], t_total], x_dtypes[j],
                         kind="ExternalInput").ap() for j in range(nj)]
    res_in = (nc.dram_tensor(pfx + "res", [fout, t_total], res_dtype,
                             kind="ExternalInput").ap()
              if res_dtype is not None else None)
    out = nc.dram_tensor(pfx + "out", [fout, t_total], out_dtype,
                         kind="ExternalOutput").ap()
    OG = fout // 128
    TC = t_total // 512
    NKT = [k // 128 for k in k_list]
    nk_tot = sum(NKT)
    with contextlib.ExitStack() as st:
        E = st.enter_context
        x_sb = [E(nc.sbuf_tensor(f"{pfx}x_sb{j}", [128, NKT[j] * t_total], BF16))
                for j in range(nj)]
        w_sb = [[E(nc.sbuf_tensor(f"{pfx}w_sb{j}_{p}", [128, NKT[j] * 128], BF16))
                 for j in range(nj)] for p in range(2)]
        r_sb = [E(nc.sbuf_tensor(f"{pfx}r_sb{p}", [128, 512],
                                 res_dtype if res_dtype else BF16))
                for p in range(2)] if res_dtype is not None else None
        o_sb = [E(nc.sbuf_tensor(f"{pfx}o_sb{p}", [128, 512], out_dtype))
                for p in range(4)]
        ps = [E(nc.psum_tensor(f"{pfx}ps{p}", [128, 512], F32)) for p in range(4)]
        xld = E(nc.semaphore(pfx + "xld"))
        wld2 = [E(nc.semaphore(f"{pfx}wld{p}")) for p in range(2)]
        rld = E(nc.semaphore(pfx + "rld"))
        mmd = E(nc.semaphore(pfx + "mmd"))
        evd = E(nc.semaphore(pfx + "evd"))
        stv4 = [E(nc.semaphore(f"{pfx}stv{p}")) for p in range(4)]
        blk = E(nc.Block())

        @blk.gpsimd
        def _(g):
            for j in range(nj):
                for kt in range(NKT[j]):
                    g.dma_start(out=x_sb[j][:, kt * t_total:(kt + 1) * t_total],
                                in_=xs[j][kt * 128:(kt + 1) * 128, :]
                                ).then_inc(xld, 16)
            for og in range(OG):
                if og >= 2:
                    g.wait_ge(mmd, (og - 1) * TC)
                for j in range(nj):
                    g.dma_start(
                        out=w_sb[og % 2][j].ap().rearrange(
                            "p (a f) -> p a f", f=128),
                        in_=ws[j][:, og * 128:(og + 1) * 128].rearrange(
                            "(a p) f -> p a f", p=128)
                    ).then_inc(wld2[og % 2], 16)

        @blk.sync
        def _(sy):
            for og in range(OG):
                for tc in range(TC):
                    gidx = og * TC + tc
                    if res_in is not None:
                        if gidx >= 2:
                            sy.wait_ge(evd, gidx - 1)
                        sy.dma_start(out=r_sb[gidx % 2][:],
                                     in_=res_in[og * 128:(og + 1) * 128,
                                                tc * 512:(tc + 1) * 512]
                                     ).then_inc(rld, 16)
                    sy.wait_ge(evd, gidx + 1)
                    sy.dma_start(out=out[og * 128:(og + 1) * 128,
                                         tc * 512:(tc + 1) * 512],
                                 in_=o_sb[gidx % 4][:]).then_inc(stv4[gidx % 4], 16)
            ngr = OG * TC
            for p in range(4):
                sy.wait_ge(stv4[p], 16 * ((ngr - 1 - p) // 4 + 1))

        @blk.tensor
        def _(t):
            t.wait_ge(xld, 16 * nk_tot)
            for og in range(OG):
                t.wait_ge(wld2[og % 2], 16 * nj * (og // 2 + 1))
                for tc in range(TC):
                    gidx = og * TC + tc
                    if gidx >= 4:
                        t.wait_ge(evd, gidx - 3)
                    p = ps[gidx % 4]
                    n = 0
                    for j in range(nj):
                        for kt in range(NKT[j]):
                            n += 1
                            ins = t.matmul(
                                p[:],
                                w_sb[og % 2][j][:, kt * 128:(kt + 1) * 128],
                                x_sb[j][:, kt * t_total + tc * 512:
                                        kt * t_total + (tc + 1) * 512],
                                start=(n == 1), stop=(n == nk_tot))
                    ins.then_inc(mmd, 1)

        @blk.vector
        def _(v):
            for og in range(OG):
                for tc in range(TC):
                    gidx = og * TC + tc
                    v.wait_ge(mmd, gidx + 1)
                    if gidx >= 4:
                        v.wait_ge(stv4[gidx % 4], 16 * ((gidx - 4) // 4 + 1))
                    if res_in is not None:
                        v.wait_ge(rld, 16 * (gidx + 1))
                        v.tensor_tensor(out=o_sb[gidx % 4][:],
                                        in0=ps[gidx % 4][:],
                                        in1=r_sb[gidx % 2][:],
                                        op=ALU.add).then_inc(evd, 1)
                    else:
                        v.tensor_copy(o_sb[gidx % 4][:],
                                      ps[gidx % 4][:]).then_inc(evd, 1)
    return nc, out


def run_gemm(key, w_full_list, x_percore_list, res_percore, out_dtype_np):
    k_list = [w.shape[0] for w in w_full_list]
    fout = w_full_list[0].shape[1]
    w_dt = [F32 if w.dtype == np.float32 else BF16 for w in w_full_list]
    x_dt = [BF16] * len(w_full_list)
    res_dt = None
    if res_percore is not None:
        res_dt = F32 if res_percore[0].dtype == np.float32 else BF16
    out_dt = F32 if out_dtype_np == np.float32 else BF16
    nc = _cached(("gemm", key), lambda: build_gemm(
        key, tuple(k_list), fout, tuple(x_dt), tuple(w_dt), res_dt, out_dt)[0])
    maps = []
    for c in range(NCORE):
        m = {}
        for j, w in enumerate(w_full_list):
            m[f"w{j}"] = w
            m[f"x{j}"] = x_percore_list[j][c]
        if res_percore is not None:
            m["res"] = res_percore[c]
        maps.append(m)
    res = _run(nc, maps)
    return [r["out"] for r in res]



def build_l0attn():
    S = 4096
    NCH = 32
    NP = 8
    SCALE = 128.0 ** -0.5
    nc = bass.Bass("TRN2", target_bir_lowering=False, detect_race_conditions=False)
    qf = nc.dram_tensor("qf", [NP, 128, S], BF16, kind="ExternalInput").ap()
    kf = nc.dram_tensor("kf", [NP, 128, S], BF16, kind="ExternalInput").ap()
    kt = nc.dram_tensor("kt", [NP, 128, S], BF16, kind="ExternalInput").ap()
    vt = nc.dram_tensor("vt", [NP, 128, S], BF16, kind="ExternalInput").ap()
    mem = nc.dram_tensor("mem", [NP, 128, 520], F32, kind="ExternalInput").ap()
    mns = nc.dram_tensor("mns", [NP, 128, 4], F32, kind="ExternalInput").ap()
    gatev = nc.dram_tensor("gatev", [NP, 1, 1], F32, kind="ExternalInput").ap()
    mnflat = nc.dram_tensor("mnflat", [1, 8192], F32, kind="ExternalInput").ap()
    outm = nc.dram_tensor("outm", [NP, 128, S], BF16, kind="ExternalOutput").ap()
    mask_np = (np.arange(128)[None, :] >= np.arange(128)[:, None]).astype(BFD)
    mask_d = nc.inline_tensor(mask_np, "mask_d").ap()
    ones1_d = nc.inline_tensor(np.ones((1, 128), BFD), "ones1_d").ap()

    VN = 4 + 3 * NCH
    TN = 2 + 2 * NCH
    EN = 10

    def v_relq(p): return VN * p + 1
    def v_b6(p): return VN * p + 2
    def v_memw(p): return VN * p + 3
    def v_state0(p): return VN * p + 4
    def v_scorem(p, c): return VN * p + 5 + 3 * c
    def v_chunkev(p, c): return VN * p + 7 + 3 * c
    def v_end(p): return VN * (p + 1)

    def t_rel(p): return TN * p + 1
    def t_b6(p): return TN * p + 2
    def t_score(p, c): return TN * p + 3 + 2 * c
    def t_chunk(p, c): return TN * p + 4 + 2 * c

    def e_elu_v(p, i): return EN * p + 2 * i + 1
    def e_elu_s(p, i): return EN * p + 2 * i + 2
    def e_rel_v(p): return EN * p + 7
    def e_rel_s(p): return EN * p + 8
    def e_gate_v(p): return EN * p + 9
    def e_gate_s(p): return EN * p + 10

    with contextlib.ExitStack() as st:
        E = st.enter_context
        q_sb = [E(nc.sbuf_tensor(f"q_sb{p}", [128, S], BF16)) for p in range(2)]
        kf_sb = [E(nc.sbuf_tensor(f"kf_sb{p}", [128, S], BF16)) for p in range(2)]
        kt_sb = [E(nc.sbuf_tensor(f"kt_sb{p}", [128, S], BF16)) for p in range(2)]
        va_sb = [E(nc.sbuf_tensor(f"va_sb{p}", [128, NCH * 132], BF16))
                 for p in range(2)]
        mem_sb = [E(nc.sbuf_tensor(f"mem_sb{p}", [128, 520], F32)) for p in range(2)]
        mns_sb = [E(nc.sbuf_tensor(f"mns_sb{p}", [128, 4], BF16)) for p in range(2)]
        gt_sb = [E(nc.sbuf_tensor(f"gt_sb{p}", [1, 1], F32)) for p in range(2)]
        sgq = E(nc.sbuf_tensor("sgq", [128, S], BF16))
        kpf = E(nc.sbuf_tensor("kpf", [128, S], BF16))
        kpt = E(nc.sbuf_tensor("kpt", [128, S], BF16))
        tmp1 = E(nc.sbuf_tensor("tmp1", [128, S], BF16))
        tmp2 = E(nc.sbuf_tensor("tmp2", [128, S], BF16))
        memw = E(nc.sbuf_tensor("memw", [128, 520], BF16))
        ob = E(nc.sbuf_tensor("ob", [128, S], BF16))
        mask_sb = E(nc.sbuf_tensor("mask_sb", [128, 128], BF16))
        ones1_sb = E(nc.sbuf_tensor("ones1_sb", [1, 128], BF16))
        mnf_sb = E(nc.sbuf_tensor("mnf_sb", [1, 8192], F32))
        emt = E(nc.sbuf_tensor("emt", [1, 16], F32))
        relb = E(nc.sbuf_tensor("relb", [1, 32], F32))
        wg6 = E(nc.sbuf_tensor("wg6", [1, 6], BF16))
        wgb = E(nc.sbuf_tensor("wgb", [128, 6], F32))
        qsum = E(nc.sbuf_tensor("qsum", [128, 1], F32))
        qsum_bf = E(nc.sbuf_tensor("qsum_bf", [128, 1], BF16))
        S_f = [E(nc.sbuf_tensor(f"S_f{p}", [128, 129], F32)) for p in range(2)]
        S_bf = E(nc.sbuf_tensor("S_bf", [128, 129], BF16))
        pstm = E(nc.sbuf_tensor("pstm", [128, 128], BF16))
        den5 = E(nc.sbuf_tensor("den5", [128, 12], F32))
        loc = E(nc.sbuf_tensor("loc", [128, 128], F32))
        acc = [E(nc.sbuf_tensor(f"acc{p}", [128, 128], F32)) for p in range(2)]
        ps_rel = E(nc.psum_tensor("ps_rel", [1, 8], F32))
        ps_b6 = E(nc.psum_tensor("ps_b6", [128, 6], F32))
        ps_s = E(nc.psum_tensor("ps_s", [128, 128], F32))
        ps_out = E(nc.psum_tensor("ps_out", [128, 129], F32))
        ps_am1 = E(nc.psum_tensor("ps_am1", [128, 260], F32))
        ps_am2 = E(nc.psum_tensor("ps_am2", [128, 260], F32))
        ps_kv = E(nc.psum_tensor("ps_kv", [128, 129], F32))
        cld = E(nc.semaphore("cld"))
        msd = E(nc.semaphore("msd"))
        pld = [E(nc.semaphore(f"pld{p}")) for p in range(2)]
        tsem = E(nc.semaphore("tsem"))
        vsem = E(nc.semaphore("vsem"))
        esem = E(nc.semaphore("esem"))
        vs = E(nc.semaphore("vs"))
        stv = E(nc.semaphore("stv"))
        blk = E(nc.Block())

        @blk.gpsimd
        def _(g):
            g.dma_start(out=mask_sb[:], in_=mask_d).then_inc(cld, 16)
            g.dma_start(out=ones1_sb[:], in_=ones1_d).then_inc(cld, 16)
            g.dma_start(out=mnf_sb[:], in_=mnflat).then_inc(cld, 16)
            g.memset(va_sb[0][:], 1.0)
            g.memset(va_sb[1][:], 1.0)
            g.memset(S_f[0][:], 0.0)
            g.memset(S_bf[:], 0.0).then_inc(msd, 1)
            for p in range(NP):
                par = p % 2
                if p >= 2:
                    g.wait_ge(vsem, v_end(p - 2))
                    g.wait_ge(tsem, t_chunk(p - 2, NCH - 1))
                g.dma_start(out=q_sb[par][:], in_=qf[p]).then_inc(pld[par], 16)
                g.dma_start(out=kf_sb[par][:], in_=kf[p]).then_inc(pld[par], 16)
                g.dma_start(out=kt_sb[par][:], in_=kt[p]).then_inc(pld[par], 16)
                g.dma_start(
                    out=va_sb[par].ap().rearrange("p (c w) -> p c w", w=132
                                                  )[:, :, 0:128],
                    in_=vt[p].rearrange("p (c w) -> p c w", w=128)
                ).then_inc(pld[par], 16)
                g.dma_start(out=mem_sb[par][:], in_=mem[p]).then_inc(pld[par], 16)
                g.dma_start(out=mns_sb[par][:], in_=mns[p]).then_inc(pld[par], 16)
                g.dma_start(out=gt_sb[par][:], in_=gatev[p]).then_inc(pld[par], 16)

        @blk.vector
        def _(v):
            nvs = 0

            def sync(ins):
                nonlocal nvs
                nvs += 1
                ins.then_inc(vs, 1)
                v.wait_ge(vs, nvs)

            v.wait_ge(cld, 48)
            sync(v.tensor_reduce(emt[:, 0:4], mnf_sb.ap().rearrange(
                "p (m d) -> p m d", d=2048), axis=AX, op=ALU.add))
            sync(v.tensor_scalar(out=emt[:, 4:8], in0=emt[:, 0:4],
                                 scalar1=1e-6, scalar2=None, op0=ALU.is_lt))
            sync(v.tensor_reduce(emt[:, 8:9], emt[:, 4:8], axis=AX, op=ALU.add))
            sync(v.tensor_scalar(out=emt[:, 9:10], in0=emt[:, 8:9],
                                 scalar1=3.5, scalar2=None, op0=ALU.is_lt))

            for p in range(NP):
                par = p % 2
                v.wait_ge(pld[par], 16 * 7 * (p // 2 + 1))
                if p >= 1:
                    v.wait_ge(stv, 16 * p)
                for i, (srcb, dstb) in enumerate(
                        ((q_sb[par], sgq), (kf_sb[par], kpf), (kt_sb[par], kpt))):
                    v.tensor_scalar(out=tmp1[:], in0=srcb[:], scalar1=0.0,
                                    scalar2=None, op0=ALU.min).then_inc(esem, 1)
                    v.wait_ge(esem, e_elu_s(p, i))
                    v.scalar_tensor_tensor(out=dstb[:], in0=srcb[:], scalar=0.0,
                                           in1=tmp2[:], op0=ALU.max, op1=ALU.add)
                sync(v.reduce_sum(qsum[:], q_sb[par][:], axis=AX))
                v.tensor_copy(qsum_bf[:], qsum[:]).then_inc(vsem, 1)
                v.wait_ge(tsem, t_rel(p))
                sync(v.tensor_scalar(out=relb[:, 0:4], in0=ps_rel[0:1, 0:4],
                                     scalar1=SCALE / 4096.0, scalar2=None,
                                     op0=ALU.mult))
                sync(v.scalar_tensor_tensor(out=relb[:, 4:8], in0=emt[:, 4:8],
                                            scalar=-1e38, in1=relb[:, 0:4],
                                            op0=ALU.mult, op1=ALU.add))
                v.reduce_max(relb[:, 8:9], relb[:, 4:8], axis=AX,
                             negate=True).then_inc(esem, 1)
                v.wait_ge(esem, e_rel_s(p))
                sync(v.reduce_sum(relb[:, 14:15], relb[:, 10:14], axis=AX))
                sync(v.reciprocal(relb[:, 15:16], relb[:, 14:15]))
                sync(v.tensor_scalar(out=relb[:, 16:20], in0=relb[:, 10:14],
                                     scalar1=relb[:, 15:16], scalar2=None,
                                     op0=ALU.mult))
                v.tensor_scalar(out=relb[:, 24:28], in0=relb[:, 16:20],
                                scalar1=emt[:, 9:10], scalar2=None,
                                op0=ALU.mult).then_inc(esem, 1)
                v.wait_ge(esem, e_gate_s(p))
                sync(v.tensor_scalar(out=relb[:, 29:30], in0=relb[:, 28:29],
                                     scalar1=-1.0, scalar2=1.0, op0=ALU.mult,
                                     op1=ALU.add))
                v.tensor_copy(wg6[:], relb[:, 24:30]).then_inc(vsem, 1)
                v.wait_ge(tsem, t_b6(p))
                sync(v.tensor_copy(wgb[:], ps_b6[:]))
                mv = memw.ap().rearrange("p (m w) -> p m w", w=130)
                ms = mem_sb[par].ap().rearrange("p (m w) -> p m w", w=130)
                for m in range(4):
                    v.tensor_scalar(out=memw[:, m * 130:m * 130 + 128],
                                    in0=mem_sb[par][:, m * 130:m * 130 + 128],
                                    scalar1=wgb[:, m:m + 1], scalar2=None,
                                    op0=ALU.mult)
                v.tensor_copy(mv[:, :, 128:129],
                              ms[:, :, 128:129]).then_inc(vsem, 1)
                if p == 0:
                    v.wait_ge(msd, 1)
                v.tensor_scalar(out=S_f[0][:], in0=S_f[0][:], scalar1=0.0,
                                scalar2=None, op0=ALU.mult)
                v.tensor_scalar(out=S_bf[:], in0=S_f[0][:], scalar1=0.0,
                                scalar2=None, op0=ALU.mult).then_inc(vsem, 1)
                for c in range(NCH):
                    cpar = c % 2
                    v.wait_ge(tsem, t_score(p, c))
                    v.tensor_tensor(out=pstm[:], in0=ps_s[:], in1=mask_sb[:],
                                    op=ALU.mult).then_inc(vsem, 1)
                    v.wait_ge(tsem, t_chunk(p, c))
                    v.tensor_tensor(out=S_f[1 - cpar][:], in0=S_f[cpar][:],
                                    in1=ps_kv[:], op=ALU.add)
                    v.tensor_copy(S_bf[:], S_f[1 - cpar][:]).then_inc(vsem, 1)
                    v.tensor_copy(den5[:, 0:1], ps_out[:, 128:129])
                    v.tensor_copy(den5[:, 1:2], ps_am1[:, 128:129])
                    v.tensor_copy(den5[:, 2:3], ps_am1[:, 258:259])
                    v.tensor_copy(den5[:, 3:4], ps_am2[:, 128:129])
                    sync(v.tensor_copy(den5[:, 4:5], ps_am2[:, 258:259]))
                    sync(v.tensor_scalar(out=den5[:, 6:11], in0=den5[:, 0:5],
                                         scalar1=1e-6, scalar2=None,
                                         op0=ALU.max))
                    sync(v.reciprocal(den5[:, 0:5], den5[:, 6:11]))
                    v.tensor_scalar(out=loc[:], in0=ps_out[:, 0:128],
                                    scalar1=den5[:, 0:1], scalar2=None,
                                    op0=ALU.mult)
                    v.tensor_scalar(out=acc[0][:], in0=ps_am1[:, 0:128],
                                    scalar1=den5[:, 1:2], scalar2=None,
                                    op0=ALU.mult)
                    v.scalar_tensor_tensor(out=acc[1][:], in0=ps_am1[:, 130:258],
                                           scalar=den5[:, 2:3], in1=acc[0][:],
                                           op0=ALU.mult, op1=ALU.add)
                    v.scalar_tensor_tensor(out=acc[0][:], in0=ps_am2[:, 0:128],
                                           scalar=den5[:, 3:4], in1=acc[1][:],
                                           op0=ALU.mult, op1=ALU.add)
                    v.scalar_tensor_tensor(out=acc[1][:], in0=ps_am2[:, 130:258],
                                           scalar=den5[:, 4:5], in1=acc[0][:],
                                           op0=ALU.mult, op1=ALU.add)
                    v.tensor_scalar(out=loc[:], in0=loc[:], scalar1=wgb[:, 5:6],
                                    scalar2=None, op0=ALU.mult)
                    v.scalar_tensor_tensor(
                        out=ob[:, c * 128:(c + 1) * 128], in0=acc[1][:],
                        scalar=wgb[:, 4:5], in1=loc[:], op0=ALU.mult,
                        op1=ALU.add).then_inc(vsem, 1)

        @blk.scalar
        def _(s):
            for p in range(NP):
                for i in range(3):
                    s.wait_ge(esem, e_elu_v(p, i))
                    s.activation(out=tmp2[:], in_=tmp1[:],
                                 func=AF.Exp).then_inc(esem, 1)
                s.wait_ge(esem, e_rel_v(p))
                s.activation(out=relb[:, 10:14], in_=relb[:, 4:8], func=AF.Exp,
                             bias=relb[:, 8:9], scale=1.0).then_inc(esem, 1)
                s.wait_ge(esem, e_gate_v(p))
                s.activation(out=relb[:, 28:29], in_=gt_sb[p % 2][:],
                             func=AF.Sigmoid).then_inc(esem, 1)

        @blk.tensor
        def _(t):
            t.wait_ge(cld, 48)
            t.wait_ge(msd, 1)
            for p in range(NP):
                par = p % 2
                t.wait_ge(vsem, v_relq(p))
                t.matmul(ps_rel[0:1, 0:4], qsum_bf[:], mns_sb[par][:],
                         start=True, stop=True).then_inc(tsem, 1)
                t.wait_ge(vsem, v_b6(p))
                t.matmul(ps_b6[:], ones1_sb[:], wg6[:],
                         start=True, stop=True).then_inc(tsem, 1)
                t.wait_ge(vsem, v_memw(p))
                for c in range(NCH):
                    cs = slice(c * 128, (c + 1) * 128)
                    vsl = slice(c * 132, c * 132 + 129)
                    if c == 0:
                        t.wait_ge(vsem, v_state0(p))
                    else:
                        t.wait_ge(vsem, v_chunkev(p, c - 1))
                    t.matmul(ps_s[:], kpf[:, cs], sgq[:, cs],
                             start=True, stop=True).then_inc(tsem, 1)
                    t.matmul(ps_out[:], sgq[:, cs], S_bf[:],
                             start=True, stop=False)
                    t.matmul(ps_am1[:, 0:129], sgq[:, cs], memw[:, 0:129],
                             start=True, stop=True)
                    t.matmul(ps_am1[:, 130:259], sgq[:, cs], memw[:, 130:259],
                             start=True, stop=True)
                    t.matmul(ps_am2[:, 0:129], sgq[:, cs], memw[:, 260:389],
                             start=True, stop=True)
                    t.matmul(ps_am2[:, 130:259], sgq[:, cs], memw[:, 390:519],
                             start=True, stop=True)
                    t.wait_ge(vsem, v_scorem(p, c))
                    t.matmul(ps_out[:], pstm[:], va_sb[par][:, vsl],
                             start=False, stop=True)
                    t.matmul(ps_kv[:], kpt[:, cs], va_sb[par][:, vsl],
                             start=True, stop=True).then_inc(tsem, 1)

        @blk.sync
        def _(sy):
            for p in range(NP):
                sy.wait_ge(vsem, v_end(p))
                sy.dma_start(out=outm[p], in_=ob[:]).then_inc(stv, 16)
            sy.wait_ge(stv, 16 * NP)
    return nc


def run_l0attn(in_percore):
    nc = _cached("l0attn", build_l0attn)
    res = _run(nc, in_percore)
    return [r["outm"] for r in res]


def build_l0ffna(nc=None, pfx=""):
    NK = D // 128
    NT = T // 128
    if nc is None:
        nc = bass.Bass("TRN2", target_bir_lowering=False,
                       detect_race_conditions=False)
    xnf = nc.dram_tensor(pfx + "xnf", [D, T], BF16, kind="ExternalInput").ap()
    rwt = nc.dram_tensor(pfx + "rwt", [D, 8], F32, kind="ExternalInput").ap()
    dwt = nc.dram_tensor(pfx + "dwt", [D, 128], F32, kind="ExternalInput").ap()
    aout_d = nc.dram_tensor(pfx + "aout", [128, T], BF16, kind="ExternalOutput").ap()
    ident_d = nc.inline_tensor(np.eye(128, dtype=BFD), pfx + "ident_d").ap()

    with contextlib.ExitStack() as st:
        E = st.enter_context
        xn_sb = E(nc.sbuf_tensor(pfx + "xn_sb", [128, NK * T], BF16))
        rw_sb = E(nc.sbuf_tensor(pfx + "rw_sb", [128, NK * 8], BF16))
        dw_sb = E(nc.sbuf_tensor(pfx + "dw_sb", [128, NK * 128], BF16))
        ident = E(nc.sbuf_tensor(pfx + "ident", [128, 128], BF16))
        p8 = E(nc.sbuf_tensor(pfx + "p8", [128, 8], F32))
        w8 = E(nc.sbuf_tensor(pfx + "w8", [128, 8], F32))
        w2b = E(nc.sbuf_tensor(pfx + "w2b", [128, 8], F32))
        g8 = E(nc.sbuf_tensor(pfx + "g8", [128, 8], F32))
        stt_ = E(nc.sbuf_tensor(pfx + "stt_", [128, 8], F32))
        a_sb = E(nc.sbuf_tensor(pfx + "a_sb", [128, 128], F32))
        asc = E(nc.sbuf_tensor(pfx + "asc", [128, 128], BF16))
        aout = E(nc.sbuf_tensor(pfx + "aout_sb", [128, T], BF16))
        ps_r = E(nc.psum_tensor(pfx + "ps_r", [128, 8], F32))
        ps_h = E(nc.psum_tensor(pfx + "ps_h", [128, 128], F32))
        ps_t = E(nc.psum_tensor(pfx + "ps_t", [128, 128], BF16))
        ld = E(nc.semaphore(pfx + "ld"))
        tsem = E(nc.semaphore(pfx + "tsem"))
        vsem = E(nc.semaphore(pfx + "vsem"))
        esem = E(nc.semaphore(pfx + "esem"))
        vs = E(nc.semaphore(pfx + "vs"))
        stv = E(nc.semaphore(pfx + "stv"))
        blk = E(nc.Block())

        @blk.gpsimd
        def _(g):
            g.dma_start(out=xn_sb.ap().rearrange("p (a t) -> p a t", t=T),
                        in_=xnf.rearrange("(a p) t -> p a t", p=128)
                        ).then_inc(ld, 16)
            g.dma_start(out=rw_sb.ap().rearrange("p (a f) -> p a f", f=8),
                        in_=rwt.rearrange("(a p) f -> p a f", p=128)
                        ).then_inc(ld, 16)
            g.dma_start(out=dw_sb.ap().rearrange("p (a f) -> p a f", f=128),
                        in_=dwt.rearrange("(a p) f -> p a f", p=128)
                        ).then_inc(ld, 16)
            g.dma_start(out=ident[:], in_=ident_d).then_inc(ld, 16)

        @blk.tensor
        def _(t):
            t.wait_ge(ld, 64)
            for i in range(NT):
                if i >= 1:
                    t.wait_ge(vsem, 2 * i)
                for k in range(NK):
                    t.matmul(ps_r[:], xn_sb[:, k * T + i * 128:k * T + (i + 1) * 128],
                             rw_sb[:, k * 8:(k + 1) * 8],
                             start=(k == 0), stop=(k == NK - 1))
                for k in range(NK):
                    ins = t.matmul(ps_h[:],
                                   xn_sb[:, k * T + i * 128:k * T + (i + 1) * 128],
                                   dw_sb[:, k * 128:(k + 1) * 128],
                                   start=(k == 0), stop=(k == NK - 1))
                ins.then_inc(tsem, 1)
                t.wait_ge(vsem, 2 * i + 1)
                t.transpose(ps_t[:], asc[:], ident[:]).then_inc(tsem, 1)

        @blk.vector
        def _(v):
            nvs = 0

            def sync(ins):
                nonlocal nvs
                nvs += 1
                ins.then_inc(vs, 1)
                v.wait_ge(vs, nvs)

            for i in range(NT):
                v.wait_ge(tsem, 2 * i + 1)
                v.reduce_max(stt_[:, 0:1], ps_r[:], axis=AX,
                             negate=True).then_inc(esem, 1)
                v.wait_ge(esem, 3 * i + 2)
                sync(v.reduce_sum(stt_[:, 1:2], p8[:], axis=AX))
                sync(v.reciprocal(stt_[:, 2:3], stt_[:, 1:2]))
                sync(v.tensor_scalar(out=w8[:], in0=p8[:], scalar1=stt_[:, 2:3],
                                     scalar2=None, op0=ALU.mult))
                sync(v.reduce_max(stt_[:, 3:4], w8[:], axis=AX))
                sync(v.tensor_scalar(out=w2b[:], in0=w8[:], scalar1=stt_[:, 3:4],
                                     scalar2=None, op0=ALU.is_ge))
                sync(v.scalar_tensor_tensor(out=w2b[:], in0=w2b[:], scalar=-1e38,
                                            in1=w8[:], op0=ALU.mult, op1=ALU.add))
                sync(v.reduce_max(stt_[:, 4:5], w2b[:], axis=AX))
                sync(v.tensor_scalar(out=g8[:], in0=w8[:], scalar1=stt_[:, 4:5],
                                     scalar2=None, op0=ALU.is_ge))
                sync(v.tensor_tensor(out=g8[:], in0=g8[:], in1=w8[:],
                                     op=ALU.mult))
                v.wait_ge(esem, 3 * i + 3)
                for e in range(8):
                    ins = v.tensor_scalar(out=asc[:, e * 16:(e + 1) * 16],
                                          in0=a_sb[:, e * 16:(e + 1) * 16],
                                          scalar1=g8[:, e:e + 1], scalar2=None,
                                          op0=ALU.mult)
                ins.then_inc(vsem, 1)
                v.wait_ge(tsem, 2 * i + 2)
                if i >= 1:
                    v.wait_ge(stv, 0)
                v.tensor_copy(aout[:, i * 128:(i + 1) * 128],
                              ps_t[:]).then_inc(vsem, 1)

        @blk.scalar
        def _(s):
            for i in range(NT):
                s.wait_ge(esem, 3 * i + 1)
                s.activation(out=p8[:], in_=ps_r[:], func=AF.Exp,
                             bias=stt_[:, 0:1], scale=1.0).then_inc(esem, 1)
                s.activation(out=a_sb[:], in_=ps_h[:],
                             func=AF.Silu).then_inc(esem, 1)

        @blk.sync
        def _(sy):
            sy.wait_ge(vsem, 2 * NT)
            sy.dma_start(out=aout_d, in_=aout[:]).then_inc(stv, 16)
            sy.wait_ge(stv, 16)
    return nc


def run_l0ffna(xnf_percore, rwt, dwt):
    nc = _cached("l0ffna", lambda: build_l0ffna())
    maps = [{"xnf": xnf_percore[c], "rwt": rwt, "dwt": dwt} for c in range(NCORE)]
    res = _run(nc, maps)
    return [r["aout"] for r in res]



def build_rope(nc=None, pfx="", qkv_src=None):
    NG = 18
    TC = T // 512
    if nc is None:
        nc = bass.Bass("TRN2", target_bir_lowering=False,
                       detect_race_conditions=False)
    if qkv_src is None:
        qkv = nc.dram_tensor(pfx + "qkv", [2304, T], BF16,
                             kind="ExternalInput").ap()
    else:
        qkv = qkv_src
    cosr = nc.dram_tensor(pfx + "cosr", [128, T], F32, kind="ExternalInput").ap()
    sins = nc.dram_tensor(pfx + "sins", [128, T], F32, kind="ExternalInput").ap()
    out = nc.dram_tensor(pfx + "out", [2304, T], BF16, kind="ExternalOutput").ap()
    pm = np.zeros((128, 128), BFD)
    for p in range(128):
        q = (p // 64) * 64 + ((p % 64) + 32) % 64
        pm[q, p] = 1.0
    perm_d = nc.inline_tensor(pm, pfx + "perm_d").ap()
    with contextlib.ExitStack() as st:
        E = st.enter_context
        xb = [E(nc.sbuf_tensor(f"{pfx}xb{p}", [128, T], BF16)) for p in range(2)]
        yb = [E(nc.sbuf_tensor(f"{pfx}yb{p}", [128, T], BF16)) for p in range(2)]
        cs = E(nc.sbuf_tensor(pfx + "cs", [128, T], F32))
        sn = E(nc.sbuf_tensor(pfx + "sn", [128, T], F32))
        pmat = E(nc.sbuf_tensor(pfx + "pmat", [128, 128], BF16))
        t1 = E(nc.sbuf_tensor(pfx + "t1", [128, 512], F32))
        t2 = E(nc.sbuf_tensor(pfx + "t2", [128, 512], F32))
        ps_sw = [E(nc.psum_tensor(f"{pfx}ps_sw{p}", [128, 512], F32)) for p in range(2)]
        cld = E(nc.semaphore(pfx + "cld"))
        ld2 = [E(nc.semaphore(f"{pfx}ld{p}")) for p in range(2)]
        ev = E(nc.semaphore(pfx + "ev"))
        tsem = E(nc.semaphore(pfx + "tsem"))
        stv = E(nc.semaphore(pfx + "stv"))
        blk = E(nc.Block())

        @blk.sync
        def _(sy):
            sy.dma_start(out=cs[:], in_=cosr).then_inc(cld, 16)
            sy.dma_start(out=sn[:], in_=sins).then_inc(cld, 16)
            sy.dma_start(out=pmat[:], in_=perm_d).then_inc(cld, 16)
            for g in range(NG):
                if g >= 2:
                    sy.wait_ge(ev, (g - 1) * TC)
                    sy.dma_start(out=out[(g - 2) * 128:(g - 1) * 128, :],
                                 in_=yb[g % 2][:]).then_inc(stv, 16)
                sy.dma_start(out=xb[g % 2][:],
                             in_=qkv[g * 128:(g + 1) * 128, :]
                             ).then_inc(ld2[g % 2], 16)
            sy.wait_ge(ev, (NG - 1) * TC)
            sy.dma_start(out=out[(NG - 2) * 128:(NG - 1) * 128, :],
                         in_=yb[0][:]).then_inc(stv, 16)
            sy.wait_ge(ev, NG * TC)
            sy.dma_start(out=out[(NG - 1) * 128:NG * 128, :],
                         in_=yb[1][:]).then_inc(stv, 16)
            sy.wait_ge(stv, 16 * NG)

        @blk.tensor
        def _(t):
            t.wait_ge(cld, 48)
            for g in range(NG):
                t.wait_ge(ld2[g % 2], 16 * (g // 2 + 1))
                for cc in range(TC):
                    gi = g * TC + cc
                    if gi >= 2:
                        t.wait_ge(ev, gi - 1)
                    t.matmul(ps_sw[gi % 2][:], pmat[:],
                             xb[g % 2][:, cc * 512:(cc + 1) * 512],
                             start=True, stop=True).then_inc(tsem, 1)

        @blk.vector
        def _(v):
            v.wait_ge(cld, 48)
            for g in range(NG):
                v.wait_ge(ld2[g % 2], 16 * (g // 2 + 1))
                if g >= 2:
                    v.wait_ge(stv, 16 * (g - 1))
                for cc in range(TC):
                    gi = g * TC + cc
                    sl = slice(cc * 512, (cc + 1) * 512)
                    v.wait_ge(tsem, gi + 1)
                    v.tensor_tensor(out=t1[:], in0=xb[g % 2][:, sl],
                                    in1=cs[:, sl], op=ALU.mult)
                    v.tensor_tensor(out=t2[:], in0=ps_sw[gi % 2][:],
                                    in1=sn[:, sl], op=ALU.mult)
                    v.scalar_tensor_tensor(out=yb[g % 2][:, sl], in0=t1[:],
                                           scalar=1.0, in1=t2[:],
                                           op0=ALU.mult,
                                           op1=ALU.add).then_inc(ev, 1)
    return nc


def run_rope(qkv_percore, cos_half, sins_half):
    nc = _cached("rope", lambda: build_rope())
    maps = [{"qkv": qkv_percore[c], "cosr": cos_half[c % 2],
             "sins": sins_half[c % 2]} for c in range(NCORE)]
    res = _run(nc, maps)
    return [r["out"] for r in res]


def _l1_sched():
    NP, NQ = 8, 8
    T_ = {}
    V_ = {}
    E_ = {}
    ct = cv = ce = 0
    cv += 1; V_["ksq"] = cv
    for cc in range(16):
        ct += 1; T_[f"kn{cc}"] = ct
        cv += 1; V_[f"knr{cc}"] = cv
    g = 0
    for u in range(NP):
        ce += 1; E_[f"sq{u}"] = ce
        for cc in range(8):
            ct += 1; T_[f"qn{u},{cc}"] = ct
            cv += 1; V_[f"qnr{u},{cc}"] = cv
        cv += 1; V_[f"biasgo{u}"] = cv
        ce += 1; E_[f"sqrt{u}"] = ce
        cv += 1; V_[f"biasmm{u}"] = cv
        ct += 1; T_[f"bb{u}"] = ct
        cv += 1; V_[f"biasrdy{u}"] = cv
        for h in range(2):
            for q in range(NQ):
                nch = 4 * q + 4
                prev = None
                for c0 in range(0, nch, 2):
                    for c in (c0, c0 + 1):
                        ct += 1; T_[f"s{g + c}"] = ct
                    ce += 1; E_[f"exp{(g + c0) // 2}"] = ce
                    for c in (c0, c0 + 1):
                        if c >= 4 * q:
                            cv += 1; V_[f"mask{g + c}"] = cv
                    if prev is not None:
                        for c in (prev, prev + 1):
                            ct += 1; T_[f"p{g + c}"] = ct
                    prev = c0
                for c in (prev, prev + 1):
                    ct += 1; T_[f"p{g + c}"] = ct
                g += nch
                cv += 1; V_[f"den{u},{h},{q}"] = cv
                ct += 1; T_[f"bc{u},{h},{q}"] = ct
                cv += 1; V_[f"evac{u},{h},{q}"] = cv
        V_[f"end{u}"] = cv
    return T_, V_, E_


def build_l1attn():
    S = 4096
    NQ = 8
    NP = 8
    T_, V_, E_ = _l1_sched()
    nc = bass.Bass("TRN2", target_bir_lowering=False, detect_race_conditions=False)
    qh = nc.dram_tensor("qh", [NP, 128, S], BF16, kind="ExternalInput").ap()
    kf2 = nc.dram_tensor("kf2", [2, 128, S], BF16, kind="ExternalInput").ap()
    vt2 = nc.dram_tensor("vt2", [2, 128, 2048], BF16, kind="ExternalInput").ap()
    oh = nc.dram_tensor("oh", [NP * 2, 64, S], BF16, kind="ExternalOutput").ap()
    stair = (np.arange(128)[:, None] <= np.arange(896)[None, :] - 384).astype(BFD)
    t8_d = nc.inline_tensor(stair, "t8_d").ap()
    ones1_d = nc.inline_tensor(np.ones((1, 128), BFD), "ones1_d").ap()
    onesc_d = nc.inline_tensor(np.ones((128, 1), BFD), "onesc_d").ap()

    with contextlib.ExitStack() as st:
        E = st.enter_context
        q_sb = [E(nc.sbuf_tensor(f"q_sb{p}", [128, S], BF16)) for p in range(2)]
        oh_sb = [E(nc.sbuf_tensor(f"oh_sb{p}", [64, S], BF16)) for p in range(2)]
        kf_sb = E(nc.sbuf_tensor("kf_sb", [128, 2 * S], BF16))
        va_sb = E(nc.sbuf_tensor("va_sb", [128, 2 * 2176], BF16))
        sq_sb = E(nc.sbuf_tensor("sq_sb", [128, 2 * S], BF16))
        t8_sb = E(nc.sbuf_tensor("t8_sb", [128, 896], BF16))
        ones1_sb = E(nc.sbuf_tensor("ones1_sb", [1, 128], BF16))
        onesc_sb = E(nc.sbuf_tensor("onesc_sb", [128, 1], BF16))
        pst = [E(nc.sbuf_tensor(f"pst{p}", [128, 1024], BF16)) for p in range(2)]
        nrm = E(nc.sbuf_tensor("nrm", [1, 20], F32))
        nrmb = E(nc.sbuf_tensor("nrmb", [1, 1], BF16))
        biasb = E(nc.sbuf_tensor("biasb", [128, 1], F32))
        denf = E(nc.sbuf_tensor("denf", [1, 512], F32))
        denr = E(nc.sbuf_tensor("denr", [1, 512], BF16))
        denb = E(nc.sbuf_tensor("denb", [64, 512], BF16))
        ps_sT = [E(nc.psum_tensor(f"ps_sT{p}", [128, 1024], F32)) for p in range(2)]
        ps_o = E(nc.psum_tensor("ps_o", [65, 512], F32))
        ps_bc = E(nc.psum_tensor("ps_bc", [64, 512], F32))
        ps_n = E(nc.psum_tensor("ps_n", [1, 512], F32))
        ps_bb = E(nc.psum_tensor("ps_bb", [128, 1], F32))
        cld = E(nc.semaphore("cld"))
        msd = E(nc.semaphore("msd"))
        pld = [E(nc.semaphore(f"pld{p}")) for p in range(2)]
        tsem = E(nc.semaphore("tsem"))
        vsem = E(nc.semaphore("vsem"))
        esem = E(nc.semaphore("esem"))
        vs = E(nc.semaphore("vs"))
        stv = E(nc.semaphore("stv"))
        blk = E(nc.Block())

        @blk.gpsimd
        def _(g):
            g.dma_start(out=t8_sb[:], in_=t8_d).then_inc(cld, 16)
            g.dma_start(out=ones1_sb[:], in_=ones1_d).then_inc(cld, 16)
            g.dma_start(out=onesc_sb[:], in_=onesc_d).then_inc(cld, 16)
            g.dma_start(out=kf_sb[:, 0:S], in_=kf2[0]).then_inc(cld, 16)
            g.dma_start(out=kf_sb[:, S:2 * S], in_=kf2[1]).then_inc(cld, 16)
            g.memset(va_sb[:], 1.0).then_inc(msd, 1)
            for kv in range(2):
                g.dma_start(
                    out=va_sb.ap()[:, kv * 2176:(kv + 1) * 2176].rearrange(
                        "p (c w) -> p c w", w=68)[:, :, 0:64],
                    in_=vt2[kv].rearrange("p (c w) -> p c w", w=64)
                ).then_inc(cld, 16)
            for u in range(NP):
                par = u % 2
                if u >= 2:
                    g.wait_ge(vsem, V_[f"end{u - 2}"])
                g.dma_start(out=q_sb[par][:], in_=qh[u]).then_inc(pld[par], 16)

        @blk.scalar
        def _(s):
            s.wait_ge(cld, 112)
            gg = 0
            for u in range(NP):
                par = u % 2
                kv = u // 4
                s.wait_ge(pld[par], 16 * (u // 2 + 1))
                if u >= 1:
                    s.wait_ge(tsem, T_[f"qn{u - 1},7"])
                else:
                    s.wait_ge(tsem, T_["kn7"])
                s.activation(out=sq_sb[:, 0:S], in_=q_sb[par][:],
                             func=AF.Square).then_inc(esem, 1)
                s.wait_ge(vsem, V_[f"biasgo{u}"])
                s.activation(out=nrm[:, 3:4], in_=nrm[:, 2:3],
                             func=AF.Sqrt).then_inc(esem, 1)
                s.wait_ge(vsem, V_[f"biasrdy{u}"])
                for h in range(2):
                    for q in range(NQ):
                        for c in range(1, 4 * q + 4, 2):
                            gi = gg + c
                            di = gi // 2
                            s.wait_ge(tsem, T_[f"s{gi}"])
                            if di >= 2:
                                s.wait_ge(tsem, T_[f"p{2 * di - 3}"])
                            s.activation(out=pst[di % 2][:],
                                         in_=ps_sT[di % 2][:], func=AF.Exp,
                                         bias=biasb[:, 0:1],
                                         scale=0.125).then_inc(esem, 1)
                        gg += 4 * q + 4

        @blk.vector
        def _(v):
            nvs = 0

            def sync(ins):
                nonlocal nvs
                nvs += 1
                ins.then_inc(vs, 1)
                v.wait_ge(vs, nvs)

            v.wait_ge(cld, 112)
            v.tensor_tensor(out=sq_sb[:], in0=kf_sb[:], in1=kf_sb[:],
                            op=ALU.mult).then_inc(vsem, 1)
            for kv in range(2):
                for cc in range(8):
                    v.wait_ge(tsem, T_[f"kn{kv * 8 + cc}"])
                    v.reduce_max(nrm[:, 8 + cc:9 + cc], ps_n[:],
                                 axis=AX).then_inc(vsem, 1)
                v.wait_ge(vsem, V_[f"knr{kv * 8 + 7}"])
                sync(v.reduce_max(nrm[:, 4 + kv:5 + kv], nrm[:, 8:16], axis=AX))
            gg = 0
            for u in range(NP):
                par = u % 2
                kv = u // 4
                for cc in range(8):
                    v.wait_ge(tsem, T_[f"qn{u},{cc}"])
                    v.reduce_max(nrm[:, 8 + cc:9 + cc], ps_n[:],
                                 axis=AX).then_inc(vsem, 1)
                v.wait_ge(vsem, V_[f"qnr{u},7"])
                sync(v.reduce_max(nrm[:, 0:1], nrm[:, 8:16], axis=AX))
                sync(v.tensor_tensor(out=nrm[:, 2:3], in0=nrm[:, 0:1],
                                     in1=nrm[:, 4 + kv:5 + kv], op=ALU.mult))
                v.tensor_copy(nrm[:, 18:19],
                              nrm[:, 8:9]).then_inc(vsem, 1)
                v.wait_ge(esem, E_[f"sqrt{u}"])
                sync(v.tensor_scalar(out=nrm[:, 6:7], in0=nrm[:, 3:4],
                                     scalar1=-0.125, scalar2=None,
                                     op0=ALU.mult))
                sync(v.tensor_copy(nrmb[:], nrm[:, 6:7]))
                v.tensor_copy(nrm[:, 18:19],
                              nrm[:, 8:9]).then_inc(vsem, 1)
                v.wait_ge(tsem, T_[f"bb{u}"])
                v.tensor_copy(biasb[:], ps_bb[:]).then_inc(vsem, 1)
                for h in range(2):
                    uh = u * 2 + h
                    if uh >= 2:
                        v.wait_ge(stv, 16 * (uh - 1))
                    for q in range(NQ):
                        for c in range(4 * q, 4 * q + 4):
                            gi = gg + c
                            di = gi // 2
                            hs = (gi % 2) * 512
                            v.wait_ge(esem, E_[f"exp{di}"])
                            delta = 128 * (4 * q - c)
                            v.tensor_tensor(
                                out=pst[di % 2][:, hs:hs + 512],
                                in0=pst[di % 2][:, hs:hs + 512],
                                in1=t8_sb[:, 384 + delta:896 + delta],
                                op=ALU.mult).then_inc(vsem, 1)
                        gg += 4 * q + 4
                        v.wait_ge(tsem, T_[f"p{gg - 1}"])
                        sync(v.reciprocal(denf[:], ps_o[64:65, :]))
                        sync(v.tensor_copy(denr[:], denf[:]))
                        v.tensor_copy(nrm[:, 18:19],
                                      nrm[:, 8:9]).then_inc(vsem, 1)
                        v.wait_ge(tsem, T_[f"bc{u},{h},{q}"])
                        v.tensor_copy(denb[:], ps_bc[:])
                        v.tensor_tensor(
                            out=oh_sb[uh % 2][:, q * 512:(q + 1) * 512],
                            in0=ps_o[0:64, :], in1=denb[:],
                            op=ALU.mult).then_inc(vsem, 1)

        @blk.tensor
        def _(t):
            t.wait_ge(cld, 112)
            t.wait_ge(msd, 1)
            t.wait_ge(vsem, V_["ksq"])
            for cc in range(16):
                if cc >= 1:
                    t.wait_ge(vsem, V_[f"knr{cc - 1}"])
                t.matmul(ps_n[:], onesc_sb[:],
                         sq_sb[:, cc * 512:(cc + 1) * 512],
                         start=True, stop=True).then_inc(tsem, 1)
            gg = 0
            for u in range(NP):
                par = u % 2
                kv = u // 4
                t.wait_ge(esem, E_[f"sq{u}"])
                for cc in range(8):
                    if cc == 0:
                        t.wait_ge(vsem, V_["knr15"] if u == 0 else
                                  V_[f"qnr{u - 1},7"])
                    else:
                        t.wait_ge(vsem, V_[f"qnr{u},{cc - 1}"])
                    t.matmul(ps_n[:], onesc_sb[:],
                             sq_sb[:, cc * 512:(cc + 1) * 512],
                             start=True, stop=True).then_inc(tsem, 1)
                t.wait_ge(vsem, V_[f"biasmm{u}"])
                t.matmul(ps_bb[:], ones1_sb[:], nrmb[:],
                         start=True, stop=True).then_inc(tsem, 1)
                for h in range(2):
                    qrow = slice(h * 64, (h + 1) * 64)
                    for q in range(NQ):
                        nch = 4 * q + 4
                        def _pvs(d0):
                            for c in (d0, d0 + 1):
                                gi = gg + c
                                di = gi // 2
                                hs = (gi % 2) * 512
                                if c == 0 and not (u == 0 and h == 0
                                                   and q == 0):
                                    pu, ph, pq = (u, h, q - 1) if q > 0 else (
                                        (u, h - 1, NQ - 1) if h > 0 else
                                        (u - 1, 1, NQ - 1))
                                    t.wait_ge(vsem,
                                              V_[f"evac{pu},{ph},{pq}"])
                                if c >= 4 * q:
                                    t.wait_ge(vsem, V_[f"mask{gi}"])
                                else:
                                    t.wait_ge(esem, E_[f"exp{di}"])
                                t.matmul(ps_o[:],
                                         va_sb[:, kv * 2176 + c * 68:
                                               kv * 2176 + c * 68 + 65],
                                         pst[di % 2][:, hs:hs + 512],
                                         start=(c == 0), stop=(c == nch - 1)
                                         ).then_inc(tsem, 1)

                        prev = None
                        for c0 in range(0, nch, 2):
                            for c in (c0, c0 + 1):
                                gi = gg + c
                                di = gi // 2
                                hs = (gi % 2) * 512
                                if gi % 2 == 0 and di >= 2:
                                    t.wait_ge(esem, E_[f"exp{di - 2}"])
                                t.matmul(ps_sT[di % 2][:, hs:hs + 512],
                                         kf_sb[qrow, kv * S + c * 128:
                                               kv * S + (c + 1) * 128],
                                         q_sb[par][qrow,
                                                   q * 512:(q + 1) * 512],
                                         start=True, stop=True
                                         ).then_inc(tsem, 1)
                            if prev is not None:
                                _pvs(prev)
                            prev = c0
                        _pvs(prev)
                        gg += nch
                        t.wait_ge(vsem, V_[f"den{u},{h},{q}"])
                        t.matmul(ps_bc[:], ones1_sb[0:1, 0:64], denr[:],
                                 start=True, stop=True
                                 ).then_inc(tsem, 1)

        @blk.sync
        def _(sy):
            for u in range(NP):
                for h in range(2):
                    uh = u * 2 + h
                    sy.wait_ge(vsem, V_[f"evac{u},{h},{NQ - 1}"])
                    sy.dma_start(out=oh[uh],
                                 in_=oh_sb[uh % 2][:]).then_inc(stv, 16)
            sy.wait_ge(stv, 16 * NP * 2)
    return nc


def run_l1attn(in_percore):
    nc = _cached("l1attn", build_l1attn)
    res = _run(nc, in_percore)
    return [r["oh"] for r in res]


def build_quant(nc=None, pfx=""):
    if nc is None:
        nc = bass.Bass("TRN2", target_bir_lowering=False,
                       detect_race_conditions=False)
    w1 = nc.dram_tensor(pfx + "w1", [2048, 2048], F32, kind="ExternalInput").ap()
    w2 = nc.dram_tensor(pfx + "w2", [1024, 2048], F32, kind="ExternalInput").ap()
    q1 = nc.dram_tensor(pfx + "q1", [2048, 2048], BF16, kind="ExternalOutput").ap()
    q2 = nc.dram_tensor(pfx + "q2", [1024, 2048], BF16, kind="ExternalOutput").ap()
    sel = np.zeros((128, 2), BFD)
    sel[0:64, 0] = 1.0
    sel[64:128, 1] = 1.0
    sel_d = nc.inline_tensor(sel, pfx + "sel_d").ap()
    selt_d = nc.inline_tensor(np.ascontiguousarray(sel.T), pfx + "selt_d").ap()
    tiles = [("w1", r, c) for r in range(16) for c in range(4)] + \
            [("w2", r, c) for r in range(8) for c in range(4)]
    NTL = len(tiles)

    with contextlib.ExitStack() as st:
        E = st.enter_context
        w_sb = [E(nc.sbuf_tensor(f"{pfx}w_sb{p}", [128, 512], BF16)) for p in range(2)]
        wa = E(nc.sbuf_tensor(pfx + "wa", [128, 512], BF16))
        sel_sb = E(nc.sbuf_tensor(pfx + "sel_sb", [128, 2], BF16))
        selt_sb = E(nc.sbuf_tensor(pfx + "selt_sb", [2, 128], BF16))
        sc = E(nc.sbuf_tensor(pfx + "sc", [2, 512 * 4], F32))
        sc_bf = E(nc.sbuf_tensor(pfx + "sc_bf", [2, 512], BF16))
        thr = E(nc.sbuf_tensor(pfx + "thr", [128, 512], F32))
        mk = E(nc.sbuf_tensor(pfx + "mk", [128, 512], BF16))
        sg = E(nc.sbuf_tensor(pfx + "sg", [128, 512], BF16))
        tq = E(nc.sbuf_tensor(pfx + "tq", [128, 512], BF16))
        wq = [E(nc.sbuf_tensor(f"{pfx}wq{p}", [128, 512], BF16)) for p in range(2)]
        ps_s = E(nc.psum_tensor(pfx + "ps_s", [2, 512], F32))
        ps_b2 = E(nc.psum_tensor(pfx + "ps_b2", [128, 512], F32))
        cld = E(nc.semaphore(pfx + "cld"))
        pld = [E(nc.semaphore(f"{pfx}pld{p}")) for p in range(2)]
        tsem = E(nc.semaphore(pfx + "tsem"))
        vsem = E(nc.semaphore(pfx + "vsem"))
        esem = E(nc.semaphore(pfx + "esem"))
        vs = E(nc.semaphore(pfx + "vs"))
        stv = [E(nc.semaphore(f"{pfx}stv{p}")) for p in range(2)]
        blk = E(nc.Block())

        def tsl(idx):
            name, r, c = tiles[idx]
            ap = w1 if name == "w1" else w2
            return ap[r * 128:(r + 1) * 128, c * 512:(c + 1) * 512]

        def osl(idx):
            name, r, c = tiles[idx]
            ap = q1 if name == "w1" else q2
            return ap[r * 128:(r + 1) * 128, c * 512:(c + 1) * 512]

        @blk.gpsimd
        def _(g):
            g.dma_start(out=sel_sb[:], in_=sel_d).then_inc(cld, 16)
            g.dma_start(out=selt_sb[:], in_=selt_d).then_inc(cld, 16)
            for i in range(NTL):
                if i >= 2:
                    g.wait_ge(stv[i % 2], 16 * (i // 2))
                g.dma_start(out=w_sb[i % 2][:], in_=tsl(i)).then_inc(pld[i % 2], 16)

        @blk.scalar
        def _(s):
            for i in range(NTL):
                s.wait_ge(pld[i % 2], 16 * (i // 2 + 1))
                if i >= 1:
                    s.wait_ge(vsem, 2 * i)
                s.activation(out=wa[:], in_=w_sb[i % 2][:],
                             func=AF.Abs).then_inc(esem, 1)
                s.activation(out=sg[:], in_=w_sb[i % 2][:],
                             func=AF.Sign).then_inc(esem, 1)

        @blk.tensor
        def _(t):
            t.wait_ge(cld, 32)
            for i in range(NTL):
                t.wait_ge(esem, 2 * i + 1)
                if i >= 1:
                    t.wait_ge(vsem, 2 * i)
                t.matmul(ps_s[:], sel_sb[:], wa[:],
                         start=True, stop=True).then_inc(tsem, 1)
                t.wait_ge(vsem, 2 * i + 1)
                t.matmul(ps_b2[:], selt_sb[:], sc_bf[:],
                         start=True, stop=True).then_inc(tsem, 1)

        @blk.vector
        def _(v):
            nvs = 0

            def sync(ins):
                nonlocal nvs
                nvs += 1
                ins.then_inc(vs, 1)
                v.wait_ge(vs, nvs)

            for i in range(NTL):
                v.wait_ge(tsem, 2 * i + 1)
                sync(v.tensor_scalar(out=sc[:, 0:512], in0=ps_s[:],
                                     scalar1=1.0 / 64, scalar2=None,
                                     op0=ALU.mult))
                v.tensor_scalar(out=sc_bf[:], in0=sc[:, 0:512],
                                scalar1=1e-8, scalar2=None,
                                op0=ALU.max).then_inc(vsem, 1)
                v.wait_ge(tsem, 2 * i + 2)
                v.tensor_scalar(out=thr[:], in0=ps_b2[:],
                                scalar1=0.5009765625, scalar2=None,
                                op0=ALU.mult)
                v.wait_ge(esem, 2 * i + 2)
                v.tensor_tensor(out=mk[:], in0=wa[:], in1=thr[:],
                                op=ALU.is_gt)
                v.tensor_tensor(out=tq[:], in0=mk[:], in1=sg[:], op=ALU.mult)
                if i >= 2:
                    v.wait_ge(stv[i % 2], 16 * (i // 2))
                v.tensor_tensor(out=wq[i % 2][:], in0=tq[:], in1=ps_b2[:],
                                op=ALU.mult).then_inc(vsem, 1)
        @blk.sync
        def _(sy):
            for i in range(NTL):
                sy.wait_ge(vsem, 2 * i + 2)
                sy.dma_start(out=osl(i), in_=wq[i % 2][:]).then_inc(stv[i % 2], 16)
            for p in range(2):
                sy.wait_ge(stv[p], 16 * ((NTL - 1 - p) // 2 + 1))
    return nc


def run_quant(w1_percore, w2_percore):
    nc = _cached("quant", lambda: build_quant())
    maps = [{"w1": w1_percore[c], "w2": w2_percore[c]} for c in range(NCORE)]
    res = _run(nc, maps)
    return [r["q1"] for r in res], [r["q2"] for r in res]


def build_ffn1(nc=None, pfx="", p2_src=None):
    NK = D // 128
    NG = 64
    TC = T // 512
    OG = D // 128
    EPSF = float(np.finfo(np.float32).eps)
    if nc is None:
        nc = bass.Bass("TRN2", target_bir_lowering=False,
                       detect_race_conditions=False)
    xnf = nc.dram_tensor(pfx + "xnf", [D, T], BF16, kind="ExternalInput").ap()
    wgu = nc.dram_tensor(pfx + "wgu", [D, 16384], BF16,
                         kind="ExternalInput").ap()
    wpr = nc.dram_tensor(pfx + "wpr", [8192, D], BF16,
                         kind="ExternalInput").ap()
    if p2_src is None:
        p2 = nc.dram_tensor(pfx + "p2", [D, T], BF16,
                            kind="ExternalInput").ap()
    else:
        p2 = p2_src
    x2 = nc.dram_tensor(pfx + "x2", [D, T], F32, kind="ExternalOutput").ap()
    ones1_d = nc.inline_tensor(np.ones((1, 128), BFD), pfx + "ones1_d").ap()
    onesc_d = nc.inline_tensor(np.ones((128, 1), BFD), pfx + "onesc_d").ap()

    def sch():
        Td, Vd, Ed = {}, {}, {}
        ct = cv = ce = 0
        for tc in range(TC):
            for gg in range(NG):
                gi = tc * NG + gg
                if gg >= 2:
                    ct += 1; Td[f"r{gi - 2}"] = ct
                ct += 1; Td[f"g{gi}"] = ct
                ce += 1; Ed[f"s{gi}"] = ce
                cv += 1; Vd[f"h{gi}"] = cv
            ct += 1; Td[f"r{tc * NG + NG - 2}"] = ct
            ct += 1; Td[f"r{tc * NG + NG - 1}"] = ct
            cv += 1; Vd[f"vsq{tc}"] = cv
            ce += 1; Ed[f"q{tc}"] = ce
            cv += 1; Vd[f"ib{tc}"] = cv
            ct += 1; Td[f"b{tc}"] = ct
            cv += 1; Vd[f"hn{tc}"] = cv
            for og in range(OG):
                oi = tc * OG + og
                ct += 1; Td[f"o{oi}"] = ct
                cv += 1; Vd[f"oev{oi}"] = cv
        return Td, Vd, Ed

    Td, Vd, Ed = sch()

    with contextlib.ExitStack() as st:
        E = st.enter_context
        xn_sb = E(nc.sbuf_tensor(pfx + "xn_sb", [128, NK * T], BF16))
        wg_sb = [[E(nc.sbuf_tensor(f"{pfx}wg_sb{p}_{j}", [128, NK * 128], BF16))
                  for j in range(2)] for p in range(2)]
        wp_sb = [E(nc.sbuf_tensor(f"{pfx}wp_sb{p}", [128, 64 * 128], BF16))
                 for p in range(2)]
        hbuf = E(nc.sbuf_tensor(pfx + "hbuf", [128, NG * 512], BF16))
        h2b = [E(nc.sbuf_tensor(f"{pfx}h2b{p}", [128, 512], BF16)) for p in range(2)]
        sil = [E(nc.sbuf_tensor(f"{pfx}sil{p}", [128, 512], F32)) for p in range(2)]
        rr0 = E(nc.sbuf_tensor(pfx + "rr0", [1, 512], F32))
        rr1 = E(nc.sbuf_tensor(pfx + "rr1", [1, 512], F32))
        rr2 = E(nc.sbuf_tensor(pfx + "rr2", [1, 512], F32))
        ib_bf = E(nc.sbuf_tensor(pfx + "ib_bf", [1, 512], BF16))
        ib_sb = E(nc.sbuf_tensor(pfx + "ib_sb", [128, 512], BF16))
        p2_sb = [E(nc.sbuf_tensor(f"{pfx}p2_sb{p}", [128, 512], BF16)) for p in range(2)]
        ob = [E(nc.sbuf_tensor(f"{pfx}ob{p}", [128, 512], F32)) for p in range(2)]
        ones1_sb = E(nc.sbuf_tensor(pfx + "ones1_sb", [1, 128], BF16))
        onesc_sb = E(nc.sbuf_tensor(pfx + "onesc_sb", [128, 1], BF16))
        ps_g = [E(nc.psum_tensor(f"{pfx}ps_g{p}", [128, 512], F32)) for p in range(2)]
        ps_u = [E(nc.psum_tensor(f"{pfx}ps_u{p}", [128, 512], F32)) for p in range(2)]
        ps_o = [E(nc.psum_tensor(f"{pfx}ps_o{p}", [128, 512], F32)) for p in range(2)]
        ps_r = E(nc.psum_tensor(pfx + "ps_r", [1, 512], F32))
        ps_ib = E(nc.psum_tensor(pfx + "ps_ib", [128, 512], F32))
        cld = E(nc.semaphore(pfx + "cld"))
        wld = [E(nc.semaphore(f"{pfx}wld{p}")) for p in range(2)]
        wpld = [E(nc.semaphore(f"{pfx}wpld{p}")) for p in range(2)]
        rld = [E(nc.semaphore(f"{pfx}rld{p}")) for p in range(2)]
        tsem = E(nc.semaphore(pfx + "tsem"))
        vsem = E(nc.semaphore(pfx + "vsem"))
        esem = E(nc.semaphore(pfx + "esem"))
        vs = E(nc.semaphore(pfx + "vs"))
        stv = [E(nc.semaphore(f"{pfx}stv{p}")) for p in range(2)]
        blk = E(nc.Block())

        @blk.gpsimd
        def _(g):
            g.dma_start(out=ones1_sb[:], in_=ones1_d).then_inc(cld, 16)
            g.dma_start(out=onesc_sb[:], in_=onesc_d).then_inc(cld, 16)
            g.dma_start(out=xn_sb.ap().rearrange("p (a t) -> p a t", t=T),
                        in_=xnf.rearrange("(a p) t -> p a t", p=128)
                        ).then_inc(cld, 16)
            for tc in range(TC):
                for gg in range(NG):
                    gi = tc * NG + gg
                    if gi >= 2:
                        g.wait_ge(tsem, Td[f"g{gi - 2}"])
                    g.dma_start(
                        out=wg_sb[gi % 2][0].ap().rearrange(
                            "p (a f) -> p a f", f=128),
                        in_=wgu[:, gg * 128:(gg + 1) * 128].rearrange(
                            "(a p) f -> p a f", p=128)
                    ).then_inc(wld[gi % 2], 16)
                    g.dma_start(
                        out=wg_sb[gi % 2][1].ap().rearrange(
                            "p (a f) -> p a f", f=128),
                        in_=wgu[:, 8192 + gg * 128:8192 + (gg + 1) * 128
                                ].rearrange("(a p) f -> p a f", p=128)
                    ).then_inc(wld[gi % 2], 16)
                for og in range(OG):
                    oi = tc * OG + og
                    if oi >= 2:
                        g.wait_ge(tsem, Td[f"o{oi - 2}"])
                    g.dma_start(
                        out=wp_sb[oi % 2].ap().rearrange(
                            "p (a f) -> p a f", f=128),
                        in_=wpr[:, og * 128:(og + 1) * 128].rearrange(
                            "(a p) f -> p a f", p=128)
                    ).then_inc(wpld[oi % 2], 16)

        @blk.sync
        def _(sy):
            for tc in range(TC):
                for og in range(OG):
                    oi = tc * OG + og
                    if oi >= 2:
                        sy.wait_ge(vsem, Vd[f"oev{oi - 1}"])
                    sy.dma_start(out=p2_sb[oi % 2][:],
                                 in_=p2[og * 128:(og + 1) * 128,
                                        tc * 512:(tc + 1) * 512]
                                 ).then_inc(rld[oi % 2], 16)
                    sy.wait_ge(vsem, Vd[f"oev{oi}"])
                    sy.dma_start(out=x2[og * 128:(og + 1) * 128,
                                        tc * 512:(tc + 1) * 512],
                                 in_=ob[oi % 2][:]).then_inc(stv[oi % 2], 16)
            for p in range(2):
                sy.wait_ge(stv[p], 16 * ((TC * OG - 1 - p) // 2 + 1))

        @blk.scalar
        def _(s):
            for tc in range(TC):
                for gg in range(NG):
                    gi = tc * NG + gg
                    s.wait_ge(tsem, Td[f"g{gi}"])
                    if gi >= 2:
                        s.wait_ge(vsem, Vd[f"h{gi - 2}"])
                    s.activation(out=sil[gi % 2][:], in_=ps_g[gi % 2][:],
                                 func=AF.Silu).then_inc(esem, 1)
                s.wait_ge(vsem, Vd[f"vsq{tc}"])
                s.activation(out=rr1[:], in_=rr0[:],
                             func=AF.Sqrt).then_inc(esem, 1)

        @blk.vector
        def _(v):
            nvs = 0

            def sync(ins):
                nonlocal nvs
                nvs += 1
                ins.then_inc(vs, 1)
                v.wait_ge(vs, nvs)

            for tc in range(TC):
                if tc >= 1:
                    v.wait_ge(tsem, Td[f"o{tc * OG - 1}"])
                for gg in range(NG):
                    gi = tc * NG + gg
                    v.wait_ge(esem, Ed[f"s{gi}"])
                    if gi >= 2:
                        v.wait_ge(tsem, Td[f"r{gi - 2}"])
                    v.tensor_tensor(out=hbuf[:, gg * 512:(gg + 1) * 512],
                                    in0=sil[gi % 2][:], in1=ps_u[gi % 2][:],
                                    op=ALU.mult)
                    v.tensor_tensor(out=h2b[gi % 2][:],
                                    in0=hbuf[:, gg * 512:(gg + 1) * 512],
                                    in1=hbuf[:, gg * 512:(gg + 1) * 512],
                                    op=ALU.mult).then_inc(vsem, 1)
                v.wait_ge(tsem, Td[f"r{tc * NG + NG - 1}"])
                sync(v.tensor_scalar(out=rr0[:], in0=ps_r[:],
                                     scalar1=1.0 / 8192, scalar2=None,
                                     op0=ALU.mult))
                v.tensor_scalar(out=rr0[:], in0=rr0[:], scalar1=EPSF,
                                scalar2=None, op0=ALU.add).then_inc(vsem, 1)
                v.wait_ge(esem, Ed[f"q{tc}"])
                sync(v.reciprocal(rr2[:], rr1[:]))
                v.tensor_copy(ib_bf[:], rr2[:]).then_inc(vsem, 1)
                v.wait_ge(tsem, Td[f"b{tc}"])
                sync(v.tensor_copy(ib_sb[:], ps_ib[:]))
                for gg in range(NG):
                    ins = v.tensor_tensor(
                        out=hbuf[:, gg * 512:(gg + 1) * 512],
                        in0=hbuf[:, gg * 512:(gg + 1) * 512],
                        in1=ib_sb[:], op=ALU.mult)
                ins.then_inc(vsem, 1)
                for og in range(OG):
                    oi = tc * OG + og
                    v.wait_ge(tsem, Td[f"o{oi}"])
                    v.wait_ge(rld[oi % 2], 16 * (oi // 2 + 1))
                    if oi >= 2:
                        v.wait_ge(stv[oi % 2], 16 * (oi // 2))
                    v.tensor_tensor(out=ob[oi % 2][:], in0=ps_o[oi % 2][:],
                                    in1=p2_sb[oi % 2][:],
                                    op=ALU.add).then_inc(vsem, 1)

        @blk.tensor
        def _(t):
            t.wait_ge(cld, 48)
            for tc in range(TC):
                for gg in range(NG):
                    gi = tc * NG + gg
                    if gg >= 2:
                        t.wait_ge(vsem, Vd[f"h{gi - 2}"])
                        t.matmul(ps_r[:], onesc_sb[:], h2b[gi % 2][:],
                                 start=(gg - 2 == 0), stop=False
                                 ).then_inc(tsem, 1)
                    t.wait_ge(wld[gi % 2], 32 * (gi // 2 + 1))
                    if gi >= 2:
                        t.wait_ge(vsem, Vd[f"h{gi - 2}"])
                    for k in range(NK):
                        t.matmul(ps_g[gi % 2][:],
                                 wg_sb[gi % 2][0][:, k * 128:(k + 1) * 128],
                                 xn_sb[:, k * T + tc * 512:
                                       k * T + (tc + 1) * 512],
                                 start=(k == 0), stop=(k == NK - 1))
                    for k in range(NK):
                        ins = t.matmul(ps_u[gi % 2][:],
                                       wg_sb[gi % 2][1][:, k * 128:(k + 1) * 128],
                                       xn_sb[:, k * T + tc * 512:
                                             k * T + (tc + 1) * 512],
                                       start=(k == 0), stop=(k == NK - 1))
                    ins.then_inc(tsem, 1)
                for gg in (NG - 2, NG - 1):
                    gi = tc * NG + gg
                    t.wait_ge(vsem, Vd[f"h{gi}"])
                    t.matmul(ps_r[:], onesc_sb[:], h2b[gi % 2][:],
                             start=False, stop=(gg == NG - 1)
                             ).then_inc(tsem, 1)
                t.wait_ge(vsem, Vd[f"ib{tc}"])
                t.matmul(ps_ib[:], ones1_sb[:], ib_bf[:],
                         start=True, stop=True).then_inc(tsem, 1)
                for og in range(OG):
                    oi = tc * OG + og
                    t.wait_ge(wpld[oi % 2], 16 * (oi // 2 + 1))
                    t.wait_ge(vsem, Vd[f"hn{tc}"])
                    if oi >= 2:
                        t.wait_ge(vsem, Vd[f"oev{oi - 2}"])
                    for g2 in range(NG):
                        ins = t.matmul(ps_o[oi % 2][:],
                                       wp_sb[oi % 2][:, g2 * 128:(g2 + 1) * 128],
                                       hbuf[:, g2 * 512:(g2 + 1) * 512],
                                       start=(g2 == 0), stop=(g2 == NG - 1))
                    ins.then_inc(tsem, 1)
    return nc



def run_ffn1(xnf_percore, wgu, wpr, p2_percore):
    nc = _cached("ffn1", lambda: build_ffn1())
    maps = [{"xnf": xnf_percore[c], "wgu": wgu, "wpr": wpr,
             "p2": p2_percore[c]} for c in range(NCORE)]
    res = _run(nc, maps)
    return [r["x2"] for r in res]



def build_g2():
    nc = bass.Bass("TRN2", target_bir_lowering=False,
                   detect_race_conditions=False)
    nc, _ = build_gemm("qkv0m", (2048,), 6144, (BF16,), (F32,), None, BF16,
                       nc=nc, pfx="a_")
    build_l0ffna(nc=nc, pfx="b_")
    return nc


def build_g6():
    nc = bass.Bass("TRN2", target_bir_lowering=False,
                   detect_race_conditions=False)
    nc, qkvout = build_gemm("qkv1m", (2048,), 2560, (BF16,), (F32,), None,
                            BF16, nc=nc, pfx="a_")
    build_rope(nc=nc, pfx="r_", qkv_src=qkvout[0:2304, :])
    build_quant(nc=nc, pfx="q_")
    return nc


def build_g8():
    nc = bass.Bass("TRN2", target_bir_lowering=False,
                   detect_race_conditions=False)
    nc, p2out = build_gemm("oproj1m", (2048,), 2048, (BF16,), (F32,), BF16,
                           BF16, nc=nc, pfx="a_")
    build_ffn1(nc=nc, pfx="f_", p2_src=p2out)
    return nc



def shard_tokens(x):
    out = []
    for c in range(NCORE):
        b, h = c // 2, c % 2
        out.append(np.ascontiguousarray(x[b, h * T:(h + 1) * T]))
    return out


def unshard_tokens(parts, d2):
    x = np.empty((4, 4096, d2), parts[0].dtype)
    for c in range(NCORE):
        b, h = c // 2, c % 2
        x[b, h * T:(h + 1) * T] = parts[c]
    return x


def tr(a):
    return np.ascontiguousarray(np.swapaxes(a, -1, -2))


def _rope_tables():
    inv = 1.0 / 10000.0 ** (np.arange(0, 64, 2, dtype=np.float64) / 64)
    sign = np.where((np.arange(128) % 64) < 32, -1.0, 1.0)
    cos_half, sin_half = [], []
    for hh in range(2):
        pos = np.arange(hh * T, (hh + 1) * T, dtype=np.float64)
        ang = pos[None, :] * inv[np.arange(128) % 32][:, None]
        cos_half.append(np.cos(ang).astype(np.float32))
        sin_half.append((sign[:, None] * np.sin(ang)).astype(np.float32))
    return cos_half, sin_half


def _tiled_tok(x, w):
    return np.ascontiguousarray(
        x.reshape(32, 128, w).transpose(1, 0, 2).reshape(128, 32 * w))


def _prep_l0attn(qkv0, memories, mnorm, gate):
    mnflat = np.ascontiguousarray(mnorm.reshape(1, 8192), np.float32)
    maps = []
    for c in range(NCORE):
        b = c // 2
        e, o = 2 * b, 2 * b + 1
        heads = range((c % 2) * 8, (c % 2) * 8 + 8)
        qf = np.empty((8, 128, 4096), BFD)
        kf = np.empty((8, 128, 4096), BFD)
        kt = np.empty((8, 128, 4096), BFD)
        vt = np.empty((8, 128, 4096), BFD)
        for j, h in enumerate(heads):
            rows = slice(h * 128, (h + 1) * 128)
            qf[j, :, 0:T] = qkv0[e][rows]
            qf[j, :, T:2 * T] = qkv0[o][rows]
            kf[j, :, 0:T] = qkv0[e][2048 + h * 128:2048 + (h + 1) * 128]
            kf[j, :, T:2 * T] = qkv0[o][2048 + h * 128:2048 + (h + 1) * 128]
            kt[j] = _tiled_tok(np.ascontiguousarray(kf[j].T), 128)
            vrows = slice(4096 + h * 128, 4096 + (h + 1) * 128)
            vfull = np.concatenate([qkv0[e][vrows], qkv0[o][vrows]], axis=1)
            vt[j] = _tiled_tok(np.ascontiguousarray(vfull.T), 128)
        memp = np.zeros((8, 128, 520), np.float32)
        mnsp = np.zeros((8, 128, 4), np.float32)
        gp = np.zeros((8, 1, 1), np.float32)
        for j, h in enumerate(heads):
            for m in range(4):
                memp[j, :, m * 130:m * 130 + 128] = memories[m, h]
                memp[j, :, m * 130 + 128] = mnorm[m, h]
                mnsp[j, :, m] = mnorm[m, h]
            gp[j, 0, 0] = gate[h]
        maps.append(dict(qf=qf, kf=kf, kt=kt, vt=vt, mem=memp, mns=mnsp,
                         gatev=gp, mnflat=mnflat))
    return maps


def _prep_l1attn(roped, v1):
    maps = []
    for c in range(NCORE):
        b, hh = c // 2, c % 2
        e, o = 2 * b, 2 * b + 1
        qh = np.empty((8, 128, 4096), BFD)
        for u in range(8):
            for j in range(2):
                h = hh * 16 + 2 * u + j
                rows = slice(h * 64, (h + 1) * 64)
                qh[u, j * 64:(j + 1) * 64, 0:T] = roped[e][rows]
                qh[u, j * 64:(j + 1) * 64, T:2 * T] = roped[o][rows]
        kf2 = np.empty((2, 128, 4096), BFD)
        vt2 = np.empty((2, 128, 2048), BFD)
        for j in range(2):
            kvh = hh * 2 + j
            rows = slice(2048 + kvh * 64, 2048 + (kvh + 1) * 64)
            kf2[j, 0:64, 0:T] = roped[e][rows]
            kf2[j, 0:64, T:2 * T] = roped[o][rows]
            kf2[j, 64:128] = kf2[j, 0:64]
            vrows = slice(kvh * 64, (kvh + 1) * 64)
            vfull = np.concatenate([v1[e][vrows], v1[o][vrows]], axis=1)
            vt2[j] = _tiled_tok(np.ascontiguousarray(vfull.T), 64)
        maps.append(dict(qh=qh, kf2=kf2, vt2=vt2))
    return maps


def kernel(L0_attn_hidden_states, L0_attn_memories, L0_attn_memory_norms,
           L0_attn_w_q, L0_attn_w_k, L0_attn_w_v, L0_attn_w_o, L0_attn_gate,
           L0_ffn_router_w, L0_ffn_down_w, L0_ffn_up_w,
           L1_attn_w_qkv, L1_attn_w_o, L1_ffn_w_gate_up, L1_ffn_w_proj):
    x = np.ascontiguousarray(np.asarray(L0_attn_hidden_states, np.float32))
    xs = shard_tokens(x)

    xn, _ = run_ln0(xs)
    xn_feat = [tr(a) for a in xn]
    wqkv0 = np.concatenate(
        [np.ascontiguousarray(np.asarray(w, np.float32).T)
         for w in (L0_attn_w_q, L0_attn_w_k, L0_attn_w_v)], axis=1)
    rwt = np.ascontiguousarray(np.asarray(L0_ffn_router_w, np.float32).T)
    dwt = np.ascontiguousarray(
        np.transpose(np.asarray(L0_ffn_down_w, np.float32),
                     (2, 0, 1)).reshape(2048, 128))
    res2 = _run(_cached("g2", build_g2),
                [{"a_w0": wqkv0, "a_x0": xn_feat[c], "b_xnf": xn_feat[c],
                  "b_rwt": rwt, "b_dwt": dwt} for c in range(NCORE)])
    qkv0 = [r["a_out"] for r in res2]
    asct = [r["b_aout"] for r in res2]

    maps0 = _prep_l0attn(qkv0, np.asarray(L0_attn_memories, np.float32),
                         np.asarray(L0_attn_memory_norms, np.float32),
                         np.asarray(L0_attn_gate, np.float32))
    outm = run_l0attn(maps0)

    mixf = []
    for c in range(NCORE):
        b, hh = c // 2, c % 2
        mf = np.empty((2048, T), BFD)
        for src_core, hbase in ((2 * b, 0), (2 * b + 1, 8)):
            om = np.asarray(outm[src_core]).reshape(8, 128, 32, 128)
            for j in range(8):
                h = hbase + j
                blk2 = om[j].transpose(1, 0, 2).reshape(4096, 128)
                mf[h * 128:(h + 1) * 128] = blk2[hh * T:(hh + 1) * T].T
        mixf.append(np.ascontiguousarray(mf))
    wot = np.ascontiguousarray(np.asarray(L0_attn_w_o, np.float32).T)
    upt = np.ascontiguousarray(
        np.transpose(np.asarray(L0_ffn_up_w, np.float32),
                     (0, 2, 1)).reshape(128, 2048))
    xres = [np.ascontiguousarray(x[c // 2, (c % 2) * T:(c % 2 + 1) * T].T)
            for c in range(NCORE)]
    x1 = run_gemm("oproj0", [wot, upt], [mixf, asct], xres, BFD)

    x1tok = [tr(a) for a in x1]
    xn1 = run_ln1(x1tok)
    xn1f = [tr(a) for a in xn1]
    wqkvt = np.ascontiguousarray(np.asarray(L1_attn_w_qkv, np.float32).T)
    cos_half, sin_half = _rope_tables()
    wguT = np.ascontiguousarray(np.asarray(L1_ffn_w_gate_up, np.float32).T)
    wprT = np.ascontiguousarray(np.asarray(L1_ffn_w_proj, np.float32).T)
    w1s = [np.ascontiguousarray(wguT[:, c * 2048:(c + 1) * 2048])
           for c in range(NCORE)]
    w2s = [np.ascontiguousarray(wprT[c * 1024:(c + 1) * 1024])
           for c in range(NCORE)]
    res6 = _run(_cached("g6", build_g6),
                [{"a_w0": wqkvt, "a_x0": xn1f[c],
                  "r_cosr": cos_half[c % 2], "r_sins": sin_half[c % 2],
                  "q_w1": w1s[c], "q_w2": w2s[c]} for c in range(NCORE)])
    roped = [r["r_out"] for r in res6]
    v1 = [np.ascontiguousarray(r["a_out"][2304:2560]) for r in res6]
    q1 = [r["q_q1"] for r in res6]
    q2 = [r["q_q2"] for r in res6]
    wgu_q = np.ascontiguousarray(np.concatenate(
        [np.asarray(q) for q in q1], axis=1))
    wpr_q = np.ascontiguousarray(np.concatenate(
        [np.asarray(q) for q in q2], axis=0))

    maps1 = _prep_l1attn(roped, v1)
    oh = run_l1attn(maps1)

    ahf = []
    for c in range(NCORE):
        b, hh = c // 2, c % 2
        af = np.empty((2048, T), BFD)
        tok = slice(hh * T, (hh + 1) * T)
        for src_core, hbase in ((2 * b, 0), (2 * b + 1, 16)):
            blk2 = np.asarray(oh[src_core]).reshape(16, 64, 4096)
            for uh in range(16):
                h = hbase + uh
                af[h * 64:(h + 1) * 64] = blk2[uh][:, tok]
        ahf.append(np.ascontiguousarray(af))
    wot1 = np.ascontiguousarray(np.asarray(L1_attn_w_o, np.float32).T)
    res8 = _run(_cached("g8", build_g8),
                [{"a_w0": wot1, "a_x0": ahf[c], "a_res": x1[c],
                  "f_xnf": xn1f[c], "f_wgu": wgu_q, "f_wpr": wpr_q}
                 for c in range(NCORE)])
    x2 = [r["f_x2"] for r in res8]
    out = unshard_tokens([np.ascontiguousarray(tr(np.asarray(a, np.float32)))
                          for a in x2], 2048)
    return out
